# version 01
# baseline (speedup 1.0000x reference)
import numpy as np
import ml_dtypes
import concourse.bass as bass
import concourse.mybir as mybir
from concourse.bass_utils import run_bass_kernel_spmd

F32 = mybir.dt.float32
BF16 = mybir.dt.bfloat16
I32 = mybir.dt.int32
AF = mybir.ActivationFunctionType
ALU = mybir.AluOpType
AX = mybir.AxisListType
BF = ml_dtypes.bfloat16

T = 4096
D = 1024
NH, DK, DV = 12, 64, 128
KD, VD = NH * DK, NH * DV
CS = 64
HID = 4096
H1, DH1 = 16, 64
EPS_LN = 1e-5



class Em:
    QUEUES = ("pe", "act", "dve", "pool", "sp")
    NLANES = 8

    def __init__(self, nc, sems):
        self.nc = nc
        self.eng = dict(pe=nc.tensor, act=nc.scalar, dve=nc.vector,
                        pool=nc.gpsimd, sp=nc.sync)
        self.sems = sems
        self.cnt = {k: 0 for k in sems}
        self.lane = {"sp": 0, "pool": 0}
        self.acc = {}
        self.waited = {q: {} for q in self.QUEUES}

    @staticmethod
    def _rng(ap):
        lo = hi = ap.offset
        for step, count in ap.ap:
            d = step * (count - 1)
            if d >= 0:
                hi += d
            else:
                lo += d
        return ap.tensor.name, lo, hi

    def _deps_for(self, ap, write):
        name, lo, hi = self._rng(ap)
        deps = []
        for rec in self.acc.get(name, ()):
            if rec[1] < lo or rec[0] > hi:
                continue
            if rec[4] or write:
                deps.append((rec[2], rec[3]))
        return deps

    def _record(self, ap, q, cnt, write):
        name, lo, hi = self._rng(ap)
        lst = self.acc.setdefault(name, [])
        if write:
            lst[:] = [r for r in lst if not (lo <= r[0] and r[1] <= hi)]
        lst.append([lo, hi, q, cnt, write])
        if len(lst) > 400:
            del lst[: len(lst) - 250]

    def _wait(self, q, deps):
        need = {}
        for dq, c in deps:
            if dq == q == "pe":
                continue
            if c is None:
                raise RuntimeError("dependency on un-inc'd (lazy) op")
            if need.get(dq, 0) < c:
                need[dq] = c
        for dq, c in need.items():
            if self.waited[q].get(dq, 0) >= c:
                continue
            self.eng[q].wait_ge(self.sems[dq], c)
            self.waited[q][dq] = c

    def op(self, q, fname, outs, ins, inc=1, _call=None, **kw):
        deps = []
        for ap in ins:
            if isinstance(ap, (int, float)) or ap is None:
                continue
            deps += self._deps_for(ap, False)
        for ap in outs:
            if ap is None:
                continue
            deps += self._deps_for(ap, True)
        self._wait(q, deps)
        inst = (_call or getattr(self.eng[q], fname))(**kw)
        c = None
        if inc:
            inst.then_inc(self.sems[q], inc)
            self.cnt[q] += inc
            c = self.cnt[q]
        else:
            c = self.cnt[q] + 1
        for ap in ins:
            if isinstance(ap, (int, float)) or ap is None:
                continue
            self._record(ap, q, c, False)
        for ap in outs:
            if ap is None:
                continue
            self._record(ap, q, c, True)
        return inst

    def _dma_issue(self, q, outs, ins, transpose=False, **kw):
        ln = self.lane[q]
        self.lane[q] = (ln + 1) % self.NLANES
        key = f"{q}{ln}"
        deps = []
        for ap in ins:
            deps += self._deps_for(ap, False)
        for ap in outs:
            deps += self._deps_for(ap, True)
        deps.append((key, self.cnt[key]))
        if not hasattr(self, "_cat"):
            self._cat = {"tr": {}, "ss": {}}
        is_ss = (not transpose and ins[0].tensor.space.name == "SBUF"
                 and outs[0].tensor.space.name == "SBUF")
        if transpose:
            deps += list(self._cat["ss"].items())
        elif is_ss:
            deps += list(self._cat["tr"].items())
        self._wait(q, deps)
        if transpose:
            kw["transpose"] = True
        inst = self.eng[q].dma_start(**kw)
        inst.then_inc(self.sems[key], 16)
        self.cnt[key] += 16
        c = self.cnt[key]
        if transpose:
            self._cat["tr"][key] = c
        elif is_ss:
            self._cat["ss"][key] = c
        for ap in ins:
            self._record(ap, key, c, False)
        for ap in outs:
            self._record(ap, key, c, True)
        return inst

    def dma(self, out, in_, q="sp", **kw):
        return self._dma_issue(q, [out], [in_], out=out, in_=in_, **kw)

    def dmat(self, out, in_, q="sp"):
        return self._dma_issue(q, [out], [in_], transpose=True, out=out, in_=in_)

    def mm(self, out, lhsT, rhs, start=True, stop=True, inc=1, tile_position=None):
        return self.op("pe", "matmul", [out], [lhsT, rhs], inc=inc, out=out,
                       lhsT=lhsT, rhs=rhs, start=start, stop=stop,
                       tile_position=tile_position)

    def tr(self, out, in_, ident, inc=1):
        return self.op("pe", "transpose", [out], [in_, ident], inc=inc,
                       out=out, in_=in_, identity=ident)

    def constf(self, value):
        value = float(value)
        if not hasattr(self, "_consts"):
            self._consts = {}
        if value not in self._consts:
            t = self.stack.enter_context(
                self.nc.sbuf_tensor(f"c_f32_{len(self._consts)}", [128, 1], F32))
            self.memset(t[:], value)
            self._consts[value] = t
        return self._consts[value][:]

    def act(self, out, in_, func, bias=None, scale=None, accum_out=None, q="act"):
        npart = out.ap[0][1]
        if isinstance(bias, (int, float)) and float(bias) != 0.0:
            bias = self.constf(bias)[0:npart, 0:1]
        if isinstance(scale, (int, float)) and float(scale) == 1.0:
            scale = None
        if isinstance(scale, (int, float)):
            scale = self.constf(scale)[0:npart, 0:1]
        kw = dict(out=out, in_=in_, func=func)
        ins = [in_]
        if bias is not None and not (isinstance(bias, float) and bias == 0.0):
            kw["bias"] = bias
            if not isinstance(bias, (int, float)):
                ins.append(bias)
        if scale is not None:
            kw["scale"] = scale
            if not isinstance(scale, (int, float)):
                ins.append(scale)
        if accum_out is not None:
            kw["accum_out"] = accum_out
        return self.op(q, "activation", [out, accum_out], ins, **kw)

    def ts(self, out, in0, s1, s2=None, op0=ALU.mult, op1=None, q="dve", accum_out=None):
        ins = [in0] + [s for s in (s1, s2) if s is not None and not isinstance(s, (int, float))]
        kw = dict(out=out, in0=in0, scalar1=s1, scalar2=s2, op0=op0)
        if op1 is not None:
            kw["op1"] = op1
        if accum_out is not None:
            kw["accum_out"] = accum_out
        return self.op(q, "tensor_scalar", [out, accum_out], ins, **kw)

    def tt(self, out, in0, in1, alu, q="dve"):
        return self.op(q, "tensor_tensor", [out], [in0, in1], out=out, in0=in0, in1=in1, op=alu)

    def stt(self, out, in0, s, in1, op0, op1, q="dve"):
        ins = [in0, in1] + ([s] if not isinstance(s, (int, float)) else [])
        return self.op(q, "scalar_tensor_tensor", [out], ins, out=out, in0=in0,
                       scalar=s, in1=in1, op0=op0, op1=op1)

    def copy(self, out, in_, q="dve"):
        return self.op(q, "tensor_copy", [out], [in_], out=out, in_=in_)

    def red(self, out, in_, alu, absval=None, q="dve"):
        return self.op(q, "tensor_reduce", [out], [in_], out=out, in_=in_,
                       axis=AX.X, op=alu, apply_absolute_value=absval)

    def scan(self, out, d0, d1, initial, op0, op1, q="dve"):
        return self.op(q, "tensor_tensor_scan", [out], [d0, d1], out=out, data0=d0,
                       data1=d1, initial=initial, op0=op0, op1=op1)

    def memset(self, ap, val, q="dve"):
        return self.op(q, "memset", [ap], [], ap=ap, constant=val)

    def recip(self, out, in_, q="dve"):
        return self.op(q, "reciprocal", [out], [in_], out=out, in_=in_)

    def sb(self, name, shape, dtype, tdest=False):
        return self.stack.enter_context(
            self.nc.sbuf_tensor(name, list(shape), dtype, is_transpose_dest=tdest))

    def ps(self, name, shape, dtype=F32):
        return self.stack.enter_context(self.nc.psum_tensor(name, list(shape), dtype))

    def quiesce(self):
        for q in self.QUEUES:
            for dk, c in self.cnt.items():
                if dk != q and c and self.waited[q].get(dk, 0) < c:
                    self.eng[q].wait_ge(self.sems[dk], c)
                    self.waited[q][dk] = c

    def barrier_store_done(self):
        for dk, c in self.cnt.items():
            if dk != "sp" and c:
                if self.waited["sp"].get(dk, 0) < c:
                    self.eng["sp"].wait_ge(self.sems[dk], c)
                    self.waited["sp"][dk] = c
        for ln in range(self.NLANES):
            key = f"sp{ln}"
            if self.cnt.get(key, 0):
                self.eng["sp"].wait_ge(self.sems[key], self.cnt[key])


def rsqrt_em(em, out, in_, eps, q="act", scratch=None):
    s = scratch if scratch is not None else out
    em.act(s, in_, AF.Sqrt, bias=float(eps), q=q)
    em.recip(out, s)



def host_consts():
    ident = np.eye(128, dtype=BF)
    ones_row = np.ones((1, 128), dtype=BF)
    t_idx = np.arange(CS)[:, None]
    s_idx = np.arange(CS)[None, :]
    ms = np.tril(np.ones((CS, CS), np.float32), -1).astype(BF)
    bigL = (30000.0 * (s_idx > t_idx)).astype(np.float32)
    bigUi = (-30000.0 * (t_idx > s_idx)).astype(np.float32)
    bigUs = (-30000.0 * (t_idx >= s_idx)).astype(np.float32)
    h = np.array([[1.0]], dtype=np.float64)
    while h.shape[0] < 1024:
        h = np.block([[h, h], [h, -h]])
    had = (h / np.sqrt(1024)).astype(np.float32).astype(BF)
    return dict(c_ident=ident, c_ones=ones_row, c_ms=ms, c_bigL=bigL,
                c_bigUi=bigUi, c_bigUs=bigUs, c_had=had)


CONST_SPECS = dict(c_ident=([128, 128], BF16), c_ones=([1, 128], BF16),
                   c_ms=([CS, CS], BF16), c_bigL=([CS, CS], F32),
                   c_bigUi=([CS, CS], F32), c_bigUs=([CS, CS], F32),
                   c_had=([1024, 1024], BF16))



def round_chain(em, out_bf, x_f32, tmp_i32, tmp_a, tmp_b, lo, hi, scale_col=None,
                post_mul_col=None, q1="dve", q2="dve"):
    if scale_col is not None:
        em.ts(tmp_a, x_f32, scale_col, op0=ALU.mult, q=q1)
        src = tmp_a
    else:
        src = x_f32
    em.copy(tmp_i32, src, q=q1)
    em.copy(tmp_b, tmp_i32, q=q1)
    em.tt(tmp_i32.bitcast(F32), src, tmp_b, ALU.subtract, q=q2)
    e = tmp_i32.bitcast(F32)
    em.ts(tmp_a, e, 0.5, op0=ALU.is_ge, q=q2)
    em.stt(tmp_a, e, -0.5, tmp_a, ALU.is_lt, ALU.subtract, q=q2)
    em.tt(tmp_b, tmp_b, tmp_a, ALU.subtract, q=q2)
    if post_mul_col is None:
        em.ts(out_bf, tmp_b, float(hi), float(lo), op0=ALU.min, op1=ALU.max, q=q1)
    else:
        em.ts(tmp_b, tmp_b, float(hi), float(lo), op0=ALU.min, op1=ALU.max, q=q1)
        em.ts(out_bf, tmp_b, post_mul_col, op0=ALU.mult, q=q1)


def emit_prep(em, IN, OUT):
    nc = em.nc
    sw = em.sb("pr_sw", [128, 4096], F32)
    swb = em.sb("pr_swb", [128, 4096], F32)
    swc = em.sb("pr_swc", [128, 4096], F32)
    swi = em.sb("pr_swi", [128, 4096], I32)
    bo0 = em.sb("pr_bo0", [128, 4096], BF16)
    bo1 = em.sb("pr_bo1", [128, 4096], BF16)
    acc = em.sb("pr_acc", [128, 1], F32)
    tpart = em.sb("pr_tp", [128, 1], F32)
    sc = em.sb("pr_sc", [128, 8], F32)
    onesf = em.sb("pr_onesf", [128, 1], F32)
    onesr = em.sb("pr_onesr", [1, 128], F32)
    ps = em.ps("pr_ps", [1, 8])
    ps2 = em.ps("pr_ps2", [128, 8])
    em.memset(onesf[:], 1.0)
    em.memset(onesr[:], 1.0)

    casts = [
        ("L0_attn_Wq", "wq_bf", 1024, 768), ("L0_attn_Wk", "wk_bf", 1024, 768),
        ("L0_attn_Wv", "wv_bf", 1024, 1536), ("L0_attn_Wg", "wg_bf", 1024, 1536),
        ("L0_attn_Wo", "wo_bf", 1536, 1024),
        ("L1_attn_wq", "wq1_bf", 1024, 1024), ("L1_attn_wk", "wk1_bf", 1024, 1024),
        ("L1_attn_wv", "wv1_bf", 1024, 1024), ("L1_attn_wo", "wo1_bf", 1024, 1024),
    ]
    bi = 0
    for src_, dst, rows, cols in casts:
        s, d = IN[src_], OUT[dst]
        for r0 in range(0, rows, 128):
            b = (bo0, bo1)[bi % 2]; bi += 1
            em.dma(b[:, 0:cols], s[r0:r0 + 128, 0:cols], q="pool")
            em.dma(d[r0:r0 + 128, 0:cols], b[:, 0:cols], q="sp")

    for src_, dst, rows in [("L0_attn_conv_q", "convq_bf", 768),
                            ("L0_attn_conv_k", "convk_bf", 768),
                            ("L0_attn_conv_v", "convv_bf", 1536)]:
        ncol = rows * 4 // 128
        em.dma(sw[:, 0:ncol], IN[src_].rearrange("a b -> (a b)").rearrange("(p n) -> p n", p=128), q="sp")
        em.copy(bo0[:, 0:ncol], sw[:, 0:ncol])
        em.dma(OUT[dst].rearrange("a b -> (a b)").rearrange("(p n) -> p n", p=128), bo0[:, 0:ncol], q="sp")

    em.memset(bo1[:, 0:352], 0.0)
    em.dma(OUT["wgkb_bf"].rearrange("a b -> (a b)").rearrange("(p n) -> p n", p=128),
           bo1[:, 0:352], q="sp")
    for src_, c0 in [("L0_attn_Wgk", 0), ("L0_attn_Wb", 32)]:
        em.dma(sw[:, 0:96], IN[src_].rearrange("a b -> (a b)").rearrange("(p n) -> p n", p=128), q="sp")
        em.copy(bo0[:, 0:96], sw[:, 0:96])
        em.dma(OUT["wgkb_bf"][:, c0:c0 + 12].rearrange("(p r) c -> p r c", p=128),
               bo0[:, 0:96].rearrange("p (r c) -> p r c", c=12), q="sp")

    for wname, oname, nelem in [("L0_ffn_w1", "w1q_rows", 4096 * 1024),
                                ("L0_ffn_w2", "w2q_rows", 1024 * 4096)]:
        w = IN[wname]
        wflat = w.rearrange("a b -> (a b)").rearrange("(p n) -> p n", p=128)
        o = OUT[oname]
        oflat = o.rearrange("a b -> (a b)").rearrange("(p n) -> p n", p=128)
        NCOL = nelem // 128
        TL = 4096
        em.memset(acc[:], 0.0)
        for c0 in range(0, NCOL, TL):
            em.dma(sw[:], wflat[:, c0:c0 + TL], q="sp")
            em.act(swb[:], sw[:], AF.Abs, accum_out=tpart[:])
            em.tt(acc[:], acc[:], tpart[:], ALU.add)
        em.mm(ps[0:1, 0:1], acc[:], onesf[:])
        em.copy(sc[0:1, 4:5], ps[0:1, 0:1])
        em.mm(ps2[:, 0:1], onesr[:], sc[0:1, 4:5])
        em.ts(sc[:, 0:1], ps2[:, 0:1], 1.0 / nelem, 1e-5, op0=ALU.mult, op1=ALU.max)
        em.recip(sc[:, 1:2], sc[:, 0:1])
        for c0 in range(0, NCOL, TL):
            em.dma(sw[:], wflat[:, c0:c0 + TL], q="sp")
            round_chain(em, bo0[:], sw[:], swi[:], swb[:], swc[:], -1.0, 1.0,
                        scale_col=sc[:, 1:2], post_mul_col=sc[:, 0:1])
            em.dma(oflat[:, c0:c0 + TL], bo0[:], q="sp")

    w = IN["L1_ffn_weight"]
    o = OUT["wf1q_rows"]
    for r0 in range(0, 1024, 128):
        em.dma(sw[:, 0:1024], w[r0:r0 + 128, :], q="sp")
        em.act(swb[:, 0:1024], sw[:, 0:1024], AF.Abs, accum_out=sc[:, 0:1])
        em.ts(sc[:, 1:2], sc[:, 0:1], 1.0 / 1024, 1e-5, op0=ALU.mult, op1=ALU.max)
        em.recip(sc[:, 2:3], sc[:, 1:2])
        em.ts(swc[:, 0:1024], sw[:, 0:1024], sc[:, 2:3], op0=ALU.mult)
        em.ts(swb[:, 0:1024], swc[:, 0:1024], 0.5, op0=ALU.is_gt)
        em.stt(swb[:, 0:1024], swc[:, 0:1024], -0.5, swb[:, 0:1024], ALU.is_lt, ALU.subtract)
        em.ts(sc[:, 3:4], sc[:, 1:2], -1.0, op0=ALU.mult)
        em.ts(bo0[:, 0:1024], swb[:, 0:1024], sc[:, 3:4], op0=ALU.mult)
        em.dma(o[r0:r0 + 128, :], bo0[:, 0:1024], q="sp")


PREP_OUTS = dict(
    wq_bf=([1024, 768], BF16), wk_bf=([1024, 768], BF16), wv_bf=([1024, 1536], BF16),
    wg_bf=([1024, 1536], BF16), wo_bf=([1536, 1024], BF16), wgkb_bf=([1024, 44], BF16),
    convq_bf=([768, 4], BF16), convk_bf=([768, 4], BF16), convv_bf=([1536, 4], BF16),
    w1q_rows=([4096, 1024], BF16), w2q_rows=([1024, 4096], BF16),
    wf1q_rows=([1024, 1024], BF16),
    wq1_bf=([1024, 1024], BF16), wk1_bf=([1024, 1024], BF16),
    wv1_bf=([1024, 1024], BF16), wo1_bf=([1024, 1024], BF16),
)

PREP_INS = ["L0_attn_Wq", "L0_attn_Wk", "L0_attn_Wv", "L0_attn_Wg", "L0_attn_Wo",
            "L0_attn_conv_q", "L0_attn_conv_k", "L0_attn_conv_v", "L0_attn_Wgk",
            "L0_attn_Wb", "L0_ffn_w1", "L0_ffn_w2", "L1_attn_wq", "L1_attn_wk",
            "L1_attn_wv", "L1_attn_wo", "L1_ffn_weight"]

INPUT_SHAPES = dict(
    L0_attn_Wq=[1024, 768], L0_attn_Wk=[1024, 768], L0_attn_Wv=[1024, 1536],
    L0_attn_Wg=[1024, 1536], L0_attn_Wo=[1536, 1024], L0_attn_conv_q=[768, 4],
    L0_attn_conv_k=[768, 4], L0_attn_conv_v=[1536, 4], L0_attn_Wgk=[1024, 12],
    L0_attn_Wb=[1024, 12], L0_ffn_w1=[4096, 1024], L0_ffn_w2=[1024, 4096],
    L1_attn_wq=[1024, 1024], L1_attn_wk=[1024, 1024], L1_attn_wv=[1024, 1024],
    L1_attn_wo=[1024, 1024], L1_ffn_weight=[1024, 1024],
)


def build_phase(emit_fn, ext_ins, ext_outs, consts=()):
    nc = bass.Bass("TRN2", target_bir_lowering=False)
    IN = {n: nc.dram_tensor(n, list(sh), dt, kind="ExternalInput").ap()
          for n, (sh, dt) in ext_ins.items()}
    OUT = {n: nc.dram_tensor(n, list(sh), dt, kind="ExternalOutput").ap()
           for n, (sh, dt) in ext_outs.items()}
    keys = list(Em.QUEUES) + [f"{q}{l}" for q in ("sp", "pool") for l in range(Em.NLANES)]
    sem_ctx = [nc.semaphore(f"q_{k}") for k in keys]
    sems = {}
    import contextlib
    with contextlib.ExitStack() as stack:
        for k, ctx in zip(keys, sem_ctx):
            sems[k] = stack.enter_context(ctx)
        with nc.Block() as block:
            @block.vector
            def _(vector):
                em = Em(nc, sems)
                with contextlib.ExitStack() as tstack:
                    em.stack = tstack
                    emit_fn(em, IN, OUT)
                    em.barrier_store_done()
    return nc


import contextlib

@contextlib.contextmanager
def emscope(em):
    with contextlib.ExitStack() as inner:
        outer = em.stack
        em.stack = inner
        try:
            yield
        finally:
            em.quiesce()
            em.stack = outer



def _bcast_const_mat(em, out_sb, vec_dram, width, ones_f, ps_bc, tmp_row):
    em.dma(tmp_row[0:1, 0:width], vec_dram.rearrange("(r c) -> r c", r=1), q="sp")
    for c0 in range(0, width, 512):
        em.mm(ps_bc[:, 0:512], ones_f[0:1, :], tmp_row[0:1, c0:c0 + 512])
        em.copy(out_sb[:, c0:c0 + 512], ps_bc[:, 0:512])


def _round_quant(em, out_bf, x_f32, scol, lo, hi, fB, fC, fD, iA, n):
    em.ts(fB[:, 0:n], x_f32, scol, op0=ALU.mult)
    em.copy(iA[:, 0:n], fB[:, 0:n])
    em.copy(fC[:, 0:n], iA[:, 0:n])
    e = iA[:, 0:n].bitcast(F32)
    em.tt(e, fB[:, 0:n], fC[:, 0:n], ALU.subtract)
    em.ts(fD[:, 0:n], e, 0.5, op0=ALU.is_ge)
    em.stt(fD[:, 0:n], e, -0.5, fD[:, 0:n], ALU.is_lt, ALU.subtract)
    em.tt(fC[:, 0:n], fC[:, 0:n], fD[:, 0:n], ALU.subtract)
    em.ts(out_bf, fC[:, 0:n], float(hi), float(lo), op0=ALU.min, op1=ALU.max)


def emit_l0f(em, IN, OUT):
    nc = em.nc
    ones = em.sb("ff_ones", [1, 128], BF16)
    em.dma(ones[:], IN["c_ones"], q="sp")
    onesf = em.sb("ff_onesf", [1, 128], F32)
    em.memset(onesf[:], 1.0)
    ident = em.sb("ff_ident", [128, 128], BF16)
    em.dma(ident[:], IN["c_ident"], q="sp")
    ps_bc = em.ps("ff_psbc", [128, 512])
    tmp_row = em.sb("ff_trow", [1, 4096], F32)

    c_ln1w = em.sb("ff_ln1w", [128, 1024], F32)
    c_ln1b = em.sb("ff_ln1b", [128, 1024], F32)
    c_ln2w = em.sb("ff_ln2w", [128, 4096], F32)
    c_ln2b = em.sb("ff_ln2b", [128, 4096], F32)
    for mat, vec, w in [(c_ln1w, "L0_ffn_ln1_w", 1024), (c_ln1b, "L0_ffn_ln1_b", 1024),
                        (c_ln2w, "L0_ffn_ln2_w", 4096), (c_ln2b, "L0_ffn_ln2_b", 4096)]:
        _bcast_const_mat(em, mat, IN[vec], w, onesf, ps_bc, tmp_row)
    b1row = em.sb("ff_b1row", [1, 4096], BF16)
    em.dma(b1row[:], IN["L0_ffn_b1"].rearrange("(r c) -> r c", r=1), q="pool")
    b2row = em.sb("ff_b2row", [1, 1024], BF16)
    em.dma(b2row[:], IN["L0_ffn_b2"].rearrange("(r c) -> r c", r=1), q="pool")

    xn_t = em.sb("ff_xn", [128, 1024], BF16)
    fA = em.sb("ff_fA", [128, 1024], F32)
    fB = em.sb("ff_fB", [128, 1024], F32)
    fC = em.sb("ff_fC", [128, 1024], F32)
    fD = em.sb("ff_fD", [128, 1024], F32)
    iA = em.sb("ff_iA", [128, 1024], I32)
    hq_t = em.sb("ff_hq", [128, 1024], BF16)
    st = em.sb("ff_st", [128, 32], F32)
    gacc = em.sb("ff_gacc", [128, 32], F32)
    sqs = em.sb("ff_sqs", [128, 512], F32)
    mq_t = em.sb("ff_mqt", [128, 1024], BF16)
    srow = em.sb("ff_srow", [1, 128], BF16)
    sbfc = em.sb("ff_sbfc", [128, 1], BF16)
    ps_tr = em.ps("ff_pstr", [128, 128], BF16)
    pu = [em.ps(f"ff_pu{i}", [128, 512]) for i in range(2)]

    xn_s, hq_s, mq_s, s2_s = IN["xn_s"], OUT["hq_s"], OUT["mq_s"], OUT["s2_s"]

    with emscope(em):
        w1sb = [em.sb(f"ff_w1_{d}", [128, 4096], BF16, tdest=True) for d in range(8)]
        for d in range(8):
            em.dmat(w1sb[d][:], IN["w1q_rows"][:, d * 128:(d + 1) * 128], q="sp")
        hqT = [em.sb(f"ff_hqT{d}", [128, 128], BF16, tdest=True) for d in range(8)]
        gm = em.sb("ff_gm", [128, 4096], BF16)

        for tt in range(32):
            trow = tt * 128
            em.dma(xn_t[:], xn_s[trow:trow + 128, :], q="sp")
            em.red(st[:, 0:1], xn_t[:], ALU.add)
            em.act(fA[:], xn_t[:], AF.Square, accum_out=st[:, 1:2])
            em.ts(st[:, 2:3], st[:, 0:1], 1.0 / 1024, op0=ALU.mult)
            em.ts(st[:, 3:4], st[:, 1:2], 1.0 / 1024, op0=ALU.mult)
            em.ts(st[:, 4:5], st[:, 2:3], st[:, 2:3], op0=ALU.mult)
            em.tt(st[:, 5:6], st[:, 3:4], st[:, 4:5], ALU.subtract)
            em.act(st[:, 6:7], st[:, 5:6], AF.Ln, bias=EPS_LN)
            em.act(st[:, 7:8], st[:, 6:7], AF.Exp, scale=-0.5)
            em.ts(st[:, 8:9], st[:, 2:3], st[:, 7:8], s2=-1.0, op0=ALU.mult, op1=ALU.mult)
            em.act(fA[:], xn_t[:], AF.Identity, scale=st[:, 7:8], bias=st[:, 8:9])
            em.stt(fB[:], fA[:], 0.0, c_ln1w[:], ALU.bypass, ALU.mult)
            em.tt(fA[:], fB[:], c_ln1b[:], ALU.add)
            em.red(st[:, 9:10], fA[:], ALU.max, absval=True)
            em.ts(st[:, 10:11], st[:, 9:10], 1e-5, op0=ALU.max)
            em.recip(st[:, 11:12], st[:, 10:11])
            em.ts(st[:, 12:13], st[:, 11:12], 127.0, op0=ALU.mult)
            em.ts(st[:, 13:14], st[:, 10:11], 1.0 / 127.0, op0=ALU.mult)
            _round_quant(em, hq_t[:], fA[:], st[:, 12:13], -128.0, 127.0, fA2 := fB, fC, fD, iA, 1024)
            em.dma(hq_s[trow:trow + 128, :], hq_t[:], q="sp")
            em.copy(sbfc[:], st[:, 12:13])
            em.tr(ps_tr[0:1, 0:128], sbfc[:], ident[:])
            em.copy(srow[:], ps_tr[0:1, 0:128])
            for d in range(8):
                em.dmat(hqT[d][:], hq_s[trow:trow + 128, d * 128:(d + 1) * 128], q="sp")
            for fc in range(8):
                bank = pu[fc % 2]
                for d in range(8):
                    em.mm(bank[:], hqT[d][:], w1sb[d][:, fc * 512:(fc + 1) * 512],
                          start=(d == 0), stop=False, inc=0)
                em.mm(bank[:], srow[0:1, :], b1row[0:1, fc * 512:(fc + 1) * 512],
                      start=False, stop=True)
                em.act(gm[:, fc * 512:(fc + 1) * 512], bank[:], AF.Gelu,
                       scale=st[:, 13:14], accum_out=gacc[:, fc:fc + 1])
                em.act(sqs[:], gm[:, fc * 512:(fc + 1) * 512], AF.Square,
                       accum_out=gacc[:, 8 + fc:9 + fc])
            em.red(st[:, 16:17], gacc[:, 0:8], ALU.add)
            em.red(st[:, 17:18], gacc[:, 8:16], ALU.add)
            em.ts(st[:, 18:19], st[:, 16:17], 1.0 / 4096, op0=ALU.mult)
            em.ts(st[:, 19:20], st[:, 17:18], 1.0 / 4096, op0=ALU.mult)
            em.ts(st[:, 20:21], st[:, 18:19], st[:, 18:19], op0=ALU.mult)
            em.tt(st[:, 21:22], st[:, 19:20], st[:, 20:21], ALU.subtract)
            em.act(st[:, 22:23], st[:, 21:22], AF.Ln, bias=EPS_LN)
            em.act(st[:, 23:24], st[:, 22:23], AF.Exp, scale=-0.5)
            em.ts(st[:, 24:25], st[:, 18:19], st[:, 23:24], s2=-1.0, op0=ALU.mult, op1=ALU.mult)
            for fc in range(8):
                sl_ = slice(fc * 512, (fc + 1) * 512)
                em.act(fA[:, 0:512], gm[:, sl_], AF.Identity, scale=st[:, 23:24], bias=st[:, 24:25])
                em.stt(fB[:, 0:512], fA[:, 0:512], 0.0, c_ln2w[:, sl_], ALU.bypass, ALU.mult)
                em.tt(gm[:, sl_], fB[:, 0:512], c_ln2b[:, sl_], ALU.add)
                em.red(gacc[:, 16 + fc:17 + fc], gm[:, sl_], ALU.max, absval=True)
            em.red(st[:, 25:26], gacc[:, 16:24], ALU.max)
            em.ts(st[:, 26:27], st[:, 25:26], 1e-5, op0=ALU.max)
            em.recip(st[:, 27:28], st[:, 26:27])
            em.ts(st[:, 28:29], st[:, 27:28], 127.0, op0=ALU.mult)
            em.dma(s2_s[trow:trow + 128, :], st[:, 28:29], q="sp")
            for cc in range(4):
                sl_ = slice(cc * 1024, (cc + 1) * 1024)
                _round_quant(em, mq_t[:], gm[:, sl_], st[:, 28:29], -128.0, 127.0, fB, fC, fD, iA, 1024)
                em.dma(mq_s[trow:trow + 128, sl_], mq_t[:], q="sp")

    with emscope(em):
        w2sb = [em.sb(f"ff_w2_{h}", [128, 1024], BF16, tdest=True) for h in range(32)]
        for h in range(32):
            em.dmat(w2sb[h][:], IN["w2q_rows"][:, h * 128:(h + 1) * 128], q="sp")
        mqT = [em.sb(f"ff_mqT{h}", [128, 128], BF16, tdest=True) for h in range(32)]
        x_t = em.sb("ff_xt", [128, 1024], F32)
        a0_t = em.sb("ff_a0", [128, 1024], BF16)
        x1_t = em.sb("ff_x1", [128, 1024], BF16)
        s2c = em.sb("ff_s2c", [128, 1], F32)
        s2r = em.sb("ff_s2r", [1, 128], BF16)
        s2cb = em.sb("ff_s2cb", [128, 1], BF16)
        rs2 = em.sb("ff_rs2", [128, 1], F32)
        po = [em.ps(f"ff_po{i}", [128, 512]) for i in range(2)]
        for tt in range(32):
            trow = tt * 128
            em.dma(s2c[:], s2_s[trow:trow + 128, :], q="sp")
            em.recip(rs2[:], s2c[:])
            em.copy(s2cb[:], s2c[:])
            em.tr(ps_tr[0:1, 0:128], s2cb[:], ident[:])
            em.copy(s2r[:], ps_tr[0:1, 0:128])
            for h in range(32):
                em.dmat(mqT[h][:], mq_s[trow:trow + 128, h * 128:(h + 1) * 128], q="sp")
            em.dma(x_t[:], IN["x"][trow:trow + 128, :], q="sp")
            em.dma(a0_t[:], IN["a0_s"][trow:trow + 128, :], q="sp")
            for oc in range(2):
                bank = po[oc]
                for h in range(32):
                    em.mm(bank[:], mqT[h][:], w2sb[h][:, oc * 512:(oc + 1) * 512],
                          start=(h == 0), stop=False, inc=0)
                em.mm(bank[:], s2r[0:1, :], b2row[0:1, oc * 512:(oc + 1) * 512],
                      start=False, stop=True)
                sl_ = slice(oc * 512, (oc + 1) * 512)
                em.stt(fA[:, 0:512], bank[:], rs2[:], x_t[:, sl_], ALU.mult, ALU.add)
                em.tt(x1_t[:, sl_], fA[:, 0:512], a0_t[:, sl_], ALU.add)
            em.dma(OUT["x1_s"][trow:trow + 128, :], x1_t[:], q="sp")


L0F_INS = dict(
    x=([T, D], F32), xn_s=([T, D], BF16), a0_s=([T, D], BF16),
    w1q_rows=([4096, 1024], BF16), w2q_rows=([1024, 4096], BF16),
    L0_ffn_ln1_w=([1024], F32), L0_ffn_ln1_b=([1024], F32),
    L0_ffn_ln2_w=([4096], F32), L0_ffn_ln2_b=([4096], F32),
    L0_ffn_b1=([4096], F32), L0_ffn_b2=([1024], F32),
    c_ones=([1, 128], BF16), c_ident=([128, 128], BF16),
)
L0F_OUTS = dict(x1_s=([T, D], BF16), hq_s=([T, D], BF16),
                mq_s=([T, HID], BF16), s2_s=([T, 1], F32))



def emit_l0a(em, IN, OUT, n_chunks=8, n_sub=8, scan_secs=99):
    nc = em.nc
    ident = em.sb("la_ident", [128, 128], BF16)
    em.dma(ident[:], IN["c_ident"], q="sp")
    onesf = em.sb("la_onesf", [1, 128], F32)
    em.memset(onesf[:], 1.0)
    onescol = em.sb("la_onescol", [64, 1], F32)
    em.memset(onescol[:], 1.0)
    negones64 = em.sb("la_no64", [1, 64], F32)
    em.memset(negones64[:], -1.0)
    c_ms = em.sb("la_ms", [CS, CS], BF16)
    em.dma(c_ms[:], IN["c_ms"], q="sp")
    c_bigL = em.sb("la_bigL", [CS, CS], F32)
    em.dma(c_bigL[:], IN["c_bigL"], q="sp")
    c_bigUi = em.sb("la_bigUi", [CS, CS], F32)
    em.dma(c_bigUi[:], IN["c_bigUi"], q="sp")
    c_bigUs = em.sb("la_bigUs", [CS, CS], F32)
    em.dma(c_bigUs[:], IN["c_bigUs"], q="sp")

    alog = em.sb("la_alog", [12, 1], F32)
    em.dma(alog[:], IN["L0_attn_A_log"].rearrange("(p c) -> p c", c=1), q="sp")
    dtb = em.sb("la_dtb", [12, 1], F32)
    em.dma(dtb[:], IN["L0_attn_dt_bias"].rearrange("(p c) -> p c", c=1), q="sp")
    bbc = em.sb("la_bb", [12, 1], F32)
    em.dma(bbc[:], IN["L0_attn_b_b"].rearrange("(p c) -> p c", c=1), q="sp")
    negA = em.sb("la_negA", [12, 1], F32)
    em.act(negA[:], alog[:], AF.Exp)
    em.ts(negA[:], negA[:], -1.0, op0=ALU.mult)

    gnwrow = em.sb("la_gnwrow", [1, 128], F32)
    em.dma(gnwrow[:], IN["L0_attn_g_norm_weight"].rearrange("(r c) -> r c", r=1), q="sp")
    gnwmat = em.sb("la_gnwmat", [64, 1536], BF16)
    ppg = em.ps("la_ppg", [128, 128])
    for h in range(12):
        em.mm(ppg[0:64, 0:128], onesf[0:1, 0:64], gnwrow[:])
        em.copy(gnwmat[:, h * 128:(h + 1) * 128], ppg[0:64, 0:128])

    convqk = em.sb("la_convqk", [64, 96], F32)
    for h in range(12):
        em.dma(convqk[:, h * 4:(h + 1) * 4], IN["L0_attn_conv_q"][h * 64:(h + 1) * 64, :], q="sp")
        em.dma(convqk[:, 48 + h * 4:48 + (h + 1) * 4], IN["L0_attn_conv_k"][h * 64:(h + 1) * 64, :], q="sp")
    convv_all = em.sb("la_convvall", [128, 48], F32)
    for vt in range(12):
        em.dma(convv_all[:, vt * 4:(vt + 1) * 4], IN["L0_attn_conv_v"][vt * 128:(vt + 1) * 128, :], q="sp")
    dgqk = []
    for hh in range(24):
        tile4 = []
        for j in range(4):
            d = em.sb(f"la_dg{hh}_{j}", [64, 64], BF16)
            em.ts(d[:], ident[0:64, 0:64], convqk[:, hh * 4 + j:hh * 4 + j + 1], op0=ALU.mult)
            tile4.append(d)
        dgqk.append(tile4)
    vdg = [[em.sb(f"la_vdg{i}_{j}", [128, 128], BF16) for j in range(4)] for i in range(2)]

    w_sb = [em.sb(f"la_w{d}", [128, 4608], BF16) for d in range(8)]
    wgkb_sb = em.sb("la_wgkb", [128, 352], BF16)
    for d in range(8):
        em.dma(w_sb[d][:, 0:768], IN["wq_bf"][d * 128:(d + 1) * 128, :], q="sp")
        em.dma(w_sb[d][:, 768:1536], IN["wk_bf"][d * 128:(d + 1) * 128, :], q="sp")
        em.dma(w_sb[d][:, 1536:3072], IN["wv_bf"][d * 128:(d + 1) * 128, :], q="sp")
        em.dma(w_sb[d][:, 3072:4608], IN["wg_bf"][d * 128:(d + 1) * 128, :], q="sp")
        em.dma(wgkb_sb[:, d * 44:(d + 1) * 44], IN["wgkb_bf"][d * 128:(d + 1) * 128, :], q="sp")

    pp = [em.ps(f"la_pp{i}", [128, 512]) for i in range(6)]
    ps_trf = em.ps("la_pstrf", [128, 128])
    identf = em.sb("la_identf", [128, 128], F32)
    em.copy(identf[:], ident[:])

    with emscope(em):
        x_t = em.sb("la_xt", [128, 512], F32)
        xn_t = em.sb("la_xnt", [128, 1024], BF16)
        st = em.sb("la_st", [128, 16], F32)
        sqf = em.sb("la_sqf", [64, 512], F32)
        xnT = [em.sb(f"la_xnT{d}", [128, 512], BF16, tdest=True) for d in range(8)]
        preqk = [em.sb(f"la_preqk{t}", [64, 516], BF16) for t in range(2)]
        histqk = [em.sb(f"la_histqk{t}", [64, 4], BF16) for t in range(24)]
        prev = [em.sb(f"la_prev{t}", [128, 516], BF16) for t in range(2)]
        histv = [em.sb(f"la_histv{t}", [128, 4], BF16) for t in range(12)]
        qn = [em.sb(f"la_qn{t}", [64, 512], BF16) for t in range(12)]
        kn = [em.sb(f"la_kn{t}", [64, 512], BF16) for t in range(12)]
        vf = [em.sb(f"la_vf{t}", [128, 512], BF16) for t in range(2)]
        rsq1 = em.sb("la_rsq1", [1, 512], F32)
        g_sc = em.sb("la_gsc", [64, 1536], BF16)
        gkb = em.sb("la_gkb", [12, 512], F32)
        beta_f = em.sb("la_betaf", [12, 512], F32)
        S = em.sb("la_S", [64, 1536], F32)
        Sbf = em.sb("la_Sbf", [64, 1536], BF16)
        em.memset(S[:], 0.0)
        em.memset(Sbf[:], 0.0)
        gam = em.sb("la_gam", [12, 64], F32)
        b12 = em.sb("la_b12", [12, 64], F32)
        bb12 = em.sb("la_bb12", [12, 64], F32)
        kds12 = em.sb("la_kds12", [12, 64], F32)
        flatg = em.sb("la_flatg", [1, 768], F32)
        flatbeta = em.sb("la_flatbeta", [1, 768], F32)
        beta_tok = em.sb("la_btok", [64, 12], F32)
        bb_tok = em.sb("la_bbtok", [64, 12], F32)
        b_tok = em.sb("la_btok2", [64, 12], F32)
        kds_tok = em.sb("la_kdstok", [64, 12], F32)
        bc_tok = em.sb("la_bctok", [64, 12], F32)
        bcrow = em.sb("la_bcrow", [1, 12], F32)
        E0 = em.sb("la_E0", [64, 768], BF16)
        E0U = em.sb("la_E0U", [64, 768], BF16)
        Amat = em.sb("la_Amat", [64, 768], BF16)
        ATmat = em.sb("la_ATmat", [64, 768], BF16)
        WTmat = em.sb("la_WTmat", [64, 768], BF16)
        powP = [em.sb(f"la_powP{i}", [64, 768], BF16) for i in range(2)]
        powPT = [None] + [em.sb(f"la_powPT{i}", [64, 768], BF16) for i in range(1, 6)]
        y_t = em.sb("la_y", [64, 1536], BF16)
        y2_t = em.sb("la_y2", [64, 1536], BF16)
        Dl = em.sb("la_Dl", [64, 1536], BF16)
        Ot = em.sb("la_O", [64, 1536], BF16)
        o2 = em.sb("la_o2", [64, 1536], BF16)
        ovar = em.sb("la_ovar", [64, 16], F32)
        ktokG = em.sb("la_ktokG", [128, 768], BF16, tdest=True)
        vtokG = em.sb("la_vtokG", [128, 1536], BF16, tdest=True)
        ktokB = em.sb("la_ktokB", [64, 768], BF16)
        vtokB = em.sb("la_vtokB", [64, 1536], BF16)

        xn_s, a0_s = OUT["xn_s"], OUT["a0_s"]
        kn_s, vf_s, og_s = OUT["kn_s"], OUT["vf_s"], OUT["og_s"]

        def stretch(tile, h0, nh, rep):
            return tile[:, h0:h0 + nh].unsqueeze(2).broadcast_to((64, nh, rep))

        def rep_mask(msk, nh):
            return msk[:].unsqueeze(1).broadcast_to((64, nh, CS))

        def h3(ap2d, nh, blk):
            return ap2d.rearrange("p (h s) -> p h s", s=blk)

        for ch in range(n_chunks):
            tbase = ch * 512
            for tt in range(4):
                trow = tbase + tt * 128
                em.dma(x_t[:], IN["x"][trow:trow + 128, 0:512], q="sp")
                em.red(st[:, 0:1], x_t[:], ALU.add)
                em.act(sqf[0:64, :], x_t[0:64, :], AF.Square, accum_out=st[0:64, 1:2])
                em.act(sqf[0:64, :], x_t[64:128, :], AF.Square, accum_out=st[64:128, 1:2])
                em.dma(x_t[:], IN["x"][trow:trow + 128, 512:1024], q="sp")
                em.red(st[:, 9:10], x_t[:], ALU.add)
                em.act(sqf[0:64, :], x_t[0:64, :], AF.Square, accum_out=st[0:64, 10:11])
                em.act(sqf[0:64, :], x_t[64:128, :], AF.Square, accum_out=st[64:128, 10:11])
                em.tt(st[:, 0:1], st[:, 0:1], st[:, 9:10], ALU.add)
                em.tt(st[:, 1:2], st[:, 1:2], st[:, 10:11], ALU.add)
                em.ts(st[:, 2:3], st[:, 0:1], 1.0 / 1024, op0=ALU.mult)
                em.ts(st[:, 3:4], st[:, 1:2], 1.0 / 1024, op0=ALU.mult)
                em.ts(st[:, 4:5], st[:, 2:3], st[:, 2:3], op0=ALU.mult)
                em.tt(st[:, 5:6], st[:, 3:4], st[:, 4:5], ALU.subtract)
                em.act(st[:, 6:7], st[:, 5:6], AF.Ln, bias=EPS_LN)
                em.act(st[:, 7:8], st[:, 6:7], AF.Exp, scale=-0.5)
                em.ts(st[:, 8:9], st[:, 2:3], st[:, 7:8], s2=-1.0, op0=ALU.mult, op1=ALU.mult)
                em.act(xn_t[:, 512:1024], x_t[:], AF.Identity, scale=st[:, 7:8], bias=st[:, 8:9])
                em.dma(x_t[:], IN["x"][trow:trow + 128, 0:512], q="sp")
                em.act(xn_t[:, 0:512], x_t[:], AF.Identity, scale=st[:, 7:8], bias=st[:, 8:9])
                em.dma(xn_s[trow:trow + 128, :], xn_t[:], q="sp")
            for d in range(8):
                em.dmat(xnT[d][:], xn_s[tbase:tbase + 512, d * 128:(d + 1) * 128], q="sp")
            for hh in range(24):
                bank = pp[hh % 2]
                prt = preqk[hh % 2]
                for d in range(8):
                    em.mm(bank[0:64, :], w_sb[d][:, hh * 64:(hh + 1) * 64], xnT[d][:],
                          start=(d == 0), stop=(d == 7), inc=(1 if d == 7 else 0))
                if ch > 0:
                    em.copy(prt[:, 0:3], histqk[hh][:, 0:3], q="pool")
                else:
                    em.memset(prt[:, 0:3], 0.0, q="pool")
                em.copy(prt[:, 3:515], bank[0:64, :])
                em.copy(histqk[hh][:, 0:3], prt[:, 512:515], q="pool")
                bank2 = pp[2 + hh % 2]
                for j in range(4):
                    em.mm(bank2[0:64, :], dgqk[hh][j][:], prt[:, j:j + 512],
                          start=(j == 0), stop=(j == 3), inc=(1 if j == 3 else 0))
                dst = qn[hh] if hh < 12 else kn[hh - 12]
                em.act(dst[:], bank2[0:64, :], AF.Silu)
                em.act(sqf[:], dst[:], AF.Square)
                em.mm(pp[4][0:1, :], onescol[:], sqf[:])
                em.act(rsq1[:], pp[4][0:1, :], AF.Ln, bias=1e-6)
                em.act(rsq1[:], rsq1[:], AF.Exp, scale=-0.5)
                if hh < 12:
                    em.ts(rsq1[:], rsq1[:], 0.125, op0=ALU.mult)
                em.mm(pp[5][0:64, 0:512], onesf[0:1, 0:64], rsq1[0:1, :])
                em.tt(dst[:], dst[:], pp[5][0:64, 0:512], ALU.mult)
                if hh >= 12:
                    em.dma(kn_s[(hh - 12) * 64:(hh - 11) * 64, tbase:tbase + 512], dst[:], q="sp")
            for ft in range(12):
                bank = pp[ft % 2]
                prt = prev[ft % 2]
                for d in range(8):
                    em.mm(bank[:], w_sb[d][:, 1536 + ft * 128:1536 + (ft + 1) * 128], xnT[d][:],
                          start=(d == 0), stop=(d == 7), inc=(1 if d == 7 else 0))
                if ch > 0:
                    em.copy(prt[:, 0:3], histv[ft][:, 0:3], q="pool")
                else:
                    em.memset(prt[:, 0:3], 0.0, q="pool")
                em.copy(prt[:, 3:515], bank[:])
                em.copy(histv[ft][:, 0:3], prt[:, 512:515], q="pool")
                dg = vdg[ft % 2]
                for j in range(4):
                    em.ts(dg[j][:], ident[:], convv_all[:, ft * 4 + j:ft * 4 + j + 1],
                          op0=ALU.mult, q="pool")
                bank2 = pp[2 + ft % 2]
                for j in range(4):
                    em.mm(bank2[:], dg[j][:], prt[:, j:j + 512],
                          start=(j == 0), stop=(j == 3), inc=(1 if j == 3 else 0))
                vt = vf[ft % 2]
                em.act(vt[:], bank2[:], AF.Silu)
                em.dma(vf_s[ft * 128:(ft + 1) * 128, tbase:tbase + 512], vt[:], q="sp")
            bank = pp[2]
            for d in range(8):
                em.mm(bank[0:44, :], wgkb_sb[:, d * 44:(d + 1) * 44], xnT[d][:],
                      start=(d == 0), stop=(d == 7), inc=(1 if d == 7 else 0))
            em.act(gkb[:], bank[0:12, :], AF.Abs, bias=dtb[:, 0:1])
            em.act(gkb[:], gkb[:], AF.Exp, scale=-1.0)
            em.act(gkb[:], gkb[:], AF.Ln, bias=1.0)
            em.act(beta_f[:], bank[0:12, :], AF.Identity, bias=dtb[:, 0:1])
            em.ts(beta_f[:], beta_f[:], 0.0, op0=ALU.max)
            em.tt(gkb[:], gkb[:], beta_f[:], ALU.add)
            em.ts(gkb[:], gkb[:], negA[:, 0:1], op0=ALU.mult)
            em.act(beta_f[:], bank[32:44, :], AF.Sigmoid, bias=bbc[:, 0:1])
            for sc in range(n_sub):
                c0 = sc * 64
                grp = sc // 2
                if sc % 2 == 0:
                    trow = tbase + grp * 128
                    em.dmat(ktokG[:], kn_s[0:768, trow:trow + 128], q="sp")
                    em.dmat(vtokG[:], vf_s[0:1536, trow:trow + 128], q="sp")
                    em.dma(ktokB[:], ktokG[64:128, :], q="sp")
                    em.dma(vtokB[:], vtokG[64:128, :], q="sp")
                ktok = ktokG if sc % 2 == 0 else ktokB
                vtok = vtokG if sc % 2 == 0 else vtokB
                for fs in range(3):
                    gb = pp[3 + fs]
                    for d in range(8):
                        em.mm(gb[0:64, :], xnT[d][:, c0:c0 + 64],
                              w_sb[d][:, 3072 + fs * 512:3072 + (fs + 1) * 512],
                              start=(d == 0), stop=(d == 7), inc=(1 if d == 7 else 0))
                    em.copy(g_sc[:, fs * 512:(fs + 1) * 512], gb[0:64, :])
                if scan_secs < 2:
                    continue
                em.scan(gam[:], gkb[:, c0:c0 + 64], gkb[:, c0:c0 + 64], 0.0,
                        ALU.add, ALU.bypass)
                em.act(b12[:], gam[:], AF.Exp)
                em.tt(bb12[:], beta_f[:, c0:c0 + 64], b12[:], ALU.mult)
                em.ts(kds12[:], gam[:], gam[:, 63:64], op0=ALU.subtract)
                em.act(kds12[:], kds12[:], AF.Exp, scale=-1.0)
                em.dma(flatg[:], gam[:], q="sp")
                em.dma(flatbeta[:], beta_f[:, c0:c0 + 64], q="sp")
                for tsrc, tdst in ((beta_f[:, c0:c0 + 64], beta_tok), (bb12[:], bb_tok),
                                   (b12[:], b_tok), (kds12[:], kds_tok)):
                    em.tr(ps_trf[0:64, 0:12], tsrc, identf[0:12, 0:12])
                    em.copy(tdst[:], ps_trf[0:64, 0:12])
                em.act(bcrow[:], flatg[0:1, 63:768:64], AF.Exp)
                em.mm(pp[5][0:64, 0:12], onesf[0:1, 0:64], bcrow[:])
                em.copy(bc_tok[:], pp[5][0:64, 0:12])
                if scan_secs < 3:
                    continue
                for h in range(12):
                    b_, cs_ = (pp[0], h * 64) if h < 8 else (pp[1], (h - 8) * 64)
                    em.mm(b_[0:64, cs_:cs_ + 64], flatg[0:1, 64 * h:64 * h + 64],
                          negones64[0:1, 0:64], start=(h in (0, 8)), stop=False, inc=0)
                    em.mm(b_[0:64, cs_:cs_ + 64], onesf[0:1, 0:64],
                          flatg[0:1, 64 * h:64 * h + 64], start=False,
                          stop=(h in (7, 11)), inc=(1 if h in (7, 11) else 0))
                scr = ((pp[0], pp[4], 512, 0), (pp[1], pp[5], 256, 512))
                for (b_, s_, w_, off) in scr:
                    em.tt(h3(s_[0:64, 0:w_], w_ // 64, 64), h3(b_[0:64, 0:w_], w_ // 64, 64),
                          rep_mask(c_bigL, w_ // 64), ALU.add)
                    em.act(E0[:, off:off + w_], s_[0:64, 0:w_], AF.Exp, scale=-1.0)
                    em.tt(h3(s_[0:64, 0:w_], w_ // 64, 64), h3(b_[0:64, 0:w_], w_ // 64, 64),
                          rep_mask(c_bigUi, w_ // 64), ALU.add)
                    em.act(E0U[:, off:off + w_], s_[0:64, 0:w_], AF.Exp)
                if scan_secs < 4:
                    continue
                for h in range(12):
                    b_, cs_ = (pp[2], h * 64) if h < 8 else (pp[3], (h - 8) * 64)
                    kf = kn[h][:, c0:c0 + 64]
                    em.mm(b_[0:64, cs_:cs_ + 64], kf, kf, inc=(1 if h in (7, 11) else 0))
                if scan_secs < 41:
                    continue
                for (b_, w_, off) in ((pp[2], 512, 0), (pp[3], 256, 512)):
                    nh = w_ // 64
                    em.tt(Amat[:, off:off + w_], E0[:, off:off + w_], b_[0:64, 0:w_], ALU.mult)
                    em.tt(h3(Amat[:, off:off + w_], nh, 64), h3(Amat[:, off:off + w_], nh, 64),
                          stretch(beta_tok, 0 + off // 64, nh, 64), ALU.mult)
                    em.tt(h3(Amat[:, off:off + w_], nh, 64), h3(Amat[:, off:off + w_], nh, 64),
                          rep_mask(c_ms, nh), ALU.mult)
                if scan_secs < 42:
                    continue
                for (b_, s_, w_, off) in scr:
                    em.tt(h3(s_[0:64, 0:w_], w_ // 64, 64), h3(b_[0:64, 0:w_], w_ // 64, 64),
                          rep_mask(c_bigUs, w_ // 64), ALU.add)
                    em.act(E0[:, off:off + w_], s_[0:64, 0:w_], AF.Exp)
                for h in range(12):
                    b_, cs_ = (pp[0], h * 64) if h < 8 else (pp[1], (h - 8) * 64)
                    em.mm(b_[0:64, cs_:cs_ + 64], onesf[0:1, 0:64],
                          flatbeta[0:1, 64 * h:64 * h + 64],
                          inc=(1 if h in (7, 11) else 0))
                if scan_secs < 43:
                    continue
                for (b_, w_, off) in ((pp[2], 512, 0), (pp[3], 256, 512)):
                    em.tt(ATmat[:, off:off + w_], E0[:, off:off + w_], b_[0:64, 0:w_], ALU.mult)
                for (b_, w_, off) in ((pp[0], 512, 0), (pp[1], 256, 512)):
                    em.tt(ATmat[:, off:off + w_], ATmat[:, off:off + w_], b_[0:64, 0:w_], ALU.mult)
                if scan_secs < 5:
                    continue
                for h in range(12):
                    b_, cs_ = (pp[2], h * 64) if h < 8 else (pp[3], (h - 8) * 64)
                    em.mm(b_[0:64, cs_:cs_ + 64], kn[h][:, c0:c0 + 64], qn[h][:, c0:c0 + 64],
                          inc=(1 if h in (7, 11) else 0))
                for (b_, w_, off) in ((pp[2], 512, 0), (pp[3], 256, 512)):
                    em.tt(WTmat[:, off:off + w_], E0U[:, off:off + w_], b_[0:64, 0:w_], ALU.mult)
                if scan_secs < 6:
                    continue
                for h in range(12):
                    b_ = (pp[2], pp[3], pp[4])[h // 4]
                    cs_ = (h % 4) * 128
                    em.mm(b_[0:64, cs_:cs_ + 128], kn[h][:, c0:c0 + 64],
                          Sbf[:, h * 128:(h + 1) * 128], inc=(1 if h % 4 == 3 else 0))
                em.tt(h3(y_t[:], 12, 128), h3(vtok[0:64, :], 12, 128),
                      stretch(beta_tok, 0, 12, 128), ALU.mult)
                for bi_, b_ in enumerate((pp[2], pp[3], pp[4])):
                    off = bi_ * 512
                    em.tt(h3(y2_t[:, off:off + 512], 4, 128), h3(b_[0:64, 0:512], 4, 128),
                          stretch(bb_tok, bi_ * 4, 4, 128), ALU.mult)
                em.tt(y_t[:], y_t[:], y2_t[:], ALU.subtract)
                if scan_secs < 7:
                    continue
                em.copy(powP[0][:], Amat[:], q="pool")
                for lv in range(1, 6):
                    pprev = powP[(lv - 1) % 2]
                    ptprev = ATmat if lv == 1 else powPT[lv - 1]
                    pcur, ptcur = powP[lv % 2], powPT[lv]
                    for h in range(12):
                        b_, cs_ = (pp[0], h * 64) if h < 8 else (pp[1], (h - 8) * 64)
                        hb = slice(h * 64, h * 64 + 64)
                        em.mm(b_[0:64, cs_:cs_ + 64], ptprev[:, hb], pprev[:, hb],
                              inc=(1 if h in (7, 11) else 0))
                    for h in range(12):
                        b_, cs_ = (pp[2], h * 64) if h < 8 else (pp[3], (h - 8) * 64)
                        hb = slice(h * 64, h * 64 + 64)
                        em.mm(b_[0:64, cs_:cs_ + 64], pprev[:, hb], ptprev[:, hb],
                              inc=(1 if h in (7, 11) else 0))
                    em.act(pcur[:, 0:512], pp[0][0:64, 0:512], AF.Identity)
                    em.act(pcur[:, 512:768], pp[1][0:64, 0:256], AF.Identity)
                    em.act(ptcur[:, 0:512], pp[2][0:64, 0:512], AF.Identity)
                    em.act(ptcur[:, 512:768], pp[3][0:64, 0:256], AF.Identity)
                if scan_secs < 8:
                    continue
                for lv in range(5, 0, -1):
                    pt = ATmat if lv == 1 else powPT[lv]
                    pt = powPT[lv]
                    for h in range(12):
                        b_ = (pp[0], pp[1], pp[2])[h // 4]
                        cs_ = (h % 4) * 128
                        em.mm(b_[0:64, cs_:cs_ + 128], pt[:, h * 64:h * 64 + 64],
                              y_t[:, h * 128:(h + 1) * 128], inc=(1 if h % 4 == 3 else 0))
                    for bi_, b_ in enumerate((pp[0], pp[1], pp[2])):
                        off = bi_ * 512
                        em.tt(y_t[:, off:off + 512], y_t[:, off:off + 512], b_[0:64, 0:512], ALU.add)
                for h in range(12):
                    b_ = (pp[0], pp[1], pp[2])[h // 4]
                    cs_ = (h % 4) * 128
                    em.mm(b_[0:64, cs_:cs_ + 128], ATmat[:, h * 64:h * 64 + 64],
                          y_t[:, h * 128:(h + 1) * 128], inc=(1 if h % 4 == 3 else 0))
                for bi_, b_ in enumerate((pp[0], pp[1], pp[2])):
                    off = bi_ * 512
                    em.tt(Dl[:, off:off + 512], y_t[:, off:off + 512], b_[0:64, 0:512], ALU.subtract)
                if scan_secs < 9:
                    continue
                for h in range(12):
                    b_ = (pp[0], pp[1], pp[2])[h // 4]
                    cs_ = (h % 4) * 128
                    em.mm(b_[0:64, cs_:cs_ + 128], qn[h][:, c0:c0 + 64],
                          Sbf[:, h * 128:(h + 1) * 128], inc=(1 if h % 4 == 3 else 0))
                for bi_, b_ in enumerate((pp[0], pp[1], pp[2])):
                    off = bi_ * 512
                    em.tt(h3(Ot[:, off:off + 512], 4, 128), h3(b_[0:64, 0:512], 4, 128),
                          stretch(b_tok, bi_ * 4, 4, 128), ALU.mult)
                for h in range(12):
                    b_ = (pp[3], pp[4], pp[5])[h // 4]
                    cs_ = (h % 4) * 128
                    em.mm(b_[0:64, cs_:cs_ + 128], WTmat[:, h * 64:h * 64 + 64],
                          Dl[:, h * 128:(h + 1) * 128], inc=(1 if h % 4 == 3 else 0))
                for bi_, b_ in enumerate((pp[3], pp[4], pp[5])):
                    off = bi_ * 512
                    em.tt(Ot[:, off:off + 512], Ot[:, off:off + 512], b_[0:64, 0:512], ALU.add)
                em.act(o2[:], Ot[:], AF.Square)
                for h in range(12):
                    em.red(ovar[:, h:h + 1], o2[:, h * 128:(h + 1) * 128], ALU.add)
                em.ts(ovar[:, 0:12], ovar[:, 0:12], 1.0 / 128, op0=ALU.mult)
                em.act(ovar[:, 0:12], ovar[:, 0:12], AF.Ln, bias=1e-5)
                em.act(ovar[:, 0:12], ovar[:, 0:12], AF.Exp, scale=-0.5)
                em.tt(h3(o2[:], 12, 128), h3(Ot[:], 12, 128),
                      ovar[:, 0:12].unsqueeze(2).broadcast_to((64, 12, 128)), ALU.mult)
                em.tt(o2[:], o2[:], gnwmat[:], ALU.mult)
                em.act(y2_t[:], g_sc[:], AF.Sigmoid)
                em.tt(y_t[:], g_sc[:], y2_t[:], ALU.mult)
                em.tt(y2_t[:], o2[:], y_t[:], ALU.mult)
                em.dma(og_s[tbase + c0:tbase + c0 + 64, :], y2_t[:], q="sp")
                if scan_secs < 10:
                    continue
                em.tt(h3(y_t[:, 0:768], 12, 64), h3(ktok[0:64, :], 12, 64),
                      stretch(kds_tok, 0, 12, 64), ALU.mult)
                for h in range(12):
                    b_ = (pp[0], pp[1], pp[2])[h // 4]
                    cs_ = (h % 4) * 128
                    em.mm(b_[0:64, cs_:cs_ + 128], y_t[:, h * 64:h * 64 + 64],
                          Dl[:, h * 128:(h + 1) * 128], inc=(1 if h % 4 == 3 else 0))
                em.tt(h3(S[:], 12, 128), h3(S[:], 12, 128),
                      stretch(bc_tok, 0, 12, 128), ALU.mult)
                for bi_, b_ in enumerate((pp[0], pp[1], pp[2])):
                    off = bi_ * 512
                    em.tt(S[:, off:off + 512], S[:, off:off + 512], b_[0:64, 0:512], ALU.add)
                em.copy(Sbf[:], S[:])
                if ch == 0 and sc == 0 and "d_gam" in OUT:
                    em.dma(OUT["d_gam"], gam[:], q="sp")
                    em.dma(OUT["d_beta"], beta_f[:, 0:64], q="sp")
                    em.dma(OUT["d_E0"], E0U[:], q="sp")
                    em.dma(OUT["d_Amat"], Amat[:], q="sp")
                    em.dma(OUT["d_ATmat"], ATmat[:], q="sp")
                    em.dma(OUT["d_WTmat"], WTmat[:], q="sp")
                    em.dma(OUT["d_Dl"], Dl[:], q="sp")
                    em.dma(OUT["d_Ot"], Ot[:], q="sp")
                    em.dma(OUT["d_S"], S[0:64, :], q="sp")
                    em.dma(OUT["d_gk"], gkb[:], q="sp")
    with emscope(em):
        ogT = [em.sb(f"la_ogT{v}", [128, 512], BF16, tdest=True) for v in range(12)]
        a0t = em.sb("la_a0t", [128, 1024], BF16)
        wo_sb = [em.sb(f"la_wo{v}", [128, 1024], BF16) for v in range(12)]
        for v in range(12):
            em.dma(wo_sb[v][:], IN["wo_bf"][v * 128:(v + 1) * 128, :], q="sp")
        for gg in range(n_chunks):
            t0 = gg * 512
            for v in range(12):
                em.dmat(ogT[v][:], og_s[t0:t0 + 512, v * 128:(v + 1) * 128], q="sp")
            for tt in range(4):
                for oc in range(2):
                    bank = pp[oc]
                    for v in range(12):
                        em.mm(bank[:], ogT[v][:, tt * 128:(tt + 1) * 128],
                              wo_sb[v][:, oc * 512:(oc + 1) * 512],
                              start=(v == 0), stop=(v == 11), inc=(1 if v == 11 else 0))
                    em.copy(a0t[:, oc * 512:(oc + 1) * 512], bank[:])
                em.dma(a0_s[t0 + tt * 128:t0 + tt * 128 + 128, :], a0t[:], q="sp")


L0A_INS = dict(
    x=([T, D], F32),
    wq_bf=([1024, 768], BF16), wk_bf=([1024, 768], BF16), wv_bf=([1024, 1536], BF16),
    wg_bf=([1024, 1536], BF16), wo_bf=([1536, 1024], BF16), wgkb_bf=([1024, 44], BF16),
    L0_attn_conv_q=([768, 4], F32), L0_attn_conv_k=([768, 4], F32),
    L0_attn_conv_v=([1536, 4], F32),
    L0_attn_A_log=([12], F32), L0_attn_dt_bias=([12], F32), L0_attn_b_b=([12], F32),
    L0_attn_g_norm_weight=([128], F32),
    c_ident=([128, 128], BF16), c_ms=([CS, CS], BF16), c_bigL=([CS, CS], F32),
    c_bigUi=([CS, CS], F32), c_bigUs=([CS, CS], F32),
)
L0A_OUTS = dict(xn_s=([T, D], BF16), a0_s=([T, D], BF16),
                kn_s=([768, T], BF16), vf_s=([1536, T], BF16), og_s=([T, 1536], BF16))
L0A_DBG = dict(d_gam=([12, 64], F32), d_beta=([12, 64], F32), d_E0=([64, 768], BF16),
               d_Amat=([64, 768], BF16), d_ATmat=([64, 768], BF16), d_WTmat=([64, 768], BF16),
               d_Dl=([64, 1536], BF16), d_Ot=([64, 1536], BF16), d_S=([128, 768], F32),
               d_gk=([12, 512], F32))



def emit_l1(em, IN, OUT, n_chunks=8):
    nc = em.nc
    ident = em.sb("l1_ident", [128, 128], BF16)
    em.dma(ident[:], IN["c_ident"], q="sp")
    onesb = em.sb("l1_onesb", [1, 128], BF16)
    em.dma(onesb[:], IN["c_ones"], q="sp")
    onescol = em.sb("l1_onescol", [128, 1], BF16)
    em.memset(onescol[:], 1.0)
    pp = [em.ps(f"l1_pp{i}", [128, 512]) for i in range(6)]
    ps_tr = em.ps("l1_pstr", [128, 128], BF16)

    x_t = em.sb("l1_xt", [128, 1024], BF16)
    yn_t = em.sb("l1_ynt", [128, 1024], BF16)
    st = em.sb("l1_st", [128, 32], F32)
    sqf = em.sb("l1_sqf", [64, 512], F32)
    fA = em.sb("l1_fA", [128, 1024], F32)
    fB = em.sb("l1_fB", [128, 1024], F32)
    fC = em.sb("l1_fC", [128, 1024], F32)
    fD = em.sb("l1_fD", [128, 1024], F32)
    iA = em.sb("l1_iA", [128, 1024], I32)
    ynT = [em.sb(f"l1_ynT{d}", [128, 512], BF16, tdest=True) for d in range(8)]
    yn_s, a1_s = OUT["yn_s"], OUT["a1_s"]
    sc_s, xq_s = OUT["sc_s"], OUT["xq_s"]

    for tt in range(32):
        trow = tt * 128
        em.dma(x_t[:], IN["x1_s"][trow:trow + 128, :], q="sp")
        em.red(st[:, 0:1], x_t[:], ALU.add)
        em.act(sqf[0:64, :], x_t[0:64, 0:512], AF.Square, accum_out=st[0:64, 1:2])
        em.act(sqf[0:64, :], x_t[64:128, 0:512], AF.Square, accum_out=st[64:128, 1:2])
        em.act(sqf[0:64, :], x_t[0:64, 512:1024], AF.Square, accum_out=st[0:64, 2:3])
        em.act(sqf[0:64, :], x_t[64:128, 512:1024], AF.Square, accum_out=st[64:128, 2:3])
        em.tt(st[:, 1:2], st[:, 1:2], st[:, 2:3], ALU.add)
        em.ts(st[:, 3:4], st[:, 0:1], 1.0 / 1024, op0=ALU.mult)
        em.ts(st[:, 4:5], st[:, 1:2], 1.0 / 1024, op0=ALU.mult)
        em.ts(st[:, 5:6], st[:, 3:4], st[:, 3:4], op0=ALU.mult)
        em.tt(st[:, 6:7], st[:, 4:5], st[:, 5:6], ALU.subtract)
        em.act(st[:, 7:8], st[:, 6:7], AF.Ln, bias=EPS_LN)
        em.act(st[:, 8:9], st[:, 7:8], AF.Exp, scale=-0.5)
        em.ts(st[:, 9:10], st[:, 3:4], st[:, 8:9], s2=-1.0, op0=ALU.mult, op1=ALU.mult)
        em.act(yn_t[:], x_t[:], AF.Identity, scale=st[:, 8:9], bias=st[:, 9:10])
        em.dma(yn_s[trow:trow + 128, :], yn_t[:], q="sp")

    with emscope(em):
        wk_sb = [em.sb(f"l1_wk{d}", [128, 1024], BF16) for d in range(8)]
        wv_sb = [em.sb(f"l1_wv{d}", [128, 1024], BF16) for d in range(8)]
        for d in range(8):
            em.dma(wk_sb[d][:], IN["wk1_bf"][d * 128:(d + 1) * 128, :], q="sp")
            em.dma(wv_sb[d][:], IN["wv1_bf"][d * 128:(d + 1) * 128, :], q="sp")
        bkrow = em.sb("l1_bkrow", [1, 1024], BF16)
        em.dma(bkrow[:], IN["L1_attn_bk"].rearrange("(r c) -> r c", r=1), q="pool")
        bvrow = em.sb("l1_bvrow", [1, 1024], BF16)
        em.dma(bvrow[:], IN["L1_attn_bv"].rearrange("(r c) -> r c", r=1), q="pool")
        k_t = em.sb("l1_kt", [128, 1024], BF16)
        v_t = em.sb("l1_vt", [128, 1024], BF16)
        for tt in range(32):
            trow = tt * 128
            for d in range(8):
                em.dmat(ynT[d][:, 0:128], yn_s[trow:trow + 128, d * 128:(d + 1) * 128], q="sp")
            for half, (wsb, brow, dst, do_elu) in enumerate(
                    [(wk_sb, bkrow, k_t, True), (wv_sb, bvrow, v_t, False)]):
                for oc in range(2):
                    bank = pp[4 + oc]
                    for d in range(8):
                        em.mm(bank[:], ynT[d][:, 0:128], wsb[d][:, oc * 512:(oc + 1) * 512],
                              start=(d == 0), stop=False, inc=0)
                    em.mm(bank[:], onesb[0:1, 0:128], brow[0:1, oc * 512:(oc + 1) * 512],
                          start=False, stop=True)
                    sl_ = slice(oc * 512, (oc + 1) * 512)
                    if do_elu:
                        em.ts(fA[:, 0:512], bank[:], 0.0, op0=ALU.min)
                        em.act(fB[:, 0:512], fA[:, 0:512], AF.Exp)
                        em.stt(dst[:, sl_], bank[:], 0.0, fB[:, 0:512], ALU.max, ALU.add)
                    else:
                        em.copy(dst[:, sl_], bank[:])
            for h in range(16):
                b_ = pp[0] if h < 8 else pp[1]
                cs_ = (h % 8) * 64
                em.mm(b_[0:64, cs_:cs_ + 64], k_t[:, h * 64:(h + 1) * 64],
                      v_t[:, h * 64:(h + 1) * 64], start=(tt == 0), stop=(tt == 31),
                      inc=(1 if (tt == 31 and h in (7, 15)) else 0))
            for oc in range(2):
                em.mm(pp[2 + 0][0:1, 0:512] if oc == 0 else pp[3][0:1, 0:512],
                      onescol[:, 0:1], k_t[:, oc * 512:(oc + 1) * 512],
                      start=(tt == 0), stop=(tt == 31),
                      inc=(1 if tt == 31 else 0))
        kv_sb = em.sb("l1_kvsb", [64, 1024], BF16)
        em.copy(kv_sb[:, 0:512], pp[0][0:64, :])
        em.copy(kv_sb[:, 512:1024], pp[1][0:64, :])
        ks_row = em.sb("l1_ksrow", [1, 1024], BF16)
        em.copy(ks_row[0:1, 0:512], pp[2][0:1, 0:512])
        em.copy(ks_row[0:1, 512:1024], pp[3][0:1, 0:512])
        ks_col = em.sb("l1_kscol", [64, 16], BF16)
        for h in range(16):
            em.tr(ps_tr[0:64, 0:1], ks_row[0:1, h * 64:(h + 1) * 64], ident[0:1, 0:1])
            em.copy(ks_col[:, h:h + 1], ps_tr[0:64, 0:1])

        wq_sb = [em.sb(f"l1_wq{d}", [128, 1024], BF16) for d in range(8)]
        for d in range(8):
            em.dma(wq_sb[d][:], IN["wq1_bf"][d * 128:(d + 1) * 128, :], q="sp")
        bq_col = em.sb("l1_bqcol", [64, 16], F32)
        for hh in range(16):
            em.dma(bq_col[:, hh:hh + 1],
                   IN["L1_attn_bq"][hh * 64:(hh + 1) * 64].rearrange("(p c) -> p c", c=1), q="sp")
        qh = [em.sb(f"l1_qh{h}", [64, 512], BF16) for h in range(16)]
        zt = em.sb("l1_zt", [128, 16], F32)
        sc_t = em.sb("l1_sct", [128, 1024], BF16)
        for chk in range(n_chunks):
            tbase = chk * 512
            for d in range(8):
                em.dmat(ynT[d][:], yn_s[tbase:tbase + 512, d * 128:(d + 1) * 128], q="sp")
            for h in range(16):
                bank = pp[4 + h % 2]
                for d in range(8):
                    em.mm(bank[0:64, :], wq_sb[d][:, h * 64:(h + 1) * 64], ynT[d][:],
                          start=(d == 0), stop=(d == 7), inc=(1 if d == 7 else 0))
                em.act(fA[0:64, 0:512], bank[0:64, :], AF.Identity,
                       bias=bq_col[:, h:h + 1])
                em.ts(fB[0:64, 0:512], fA[0:64, 0:512], 0.0, op0=ALU.min)
                em.act(fC[0:64, 0:512], fB[0:64, 0:512], AF.Exp)
                em.stt(qh[h][:], fA[0:64, 0:512], 0.0, fC[0:64, 0:512], ALU.max, ALU.add)
            for ts4 in range(4):
                t0 = ts4 * 128
                for h in range(16):
                    em.mm(pp[2][:, h:h + 1], qh[h][:, t0:t0 + 128], ks_col[:, h:h + 1],
                          inc=(1 if h == 15 else 0))
                em.ts(zt[:], pp[2][:, 0:16], 1e-6, op0=ALU.add)
                em.recip(zt[:], zt[:])
                for h in range(16):
                    b_ = pp[0] if h < 8 else pp[1]
                    cs_ = (h % 8) * 64
                    em.mm(b_[:, cs_:cs_ + 64], qh[h][:, t0:t0 + 128],
                          kv_sb[:, h * 64:(h + 1) * 64], inc=(1 if h in (7, 15) else 0))
                for bi_, b_ in enumerate((pp[0], pp[1])):
                    em.tt(sc_t[:, bi_ * 512:(bi_ + 1) * 512].rearrange("p (h s) -> p h s", s=64),
                          b_[:, 0:512].rearrange("p (h s) -> p h s", s=64),
                          zt[:, bi_ * 8:(bi_ + 1) * 8].unsqueeze(2).broadcast_to((128, 8, 64)),
                          ALU.mult)
                em.dma(sc_s[tbase + t0:tbase + t0 + 128, :], sc_t[:], q="sp")

    with emscope(em):
        wo_sb = [em.sb(f"l1_wo{d}", [128, 1024], BF16) for d in range(8)]
        for d in range(8):
            em.dma(wo_sb[d][:], IN["wo1_bf"][d * 128:(d + 1) * 128, :], q="sp")
        borow = em.sb("l1_borow", [1, 1024], BF16)
        em.dma(borow[:], IN["L1_attn_bo"].rearrange("(r c) -> r c", r=1), q="pool")
        scT = [em.sb(f"l1_scT{d}", [128, 512], BF16, tdest=True) for d in range(8)]
        a1_t = em.sb("l1_a1t", [128, 1024], BF16)
        for chk in range(n_chunks):
            tbase = chk * 512
            for d in range(8):
                em.dmat(scT[d][:], sc_s[tbase:tbase + 512, d * 128:(d + 1) * 128], q="sp")
            for tt in range(4):
                for oc in range(2):
                    bank = pp[4 + oc]
                    for d in range(8):
                        em.mm(bank[:], scT[d][:, tt * 128:(tt + 1) * 128],
                              wo_sb[d][:, oc * 512:(oc + 1) * 512],
                              start=(d == 0), stop=False, inc=0)
                    em.mm(bank[:], onesb[0:1, 0:128], borow[0:1, oc * 512:(oc + 1) * 512],
                          start=False, stop=True)
                    em.copy(a1_t[:, oc * 512:(oc + 1) * 512], bank[:])
                em.dma(a1_s[tbase + tt * 128:tbase + tt * 128 + 128, :], a1_t[:], q="sp")

    with emscope(em):
        h_sb = [em.sb(f"l1_h{d}", [128, 1024], BF16) for d in range(8)]
        for d in range(8):
            em.dma(h_sb[d][:], IN["c_had"][d * 128:(d + 1) * 128, :], q="sp")
        xq_t = em.sb("l1_xqt", [128, 1024], BF16)
        for chk in range(n_chunks):
            tbase = chk * 512
            for d in range(8):
                em.dmat(ynT[d][:], yn_s[tbase:tbase + 512, d * 128:(d + 1) * 128], q="sp")
            for tt in range(4):
                for oc in range(2):
                    bank = pp[4 + oc]
                    for d in range(8):
                        em.mm(bank[:], ynT[d][:, tt * 128:(tt + 1) * 128],
                              h_sb[d][:, oc * 512:(oc + 1) * 512],
                              start=(d == 0), stop=(d == 7), inc=(1 if d == 7 else 0))
                    em.copy(fA[:, oc * 512:(oc + 1) * 512], bank[:])
                em.red(st[:, 0:1], fA[:], ALU.max, absval=True)
                em.ts(st[:, 1:2], st[:, 0:1], 1e-5, op0=ALU.max)
                em.ts(st[:, 2:3], st[:, 1:2], 1.0 / 7.0, op0=ALU.mult)
                em.recip(st[:, 3:4], st[:, 2:3])
                em.ts(fB[:], fA[:], st[:, 3:4], op0=ALU.mult)
                em.copy(iA[:], fB[:])
                em.copy(fC[:], iA[:])
                e = iA[:].bitcast(F32)
                em.tt(e, fB[:], fC[:], ALU.subtract)
                em.ts(fD[:], e, 0.5, op0=ALU.is_ge)
                em.stt(fD[:], e, -0.5, fD[:], ALU.is_lt, ALU.subtract)
                em.tt(fC[:], fC[:], fD[:], ALU.subtract)
                em.ts(xq_t[:], fC[:], 7.0, -8.0, op0=ALU.min, op1=ALU.max)
                trow = tbase + tt * 128
                em.dma(xq_s[trow:trow + 128, :], xq_t[:], q="sp")
                em.dma(OUT["s1q_s"][trow:trow + 128, :], st[:, 2:3], q="sp")

    with emscope(em):
        wf_sb = [em.sb(f"l1_wf{d}", [128, 1024], BF16, tdest=True) for d in range(8)]
        for d in range(8):
            em.dmat(wf_sb[d][:], IN["wf1q_rows"][:, d * 128:(d + 1) * 128], q="sp")
        xqT = [em.sb(f"l1_xqT{d}", [128, 512], BF16, tdest=True) for d in range(8)]
        x1_t = em.sb("l1_x1t", [128, 1024], BF16)
        a1r_t = em.sb("l1_a1r", [128, 1024], BF16)
        sq_c = em.sb("l1_sqc", [128, 1], F32)
        out_t = em.sb("l1_outt", [128, 1024], F32)
        for chk in range(n_chunks):
            tbase = chk * 512
            for d in range(8):
                em.dmat(xqT[d][:], xq_s[tbase:tbase + 512, d * 128:(d + 1) * 128], q="sp")
            for tt in range(4):
                trow = tbase + tt * 128
                em.dma(x1_t[:], IN["x1_s"][trow:trow + 128, :], q="sp")
                em.dma(a1r_t[:], a1_s[trow:trow + 128, :], q="sp")
                em.dma(sq_c[:], OUT["s1q_s"][trow:trow + 128, :], q="sp")
                for oc in range(2):
                    bank = pp[4 + oc]
                    for d in range(8):
                        em.mm(bank[:], xqT[d][:, tt * 128:(tt + 1) * 128],
                              wf_sb[d][:, oc * 512:(oc + 1) * 512],
                              start=(d == 0), stop=(d == 7), inc=(1 if d == 7 else 0))
                    sl_ = slice(oc * 512, (oc + 1) * 512)
                    em.stt(fA[:, 0:512], bank[:], sq_c[:, 0:1], x1_t[:, sl_], ALU.mult, ALU.add)
                    em.tt(out_t[:, sl_], fA[:, 0:512], a1r_t[:, sl_], ALU.add)
                em.dma(OUT["out"][trow:trow + 128, :], out_t[:], q="sp")


L1_INS = dict(
    x1_s=([T, D], BF16),
    wq1_bf=([1024, 1024], BF16), wk1_bf=([1024, 1024], BF16),
    wv1_bf=([1024, 1024], BF16), wo1_bf=([1024, 1024], BF16),
    L1_attn_bq=([1024], F32), L1_attn_bk=([1024], F32),
    L1_attn_bv=([1024], F32), L1_attn_bo=([1024], F32),
    wf1q_rows=([1024, 1024], BF16),
    c_ident=([128, 128], BF16), c_ones=([1, 128], BF16), c_had=([1024, 1024], BF16),
)
L1_OUTS = dict(out=([T, D], F32), yn_s=([T, D], BF16), a1_s=([T, D], BF16),
               sc_s=([T, D], BF16), xq_s=([T, D], BF16), s1q_s=([T, 1], F32))



_CACHE = {}


def _phase(name, emit_fn, ins_spec, outs_spec):
    if name not in _CACHE:
        _CACHE[name] = build_phase(emit_fn, ins_spec, outs_spec)
    return _CACHE[name]


def kernel(**inputs):
    inputs = {k: np.ascontiguousarray(v) for k, v in inputs.items()}
    consts = host_consts()
    NCORE = 8
    cores = list(range(NCORE))

    prep_nc = _phase("prep", emit_prep, {n: (INPUT_SHAPES[n], F32) for n in PREP_INS}, PREP_OUTS)
    prep_in = {n: inputs[n].astype(np.float32) for n in PREP_INS}
    prep_res = run_bass_kernel_spmd(prep_nc, [prep_in] * NCORE, core_ids=cores)
    pw = prep_res.results

    l0a_nc = _phase("l0a", emit_l0a, L0A_INS, L0A_OUTS)
    l0a_maps = []
    for c in cores:
        m = dict(x=inputs["L0_attn_hidden_states"][c].astype(np.float32),
                 wq_bf=pw[c]["wq_bf"], wk_bf=pw[c]["wk_bf"], wv_bf=pw[c]["wv_bf"],
                 wg_bf=pw[c]["wg_bf"], wo_bf=pw[c]["wo_bf"], wgkb_bf=pw[c]["wgkb_bf"],
                 L0_attn_conv_q=inputs["L0_attn_conv_q"], L0_attn_conv_k=inputs["L0_attn_conv_k"],
                 L0_attn_conv_v=inputs["L0_attn_conv_v"],
                 L0_attn_A_log=inputs["L0_attn_A_log"], L0_attn_dt_bias=inputs["L0_attn_dt_bias"],
                 L0_attn_b_b=inputs["L0_attn_b_b"],
                 L0_attn_g_norm_weight=inputs["L0_attn_g_norm_weight"],
                 c_ident=consts["c_ident"], c_ms=consts["c_ms"], c_bigL=consts["c_bigL"],
                 c_bigUi=consts["c_bigUi"], c_bigUs=consts["c_bigUs"])
        l0a_maps.append(m)
    l0a_res = run_bass_kernel_spmd(l0a_nc, l0a_maps, core_ids=cores).results

    l0f_nc = _phase("l0f", emit_l0f, L0F_INS, L0F_OUTS)
    l0f_maps = []
    for c in cores:
        m = dict(x=inputs["L0_attn_hidden_states"][c].astype(np.float32),
                 xn_s=l0a_res[c]["xn_s"], a0_s=l0a_res[c]["a0_s"],
                 w1q_rows=pw[c]["w1q_rows"], w2q_rows=pw[c]["w2q_rows"],
                 L0_ffn_ln1_w=inputs["L0_ffn_ln1_w"], L0_ffn_ln1_b=inputs["L0_ffn_ln1_b"],
                 L0_ffn_ln2_w=inputs["L0_ffn_ln2_w"], L0_ffn_ln2_b=inputs["L0_ffn_ln2_b"],
                 L0_ffn_b1=inputs["L0_ffn_b1"], L0_ffn_b2=inputs["L0_ffn_b2"],
                 c_ones=consts["c_ones"], c_ident=consts["c_ident"])
        l0f_maps.append(m)
    l0f_res = run_bass_kernel_spmd(l0f_nc, l0f_maps, core_ids=cores).results

    l1_nc = _phase("l1", emit_l1, L1_INS, L1_OUTS)
    l1_maps = []
    for c in cores:
        m = dict(x1_s=l0f_res[c]["x1_s"],
                 wq1_bf=pw[c]["wq1_bf"], wk1_bf=pw[c]["wk1_bf"],
                 wv1_bf=pw[c]["wv1_bf"], wo1_bf=pw[c]["wo1_bf"],
                 L1_attn_bq=inputs["L1_attn_bq"], L1_attn_bk=inputs["L1_attn_bk"],
                 L1_attn_bv=inputs["L1_attn_bv"], L1_attn_bo=inputs["L1_attn_bo"],
                 wf1q_rows=pw[c]["wf1q_rows"],
                 c_ident=consts["c_ident"], c_ones=consts["c_ones"], c_had=consts["c_had"])
        l1_maps.append(m)
    l1_res = run_bass_kernel_spmd(l1_nc, l1_maps, core_ids=cores).results

    out = np.stack([l1_res[c]["out"] for c in cores], axis=0)
    return out


# version 02
# speedup vs baseline: 1.0028x; 1.0028x over previous
import numpy as np
import ml_dtypes
import concourse.bass as bass
import concourse.mybir as mybir
from concourse.bass_utils import run_bass_kernel_spmd

F32 = mybir.dt.float32
BF16 = mybir.dt.bfloat16
I32 = mybir.dt.int32
AF = mybir.ActivationFunctionType
ALU = mybir.AluOpType
AX = mybir.AxisListType
BF = ml_dtypes.bfloat16

T = 4096
D = 1024
NH, DK, DV = 12, 64, 128
KD, VD = NH * DK, NH * DV
CS = 64
HID = 4096
H1, DH1 = 16, 64
EPS_LN = 1e-5



class Em:
    QUEUES = ("pe", "act", "dve", "pool", "sp")
    NLANES = 8

    def __init__(self, nc, sems):
        self.nc = nc
        self.eng = dict(pe=nc.tensor, act=nc.scalar, dve=nc.vector,
                        pool=nc.gpsimd, sp=nc.sync)
        self.sems = sems
        self.cnt = {k: 0 for k in sems}
        self.lane = {"sp": 0, "pool": 0}
        self.acc = {}
        self.waited = {q: {} for q in self.QUEUES}

    @staticmethod
    def _rng(ap):
        lo = hi = ap.offset
        for step, count in ap.ap:
            d = step * (count - 1)
            if d >= 0:
                hi += d
            else:
                lo += d
        return ap.tensor.name, lo, hi

    def _deps_for(self, ap, write):
        name, lo, hi = self._rng(ap)
        deps = []
        for rec in self.acc.get(name, ()):
            if rec[1] < lo or rec[0] > hi:
                continue
            if rec[4] or write:
                deps.append((rec[2], rec[3]))
        return deps

    def _record(self, ap, q, cnt, write):
        name, lo, hi = self._rng(ap)
        lst = self.acc.setdefault(name, [])
        if write:
            lst[:] = [r for r in lst if not (lo <= r[0] and r[1] <= hi)]
        lst.append([lo, hi, q, cnt, write])
        if len(lst) > 400:
            del lst[: len(lst) - 250]

    def _wait(self, q, deps):
        need = {}
        for dq, c in deps:
            if dq == q == "pe":
                continue
            if c is None:
                raise RuntimeError("dependency on un-inc'd (lazy) op")
            if need.get(dq, 0) < c:
                need[dq] = c
        for dq, c in need.items():
            if self.waited[q].get(dq, 0) >= c:
                continue
            self.eng[q].wait_ge(self.sems[dq], c)
            self.waited[q][dq] = c

    def op(self, q, fname, outs, ins, inc=1, _call=None, **kw):
        deps = []
        for ap in ins:
            if isinstance(ap, (int, float)) or ap is None:
                continue
            deps += self._deps_for(ap, False)
        for ap in outs:
            if ap is None:
                continue
            deps += self._deps_for(ap, True)
        self._wait(q, deps)
        inst = (_call or getattr(self.eng[q], fname))(**kw)
        c = None
        if inc:
            inst.then_inc(self.sems[q], inc)
            self.cnt[q] += inc
            c = self.cnt[q]
        else:
            c = self.cnt[q] + 1
        for ap in ins:
            if isinstance(ap, (int, float)) or ap is None:
                continue
            self._record(ap, q, c, False)
        for ap in outs:
            if ap is None:
                continue
            self._record(ap, q, c, True)
        return inst

    def _dma_issue(self, q, outs, ins, transpose=False, **kw):
        ln = self.lane[q]
        self.lane[q] = (ln + 1) % self.NLANES
        key = f"{q}{ln}"
        deps = []
        for ap in ins:
            deps += self._deps_for(ap, False)
        for ap in outs:
            deps += self._deps_for(ap, True)
        deps.append((key, self.cnt[key]))
        if not hasattr(self, "_cat"):
            self._cat = {"tr": {}, "ss": {}}
        is_ss = (not transpose and ins[0].tensor.space.name == "SBUF"
                 and outs[0].tensor.space.name == "SBUF")
        if transpose:
            deps += list(self._cat["ss"].items())
        elif is_ss:
            deps += list(self._cat["tr"].items())
        self._wait(q, deps)
        if transpose:
            kw["transpose"] = True
        inst = self.eng[q].dma_start(**kw)
        inst.then_inc(self.sems[key], 16)
        self.cnt[key] += 16
        c = self.cnt[key]
        if transpose:
            self._cat["tr"][key] = c
        elif is_ss:
            self._cat["ss"][key] = c
        for ap in ins:
            self._record(ap, key, c, False)
        for ap in outs:
            self._record(ap, key, c, True)
        return inst

    def dma(self, out, in_, q="sp", **kw):
        return self._dma_issue(q, [out], [in_], out=out, in_=in_, **kw)

    def dmat(self, out, in_, q="sp"):
        return self._dma_issue(q, [out], [in_], transpose=True, out=out, in_=in_)

    def mm(self, out, lhsT, rhs, start=True, stop=True, inc=1, tile_position=None):
        return self.op("pe", "matmul", [out], [lhsT, rhs], inc=inc, out=out,
                       lhsT=lhsT, rhs=rhs, start=start, stop=stop,
                       tile_position=tile_position)

    def tr(self, out, in_, ident, inc=1):
        return self.op("pe", "transpose", [out], [in_, ident], inc=inc,
                       out=out, in_=in_, identity=ident)

    def constf(self, value):
        value = float(value)
        if not hasattr(self, "_consts"):
            self._consts = {}
        if value not in self._consts:
            t = self.stack.enter_context(
                self.nc.sbuf_tensor(f"c_f32_{len(self._consts)}", [128, 1], F32))
            self.memset(t[:], value)
            self._consts[value] = t
        return self._consts[value][:]

    def act(self, out, in_, func, bias=None, scale=None, accum_out=None, q="act"):
        npart = out.ap[0][1]
        if isinstance(bias, (int, float)) and float(bias) != 0.0:
            bias = self.constf(bias)[0:npart, 0:1]
        if isinstance(scale, (int, float)) and float(scale) == 1.0:
            scale = None
        if isinstance(scale, (int, float)):
            scale = self.constf(scale)[0:npart, 0:1]
        kw = dict(out=out, in_=in_, func=func)
        ins = [in_]
        if bias is not None and not (isinstance(bias, float) and bias == 0.0):
            kw["bias"] = bias
            if not isinstance(bias, (int, float)):
                ins.append(bias)
        if scale is not None:
            kw["scale"] = scale
            if not isinstance(scale, (int, float)):
                ins.append(scale)
        if accum_out is not None:
            kw["accum_out"] = accum_out
        return self.op(q, "activation", [out, accum_out], ins, **kw)

    def ts(self, out, in0, s1, s2=None, op0=ALU.mult, op1=None, q="dve", accum_out=None):
        ins = [in0] + [s for s in (s1, s2) if s is not None and not isinstance(s, (int, float))]
        kw = dict(out=out, in0=in0, scalar1=s1, scalar2=s2, op0=op0)
        if op1 is not None:
            kw["op1"] = op1
        if accum_out is not None:
            kw["accum_out"] = accum_out
        return self.op(q, "tensor_scalar", [out, accum_out], ins, **kw)

    def tt(self, out, in0, in1, alu, q="dve"):
        return self.op(q, "tensor_tensor", [out], [in0, in1], out=out, in0=in0, in1=in1, op=alu)

    def stt(self, out, in0, s, in1, op0, op1, q="dve"):
        ins = [in0, in1] + ([s] if not isinstance(s, (int, float)) else [])
        return self.op(q, "scalar_tensor_tensor", [out], ins, out=out, in0=in0,
                       scalar=s, in1=in1, op0=op0, op1=op1)

    def copy(self, out, in_, q="dve"):
        return self.op(q, "tensor_copy", [out], [in_], out=out, in_=in_)

    def red(self, out, in_, alu, absval=None, q="dve"):
        return self.op(q, "tensor_reduce", [out], [in_], out=out, in_=in_,
                       axis=AX.X, op=alu, apply_absolute_value=absval)

    def scan(self, out, d0, d1, initial, op0, op1, q="dve"):
        return self.op(q, "tensor_tensor_scan", [out], [d0, d1], out=out, data0=d0,
                       data1=d1, initial=initial, op0=op0, op1=op1)

    def memset(self, ap, val, q="dve"):
        return self.op(q, "memset", [ap], [], ap=ap, constant=val)

    def recip(self, out, in_, q="dve"):
        return self.op(q, "reciprocal", [out], [in_], out=out, in_=in_)

    def sb(self, name, shape, dtype, tdest=False):
        return self.stack.enter_context(
            self.nc.sbuf_tensor(name, list(shape), dtype, is_transpose_dest=tdest))

    def ps(self, name, shape, dtype=F32):
        return self.stack.enter_context(self.nc.psum_tensor(name, list(shape), dtype))

    def quiesce(self):
        for q in self.QUEUES:
            for dk, c in self.cnt.items():
                if dk != q and c and self.waited[q].get(dk, 0) < c:
                    self.eng[q].wait_ge(self.sems[dk], c)
                    self.waited[q][dk] = c

    def barrier_store_done(self):
        for dk, c in self.cnt.items():
            if dk != "sp" and c:
                if self.waited["sp"].get(dk, 0) < c:
                    self.eng["sp"].wait_ge(self.sems[dk], c)
                    self.waited["sp"][dk] = c
        for ln in range(self.NLANES):
            key = f"sp{ln}"
            if self.cnt.get(key, 0):
                self.eng["sp"].wait_ge(self.sems[key], self.cnt[key])


def rsqrt_em(em, out, in_, eps, q="act", scratch=None):
    s = scratch if scratch is not None else out
    em.act(s, in_, AF.Sqrt, bias=float(eps), q=q)
    em.recip(out, s)



def host_consts():
    ident = np.eye(128, dtype=BF)
    ones_row = np.ones((1, 128), dtype=BF)
    t_idx = np.arange(CS)[:, None]
    s_idx = np.arange(CS)[None, :]
    ms = np.tril(np.ones((CS, CS), np.float32), -1).astype(BF)
    bigL = (30000.0 * (s_idx > t_idx)).astype(np.float32)
    bigUi = (-30000.0 * (t_idx > s_idx)).astype(np.float32)
    bigUs = (-30000.0 * (t_idx >= s_idx)).astype(np.float32)
    h = np.array([[1.0]], dtype=np.float64)
    while h.shape[0] < 1024:
        h = np.block([[h, h], [h, -h]])
    had = (h / np.sqrt(1024)).astype(np.float32).astype(BF)
    return dict(c_ident=ident, c_ones=ones_row, c_ms=ms, c_bigL=bigL,
                c_bigUi=bigUi, c_bigUs=bigUs, c_had=had)


CONST_SPECS = dict(c_ident=([128, 128], BF16), c_ones=([1, 128], BF16),
                   c_ms=([CS, CS], BF16), c_bigL=([CS, CS], F32),
                   c_bigUi=([CS, CS], F32), c_bigUs=([CS, CS], F32),
                   c_had=([1024, 1024], BF16))



def round_chain(em, out_bf, x_f32, tmp_i32, tmp_a, tmp_b, lo, hi, scale_col=None,
                post_mul_col=None, q1="dve", q2="dve"):
    if scale_col is not None:
        em.ts(tmp_a, x_f32, scale_col, op0=ALU.mult, q=q1)
        src = tmp_a
    else:
        src = x_f32
    em.copy(tmp_i32, src, q=q1)
    em.copy(tmp_b, tmp_i32, q=q1)
    em.tt(tmp_i32.bitcast(F32), src, tmp_b, ALU.subtract, q=q2)
    e = tmp_i32.bitcast(F32)
    em.ts(tmp_a, e, 0.5, op0=ALU.is_ge, q=q2)
    em.stt(tmp_a, e, -0.5, tmp_a, ALU.is_lt, ALU.subtract, q=q2)
    em.tt(tmp_b, tmp_b, tmp_a, ALU.subtract, q=q2)
    if post_mul_col is None:
        em.ts(out_bf, tmp_b, float(hi), float(lo), op0=ALU.min, op1=ALU.max, q=q1)
    else:
        em.ts(tmp_b, tmp_b, float(hi), float(lo), op0=ALU.min, op1=ALU.max, q=q1)
        em.ts(out_bf, tmp_b, post_mul_col, op0=ALU.mult, q=q1)


def emit_prep(em, IN, OUT):
    nc = em.nc
    sw = em.sb("pr_sw", [128, 4096], F32)
    swb = em.sb("pr_swb", [128, 4096], F32)
    swc = em.sb("pr_swc", [128, 4096], F32)
    swi = em.sb("pr_swi", [128, 4096], I32)
    bo0 = em.sb("pr_bo0", [128, 4096], BF16)
    bo1 = em.sb("pr_bo1", [128, 4096], BF16)
    acc = em.sb("pr_acc", [128, 1], F32)
    tpart = em.sb("pr_tp", [128, 1], F32)
    sc = em.sb("pr_sc", [128, 8], F32)
    onesf = em.sb("pr_onesf", [128, 1], F32)
    onesr = em.sb("pr_onesr", [1, 128], F32)
    ps = em.ps("pr_ps", [1, 8])
    ps2 = em.ps("pr_ps2", [128, 8])
    em.memset(onesf[:], 1.0)
    em.memset(onesr[:], 1.0)

    casts = [
        ("L0_attn_Wq", "wq_bf", 1024, 768), ("L0_attn_Wk", "wk_bf", 1024, 768),
        ("L0_attn_Wv", "wv_bf", 1024, 1536), ("L0_attn_Wg", "wg_bf", 1024, 1536),
        ("L0_attn_Wo", "wo_bf", 1536, 1024),
        ("L1_attn_wq", "wq1_bf", 1024, 1024), ("L1_attn_wk", "wk1_bf", 1024, 1024),
        ("L1_attn_wv", "wv1_bf", 1024, 1024), ("L1_attn_wo", "wo1_bf", 1024, 1024),
    ]
    bi = 0
    for src_, dst, rows, cols in casts:
        s, d = IN[src_], OUT[dst]
        for r0 in range(0, rows, 128):
            b = (bo0, bo1)[bi % 2]; bi += 1
            em.dma(b[:, 0:cols], s[r0:r0 + 128, 0:cols], q="pool")
            em.dma(d[r0:r0 + 128, 0:cols], b[:, 0:cols], q="sp")

    for src_, dst, rows in [("L0_attn_conv_q", "convq_bf", 768),
                            ("L0_attn_conv_k", "convk_bf", 768),
                            ("L0_attn_conv_v", "convv_bf", 1536)]:
        ncol = rows * 4 // 128
        em.dma(sw[:, 0:ncol], IN[src_].rearrange("a b -> (a b)").rearrange("(p n) -> p n", p=128), q="sp")
        em.copy(bo0[:, 0:ncol], sw[:, 0:ncol])
        em.dma(OUT[dst].rearrange("a b -> (a b)").rearrange("(p n) -> p n", p=128), bo0[:, 0:ncol], q="sp")

    em.memset(bo1[:, 0:352], 0.0)
    em.dma(OUT["wgkb_bf"].rearrange("a b -> (a b)").rearrange("(p n) -> p n", p=128),
           bo1[:, 0:352], q="sp")
    for src_, c0 in [("L0_attn_Wgk", 0), ("L0_attn_Wb", 32)]:
        em.dma(sw[:, 0:96], IN[src_].rearrange("a b -> (a b)").rearrange("(p n) -> p n", p=128), q="sp")
        em.copy(bo0[:, 0:96], sw[:, 0:96])
        em.dma(OUT["wgkb_bf"][:, c0:c0 + 12].rearrange("(p r) c -> p r c", p=128),
               bo0[:, 0:96].rearrange("p (r c) -> p r c", c=12), q="sp")

    for wname, oname, nelem in [("L0_ffn_w1", "w1q_rows", 4096 * 1024),
                                ("L0_ffn_w2", "w2q_rows", 1024 * 4096)]:
        w = IN[wname]
        wflat = w.rearrange("a b -> (a b)").rearrange("(p n) -> p n", p=128)
        o = OUT[oname]
        oflat = o.rearrange("a b -> (a b)").rearrange("(p n) -> p n", p=128)
        NCOL = nelem // 128
        TL = 4096
        em.memset(acc[:], 0.0)
        for c0 in range(0, NCOL, TL):
            em.dma(sw[:], wflat[:, c0:c0 + TL], q="sp")
            em.act(swb[:], sw[:], AF.Abs, accum_out=tpart[:])
            em.tt(acc[:], acc[:], tpart[:], ALU.add)
        em.mm(ps[0:1, 0:1], acc[:], onesf[:])
        em.copy(sc[0:1, 4:5], ps[0:1, 0:1])
        em.mm(ps2[:, 0:1], onesr[:], sc[0:1, 4:5])
        em.ts(sc[:, 0:1], ps2[:, 0:1], 1.0 / nelem, 1e-5, op0=ALU.mult, op1=ALU.max)
        em.recip(sc[:, 1:2], sc[:, 0:1])
        for c0 in range(0, NCOL, TL):
            em.dma(sw[:], wflat[:, c0:c0 + TL], q="sp")
            round_chain(em, bo0[:], sw[:], swi[:], swb[:], swc[:], -1.0, 1.0,
                        scale_col=sc[:, 1:2], post_mul_col=sc[:, 0:1])
            em.dma(oflat[:, c0:c0 + TL], bo0[:], q="sp")

    w = IN["L1_ffn_weight"]
    o = OUT["wf1q_rows"]
    for r0 in range(0, 1024, 128):
        em.dma(sw[:, 0:1024], w[r0:r0 + 128, :], q="sp")
        em.act(swb[:, 0:1024], sw[:, 0:1024], AF.Abs, accum_out=sc[:, 0:1])
        em.ts(sc[:, 1:2], sc[:, 0:1], 1.0 / 1024, 1e-5, op0=ALU.mult, op1=ALU.max)
        em.recip(sc[:, 2:3], sc[:, 1:2])
        em.ts(swc[:, 0:1024], sw[:, 0:1024], sc[:, 2:3], op0=ALU.mult)
        em.ts(swb[:, 0:1024], swc[:, 0:1024], 0.5, op0=ALU.is_gt)
        em.stt(swb[:, 0:1024], swc[:, 0:1024], -0.5, swb[:, 0:1024], ALU.is_lt, ALU.subtract)
        em.ts(sc[:, 3:4], sc[:, 1:2], -1.0, op0=ALU.mult)
        em.ts(bo0[:, 0:1024], swb[:, 0:1024], sc[:, 3:4], op0=ALU.mult)
        em.dma(o[r0:r0 + 128, :], bo0[:, 0:1024], q="sp")


PREP_OUTS = dict(
    wq_bf=([1024, 768], BF16), wk_bf=([1024, 768], BF16), wv_bf=([1024, 1536], BF16),
    wg_bf=([1024, 1536], BF16), wo_bf=([1536, 1024], BF16), wgkb_bf=([1024, 44], BF16),
    convq_bf=([768, 4], BF16), convk_bf=([768, 4], BF16), convv_bf=([1536, 4], BF16),
    w1q_rows=([4096, 1024], BF16), w2q_rows=([1024, 4096], BF16),
    wf1q_rows=([1024, 1024], BF16),
    wq1_bf=([1024, 1024], BF16), wk1_bf=([1024, 1024], BF16),
    wv1_bf=([1024, 1024], BF16), wo1_bf=([1024, 1024], BF16),
)

PREP_INS = ["L0_attn_Wq", "L0_attn_Wk", "L0_attn_Wv", "L0_attn_Wg", "L0_attn_Wo",
            "L0_attn_conv_q", "L0_attn_conv_k", "L0_attn_conv_v", "L0_attn_Wgk",
            "L0_attn_Wb", "L0_ffn_w1", "L0_ffn_w2", "L1_attn_wq", "L1_attn_wk",
            "L1_attn_wv", "L1_attn_wo", "L1_ffn_weight"]

INPUT_SHAPES = dict(
    L0_attn_Wq=[1024, 768], L0_attn_Wk=[1024, 768], L0_attn_Wv=[1024, 1536],
    L0_attn_Wg=[1024, 1536], L0_attn_Wo=[1536, 1024], L0_attn_conv_q=[768, 4],
    L0_attn_conv_k=[768, 4], L0_attn_conv_v=[1536, 4], L0_attn_Wgk=[1024, 12],
    L0_attn_Wb=[1024, 12], L0_ffn_w1=[4096, 1024], L0_ffn_w2=[1024, 4096],
    L1_attn_wq=[1024, 1024], L1_attn_wk=[1024, 1024], L1_attn_wv=[1024, 1024],
    L1_attn_wo=[1024, 1024], L1_ffn_weight=[1024, 1024],
)


def build_phase(emit_fn, ext_ins, ext_outs, consts=()):
    nc = bass.Bass("TRN2", target_bir_lowering=False)
    IN = {n: nc.dram_tensor(n, list(sh), dt, kind="ExternalInput").ap()
          for n, (sh, dt) in ext_ins.items()}
    OUT = {n: nc.dram_tensor(n, list(sh), dt, kind="ExternalOutput").ap()
           for n, (sh, dt) in ext_outs.items()}
    keys = list(Em.QUEUES) + [f"{q}{l}" for q in ("sp", "pool") for l in range(Em.NLANES)]
    sem_ctx = [nc.semaphore(f"q_{k}") for k in keys]
    sems = {}
    import contextlib
    with contextlib.ExitStack() as stack:
        for k, ctx in zip(keys, sem_ctx):
            sems[k] = stack.enter_context(ctx)
        with nc.Block() as block:
            @block.vector
            def _(vector):
                em = Em(nc, sems)
                with contextlib.ExitStack() as tstack:
                    em.stack = tstack
                    emit_fn(em, IN, OUT)
                    em.barrier_store_done()
    return nc


import contextlib

@contextlib.contextmanager
def emscope(em):
    with contextlib.ExitStack() as inner:
        outer = em.stack
        em.stack = inner
        try:
            yield
        finally:
            em.quiesce()
            em.stack = outer



def _bcast_const_mat(em, out_sb, vec_dram, width, ones_f, ps_bc, tmp_row):
    em.dma(tmp_row[0:1, 0:width], vec_dram.rearrange("(r c) -> r c", r=1), q="sp")
    for c0 in range(0, width, 512):
        em.mm(ps_bc[:, 0:512], ones_f[0:1, :], tmp_row[0:1, c0:c0 + 512])
        em.copy(out_sb[:, c0:c0 + 512], ps_bc[:, 0:512])


def _round_quant(em, out_bf, x_f32, scol, lo, hi, fB, fC, fD, iA, n):
    em.ts(fB[:, 0:n], x_f32, scol, op0=ALU.mult)
    em.copy(iA[:, 0:n], fB[:, 0:n])
    em.copy(fC[:, 0:n], iA[:, 0:n])
    e = iA[:, 0:n].bitcast(F32)
    em.tt(e, fB[:, 0:n], fC[:, 0:n], ALU.subtract)
    em.ts(fD[:, 0:n], e, 0.5, op0=ALU.is_ge)
    em.stt(fD[:, 0:n], e, -0.5, fD[:, 0:n], ALU.is_lt, ALU.subtract)
    em.tt(fC[:, 0:n], fC[:, 0:n], fD[:, 0:n], ALU.subtract)
    em.ts(out_bf, fC[:, 0:n], float(hi), float(lo), op0=ALU.min, op1=ALU.max)


def emit_l0f(em, IN, OUT):
    nc = em.nc
    ones = em.sb("ff_ones", [1, 128], BF16)
    em.dma(ones[:], IN["c_ones"], q="sp")
    onesf = em.sb("ff_onesf", [1, 128], F32)
    em.memset(onesf[:], 1.0)
    ident = em.sb("ff_ident", [128, 128], BF16)
    em.dma(ident[:], IN["c_ident"], q="sp")
    ps_bc = em.ps("ff_psbc", [128, 512])
    tmp_row = em.sb("ff_trow", [1, 4096], F32)

    c_ln1w = em.sb("ff_ln1w", [128, 1024], F32)
    c_ln1b = em.sb("ff_ln1b", [128, 1024], F32)
    c_ln2w = em.sb("ff_ln2w", [128, 4096], F32)
    c_ln2b = em.sb("ff_ln2b", [128, 4096], F32)
    for mat, vec, w in [(c_ln1w, "L0_ffn_ln1_w", 1024), (c_ln1b, "L0_ffn_ln1_b", 1024),
                        (c_ln2w, "L0_ffn_ln2_w", 4096), (c_ln2b, "L0_ffn_ln2_b", 4096)]:
        _bcast_const_mat(em, mat, IN[vec], w, onesf, ps_bc, tmp_row)
    b1row = em.sb("ff_b1row", [1, 4096], BF16)
    em.dma(b1row[:], IN["L0_ffn_b1"].rearrange("(r c) -> r c", r=1), q="pool")
    b2row = em.sb("ff_b2row", [1, 1024], BF16)
    em.dma(b2row[:], IN["L0_ffn_b2"].rearrange("(r c) -> r c", r=1), q="pool")

    xn_t = em.sb("ff_xn", [128, 1024], BF16)
    fA = em.sb("ff_fA", [128, 1024], F32)
    fB = em.sb("ff_fB", [128, 1024], F32)
    fC = em.sb("ff_fC", [128, 1024], F32)
    fD = em.sb("ff_fD", [128, 1024], F32)
    iA = em.sb("ff_iA", [128, 1024], I32)
    hq_t = em.sb("ff_hq", [128, 1024], BF16)
    st = em.sb("ff_st", [128, 32], F32)
    gacc = em.sb("ff_gacc", [128, 32], F32)
    sqs = em.sb("ff_sqs", [128, 512], F32)
    mq_t = em.sb("ff_mqt", [128, 1024], BF16)
    srow = em.sb("ff_srow", [1, 128], BF16)
    sbfc = em.sb("ff_sbfc", [128, 1], BF16)
    ps_tr = em.ps("ff_pstr", [128, 128], BF16)
    pu = [em.ps(f"ff_pu{i}", [128, 512]) for i in range(2)]

    xn_s, hq_s, mq_s, s2_s = IN["xn_s"], OUT["hq_s"], OUT["mq_s"], OUT["s2_s"]

    with emscope(em):
        w1sb = [em.sb(f"ff_w1_{d}", [128, 4096], BF16, tdest=True) for d in range(8)]
        for d in range(8):
            em.dmat(w1sb[d][:], IN["w1q_rows"][:, d * 128:(d + 1) * 128], q="sp")
        hqT = [em.sb(f"ff_hqT{d}", [128, 128], BF16, tdest=True) for d in range(8)]
        gm = em.sb("ff_gm", [128, 4096], BF16)

        for tt in range(32):
            trow = tt * 128
            em.dma(xn_t[:], xn_s[trow:trow + 128, :], q="sp")
            em.red(st[:, 0:1], xn_t[:], ALU.add)
            em.act(fA[:], xn_t[:], AF.Square, accum_out=st[:, 1:2])
            em.ts(st[:, 2:3], st[:, 0:1], 1.0 / 1024, op0=ALU.mult)
            em.ts(st[:, 3:4], st[:, 1:2], 1.0 / 1024, op0=ALU.mult)
            em.ts(st[:, 4:5], st[:, 2:3], st[:, 2:3], op0=ALU.mult)
            em.tt(st[:, 5:6], st[:, 3:4], st[:, 4:5], ALU.subtract)
            em.act(st[:, 6:7], st[:, 5:6], AF.Ln, bias=EPS_LN)
            em.act(st[:, 7:8], st[:, 6:7], AF.Exp, scale=-0.5)
            em.ts(st[:, 8:9], st[:, 2:3], st[:, 7:8], s2=-1.0, op0=ALU.mult, op1=ALU.mult)
            em.act(fA[:], xn_t[:], AF.Identity, scale=st[:, 7:8], bias=st[:, 8:9])
            em.stt(fB[:], fA[:], 0.0, c_ln1w[:], ALU.bypass, ALU.mult)
            em.tt(fA[:], fB[:], c_ln1b[:], ALU.add)
            em.red(st[:, 9:10], fA[:], ALU.max, absval=True)
            em.ts(st[:, 10:11], st[:, 9:10], 1e-5, op0=ALU.max)
            em.recip(st[:, 11:12], st[:, 10:11])
            em.ts(st[:, 12:13], st[:, 11:12], 127.0, op0=ALU.mult)
            em.ts(st[:, 13:14], st[:, 10:11], 1.0 / 127.0, op0=ALU.mult)
            _round_quant(em, hq_t[:], fA[:], st[:, 12:13], -128.0, 127.0, fA2 := fB, fC, fD, iA, 1024)
            em.dma(hq_s[trow:trow + 128, :], hq_t[:], q="sp")
            em.copy(sbfc[:], st[:, 12:13])
            em.tr(ps_tr[0:1, 0:128], sbfc[:], ident[:])
            em.copy(srow[:], ps_tr[0:1, 0:128])
            for d in range(8):
                em.dmat(hqT[d][:], hq_s[trow:trow + 128, d * 128:(d + 1) * 128], q="sp")
            for fc in range(8):
                bank = pu[fc % 2]
                for d in range(8):
                    em.mm(bank[:], hqT[d][:], w1sb[d][:, fc * 512:(fc + 1) * 512],
                          start=(d == 0), stop=False, inc=0)
                em.mm(bank[:], srow[0:1, :], b1row[0:1, fc * 512:(fc + 1) * 512],
                      start=False, stop=True)
                em.act(gm[:, fc * 512:(fc + 1) * 512], bank[:], AF.Gelu,
                       scale=st[:, 13:14], accum_out=gacc[:, fc:fc + 1])
                em.act(sqs[:], gm[:, fc * 512:(fc + 1) * 512], AF.Square,
                       accum_out=gacc[:, 8 + fc:9 + fc])
            em.red(st[:, 16:17], gacc[:, 0:8], ALU.add)
            em.red(st[:, 17:18], gacc[:, 8:16], ALU.add)
            em.ts(st[:, 18:19], st[:, 16:17], 1.0 / 4096, op0=ALU.mult)
            em.ts(st[:, 19:20], st[:, 17:18], 1.0 / 4096, op0=ALU.mult)
            em.ts(st[:, 20:21], st[:, 18:19], st[:, 18:19], op0=ALU.mult)
            em.tt(st[:, 21:22], st[:, 19:20], st[:, 20:21], ALU.subtract)
            em.act(st[:, 22:23], st[:, 21:22], AF.Ln, bias=EPS_LN)
            em.act(st[:, 23:24], st[:, 22:23], AF.Exp, scale=-0.5)
            em.ts(st[:, 24:25], st[:, 18:19], st[:, 23:24], s2=-1.0, op0=ALU.mult, op1=ALU.mult)
            for fc in range(8):
                sl_ = slice(fc * 512, (fc + 1) * 512)
                em.act(fA[:, 0:512], gm[:, sl_], AF.Identity, scale=st[:, 23:24], bias=st[:, 24:25])
                em.stt(fB[:, 0:512], fA[:, 0:512], 0.0, c_ln2w[:, sl_], ALU.bypass, ALU.mult)
                em.tt(gm[:, sl_], fB[:, 0:512], c_ln2b[:, sl_], ALU.add)
                em.red(gacc[:, 16 + fc:17 + fc], gm[:, sl_], ALU.max, absval=True)
            em.red(st[:, 25:26], gacc[:, 16:24], ALU.max)
            em.ts(st[:, 26:27], st[:, 25:26], 1e-5, op0=ALU.max)
            em.recip(st[:, 27:28], st[:, 26:27])
            em.ts(st[:, 28:29], st[:, 27:28], 127.0, op0=ALU.mult)
            em.dma(s2_s[trow:trow + 128, :], st[:, 28:29], q="sp")
            for cc in range(4):
                sl_ = slice(cc * 1024, (cc + 1) * 1024)
                _round_quant(em, mq_t[:], gm[:, sl_], st[:, 28:29], -128.0, 127.0, fB, fC, fD, iA, 1024)
                em.dma(mq_s[trow:trow + 128, sl_], mq_t[:], q="sp")

    with emscope(em):
        w2sb = [em.sb(f"ff_w2_{h}", [128, 1024], BF16, tdest=True) for h in range(32)]
        for h in range(32):
            em.dmat(w2sb[h][:], IN["w2q_rows"][:, h * 128:(h + 1) * 128], q="sp")
        mqT = [em.sb(f"ff_mqT{h}", [128, 128], BF16, tdest=True) for h in range(32)]
        x_t = em.sb("ff_xt", [128, 1024], F32)
        a0_t = em.sb("ff_a0", [128, 1024], BF16)
        x1_t = em.sb("ff_x1", [128, 1024], BF16)
        s2c = em.sb("ff_s2c", [128, 1], F32)
        s2r = em.sb("ff_s2r", [1, 128], BF16)
        s2cb = em.sb("ff_s2cb", [128, 1], BF16)
        rs2 = em.sb("ff_rs2", [128, 1], F32)
        po = [em.ps(f"ff_po{i}", [128, 512]) for i in range(2)]
        for tt in range(32):
            trow = tt * 128
            em.dma(s2c[:], s2_s[trow:trow + 128, :], q="sp")
            em.recip(rs2[:], s2c[:])
            em.copy(s2cb[:], s2c[:])
            em.tr(ps_tr[0:1, 0:128], s2cb[:], ident[:])
            em.copy(s2r[:], ps_tr[0:1, 0:128])
            for h in range(32):
                em.dmat(mqT[h][:], mq_s[trow:trow + 128, h * 128:(h + 1) * 128], q="sp")
            em.dma(x_t[:], IN["x"][trow:trow + 128, :], q="sp")
            em.dma(a0_t[:], IN["a0_s"][trow:trow + 128, :], q="sp")
            for oc in range(2):
                bank = po[oc]
                for h in range(32):
                    em.mm(bank[:], mqT[h][:], w2sb[h][:, oc * 512:(oc + 1) * 512],
                          start=(h == 0), stop=False, inc=0)
                em.mm(bank[:], s2r[0:1, :], b2row[0:1, oc * 512:(oc + 1) * 512],
                      start=False, stop=True)
                sl_ = slice(oc * 512, (oc + 1) * 512)
                em.stt(fA[:, 0:512], bank[:], rs2[:], x_t[:, sl_], ALU.mult, ALU.add)
                em.tt(x1_t[:, sl_], fA[:, 0:512], a0_t[:, sl_], ALU.add)
            em.dma(OUT["x1_s"][trow:trow + 128, :], x1_t[:], q="sp")


L0F_INS = dict(
    x=([T, D], F32), xn_s=([T, D], BF16), a0_s=([T, D], BF16),
    w1q_rows=([4096, 1024], BF16), w2q_rows=([1024, 4096], BF16),
    L0_ffn_ln1_w=([1024], F32), L0_ffn_ln1_b=([1024], F32),
    L0_ffn_ln2_w=([4096], F32), L0_ffn_ln2_b=([4096], F32),
    L0_ffn_b1=([4096], F32), L0_ffn_b2=([1024], F32),
    c_ones=([1, 128], BF16), c_ident=([128, 128], BF16),
)
L0F_OUTS = dict(x1_s=([T, D], BF16), hq_s=([T, D], BF16),
                mq_s=([T, HID], BF16), s2_s=([T, 1], F32))



def emit_l0a(em, IN, OUT, n_chunks=8, n_sub=8, scan_secs=99):
    nc = em.nc
    ident = em.sb("la_ident", [128, 128], BF16)
    em.dma(ident[:], IN["c_ident"], q="sp")
    onesf = em.sb("la_onesf", [1, 128], F32)
    em.memset(onesf[:], 1.0)
    onescol = em.sb("la_onescol", [64, 1], F32)
    em.memset(onescol[:], 1.0)
    negones64 = em.sb("la_no64", [1, 64], F32)
    em.memset(negones64[:], -1.0)
    c_ms = em.sb("la_ms", [CS, CS], BF16)
    em.dma(c_ms[:], IN["c_ms"], q="sp")
    c_bigL = em.sb("la_bigL", [CS, CS], F32)
    em.dma(c_bigL[:], IN["c_bigL"], q="sp")
    c_bigUi = em.sb("la_bigUi", [CS, CS], F32)
    em.dma(c_bigUi[:], IN["c_bigUi"], q="sp")
    c_bigUs = em.sb("la_bigUs", [CS, CS], F32)
    em.dma(c_bigUs[:], IN["c_bigUs"], q="sp")

    alog = em.sb("la_alog", [12, 1], F32)
    em.dma(alog[:], IN["L0_attn_A_log"].rearrange("(p c) -> p c", c=1), q="sp")
    dtb = em.sb("la_dtb", [12, 1], F32)
    em.dma(dtb[:], IN["L0_attn_dt_bias"].rearrange("(p c) -> p c", c=1), q="sp")
    bbc = em.sb("la_bb", [12, 1], F32)
    em.dma(bbc[:], IN["L0_attn_b_b"].rearrange("(p c) -> p c", c=1), q="sp")
    negA = em.sb("la_negA", [12, 1], F32)
    em.act(negA[:], alog[:], AF.Exp)
    em.ts(negA[:], negA[:], -1.0, op0=ALU.mult)

    gnwrow = em.sb("la_gnwrow", [1, 128], F32)
    em.dma(gnwrow[:], IN["L0_attn_g_norm_weight"].rearrange("(r c) -> r c", r=1), q="sp")
    gnwmat = em.sb("la_gnwmat", [64, 1536], BF16)
    ppg = em.ps("la_ppg", [128, 128])
    for h in range(12):
        em.mm(ppg[0:64, 0:128], onesf[0:1, 0:64], gnwrow[:])
        em.copy(gnwmat[:, h * 128:(h + 1) * 128], ppg[0:64, 0:128])

    convqk = em.sb("la_convqk", [64, 96], F32)
    for h in range(12):
        em.dma(convqk[:, h * 4:(h + 1) * 4], IN["L0_attn_conv_q"][h * 64:(h + 1) * 64, :], q="sp")
        em.dma(convqk[:, 48 + h * 4:48 + (h + 1) * 4], IN["L0_attn_conv_k"][h * 64:(h + 1) * 64, :], q="sp")
    convv_all = em.sb("la_convvall", [128, 48], F32)
    for vt in range(12):
        em.dma(convv_all[:, vt * 4:(vt + 1) * 4], IN["L0_attn_conv_v"][vt * 128:(vt + 1) * 128, :], q="sp")
    dgqk = []
    for hh in range(24):
        tile4 = []
        for j in range(4):
            d = em.sb(f"la_dg{hh}_{j}", [64, 64], BF16)
            em.ts(d[:], ident[0:64, 0:64], convqk[:, hh * 4 + j:hh * 4 + j + 1], op0=ALU.mult)
            tile4.append(d)
        dgqk.append(tile4)
    vdg = [[em.sb(f"la_vdg{i}_{j}", [128, 128], BF16) for j in range(4)] for i in range(2)]

    w_sb = [em.sb(f"la_w{d}", [128, 4608], BF16) for d in range(8)]
    wgkb_sb = em.sb("la_wgkb", [128, 352], BF16)
    for d in range(8):
        em.dma(w_sb[d][:, 0:768], IN["wq_bf"][d * 128:(d + 1) * 128, :], q="sp")
        em.dma(w_sb[d][:, 768:1536], IN["wk_bf"][d * 128:(d + 1) * 128, :], q="sp")
        em.dma(w_sb[d][:, 1536:3072], IN["wv_bf"][d * 128:(d + 1) * 128, :], q="sp")
        em.dma(w_sb[d][:, 3072:4608], IN["wg_bf"][d * 128:(d + 1) * 128, :], q="sp")
        em.dma(wgkb_sb[:, d * 44:(d + 1) * 44], IN["wgkb_bf"][d * 128:(d + 1) * 128, :], q="sp")

    pp = [em.ps(f"la_pp{i}", [128, 512]) for i in range(6)]
    ps_trf = em.ps("la_pstrf", [128, 128])
    identf = em.sb("la_identf", [128, 128], F32)
    em.copy(identf[:], ident[:])

    with emscope(em):
        x_t = em.sb("la_xt", [128, 512], F32)
        xn_t = em.sb("la_xnt", [128, 1024], BF16)
        st = em.sb("la_st", [128, 16], F32)
        sqf = em.sb("la_sqf", [64, 512], F32)
        xnT = [em.sb(f"la_xnT{d}", [128, 512], BF16, tdest=True) for d in range(8)]
        preqk = [em.sb(f"la_preqk{t}", [64, 516], BF16) for t in range(2)]
        histqk = [em.sb(f"la_histqk{t}", [64, 4], BF16) for t in range(24)]
        prev = [em.sb(f"la_prev{t}", [128, 516], BF16) for t in range(2)]
        histv = [em.sb(f"la_histv{t}", [128, 4], BF16) for t in range(12)]
        qn = [em.sb(f"la_qn{t}", [64, 512], BF16) for t in range(12)]
        kn = [em.sb(f"la_kn{t}", [64, 512], BF16) for t in range(12)]
        vf = [em.sb(f"la_vf{t}", [128, 512], BF16) for t in range(2)]
        rsq1 = em.sb("la_rsq1", [1, 512], F32)
        g_sc = em.sb("la_gsc", [64, 1536], BF16)
        gkb = em.sb("la_gkb", [12, 512], F32)
        beta_f = em.sb("la_betaf", [12, 512], F32)
        S = em.sb("la_S", [64, 1536], F32)
        Sbf = em.sb("la_Sbf", [64, 1536], BF16)
        em.memset(S[:], 0.0)
        em.memset(Sbf[:], 0.0)
        gam = em.sb("la_gam", [12, 64], F32)
        b12 = em.sb("la_b12", [12, 64], F32)
        bb12 = em.sb("la_bb12", [12, 64], F32)
        kds12 = em.sb("la_kds12", [12, 64], F32)
        flatg = em.sb("la_flatg", [1, 768], F32)
        flatbeta = em.sb("la_flatbeta", [1, 768], F32)
        beta_tok = em.sb("la_btok", [64, 12], F32)
        bb_tok = em.sb("la_bbtok", [64, 12], F32)
        b_tok = em.sb("la_btok2", [64, 12], F32)
        kds_tok = em.sb("la_kdstok", [64, 12], F32)
        bc_tok = em.sb("la_bctok", [64, 12], F32)
        bcrow = em.sb("la_bcrow", [1, 12], F32)
        E0 = em.sb("la_E0", [64, 768], BF16)
        E0U = em.sb("la_E0U", [64, 768], BF16)
        Amat = em.sb("la_Amat", [64, 768], BF16)
        ATmat = em.sb("la_ATmat", [64, 768], BF16)
        WTmat = em.sb("la_WTmat", [64, 768], BF16)
        powP = [em.sb(f"la_powP{i}", [64, 768], BF16) for i in range(2)]
        powPT = [None] + [em.sb(f"la_powPT{i}", [64, 768], BF16) for i in range(1, 6)]
        y_t = em.sb("la_y", [64, 1536], BF16)
        y2_t = em.sb("la_y2", [64, 1536], BF16)
        Dl = em.sb("la_Dl", [64, 1536], BF16)
        Ot = em.sb("la_O", [64, 1536], BF16)
        o2 = em.sb("la_o2", [64, 1536], BF16)
        ovar = em.sb("la_ovar", [64, 16], F32)
        ktokG = em.sb("la_ktokG", [128, 768], BF16, tdest=True)
        vtokG = em.sb("la_vtokG", [128, 1536], BF16, tdest=True)
        ktokB = em.sb("la_ktokB", [64, 768], BF16)
        vtokB = em.sb("la_vtokB", [64, 1536], BF16)

        xn_s, a0_s = OUT["xn_s"], OUT["a0_s"]
        kn_s, vf_s, og_s = OUT["kn_s"], OUT["vf_s"], OUT["og_s"]

        def stretch(tile, h0, nh, rep):
            return tile[:, h0:h0 + nh].unsqueeze(2).broadcast_to((64, nh, rep))

        def rep_mask(msk, nh):
            return msk[:].unsqueeze(1).broadcast_to((64, nh, CS))

        def h3(ap2d, nh, blk):
            return ap2d.rearrange("p (h s) -> p h s", s=blk)

        for ch in range(n_chunks):
            tbase = ch * 512
            for tt in range(4):
                trow = tbase + tt * 128
                em.dma(x_t[:], IN["x"][trow:trow + 128, 0:512], q="sp")
                em.red(st[:, 0:1], x_t[:], ALU.add)
                em.act(sqf[0:64, :], x_t[0:64, :], AF.Square, accum_out=st[0:64, 1:2])
                em.act(sqf[0:64, :], x_t[64:128, :], AF.Square, accum_out=st[64:128, 1:2])
                em.dma(x_t[:], IN["x"][trow:trow + 128, 512:1024], q="sp")
                em.red(st[:, 9:10], x_t[:], ALU.add)
                em.act(sqf[0:64, :], x_t[0:64, :], AF.Square, accum_out=st[0:64, 10:11])
                em.act(sqf[0:64, :], x_t[64:128, :], AF.Square, accum_out=st[64:128, 10:11])
                em.tt(st[:, 0:1], st[:, 0:1], st[:, 9:10], ALU.add)
                em.tt(st[:, 1:2], st[:, 1:2], st[:, 10:11], ALU.add)
                em.ts(st[:, 2:3], st[:, 0:1], 1.0 / 1024, op0=ALU.mult)
                em.ts(st[:, 3:4], st[:, 1:2], 1.0 / 1024, op0=ALU.mult)
                em.ts(st[:, 4:5], st[:, 2:3], st[:, 2:3], op0=ALU.mult)
                em.tt(st[:, 5:6], st[:, 3:4], st[:, 4:5], ALU.subtract)
                em.act(st[:, 6:7], st[:, 5:6], AF.Ln, bias=EPS_LN)
                em.act(st[:, 7:8], st[:, 6:7], AF.Exp, scale=-0.5)
                em.ts(st[:, 8:9], st[:, 2:3], st[:, 7:8], s2=-1.0, op0=ALU.mult, op1=ALU.mult)
                em.act(xn_t[:, 512:1024], x_t[:], AF.Identity, scale=st[:, 7:8], bias=st[:, 8:9])
                em.dma(x_t[:], IN["x"][trow:trow + 128, 0:512], q="sp")
                em.act(xn_t[:, 0:512], x_t[:], AF.Identity, scale=st[:, 7:8], bias=st[:, 8:9])
                em.dma(xn_s[trow:trow + 128, :], xn_t[:], q="sp")
            for d in range(8):
                em.dmat(xnT[d][:], xn_s[tbase:tbase + 512, d * 128:(d + 1) * 128], q="sp")
            for hh in range(24):
                bank = pp[hh % 2]
                prt = preqk[hh % 2]
                for d in range(8):
                    em.mm(bank[0:64, :], w_sb[d][:, hh * 64:(hh + 1) * 64], xnT[d][:],
                          start=(d == 0), stop=(d == 7), inc=(1 if d == 7 else 0))
                if ch > 0:
                    em.copy(prt[:, 0:3], histqk[hh][:, 0:3], q="pool")
                else:
                    em.memset(prt[:, 0:3], 0.0, q="pool")
                em.copy(prt[:, 3:515], bank[0:64, :])
                em.copy(histqk[hh][:, 0:3], prt[:, 512:515], q="pool")
                bank2 = pp[2 + hh % 2]
                for j in range(4):
                    em.mm(bank2[0:64, :], dgqk[hh][j][:], prt[:, j:j + 512],
                          start=(j == 0), stop=(j == 3), inc=(1 if j == 3 else 0))
                dst = qn[hh] if hh < 12 else kn[hh - 12]
                em.act(dst[:], bank2[0:64, :], AF.Silu)
                em.act(sqf[:], dst[:], AF.Square)
                em.mm(pp[4][0:1, :], onescol[:], sqf[:])
                em.act(rsq1[:], pp[4][0:1, :], AF.Ln, bias=1e-6)
                em.act(rsq1[:], rsq1[:], AF.Exp, scale=-0.5)
                if hh < 12:
                    em.ts(rsq1[:], rsq1[:], 0.125, op0=ALU.mult)
                em.mm(pp[5][0:64, 0:512], onesf[0:1, 0:64], rsq1[0:1, :])
                em.tt(dst[:], dst[:], pp[5][0:64, 0:512], ALU.mult)
                if hh >= 12:
                    em.dma(kn_s[(hh - 12) * 64:(hh - 11) * 64, tbase:tbase + 512], dst[:], q="sp")
            for ft in range(12):
                bank = pp[ft % 2]
                prt = prev[ft % 2]
                for d in range(8):
                    em.mm(bank[:], w_sb[d][:, 1536 + ft * 128:1536 + (ft + 1) * 128], xnT[d][:],
                          start=(d == 0), stop=(d == 7), inc=(1 if d == 7 else 0))
                if ch > 0:
                    em.copy(prt[:, 0:3], histv[ft][:, 0:3], q="pool")
                else:
                    em.memset(prt[:, 0:3], 0.0, q="pool")
                em.copy(prt[:, 3:515], bank[:])
                em.copy(histv[ft][:, 0:3], prt[:, 512:515], q="pool")
                dg = vdg[ft % 2]
                for j in range(4):
                    em.ts(dg[j][:], ident[:], convv_all[:, ft * 4 + j:ft * 4 + j + 1],
                          op0=ALU.mult, q="pool")
                bank2 = pp[2 + ft % 2]
                for j in range(4):
                    em.mm(bank2[:], dg[j][:], prt[:, j:j + 512],
                          start=(j == 0), stop=(j == 3), inc=(1 if j == 3 else 0))
                vt = vf[ft % 2]
                em.act(vt[:], bank2[:], AF.Silu)
                em.dma(vf_s[ft * 128:(ft + 1) * 128, tbase:tbase + 512], vt[:], q="sp")
            bank = pp[2]
            for d in range(8):
                em.mm(bank[0:44, :], wgkb_sb[:, d * 44:(d + 1) * 44], xnT[d][:],
                      start=(d == 0), stop=(d == 7), inc=(1 if d == 7 else 0))
            em.act(gkb[:], bank[0:12, :], AF.Abs, bias=dtb[:, 0:1])
            em.act(gkb[:], gkb[:], AF.Exp, scale=-1.0)
            em.act(gkb[:], gkb[:], AF.Ln, bias=1.0)
            em.act(beta_f[:], bank[0:12, :], AF.Identity, bias=dtb[:, 0:1])
            em.ts(beta_f[:], beta_f[:], 0.0, op0=ALU.max)
            em.tt(gkb[:], gkb[:], beta_f[:], ALU.add)
            em.ts(gkb[:], gkb[:], negA[:, 0:1], op0=ALU.mult)
            em.act(beta_f[:], bank[32:44, :], AF.Sigmoid, bias=bbc[:, 0:1])
            for sc in range(n_sub):
                c0 = sc * 64
                grp = sc // 2
                if sc % 2 == 0:
                    trow = tbase + grp * 128
                    em.dmat(ktokG[:], kn_s[0:768, trow:trow + 128], q="sp")
                    em.dmat(vtokG[:], vf_s[0:1536, trow:trow + 128], q="sp")
                    em.dma(ktokB[:], ktokG[64:128, :], q="sp")
                    em.dma(vtokB[:], vtokG[64:128, :], q="sp")
                ktok = ktokG if sc % 2 == 0 else ktokB
                vtok = vtokG if sc % 2 == 0 else vtokB
                for fs in range(3):
                    gb = pp[3 + fs]
                    for d in range(8):
                        em.mm(gb[0:64, :], xnT[d][:, c0:c0 + 64],
                              w_sb[d][:, 3072 + fs * 512:3072 + (fs + 1) * 512],
                              start=(d == 0), stop=(d == 7), inc=(1 if d == 7 else 0))
                    em.copy(g_sc[:, fs * 512:(fs + 1) * 512], gb[0:64, :])
                if scan_secs < 2:
                    continue
                em.scan(gam[:], gkb[:, c0:c0 + 64], gkb[:, c0:c0 + 64], 0.0,
                        ALU.add, ALU.bypass)
                em.act(b12[:], gam[:], AF.Exp)
                em.tt(bb12[:], beta_f[:, c0:c0 + 64], b12[:], ALU.mult)
                em.ts(kds12[:], gam[:], gam[:, 63:64], op0=ALU.subtract)
                em.act(kds12[:], kds12[:], AF.Exp, scale=-1.0)
                em.dma(flatg[:], gam[:], q="sp")
                em.dma(flatbeta[:], beta_f[:, c0:c0 + 64], q="sp")
                for tsrc, tdst in ((beta_f[:, c0:c0 + 64], beta_tok), (bb12[:], bb_tok),
                                   (b12[:], b_tok), (kds12[:], kds_tok)):
                    em.tr(ps_trf[0:64, 0:12], tsrc, identf[0:12, 0:12])
                    em.copy(tdst[:], ps_trf[0:64, 0:12])
                em.act(bcrow[:], flatg[0:1, 63:768:64], AF.Exp)
                em.mm(pp[5][0:64, 0:12], onesf[0:1, 0:64], bcrow[:])
                em.copy(bc_tok[:], pp[5][0:64, 0:12])
                if scan_secs < 3:
                    continue
                for h in range(12):
                    b_, cs_ = (pp[0], h * 64) if h < 8 else (pp[1], (h - 8) * 64)
                    em.mm(b_[0:64, cs_:cs_ + 64], flatg[0:1, 64 * h:64 * h + 64],
                          negones64[0:1, 0:64], start=(h in (0, 8)), stop=False, inc=0)
                    em.mm(b_[0:64, cs_:cs_ + 64], onesf[0:1, 0:64],
                          flatg[0:1, 64 * h:64 * h + 64], start=False,
                          stop=(h in (7, 11)), inc=(1 if h in (7, 11) else 0))
                scr = ((pp[0], pp[4], 512, 0), (pp[1], pp[5], 256, 512))
                for (b_, s_, w_, off) in scr:
                    em.tt(h3(s_[0:64, 0:w_], w_ // 64, 64), h3(b_[0:64, 0:w_], w_ // 64, 64),
                          rep_mask(c_bigL, w_ // 64), ALU.add)
                    em.act(E0[:, off:off + w_], s_[0:64, 0:w_], AF.Exp, scale=-1.0)
                    em.tt(h3(s_[0:64, 0:w_], w_ // 64, 64), h3(b_[0:64, 0:w_], w_ // 64, 64),
                          rep_mask(c_bigUi, w_ // 64), ALU.add)
                    em.act(E0U[:, off:off + w_], s_[0:64, 0:w_], AF.Exp)
                if scan_secs < 4:
                    continue
                for h in range(12):
                    b_, cs_ = (pp[2], h * 64) if h < 8 else (pp[3], (h - 8) * 64)
                    kf = kn[h][:, c0:c0 + 64]
                    em.mm(b_[0:64, cs_:cs_ + 64], kf, kf, inc=(1 if h in (7, 11) else 0))
                if scan_secs < 41:
                    continue
                for (b_, w_, off) in ((pp[2], 512, 0), (pp[3], 256, 512)):
                    nh = w_ // 64
                    em.tt(Amat[:, off:off + w_], E0[:, off:off + w_], b_[0:64, 0:w_], ALU.mult)
                    em.tt(h3(Amat[:, off:off + w_], nh, 64), h3(Amat[:, off:off + w_], nh, 64),
                          stretch(beta_tok, 0 + off // 64, nh, 64), ALU.mult)
                    em.tt(h3(Amat[:, off:off + w_], nh, 64), h3(Amat[:, off:off + w_], nh, 64),
                          rep_mask(c_ms, nh), ALU.mult)
                if scan_secs < 42:
                    continue
                for (b_, s_, w_, off) in scr:
                    em.tt(h3(s_[0:64, 0:w_], w_ // 64, 64), h3(b_[0:64, 0:w_], w_ // 64, 64),
                          rep_mask(c_bigUs, w_ // 64), ALU.add)
                    em.act(E0[:, off:off + w_], s_[0:64, 0:w_], AF.Exp)
                for h in range(12):
                    b_, cs_ = (pp[0], h * 64) if h < 8 else (pp[1], (h - 8) * 64)
                    em.mm(b_[0:64, cs_:cs_ + 64], onesf[0:1, 0:64],
                          flatbeta[0:1, 64 * h:64 * h + 64],
                          inc=(1 if h in (7, 11) else 0))
                if scan_secs < 43:
                    continue
                for (b_, w_, off) in ((pp[2], 512, 0), (pp[3], 256, 512)):
                    em.tt(ATmat[:, off:off + w_], E0[:, off:off + w_], b_[0:64, 0:w_], ALU.mult)
                for (b_, w_, off) in ((pp[0], 512, 0), (pp[1], 256, 512)):
                    em.tt(ATmat[:, off:off + w_], ATmat[:, off:off + w_], b_[0:64, 0:w_], ALU.mult)
                if scan_secs < 5:
                    continue
                for h in range(12):
                    b_, cs_ = (pp[2], h * 64) if h < 8 else (pp[3], (h - 8) * 64)
                    em.mm(b_[0:64, cs_:cs_ + 64], kn[h][:, c0:c0 + 64], qn[h][:, c0:c0 + 64],
                          inc=(1 if h in (7, 11) else 0))
                for (b_, w_, off) in ((pp[2], 512, 0), (pp[3], 256, 512)):
                    em.tt(WTmat[:, off:off + w_], E0U[:, off:off + w_], b_[0:64, 0:w_], ALU.mult)
                if scan_secs < 6:
                    continue
                for h in range(12):
                    b_ = (pp[2], pp[3], pp[4])[h // 4]
                    cs_ = (h % 4) * 128
                    em.mm(b_[0:64, cs_:cs_ + 128], kn[h][:, c0:c0 + 64],
                          Sbf[:, h * 128:(h + 1) * 128], inc=(1 if h % 4 == 3 else 0))
                em.tt(h3(y_t[:], 12, 128), h3(vtok[0:64, :], 12, 128),
                      stretch(beta_tok, 0, 12, 128), ALU.mult)
                for bi_, b_ in enumerate((pp[2], pp[3], pp[4])):
                    off = bi_ * 512
                    em.tt(h3(y2_t[:, off:off + 512], 4, 128), h3(b_[0:64, 0:512], 4, 128),
                          stretch(bb_tok, bi_ * 4, 4, 128), ALU.mult)
                em.tt(y_t[:], y_t[:], y2_t[:], ALU.subtract)
                if scan_secs < 7:
                    continue
                em.copy(powP[0][:], Amat[:], q="pool")
                for lv in range(1, 6):
                    pprev = powP[(lv - 1) % 2]
                    ptprev = ATmat if lv == 1 else powPT[lv - 1]
                    pcur, ptcur = powP[lv % 2], powPT[lv]
                    for h in range(12):
                        b_, cs_ = (pp[0], h * 64) if h < 8 else (pp[1], (h - 8) * 64)
                        hb = slice(h * 64, h * 64 + 64)
                        em.mm(b_[0:64, cs_:cs_ + 64], ptprev[:, hb], pprev[:, hb],
                              inc=(1 if h in (7, 11) else 0))
                    for h in range(12):
                        b_, cs_ = (pp[2], h * 64) if h < 8 else (pp[3], (h - 8) * 64)
                        hb = slice(h * 64, h * 64 + 64)
                        em.mm(b_[0:64, cs_:cs_ + 64], pprev[:, hb], ptprev[:, hb],
                              inc=(1 if h in (7, 11) else 0))
                    em.act(pcur[:, 0:512], pp[0][0:64, 0:512], AF.Identity)
                    em.act(pcur[:, 512:768], pp[1][0:64, 0:256], AF.Identity)
                    em.act(ptcur[:, 0:512], pp[2][0:64, 0:512], AF.Identity)
                    em.act(ptcur[:, 512:768], pp[3][0:64, 0:256], AF.Identity)
                if scan_secs < 8:
                    continue
                for lv in range(5, 0, -1):
                    pt = ATmat if lv == 1 else powPT[lv]
                    pt = powPT[lv]
                    for h in range(12):
                        b_ = (pp[0], pp[1], pp[2])[h // 4]
                        cs_ = (h % 4) * 128
                        em.mm(b_[0:64, cs_:cs_ + 128], pt[:, h * 64:h * 64 + 64],
                              y_t[:, h * 128:(h + 1) * 128], inc=(1 if h % 4 == 3 else 0))
                    for bi_, b_ in enumerate((pp[0], pp[1], pp[2])):
                        off = bi_ * 512
                        em.tt(y_t[:, off:off + 512], y_t[:, off:off + 512], b_[0:64, 0:512], ALU.add)
                for h in range(12):
                    b_ = (pp[0], pp[1], pp[2])[h // 4]
                    cs_ = (h % 4) * 128
                    em.mm(b_[0:64, cs_:cs_ + 128], ATmat[:, h * 64:h * 64 + 64],
                          y_t[:, h * 128:(h + 1) * 128], inc=(1 if h % 4 == 3 else 0))
                for bi_, b_ in enumerate((pp[0], pp[1], pp[2])):
                    off = bi_ * 512
                    em.tt(Dl[:, off:off + 512], y_t[:, off:off + 512], b_[0:64, 0:512], ALU.subtract)
                if scan_secs < 9:
                    continue
                for h in range(12):
                    b_ = (pp[0], pp[1], pp[2])[h // 4]
                    cs_ = (h % 4) * 128
                    em.mm(b_[0:64, cs_:cs_ + 128], qn[h][:, c0:c0 + 64],
                          Sbf[:, h * 128:(h + 1) * 128], inc=(1 if h % 4 == 3 else 0))
                for bi_, b_ in enumerate((pp[0], pp[1], pp[2])):
                    off = bi_ * 512
                    em.tt(h3(Ot[:, off:off + 512], 4, 128), h3(b_[0:64, 0:512], 4, 128),
                          stretch(b_tok, bi_ * 4, 4, 128), ALU.mult)
                for h in range(12):
                    b_ = (pp[3], pp[4], pp[5])[h // 4]
                    cs_ = (h % 4) * 128
                    em.mm(b_[0:64, cs_:cs_ + 128], WTmat[:, h * 64:h * 64 + 64],
                          Dl[:, h * 128:(h + 1) * 128], inc=(1 if h % 4 == 3 else 0))
                for bi_, b_ in enumerate((pp[3], pp[4], pp[5])):
                    off = bi_ * 512
                    em.tt(Ot[:, off:off + 512], Ot[:, off:off + 512], b_[0:64, 0:512], ALU.add)
                em.act(o2[:], Ot[:], AF.Square)
                for h in range(12):
                    em.red(ovar[:, h:h + 1], o2[:, h * 128:(h + 1) * 128], ALU.add)
                em.ts(ovar[:, 0:12], ovar[:, 0:12], 1.0 / 128, op0=ALU.mult)
                em.act(ovar[:, 0:12], ovar[:, 0:12], AF.Ln, bias=1e-5)
                em.act(ovar[:, 0:12], ovar[:, 0:12], AF.Exp, scale=-0.5)
                em.tt(h3(o2[:], 12, 128), h3(Ot[:], 12, 128),
                      ovar[:, 0:12].unsqueeze(2).broadcast_to((64, 12, 128)), ALU.mult)
                em.tt(o2[:], o2[:], gnwmat[:], ALU.mult)
                em.act(y2_t[:], g_sc[:], AF.Sigmoid)
                em.tt(y_t[:], g_sc[:], y2_t[:], ALU.mult)
                em.tt(y2_t[:], o2[:], y_t[:], ALU.mult)
                em.dma(og_s[tbase + c0:tbase + c0 + 64, :], y2_t[:], q="sp")
                if scan_secs < 10:
                    continue
                em.tt(h3(y_t[:, 0:768], 12, 64), h3(ktok[0:64, :], 12, 64),
                      stretch(kds_tok, 0, 12, 64), ALU.mult)
                for h in range(12):
                    b_ = (pp[0], pp[1], pp[2])[h // 4]
                    cs_ = (h % 4) * 128
                    em.mm(b_[0:64, cs_:cs_ + 128], y_t[:, h * 64:h * 64 + 64],
                          Dl[:, h * 128:(h + 1) * 128], inc=(1 if h % 4 == 3 else 0))
                em.tt(h3(S[:], 12, 128), h3(S[:], 12, 128),
                      stretch(bc_tok, 0, 12, 128), ALU.mult)
                for bi_, b_ in enumerate((pp[0], pp[1], pp[2])):
                    off = bi_ * 512
                    em.tt(S[:, off:off + 512], S[:, off:off + 512], b_[0:64, 0:512], ALU.add)
                em.copy(Sbf[:], S[:])
                if ch == 0 and sc == 0 and "d_gam" in OUT:
                    em.dma(OUT["d_gam"], gam[:], q="sp")
                    em.dma(OUT["d_beta"], beta_f[:, 0:64], q="sp")
                    em.dma(OUT["d_E0"], E0U[:], q="sp")
                    em.dma(OUT["d_Amat"], Amat[:], q="sp")
                    em.dma(OUT["d_ATmat"], ATmat[:], q="sp")
                    em.dma(OUT["d_WTmat"], WTmat[:], q="sp")
                    em.dma(OUT["d_Dl"], Dl[:], q="sp")
                    em.dma(OUT["d_Ot"], Ot[:], q="sp")
                    em.dma(OUT["d_S"], S[0:64, :], q="sp")
                    em.dma(OUT["d_gk"], gkb[:], q="sp")
    with emscope(em):
        ogT = [em.sb(f"la_ogT{v}", [128, 512], BF16, tdest=True) for v in range(12)]
        a0t = em.sb("la_a0t", [128, 1024], BF16)
        wo_sb = [em.sb(f"la_wo{v}", [128, 1024], BF16) for v in range(12)]
        for v in range(12):
            em.dma(wo_sb[v][:], IN["wo_bf"][v * 128:(v + 1) * 128, :], q="sp")
        for gg in range(n_chunks):
            t0 = gg * 512
            for v in range(12):
                em.dmat(ogT[v][:], og_s[t0:t0 + 512, v * 128:(v + 1) * 128], q="sp")
            for tt in range(4):
                for oc in range(2):
                    bank = pp[oc]
                    for v in range(12):
                        em.mm(bank[:], ogT[v][:, tt * 128:(tt + 1) * 128],
                              wo_sb[v][:, oc * 512:(oc + 1) * 512],
                              start=(v == 0), stop=(v == 11), inc=(1 if v == 11 else 0))
                    em.copy(a0t[:, oc * 512:(oc + 1) * 512], bank[:])
                em.dma(a0_s[t0 + tt * 128:t0 + tt * 128 + 128, :], a0t[:], q="sp")


L0A_INS = dict(
    x=([T, D], F32),
    wq_bf=([1024, 768], BF16), wk_bf=([1024, 768], BF16), wv_bf=([1024, 1536], BF16),
    wg_bf=([1024, 1536], BF16), wo_bf=([1536, 1024], BF16), wgkb_bf=([1024, 44], BF16),
    L0_attn_conv_q=([768, 4], F32), L0_attn_conv_k=([768, 4], F32),
    L0_attn_conv_v=([1536, 4], F32),
    L0_attn_A_log=([12], F32), L0_attn_dt_bias=([12], F32), L0_attn_b_b=([12], F32),
    L0_attn_g_norm_weight=([128], F32),
    c_ident=([128, 128], BF16), c_ms=([CS, CS], BF16), c_bigL=([CS, CS], F32),
    c_bigUi=([CS, CS], F32), c_bigUs=([CS, CS], F32),
)
L0A_OUTS = dict(xn_s=([T, D], BF16), a0_s=([T, D], BF16),
                kn_s=([768, T], BF16), vf_s=([1536, T], BF16), og_s=([T, 1536], BF16))
L0A_DBG = dict(d_gam=([12, 64], F32), d_beta=([12, 64], F32), d_E0=([64, 768], BF16),
               d_Amat=([64, 768], BF16), d_ATmat=([64, 768], BF16), d_WTmat=([64, 768], BF16),
               d_Dl=([64, 1536], BF16), d_Ot=([64, 1536], BF16), d_S=([128, 768], F32),
               d_gk=([12, 512], F32))



def emit_l1(em, IN, OUT, n_chunks=8):
    nc = em.nc
    ident = em.sb("l1_ident", [128, 128], BF16)
    em.dma(ident[:], IN["c_ident"], q="sp")
    onesb = em.sb("l1_onesb", [1, 128], BF16)
    em.dma(onesb[:], IN["c_ones"], q="sp")
    onescol = em.sb("l1_onescol", [128, 1], BF16)
    em.memset(onescol[:], 1.0)
    pp = [em.ps(f"l1_pp{i}", [128, 512]) for i in range(6)]
    ps_tr = em.ps("l1_pstr", [128, 128], BF16)

    x_t = em.sb("l1_xt", [128, 1024], BF16)
    yn_t = em.sb("l1_ynt", [128, 1024], BF16)
    st = em.sb("l1_st", [128, 32], F32)
    sqf = em.sb("l1_sqf", [64, 512], F32)
    fA = em.sb("l1_fA", [128, 1024], F32)
    fB = em.sb("l1_fB", [128, 1024], F32)
    fC = em.sb("l1_fC", [128, 1024], F32)
    fD = em.sb("l1_fD", [128, 1024], F32)
    iA = em.sb("l1_iA", [128, 1024], I32)
    ynT = [em.sb(f"l1_ynT{d}", [128, 512], BF16, tdest=True) for d in range(8)]
    yn_s, a1_s = OUT["yn_s"], OUT["a1_s"]
    sc_s, xq_s = OUT["sc_s"], OUT["xq_s"]

    for tt in range(32):
        trow = tt * 128
        em.dma(x_t[:], IN["x1_s"][trow:trow + 128, :], q="sp")
        em.red(st[:, 0:1], x_t[:], ALU.add)
        em.act(sqf[0:64, :], x_t[0:64, 0:512], AF.Square, accum_out=st[0:64, 1:2])
        em.act(sqf[0:64, :], x_t[64:128, 0:512], AF.Square, accum_out=st[64:128, 1:2])
        em.act(sqf[0:64, :], x_t[0:64, 512:1024], AF.Square, accum_out=st[0:64, 2:3])
        em.act(sqf[0:64, :], x_t[64:128, 512:1024], AF.Square, accum_out=st[64:128, 2:3])
        em.tt(st[:, 1:2], st[:, 1:2], st[:, 2:3], ALU.add)
        em.ts(st[:, 3:4], st[:, 0:1], 1.0 / 1024, op0=ALU.mult)
        em.ts(st[:, 4:5], st[:, 1:2], 1.0 / 1024, op0=ALU.mult)
        em.ts(st[:, 5:6], st[:, 3:4], st[:, 3:4], op0=ALU.mult)
        em.tt(st[:, 6:7], st[:, 4:5], st[:, 5:6], ALU.subtract)
        em.act(st[:, 7:8], st[:, 6:7], AF.Ln, bias=EPS_LN)
        em.act(st[:, 8:9], st[:, 7:8], AF.Exp, scale=-0.5)
        em.ts(st[:, 9:10], st[:, 3:4], st[:, 8:9], s2=-1.0, op0=ALU.mult, op1=ALU.mult)
        em.act(yn_t[:], x_t[:], AF.Identity, scale=st[:, 8:9], bias=st[:, 9:10])
        em.dma(yn_s[trow:trow + 128, :], yn_t[:], q="sp")

    with emscope(em):
        wk_sb = [em.sb(f"l1_wk{d}", [128, 1024], BF16) for d in range(8)]
        wv_sb = [em.sb(f"l1_wv{d}", [128, 1024], BF16) for d in range(8)]
        for d in range(8):
            em.dma(wk_sb[d][:], IN["wk1_bf"][d * 128:(d + 1) * 128, :], q="sp")
            em.dma(wv_sb[d][:], IN["wv1_bf"][d * 128:(d + 1) * 128, :], q="sp")
        bkrow = em.sb("l1_bkrow", [1, 1024], BF16)
        em.dma(bkrow[:], IN["L1_attn_bk"].rearrange("(r c) -> r c", r=1), q="pool")
        bvrow = em.sb("l1_bvrow", [1, 1024], BF16)
        em.dma(bvrow[:], IN["L1_attn_bv"].rearrange("(r c) -> r c", r=1), q="pool")
        k_t = em.sb("l1_kt", [128, 1024], BF16)
        v_t = em.sb("l1_vt", [128, 1024], BF16)
        for tt in range(32):
            trow = tt * 128
            for d in range(8):
                em.dmat(ynT[d][:, 0:128], yn_s[trow:trow + 128, d * 128:(d + 1) * 128], q="sp")
            for half, (wsb, brow, dst, do_elu) in enumerate(
                    [(wk_sb, bkrow, k_t, True), (wv_sb, bvrow, v_t, False)]):
                for oc in range(2):
                    bank = pp[4 + oc]
                    for d in range(8):
                        em.mm(bank[:], ynT[d][:, 0:128], wsb[d][:, oc * 512:(oc + 1) * 512],
                              start=(d == 0), stop=False, inc=0)
                    em.mm(bank[:], onesb[0:1, 0:128], brow[0:1, oc * 512:(oc + 1) * 512],
                          start=False, stop=True)
                    sl_ = slice(oc * 512, (oc + 1) * 512)
                    if do_elu:
                        em.ts(fA[:, 0:512], bank[:], 0.0, op0=ALU.min)
                        em.act(fB[:, 0:512], fA[:, 0:512], AF.Exp)
                        em.stt(dst[:, sl_], bank[:], 0.0, fB[:, 0:512], ALU.max, ALU.add)
                    else:
                        em.copy(dst[:, sl_], bank[:])
            for h in range(16):
                b_ = pp[0] if h < 8 else pp[1]
                cs_ = (h % 8) * 64
                em.mm(b_[0:64, cs_:cs_ + 64], k_t[:, h * 64:(h + 1) * 64],
                      v_t[:, h * 64:(h + 1) * 64], start=(tt == 0), stop=(tt == 31),
                      inc=(1 if (tt == 31 and h in (7, 15)) else 0))
            for oc in range(2):
                em.mm(pp[2 + 0][0:1, 0:512] if oc == 0 else pp[3][0:1, 0:512],
                      onescol[:, 0:1], k_t[:, oc * 512:(oc + 1) * 512],
                      start=(tt == 0), stop=(tt == 31),
                      inc=(1 if tt == 31 else 0))
        kv_sb = em.sb("l1_kvsb", [64, 1024], BF16)
        em.copy(kv_sb[:, 0:512], pp[0][0:64, :])
        em.copy(kv_sb[:, 512:1024], pp[1][0:64, :])
        ks_row = em.sb("l1_ksrow", [1, 1024], BF16)
        em.copy(ks_row[0:1, 0:512], pp[2][0:1, 0:512])
        em.copy(ks_row[0:1, 512:1024], pp[3][0:1, 0:512])
        ks_col = em.sb("l1_kscol", [64, 16], BF16)
        for h in range(16):
            em.tr(ps_tr[0:64, 0:1], ks_row[0:1, h * 64:(h + 1) * 64], ident[0:1, 0:1])
            em.copy(ks_col[:, h:h + 1], ps_tr[0:64, 0:1])

        wq_sb = [em.sb(f"l1_wq{d}", [128, 1024], BF16) for d in range(8)]
        for d in range(8):
            em.dma(wq_sb[d][:], IN["wq1_bf"][d * 128:(d + 1) * 128, :], q="sp")
        bq_col = em.sb("l1_bqcol", [64, 16], F32)
        for hh in range(16):
            em.dma(bq_col[:, hh:hh + 1],
                   IN["L1_attn_bq"][hh * 64:(hh + 1) * 64].rearrange("(p c) -> p c", c=1), q="sp")
        qh = [em.sb(f"l1_qh{h}", [64, 512], BF16) for h in range(16)]
        zt = em.sb("l1_zt", [128, 16], F32)
        sc_t = em.sb("l1_sct", [128, 1024], BF16)
        for chk in range(n_chunks):
            tbase = chk * 512
            for d in range(8):
                em.dmat(ynT[d][:], yn_s[tbase:tbase + 512, d * 128:(d + 1) * 128], q="sp")
            for h in range(16):
                bank = pp[4 + h % 2]
                for d in range(8):
                    em.mm(bank[0:64, :], wq_sb[d][:, h * 64:(h + 1) * 64], ynT[d][:],
                          start=(d == 0), stop=(d == 7), inc=(1 if d == 7 else 0))
                em.act(fA[0:64, 0:512], bank[0:64, :], AF.Identity,
                       bias=bq_col[:, h:h + 1])
                em.ts(fB[0:64, 0:512], fA[0:64, 0:512], 0.0, op0=ALU.min)
                em.act(fC[0:64, 0:512], fB[0:64, 0:512], AF.Exp)
                em.stt(qh[h][:], fA[0:64, 0:512], 0.0, fC[0:64, 0:512], ALU.max, ALU.add)
            for ts4 in range(4):
                t0 = ts4 * 128
                for h in range(16):
                    em.mm(pp[2][:, h:h + 1], qh[h][:, t0:t0 + 128], ks_col[:, h:h + 1],
                          inc=(1 if h == 15 else 0))
                em.ts(zt[:], pp[2][:, 0:16], 1e-6, op0=ALU.add)
                em.recip(zt[:], zt[:])
                for h in range(16):
                    b_ = pp[0] if h < 8 else pp[1]
                    cs_ = (h % 8) * 64
                    em.mm(b_[:, cs_:cs_ + 64], qh[h][:, t0:t0 + 128],
                          kv_sb[:, h * 64:(h + 1) * 64], inc=(1 if h in (7, 15) else 0))
                for bi_, b_ in enumerate((pp[0], pp[1])):
                    em.tt(sc_t[:, bi_ * 512:(bi_ + 1) * 512].rearrange("p (h s) -> p h s", s=64),
                          b_[:, 0:512].rearrange("p (h s) -> p h s", s=64),
                          zt[:, bi_ * 8:(bi_ + 1) * 8].unsqueeze(2).broadcast_to((128, 8, 64)),
                          ALU.mult)
                em.dma(sc_s[tbase + t0:tbase + t0 + 128, :], sc_t[:], q="sp")

    with emscope(em):
        wo_sb = [em.sb(f"l1_wo{d}", [128, 1024], BF16) for d in range(8)]
        for d in range(8):
            em.dma(wo_sb[d][:], IN["wo1_bf"][d * 128:(d + 1) * 128, :], q="sp")
        borow = em.sb("l1_borow", [1, 1024], BF16)
        em.dma(borow[:], IN["L1_attn_bo"].rearrange("(r c) -> r c", r=1), q="pool")
        scT = [em.sb(f"l1_scT{d}", [128, 512], BF16, tdest=True) for d in range(8)]
        a1_t = em.sb("l1_a1t", [128, 1024], BF16)
        for chk in range(n_chunks):
            tbase = chk * 512
            for d in range(8):
                em.dmat(scT[d][:], sc_s[tbase:tbase + 512, d * 128:(d + 1) * 128], q="sp")
            for tt in range(4):
                for oc in range(2):
                    bank = pp[4 + oc]
                    for d in range(8):
                        em.mm(bank[:], scT[d][:, tt * 128:(tt + 1) * 128],
                              wo_sb[d][:, oc * 512:(oc + 1) * 512],
                              start=(d == 0), stop=False, inc=0)
                    em.mm(bank[:], onesb[0:1, 0:128], borow[0:1, oc * 512:(oc + 1) * 512],
                          start=False, stop=True)
                    em.copy(a1_t[:, oc * 512:(oc + 1) * 512], bank[:])
                em.dma(a1_s[tbase + tt * 128:tbase + tt * 128 + 128, :], a1_t[:], q="sp")

    with emscope(em):
        h_sb = [em.sb(f"l1_h{d}", [128, 1024], BF16) for d in range(8)]
        for d in range(8):
            em.dma(h_sb[d][:], IN["c_had"][d * 128:(d + 1) * 128, :], q="sp")
        xq_t = em.sb("l1_xqt", [128, 1024], BF16)
        for chk in range(n_chunks):
            tbase = chk * 512
            for d in range(8):
                em.dmat(ynT[d][:], yn_s[tbase:tbase + 512, d * 128:(d + 1) * 128], q="sp")
            for tt in range(4):
                for oc in range(2):
                    bank = pp[4 + oc]
                    for d in range(8):
                        em.mm(bank[:], ynT[d][:, tt * 128:(tt + 1) * 128],
                              h_sb[d][:, oc * 512:(oc + 1) * 512],
                              start=(d == 0), stop=(d == 7), inc=(1 if d == 7 else 0))
                    em.copy(fA[:, oc * 512:(oc + 1) * 512], bank[:])
                em.red(st[:, 0:1], fA[:], ALU.max, absval=True)
                em.ts(st[:, 1:2], st[:, 0:1], 1e-5, op0=ALU.max)
                em.ts(st[:, 2:3], st[:, 1:2], 1.0 / 7.0, op0=ALU.mult)
                em.recip(st[:, 3:4], st[:, 2:3])
                em.ts(fB[:], fA[:], st[:, 3:4], op0=ALU.mult)
                em.copy(iA[:], fB[:])
                em.copy(fC[:], iA[:])
                e = iA[:].bitcast(F32)
                em.tt(e, fB[:], fC[:], ALU.subtract)
                em.ts(fD[:], e, 0.5, op0=ALU.is_ge)
                em.stt(fD[:], e, -0.5, fD[:], ALU.is_lt, ALU.subtract)
                em.tt(fC[:], fC[:], fD[:], ALU.subtract)
                em.ts(xq_t[:], fC[:], 7.0, -8.0, op0=ALU.min, op1=ALU.max)
                trow = tbase + tt * 128
                em.dma(xq_s[trow:trow + 128, :], xq_t[:], q="sp")
                em.dma(OUT["s1q_s"][trow:trow + 128, :], st[:, 2:3], q="sp")

    with emscope(em):
        wf_sb = [em.sb(f"l1_wf{d}", [128, 1024], BF16, tdest=True) for d in range(8)]
        for d in range(8):
            em.dmat(wf_sb[d][:], IN["wf1q_rows"][:, d * 128:(d + 1) * 128], q="sp")
        xqT = [em.sb(f"l1_xqT{d}", [128, 512], BF16, tdest=True) for d in range(8)]
        x1_t = em.sb("l1_x1t", [128, 1024], BF16)
        a1r_t = em.sb("l1_a1r", [128, 1024], BF16)
        sq_c = em.sb("l1_sqc", [128, 1], F32)
        out_t = em.sb("l1_outt", [128, 1024], F32)
        for chk in range(n_chunks):
            tbase = chk * 512
            for d in range(8):
                em.dmat(xqT[d][:], xq_s[tbase:tbase + 512, d * 128:(d + 1) * 128], q="sp")
            for tt in range(4):
                trow = tbase + tt * 128
                em.dma(x1_t[:], IN["x1_s"][trow:trow + 128, :], q="sp")
                em.dma(a1r_t[:], a1_s[trow:trow + 128, :], q="sp")
                em.dma(sq_c[:], OUT["s1q_s"][trow:trow + 128, :], q="sp")
                for oc in range(2):
                    bank = pp[4 + oc]
                    for d in range(8):
                        em.mm(bank[:], xqT[d][:, tt * 128:(tt + 1) * 128],
                              wf_sb[d][:, oc * 512:(oc + 1) * 512],
                              start=(d == 0), stop=(d == 7), inc=(1 if d == 7 else 0))
                    sl_ = slice(oc * 512, (oc + 1) * 512)
                    em.stt(fA[:, 0:512], bank[:], sq_c[:, 0:1], x1_t[:, sl_], ALU.mult, ALU.add)
                    em.tt(out_t[:, sl_], fA[:, 0:512], a1r_t[:, sl_], ALU.add)
                em.dma(OUT["out"][trow:trow + 128, :], out_t[:], q="sp")


L1_INS = dict(
    x1_s=([T, D], BF16),
    wq1_bf=([1024, 1024], BF16), wk1_bf=([1024, 1024], BF16),
    wv1_bf=([1024, 1024], BF16), wo1_bf=([1024, 1024], BF16),
    L1_attn_bq=([1024], F32), L1_attn_bk=([1024], F32),
    L1_attn_bv=([1024], F32), L1_attn_bo=([1024], F32),
    wf1q_rows=([1024, 1024], BF16),
    c_ident=([128, 128], BF16), c_ones=([1, 128], BF16), c_had=([1024, 1024], BF16),
)
L1_OUTS = dict(out=([T, D], F32), yn_s=([T, D], BF16), a1_s=([T, D], BF16),
               sc_s=([T, D], BF16), xq_s=([T, D], BF16), s1q_s=([T, 1], F32))



_CACHE = {}


def _phase(name, emit_fn, ins_spec, outs_spec):
    if name not in _CACHE:
        _CACHE[name] = build_phase(emit_fn, ins_spec, outs_spec)
    return _CACHE[name]


def kernel_multilaunch(**inputs):
    inputs = {k: np.ascontiguousarray(v) for k, v in inputs.items()}
    consts = host_consts()
    NCORE = 8
    cores = list(range(NCORE))

    prep_nc = _phase("prep", emit_prep, {n: (INPUT_SHAPES[n], F32) for n in PREP_INS}, PREP_OUTS)
    prep_in = {n: inputs[n].astype(np.float32) for n in PREP_INS}
    prep_res = run_bass_kernel_spmd(prep_nc, [prep_in] * NCORE, core_ids=cores)
    pw = prep_res.results

    l0a_nc = _phase("l0a", emit_l0a, L0A_INS, L0A_OUTS)
    l0a_maps = []
    for c in cores:
        m = dict(x=inputs["L0_attn_hidden_states"][c].astype(np.float32),
                 wq_bf=pw[c]["wq_bf"], wk_bf=pw[c]["wk_bf"], wv_bf=pw[c]["wv_bf"],
                 wg_bf=pw[c]["wg_bf"], wo_bf=pw[c]["wo_bf"], wgkb_bf=pw[c]["wgkb_bf"],
                 L0_attn_conv_q=inputs["L0_attn_conv_q"], L0_attn_conv_k=inputs["L0_attn_conv_k"],
                 L0_attn_conv_v=inputs["L0_attn_conv_v"],
                 L0_attn_A_log=inputs["L0_attn_A_log"], L0_attn_dt_bias=inputs["L0_attn_dt_bias"],
                 L0_attn_b_b=inputs["L0_attn_b_b"],
                 L0_attn_g_norm_weight=inputs["L0_attn_g_norm_weight"],
                 c_ident=consts["c_ident"], c_ms=consts["c_ms"], c_bigL=consts["c_bigL"],
                 c_bigUi=consts["c_bigUi"], c_bigUs=consts["c_bigUs"])
        l0a_maps.append(m)
    l0a_res = run_bass_kernel_spmd(l0a_nc, l0a_maps, core_ids=cores).results

    l0f_nc = _phase("l0f", emit_l0f, L0F_INS, L0F_OUTS)
    l0f_maps = []
    for c in cores:
        m = dict(x=inputs["L0_attn_hidden_states"][c].astype(np.float32),
                 xn_s=l0a_res[c]["xn_s"], a0_s=l0a_res[c]["a0_s"],
                 w1q_rows=pw[c]["w1q_rows"], w2q_rows=pw[c]["w2q_rows"],
                 L0_ffn_ln1_w=inputs["L0_ffn_ln1_w"], L0_ffn_ln1_b=inputs["L0_ffn_ln1_b"],
                 L0_ffn_ln2_w=inputs["L0_ffn_ln2_w"], L0_ffn_ln2_b=inputs["L0_ffn_ln2_b"],
                 L0_ffn_b1=inputs["L0_ffn_b1"], L0_ffn_b2=inputs["L0_ffn_b2"],
                 c_ones=consts["c_ones"], c_ident=consts["c_ident"])
        l0f_maps.append(m)
    l0f_res = run_bass_kernel_spmd(l0f_nc, l0f_maps, core_ids=cores).results

    l1_nc = _phase("l1", emit_l1, L1_INS, L1_OUTS)
    l1_maps = []
    for c in cores:
        m = dict(x1_s=l0f_res[c]["x1_s"],
                 wq1_bf=pw[c]["wq1_bf"], wk1_bf=pw[c]["wk1_bf"],
                 wv1_bf=pw[c]["wv1_bf"], wo1_bf=pw[c]["wo1_bf"],
                 L1_attn_bq=inputs["L1_attn_bq"], L1_attn_bk=inputs["L1_attn_bk"],
                 L1_attn_bv=inputs["L1_attn_bv"], L1_attn_bo=inputs["L1_attn_bo"],
                 wf1q_rows=pw[c]["wf1q_rows"],
                 c_ident=consts["c_ident"], c_ones=consts["c_ones"], c_had=consts["c_had"])
        l1_maps.append(m)
    l1_res = run_bass_kernel_spmd(l1_nc, l1_maps, core_ids=cores).results

    out = np.stack([l1_res[c]["out"] for c in cores], axis=0)
    return out



def build_fused():
    nc = bass.Bass("TRN2", target_bir_lowering=False)
    ext_in = {}
    for n in PREP_INS:
        ext_in[n] = nc.dram_tensor(n, INPUT_SHAPES[n], F32, kind="ExternalInput").ap()
    for n, sh in [("x", [T, D]), ("L0_attn_A_log", [12]), ("L0_attn_dt_bias", [12]),
                  ("L0_attn_b_b", [12]), ("L0_attn_g_norm_weight", [128]),
                  ("L0_ffn_ln1_w", [1024]), ("L0_ffn_ln1_b", [1024]),
                  ("L0_ffn_ln2_w", [4096]), ("L0_ffn_ln2_b", [4096]),
                  ("L0_ffn_b1", [4096]), ("L0_ffn_b2", [1024]),
                  ("L1_attn_bq", [1024]), ("L1_attn_bk", [1024]),
                  ("L1_attn_bv", [1024]), ("L1_attn_bo", [1024])]:
        if n not in ext_in:
            ext_in[n] = nc.dram_tensor(n, sh, F32, kind="ExternalInput").ap()
    for n, (sh, dt) in CONST_SPECS.items():
        ext_in[n] = nc.dram_tensor(n, list(sh), dt, kind="ExternalInput").ap()
    out = nc.dram_tensor("out", [T, D], F32, kind="ExternalOutput").ap()

    MID = {}
    for spec in (PREP_OUTS, L0A_OUTS, L0F_OUTS, L1_OUTS):
        for n, (sh, dt) in spec.items():
            if n == "out":
                continue
            if n not in MID:
                MID[n] = nc.dram_tensor(n, list(sh), dt, kind="Internal").ap()
    MID["out"] = out

    keys = list(Em.QUEUES) + [f"{q}{l}" for q in ("sp", "pool") for l in range(Em.NLANES)]
    sem_ctx = [nc.semaphore(f"q_{k}") for k in keys]
    sems = {}
    import contextlib
    with contextlib.ExitStack() as stack:
        for k, ctx in zip(keys, sem_ctx):
            sems[k] = stack.enter_context(ctx)
        with nc.Block() as block:
            @block.vector
            def _(vector):
                em = Em(nc, sems)
                with contextlib.ExitStack() as tstack:
                    em.stack = tstack
                    allin = {**ext_in, **MID}
                    with emscope(em):
                        emit_prep(em, ext_in, MID)
                    with emscope(em):
                        emit_l0a(em, allin, MID)
                    with emscope(em):
                        emit_l0f(em, allin, MID)
                    with emscope(em):
                        emit_l1(em, allin, MID)
                    em.barrier_store_done()
    return nc


def kernel(**inputs):
    inputs = {k: np.ascontiguousarray(v) for k, v in inputs.items()}
    consts = host_consts()
    NCORE = 8
    if "fused" not in _CACHE:
        _CACHE["fused"] = build_fused()
    nc = _CACHE["fused"]
    maps = []
    for c in range(NCORE):
        m = {n: inputs[n].astype(np.float32) for n in PREP_INS}
        m["x"] = inputs["L0_attn_hidden_states"][c].astype(np.float32)
        for n in ["L0_attn_A_log", "L0_attn_dt_bias", "L0_attn_b_b", "L0_attn_g_norm_weight",
                  "L0_ffn_ln1_w", "L0_ffn_ln1_b", "L0_ffn_ln2_w", "L0_ffn_ln2_b",
                  "L0_ffn_b1", "L0_ffn_b2", "L1_attn_bq", "L1_attn_bk",
                  "L1_attn_bv", "L1_attn_bo"]:
            m[n] = inputs[n].astype(np.float32)
        for n in CONST_SPECS:
            m[n] = consts[n]
        maps.append(m)
    res = run_bass_kernel_spmd(nc, maps, core_ids=list(range(NCORE)))
    return np.stack([res.results[c]["out"] for c in range(NCORE)], axis=0)


# version 03
# speedup vs baseline: 1.0041x; 1.0013x over previous
import numpy as np
import ml_dtypes
import concourse.bass as bass
import concourse.mybir as mybir
from concourse.bass_utils import run_bass_kernel_spmd

F32 = mybir.dt.float32
BF16 = mybir.dt.bfloat16
I32 = mybir.dt.int32
AF = mybir.ActivationFunctionType
ALU = mybir.AluOpType
AX = mybir.AxisListType
BF = ml_dtypes.bfloat16

T = 4096
D = 1024
NH, DK, DV = 12, 64, 128
KD, VD = NH * DK, NH * DV
CS = 64
HID = 4096
H1, DH1 = 16, 64
EPS_LN = 1e-5



class Em:
    QUEUES = ("pe", "act", "dve", "pool", "sp")
    NLANES = 8

    def __init__(self, nc, sems):
        self.nc = nc
        self.eng = dict(pe=nc.tensor, act=nc.scalar, dve=nc.vector,
                        pool=nc.gpsimd, sp=nc.sync)
        self.sems = sems
        self.cnt = {k: 0 for k in sems}
        self.lane = {"sp": 0, "pool": 0}
        self.acc = {}
        self.waited = {q: {} for q in self.QUEUES}

    @staticmethod
    def _rng(ap):
        lo = hi = ap.offset
        for step, count in ap.ap:
            d = step * (count - 1)
            if d >= 0:
                hi += d
            else:
                lo += d
        return ap.tensor.name, lo, hi

    def _deps_for(self, ap, write):
        name, lo, hi = self._rng(ap)
        deps = []
        for rec in self.acc.get(name, ()):
            if rec[1] < lo or rec[0] > hi:
                continue
            if rec[4] or write:
                deps.append((rec[2], rec[3]))
        return deps

    def _record(self, ap, q, cnt, write):
        name, lo, hi = self._rng(ap)
        lst = self.acc.setdefault(name, [])
        if write:
            lst[:] = [r for r in lst if not (lo <= r[0] and r[1] <= hi)]
        lst.append([lo, hi, q, cnt, write])
        if len(lst) > 400:
            del lst[: len(lst) - 250]

    def _wait(self, q, deps):
        need = {}
        for dq, c in deps:
            if dq == q == "pe":
                continue
            if c is None:
                raise RuntimeError("dependency on un-inc'd (lazy) op")
            if need.get(dq, 0) < c:
                need[dq] = c
        for dq, c in need.items():
            if self.waited[q].get(dq, 0) >= c:
                continue
            self.eng[q].wait_ge(self.sems[dq], c)
            self.waited[q][dq] = c

    def op(self, q, fname, outs, ins, inc=1, _call=None, **kw):
        deps = []
        for ap in ins:
            if isinstance(ap, (int, float)) or ap is None:
                continue
            deps += self._deps_for(ap, False)
        for ap in outs:
            if ap is None:
                continue
            deps += self._deps_for(ap, True)
        self._wait(q, deps)
        inst = (_call or getattr(self.eng[q], fname))(**kw)
        c = None
        if inc:
            inst.then_inc(self.sems[q], inc)
            self.cnt[q] += inc
            c = self.cnt[q]
        else:
            c = self.cnt[q] + 1
        for ap in ins:
            if isinstance(ap, (int, float)) or ap is None:
                continue
            self._record(ap, q, c, False)
        for ap in outs:
            if ap is None:
                continue
            self._record(ap, q, c, True)
        return inst

    def _dma_issue(self, q, outs, ins, transpose=False, **kw):
        ln = self.lane[q]
        self.lane[q] = (ln + 1) % self.NLANES
        key = f"{q}{ln}"
        deps = []
        for ap in ins:
            deps += self._deps_for(ap, False)
        for ap in outs:
            deps += self._deps_for(ap, True)
        deps.append((key, self.cnt[key]))
        if not hasattr(self, "_cat"):
            self._cat = {"tr": {}, "ss": {}}
        is_ss = (not transpose and ins[0].tensor.space.name == "SBUF"
                 and outs[0].tensor.space.name == "SBUF")
        if transpose:
            deps += list(self._cat["ss"].items())
        elif is_ss:
            deps += list(self._cat["tr"].items())
        self._wait(q, deps)
        if transpose:
            kw["transpose"] = True
        inst = self.eng[q].dma_start(**kw)
        inst.then_inc(self.sems[key], 16)
        self.cnt[key] += 16
        c = self.cnt[key]
        if transpose:
            self._cat["tr"][key] = c
        elif is_ss:
            self._cat["ss"][key] = c
        for ap in ins:
            self._record(ap, key, c, False)
        for ap in outs:
            self._record(ap, key, c, True)
        return inst

    def dma(self, out, in_, q="sp", **kw):
        return self._dma_issue(q, [out], [in_], out=out, in_=in_, **kw)

    def dmat(self, out, in_, q="sp"):
        return self._dma_issue(q, [out], [in_], transpose=True, out=out, in_=in_)

    def mm(self, out, lhsT, rhs, start=True, stop=True, inc=1, tile_position=None):
        return self.op("pe", "matmul", [out], [lhsT, rhs], inc=inc, out=out,
                       lhsT=lhsT, rhs=rhs, start=start, stop=stop,
                       tile_position=tile_position)

    def tr(self, out, in_, ident, inc=1):
        return self.op("pe", "transpose", [out], [in_, ident], inc=inc,
                       out=out, in_=in_, identity=ident)

    def constf(self, value):
        value = float(value)
        if not hasattr(self, "_consts"):
            self._consts = {}
        if value not in self._consts:
            t = self.stack.enter_context(
                self.nc.sbuf_tensor(f"c_f32_{len(self._consts)}", [128, 1], F32))
            self.memset(t[:], value)
            self._consts[value] = t
        return self._consts[value][:]

    def act(self, out, in_, func, bias=None, scale=None, accum_out=None, q="act"):
        npart = out.ap[0][1]
        if isinstance(bias, (int, float)) and float(bias) != 0.0:
            bias = self.constf(bias)[0:npart, 0:1]
        if isinstance(scale, (int, float)) and float(scale) == 1.0:
            scale = None
        if isinstance(scale, (int, float)):
            scale = self.constf(scale)[0:npart, 0:1]
        kw = dict(out=out, in_=in_, func=func)
        ins = [in_]
        if bias is not None and not (isinstance(bias, float) and bias == 0.0):
            kw["bias"] = bias
            if not isinstance(bias, (int, float)):
                ins.append(bias)
        if scale is not None:
            kw["scale"] = scale
            if not isinstance(scale, (int, float)):
                ins.append(scale)
        if accum_out is not None:
            kw["accum_out"] = accum_out
        return self.op(q, "activation", [out, accum_out], ins, **kw)

    def ts(self, out, in0, s1, s2=None, op0=ALU.mult, op1=None, q="dve", accum_out=None):
        ins = [in0] + [s for s in (s1, s2) if s is not None and not isinstance(s, (int, float))]
        kw = dict(out=out, in0=in0, scalar1=s1, scalar2=s2, op0=op0)
        if op1 is not None:
            kw["op1"] = op1
        if accum_out is not None:
            kw["accum_out"] = accum_out
        return self.op(q, "tensor_scalar", [out, accum_out], ins, **kw)

    def tt(self, out, in0, in1, alu, q="dve"):
        return self.op(q, "tensor_tensor", [out], [in0, in1], out=out, in0=in0, in1=in1, op=alu)

    def stt(self, out, in0, s, in1, op0, op1, q="dve"):
        ins = [in0, in1] + ([s] if not isinstance(s, (int, float)) else [])
        return self.op(q, "scalar_tensor_tensor", [out], ins, out=out, in0=in0,
                       scalar=s, in1=in1, op0=op0, op1=op1)

    def copy(self, out, in_, q="dve"):
        return self.op(q, "tensor_copy", [out], [in_], out=out, in_=in_)

    def red(self, out, in_, alu, absval=None, q="dve"):
        return self.op(q, "tensor_reduce", [out], [in_], out=out, in_=in_,
                       axis=AX.X, op=alu, apply_absolute_value=absval)

    def scan(self, out, d0, d1, initial, op0, op1, q="dve"):
        return self.op(q, "tensor_tensor_scan", [out], [d0, d1], out=out, data0=d0,
                       data1=d1, initial=initial, op0=op0, op1=op1)

    def memset(self, ap, val, q="dve"):
        return self.op(q, "memset", [ap], [], ap=ap, constant=val)

    def recip(self, out, in_, q="dve"):
        return self.op(q, "reciprocal", [out], [in_], out=out, in_=in_)

    def sb(self, name, shape, dtype, tdest=False):
        return self.stack.enter_context(
            self.nc.sbuf_tensor(name, list(shape), dtype, is_transpose_dest=tdest))

    def ps(self, name, shape, dtype=F32):
        return self.stack.enter_context(self.nc.psum_tensor(name, list(shape), dtype))

    def quiesce(self):
        for q in self.QUEUES:
            for dk, c in self.cnt.items():
                if dk != q and c and self.waited[q].get(dk, 0) < c:
                    self.eng[q].wait_ge(self.sems[dk], c)
                    self.waited[q][dk] = c

    def barrier_store_done(self):
        for dk, c in self.cnt.items():
            if dk != "sp" and c:
                if self.waited["sp"].get(dk, 0) < c:
                    self.eng["sp"].wait_ge(self.sems[dk], c)
                    self.waited["sp"][dk] = c
        for ln in range(self.NLANES):
            key = f"sp{ln}"
            if self.cnt.get(key, 0):
                self.eng["sp"].wait_ge(self.sems[key], self.cnt[key])


def rsqrt_em(em, out, in_, eps, q="act", scratch=None):
    s = scratch if scratch is not None else out
    em.act(s, in_, AF.Sqrt, bias=float(eps), q=q)
    em.recip(out, s)



def host_consts():
    ident = np.eye(128, dtype=BF)
    ones_row = np.ones((1, 128), dtype=BF)
    t_idx = np.arange(CS)[:, None]
    s_idx = np.arange(CS)[None, :]
    ms = np.tril(np.ones((CS, CS), np.float32), -1).astype(BF)
    bigL = (30000.0 * (s_idx > t_idx)).astype(np.float32)
    bigUi = (-30000.0 * (t_idx > s_idx)).astype(np.float32)
    bigUs = (-30000.0 * (t_idx >= s_idx)).astype(np.float32)
    h = np.array([[1.0]], dtype=np.float64)
    while h.shape[0] < 1024:
        h = np.block([[h, h], [h, -h]])
    had = (h / np.sqrt(1024)).astype(np.float32).astype(BF)
    return dict(c_ident=ident, c_ones=ones_row, c_ms=ms, c_bigL=bigL,
                c_bigUi=bigUi, c_bigUs=bigUs, c_had=had)


CONST_SPECS = dict(c_ident=([128, 128], BF16), c_ones=([1, 128], BF16),
                   c_ms=([CS, CS], BF16), c_bigL=([CS, CS], F32),
                   c_bigUi=([CS, CS], F32), c_bigUs=([CS, CS], F32),
                   c_had=([1024, 1024], BF16))



def round_chain(em, out_bf, x_f32, tmp_i32, tmp_a, tmp_b, lo, hi, scale_col=None,
                post_mul_col=None, q1="dve", q2="dve"):
    if scale_col is not None:
        em.ts(tmp_a, x_f32, scale_col, op0=ALU.mult, q=q1)
        src = tmp_a
    else:
        src = x_f32
    em.copy(tmp_i32, src, q=q1)
    em.copy(tmp_b, tmp_i32, q=q1)
    em.tt(tmp_i32.bitcast(F32), src, tmp_b, ALU.subtract, q=q2)
    e = tmp_i32.bitcast(F32)
    em.ts(tmp_a, e, 0.5, op0=ALU.is_ge, q=q2)
    em.stt(tmp_a, e, -0.5, tmp_a, ALU.is_lt, ALU.subtract, q=q2)
    em.tt(tmp_b, tmp_b, tmp_a, ALU.subtract, q=q2)
    if post_mul_col is None:
        em.ts(out_bf, tmp_b, float(hi), float(lo), op0=ALU.min, op1=ALU.max, q=q1)
    else:
        em.ts(tmp_b, tmp_b, float(hi), float(lo), op0=ALU.min, op1=ALU.max, q=q1)
        em.ts(out_bf, tmp_b, post_mul_col, op0=ALU.mult, q=q1)


def emit_prep(em, IN, OUT):
    nc = em.nc
    sw = em.sb("pr_sw", [128, 4096], F32)
    swb = em.sb("pr_swb", [128, 4096], F32)
    swc = em.sb("pr_swc", [128, 4096], F32)
    swi = em.sb("pr_swi", [128, 4096], I32)
    bo0 = em.sb("pr_bo0", [128, 4096], BF16)
    bo1 = em.sb("pr_bo1", [128, 4096], BF16)
    acc = em.sb("pr_acc", [128, 1], F32)
    tpart = em.sb("pr_tp", [128, 1], F32)
    sc = em.sb("pr_sc", [128, 8], F32)
    onesf = em.sb("pr_onesf", [128, 1], F32)
    onesr = em.sb("pr_onesr", [1, 128], F32)
    ps = em.ps("pr_ps", [1, 8])
    ps2 = em.ps("pr_ps2", [128, 8])
    em.memset(onesf[:], 1.0)
    em.memset(onesr[:], 1.0)

    casts = [
        ("L0_attn_Wq", "wq_bf", 1024, 768), ("L0_attn_Wk", "wk_bf", 1024, 768),
        ("L0_attn_Wv", "wv_bf", 1024, 1536), ("L0_attn_Wg", "wg_bf", 1024, 1536),
        ("L0_attn_Wo", "wo_bf", 1536, 1024),
        ("L1_attn_wq", "wq1_bf", 1024, 1024), ("L1_attn_wk", "wk1_bf", 1024, 1024),
        ("L1_attn_wv", "wv1_bf", 1024, 1024), ("L1_attn_wo", "wo1_bf", 1024, 1024),
    ]
    bi = 0
    for src_, dst, rows, cols in casts:
        s, d = IN[src_], OUT[dst]
        for r0 in range(0, rows, 128):
            b = (bo0, bo1)[bi % 2]; bi += 1
            em.dma(b[:, 0:cols], s[r0:r0 + 128, 0:cols], q="pool")
            em.dma(d[r0:r0 + 128, 0:cols], b[:, 0:cols], q="sp")

    for src_, dst, rows in [("L0_attn_conv_q", "convq_bf", 768),
                            ("L0_attn_conv_k", "convk_bf", 768),
                            ("L0_attn_conv_v", "convv_bf", 1536)]:
        ncol = rows * 4 // 128
        em.dma(sw[:, 0:ncol], IN[src_].rearrange("a b -> (a b)").rearrange("(p n) -> p n", p=128), q="sp")
        em.copy(bo0[:, 0:ncol], sw[:, 0:ncol])
        em.dma(OUT[dst].rearrange("a b -> (a b)").rearrange("(p n) -> p n", p=128), bo0[:, 0:ncol], q="sp")

    em.memset(bo1[:, 0:352], 0.0)
    em.dma(OUT["wgkb_bf"].rearrange("a b -> (a b)").rearrange("(p n) -> p n", p=128),
           bo1[:, 0:352], q="sp")
    for src_, c0 in [("L0_attn_Wgk", 0), ("L0_attn_Wb", 32)]:
        em.dma(sw[:, 0:96], IN[src_].rearrange("a b -> (a b)").rearrange("(p n) -> p n", p=128), q="sp")
        em.copy(bo0[:, 0:96], sw[:, 0:96])
        em.dma(OUT["wgkb_bf"][:, c0:c0 + 12].rearrange("(p r) c -> p r c", p=128),
               bo0[:, 0:96].rearrange("p (r c) -> p r c", c=12), q="sp")

    for wname, oname, nelem in [("L0_ffn_w1", "w1q_rows", 4096 * 1024),
                                ("L0_ffn_w2", "w2q_rows", 1024 * 4096)]:
        w = IN[wname]
        wflat = w.rearrange("a b -> (a b)").rearrange("(p n) -> p n", p=128)
        o = OUT[oname]
        oflat = o.rearrange("a b -> (a b)").rearrange("(p n) -> p n", p=128)
        NCOL = nelem // 128
        TL = 4096
        em.memset(acc[:], 0.0)
        for c0 in range(0, NCOL, TL):
            em.dma(sw[:], wflat[:, c0:c0 + TL], q="sp")
            em.act(swb[:], sw[:], AF.Abs, accum_out=tpart[:])
            em.tt(acc[:], acc[:], tpart[:], ALU.add)
        em.mm(ps[0:1, 0:1], acc[:], onesf[:])
        em.copy(sc[0:1, 4:5], ps[0:1, 0:1])
        em.mm(ps2[:, 0:1], onesr[:], sc[0:1, 4:5])
        em.ts(sc[:, 0:1], ps2[:, 0:1], 1.0 / nelem, 1e-5, op0=ALU.mult, op1=ALU.max)
        em.recip(sc[:, 1:2], sc[:, 0:1])
        for c0 in range(0, NCOL, TL):
            em.dma(sw[:], wflat[:, c0:c0 + TL], q="sp")
            round_chain(em, bo0[:], sw[:], swi[:], swb[:], swc[:], -1.0, 1.0,
                        scale_col=sc[:, 1:2], post_mul_col=sc[:, 0:1])
            em.dma(oflat[:, c0:c0 + TL], bo0[:], q="sp")

    w = IN["L1_ffn_weight"]
    o = OUT["wf1q_rows"]
    for r0 in range(0, 1024, 128):
        em.dma(sw[:, 0:1024], w[r0:r0 + 128, :], q="sp")
        em.act(swb[:, 0:1024], sw[:, 0:1024], AF.Abs, accum_out=sc[:, 0:1])
        em.ts(sc[:, 1:2], sc[:, 0:1], 1.0 / 1024, 1e-5, op0=ALU.mult, op1=ALU.max)
        em.recip(sc[:, 2:3], sc[:, 1:2])
        em.ts(swc[:, 0:1024], sw[:, 0:1024], sc[:, 2:3], op0=ALU.mult)
        em.ts(swb[:, 0:1024], swc[:, 0:1024], 0.5, op0=ALU.is_gt)
        em.stt(swb[:, 0:1024], swc[:, 0:1024], -0.5, swb[:, 0:1024], ALU.is_lt, ALU.subtract)
        em.ts(sc[:, 3:4], sc[:, 1:2], -1.0, op0=ALU.mult)
        em.ts(bo0[:, 0:1024], swb[:, 0:1024], sc[:, 3:4], op0=ALU.mult)
        em.dma(o[r0:r0 + 128, :], bo0[:, 0:1024], q="sp")


PREP_OUTS = dict(
    wq_bf=([1024, 768], BF16), wk_bf=([1024, 768], BF16), wv_bf=([1024, 1536], BF16),
    wg_bf=([1024, 1536], BF16), wo_bf=([1536, 1024], BF16), wgkb_bf=([1024, 44], BF16),
    convq_bf=([768, 4], BF16), convk_bf=([768, 4], BF16), convv_bf=([1536, 4], BF16),
    w1q_rows=([4096, 1024], BF16), w2q_rows=([1024, 4096], BF16),
    wf1q_rows=([1024, 1024], BF16),
    wq1_bf=([1024, 1024], BF16), wk1_bf=([1024, 1024], BF16),
    wv1_bf=([1024, 1024], BF16), wo1_bf=([1024, 1024], BF16),
)

PREP_INS = ["L0_attn_Wq", "L0_attn_Wk", "L0_attn_Wv", "L0_attn_Wg", "L0_attn_Wo",
            "L0_attn_conv_q", "L0_attn_conv_k", "L0_attn_conv_v", "L0_attn_Wgk",
            "L0_attn_Wb", "L0_ffn_w1", "L0_ffn_w2", "L1_attn_wq", "L1_attn_wk",
            "L1_attn_wv", "L1_attn_wo", "L1_ffn_weight"]

INPUT_SHAPES = dict(
    L0_attn_Wq=[1024, 768], L0_attn_Wk=[1024, 768], L0_attn_Wv=[1024, 1536],
    L0_attn_Wg=[1024, 1536], L0_attn_Wo=[1536, 1024], L0_attn_conv_q=[768, 4],
    L0_attn_conv_k=[768, 4], L0_attn_conv_v=[1536, 4], L0_attn_Wgk=[1024, 12],
    L0_attn_Wb=[1024, 12], L0_ffn_w1=[4096, 1024], L0_ffn_w2=[1024, 4096],
    L1_attn_wq=[1024, 1024], L1_attn_wk=[1024, 1024], L1_attn_wv=[1024, 1024],
    L1_attn_wo=[1024, 1024], L1_ffn_weight=[1024, 1024],
)


def build_phase(emit_fn, ext_ins, ext_outs, consts=()):
    nc = bass.Bass("TRN2", target_bir_lowering=False)
    IN = {n: nc.dram_tensor(n, list(sh), dt, kind="ExternalInput").ap()
          for n, (sh, dt) in ext_ins.items()}
    OUT = {n: nc.dram_tensor(n, list(sh), dt, kind="ExternalOutput").ap()
           for n, (sh, dt) in ext_outs.items()}
    keys = list(Em.QUEUES) + [f"{q}{l}" for q in ("sp", "pool") for l in range(Em.NLANES)]
    sem_ctx = [nc.semaphore(f"q_{k}") for k in keys]
    sems = {}
    import contextlib
    with contextlib.ExitStack() as stack:
        for k, ctx in zip(keys, sem_ctx):
            sems[k] = stack.enter_context(ctx)
        with nc.Block() as block:
            @block.vector
            def _(vector):
                em = Em(nc, sems)
                with contextlib.ExitStack() as tstack:
                    em.stack = tstack
                    emit_fn(em, IN, OUT)
                    em.barrier_store_done()
    return nc


import contextlib

@contextlib.contextmanager
def emscope(em):
    with contextlib.ExitStack() as inner:
        outer = em.stack
        em.stack = inner
        try:
            yield
        finally:
            em.quiesce()
            em.stack = outer



def _bcast_const_mat(em, out_sb, vec_dram, width, ones_f, ps_bc, tmp_row):
    em.dma(tmp_row[0:1, 0:width], vec_dram.rearrange("(r c) -> r c", r=1), q="sp")
    for c0 in range(0, width, 512):
        em.mm(ps_bc[:, 0:512], ones_f[0:1, :], tmp_row[0:1, c0:c0 + 512])
        em.copy(out_sb[:, c0:c0 + 512], ps_bc[:, 0:512])


def _round_quant(em, out_bf, x_f32, scol, lo, hi, fB, fC, fD, iA, n):
    em.ts(fB[:, 0:n], x_f32, scol, op0=ALU.mult)
    em.copy(iA[:, 0:n], fB[:, 0:n])
    em.copy(fC[:, 0:n], iA[:, 0:n])
    e = iA[:, 0:n].bitcast(F32)
    em.tt(e, fB[:, 0:n], fC[:, 0:n], ALU.subtract)
    em.ts(fD[:, 0:n], e, 0.5, op0=ALU.is_ge)
    em.stt(fD[:, 0:n], e, -0.5, fD[:, 0:n], ALU.is_lt, ALU.subtract)
    em.tt(fC[:, 0:n], fC[:, 0:n], fD[:, 0:n], ALU.subtract)
    em.ts(out_bf, fC[:, 0:n], float(hi), float(lo), op0=ALU.min, op1=ALU.max)


def emit_l0f(em, IN, OUT):
    nc = em.nc
    ones = em.sb("ff_ones", [1, 128], BF16)
    em.dma(ones[:], IN["c_ones"], q="sp")
    onesf = em.sb("ff_onesf", [1, 128], F32)
    em.memset(onesf[:], 1.0)
    ident = em.sb("ff_ident", [128, 128], BF16)
    em.dma(ident[:], IN["c_ident"], q="sp")
    ps_bc = em.ps("ff_psbc", [128, 512])
    tmp_row = em.sb("ff_trow", [1, 4096], F32)

    c_ln1w = em.sb("ff_ln1w", [128, 1024], F32)
    c_ln1b = em.sb("ff_ln1b", [128, 1024], F32)
    c_ln2w = em.sb("ff_ln2w", [128, 4096], F32)
    c_ln2b = em.sb("ff_ln2b", [128, 4096], F32)
    for mat, vec, w in [(c_ln1w, "L0_ffn_ln1_w", 1024), (c_ln1b, "L0_ffn_ln1_b", 1024),
                        (c_ln2w, "L0_ffn_ln2_w", 4096), (c_ln2b, "L0_ffn_ln2_b", 4096)]:
        _bcast_const_mat(em, mat, IN[vec], w, onesf, ps_bc, tmp_row)
    b1row = em.sb("ff_b1row", [1, 4096], BF16)
    em.dma(b1row[:], IN["L0_ffn_b1"].rearrange("(r c) -> r c", r=1), q="pool")
    b2row = em.sb("ff_b2row", [1, 1024], BF16)
    em.dma(b2row[:], IN["L0_ffn_b2"].rearrange("(r c) -> r c", r=1), q="pool")

    xn_t = em.sb("ff_xn", [128, 1024], BF16)
    fA = em.sb("ff_fA", [128, 1024], F32)
    fB = em.sb("ff_fB", [128, 1024], F32)
    fC = em.sb("ff_fC", [128, 1024], F32)
    fD = em.sb("ff_fD", [128, 1024], F32)
    iA = em.sb("ff_iA", [128, 1024], I32)
    hq_t = em.sb("ff_hq", [128, 1024], BF16)
    st = em.sb("ff_st", [128, 32], F32)
    gacc = em.sb("ff_gacc", [128, 32], F32)
    sqs = em.sb("ff_sqs", [128, 512], F32)
    mq_t = em.sb("ff_mqt", [128, 1024], BF16)
    srow = em.sb("ff_srow", [1, 128], BF16)
    sbfc = em.sb("ff_sbfc", [128, 1], BF16)
    ps_tr = em.ps("ff_pstr", [128, 128], BF16)
    pu = [em.ps(f"ff_pu{i}", [128, 512]) for i in range(2)]

    xn_s, mq_s, s2_s = IN["xn_s"], OUT["mq_s"], OUT["s2_s"]

    with emscope(em):
        w1sb = [em.sb(f"ff_w1_{d}", [128, 4096], BF16, tdest=True) for d in range(8)]
        for d in range(8):
            em.dmat(w1sb[d][:], IN["w1q_rows"][:, d * 128:(d + 1) * 128], q="sp")
        hqT = [em.sb(f"ff_hqT{d}", [128, 128], BF16, tdest=True) for d in range(8)]
        gm = em.sb("ff_gm", [128, 4096], BF16)

        for tt in range(32):
            trow = tt * 128
            em.dma(xn_t[:], xn_s[trow:trow + 128, :], q="sp")
            em.red(st[:, 0:1], xn_t[:], ALU.add)
            em.act(fA[:], xn_t[:], AF.Square, accum_out=st[:, 1:2])
            em.ts(st[:, 2:3], st[:, 0:1], 1.0 / 1024, op0=ALU.mult)
            em.ts(st[:, 3:4], st[:, 1:2], 1.0 / 1024, op0=ALU.mult)
            em.ts(st[:, 4:5], st[:, 2:3], st[:, 2:3], op0=ALU.mult)
            em.tt(st[:, 5:6], st[:, 3:4], st[:, 4:5], ALU.subtract)
            em.act(st[:, 6:7], st[:, 5:6], AF.Ln, bias=EPS_LN)
            em.act(st[:, 7:8], st[:, 6:7], AF.Exp, scale=-0.5)
            em.ts(st[:, 8:9], st[:, 2:3], st[:, 7:8], s2=-1.0, op0=ALU.mult, op1=ALU.mult)
            em.act(fA[:], xn_t[:], AF.Identity, scale=st[:, 7:8], bias=st[:, 8:9])
            em.stt(fB[:], fA[:], 0.0, c_ln1w[:], ALU.bypass, ALU.mult)
            em.tt(fA[:], fB[:], c_ln1b[:], ALU.add)
            em.red(st[:, 9:10], fA[:], ALU.max, absval=True)
            em.ts(st[:, 10:11], st[:, 9:10], 1e-5, op0=ALU.max)
            em.recip(st[:, 11:12], st[:, 10:11])
            em.ts(st[:, 12:13], st[:, 11:12], 127.0, op0=ALU.mult)
            em.ts(st[:, 13:14], st[:, 10:11], 1.0 / 127.0, op0=ALU.mult)
            _round_quant(em, hq_t[:], fA[:], st[:, 12:13], -128.0, 127.0, fA2 := fB, fC, fD, iA, 1024)
            em.copy(sbfc[:], st[:, 12:13])
            em.tr(ps_tr[0:1, 0:128], sbfc[:], ident[:])
            em.copy(srow[:], ps_tr[0:1, 0:128])
            for d in range(8):
                em.tr(ps_tr[:], hq_t[:, d * 128:(d + 1) * 128], ident[:])
                if d % 2:
                    em.act(hqT[d][:], ps_tr[:], AF.Identity)
                else:
                    em.copy(hqT[d][:], ps_tr[:])
            for fc in range(8):
                bank = pu[fc % 2]
                for d in range(8):
                    em.mm(bank[:], hqT[d][:], w1sb[d][:, fc * 512:(fc + 1) * 512],
                          start=(d == 0), stop=False, inc=0)
                em.mm(bank[:], srow[0:1, :], b1row[0:1, fc * 512:(fc + 1) * 512],
                      start=False, stop=True)
                em.act(gm[:, fc * 512:(fc + 1) * 512], bank[:], AF.Gelu,
                       scale=st[:, 13:14], accum_out=gacc[:, fc:fc + 1])
                em.act(sqs[:], gm[:, fc * 512:(fc + 1) * 512], AF.Square,
                       accum_out=gacc[:, 8 + fc:9 + fc])
            em.red(st[:, 16:17], gacc[:, 0:8], ALU.add)
            em.red(st[:, 17:18], gacc[:, 8:16], ALU.add)
            em.ts(st[:, 18:19], st[:, 16:17], 1.0 / 4096, op0=ALU.mult)
            em.ts(st[:, 19:20], st[:, 17:18], 1.0 / 4096, op0=ALU.mult)
            em.ts(st[:, 20:21], st[:, 18:19], st[:, 18:19], op0=ALU.mult)
            em.tt(st[:, 21:22], st[:, 19:20], st[:, 20:21], ALU.subtract)
            em.act(st[:, 22:23], st[:, 21:22], AF.Ln, bias=EPS_LN)
            em.act(st[:, 23:24], st[:, 22:23], AF.Exp, scale=-0.5)
            em.ts(st[:, 24:25], st[:, 18:19], st[:, 23:24], s2=-1.0, op0=ALU.mult, op1=ALU.mult)
            for fc in range(8):
                sl_ = slice(fc * 512, (fc + 1) * 512)
                em.act(fA[:, 0:512], gm[:, sl_], AF.Identity, scale=st[:, 23:24], bias=st[:, 24:25])
                em.stt(fB[:, 0:512], fA[:, 0:512], 0.0, c_ln2w[:, sl_], ALU.bypass, ALU.mult)
                em.tt(gm[:, sl_], fB[:, 0:512], c_ln2b[:, sl_], ALU.add)
                em.red(gacc[:, 16 + fc:17 + fc], gm[:, sl_], ALU.max, absval=True)
            em.red(st[:, 25:26], gacc[:, 16:24], ALU.max)
            em.ts(st[:, 26:27], st[:, 25:26], 1e-5, op0=ALU.max)
            em.recip(st[:, 27:28], st[:, 26:27])
            em.ts(st[:, 28:29], st[:, 27:28], 127.0, op0=ALU.mult)
            em.dma(s2_s[trow:trow + 128, :], st[:, 28:29], q="sp")
            for cc in range(4):
                sl_ = slice(cc * 1024, (cc + 1) * 1024)
                _round_quant(em, mq_t[:], gm[:, sl_], st[:, 28:29], -128.0, 127.0, fB, fC, fD, iA, 1024)
                em.dma(mq_s[trow:trow + 128, sl_], mq_t[:], q="sp")

    with emscope(em):
        w2sb = [em.sb(f"ff_w2_{h}", [128, 1024], BF16, tdest=True) for h in range(32)]
        for h in range(32):
            em.dmat(w2sb[h][:], IN["w2q_rows"][:, h * 128:(h + 1) * 128], q="sp")
        mqT = [em.sb(f"ff_mqT{h}", [128, 128], BF16, tdest=True) for h in range(32)]
        x_t = em.sb("ff_xt", [128, 1024], F32)
        a0_t = em.sb("ff_a0", [128, 1024], BF16)
        x1_t = em.sb("ff_x1", [128, 1024], BF16)
        s2c = em.sb("ff_s2c", [128, 1], F32)
        s2r = em.sb("ff_s2r", [1, 128], BF16)
        s2cb = em.sb("ff_s2cb", [128, 1], BF16)
        rs2 = em.sb("ff_rs2", [128, 1], F32)
        po = [em.ps(f"ff_po{i}", [128, 512]) for i in range(2)]
        for tt in range(32):
            trow = tt * 128
            em.dma(s2c[:], s2_s[trow:trow + 128, :], q="sp")
            em.recip(rs2[:], s2c[:])
            em.copy(s2cb[:], s2c[:])
            em.tr(ps_tr[0:1, 0:128], s2cb[:], ident[:])
            em.copy(s2r[:], ps_tr[0:1, 0:128])
            for h in range(32):
                em.dmat(mqT[h][:], mq_s[trow:trow + 128, h * 128:(h + 1) * 128], q="sp")
            em.dma(x_t[:], IN["x"][trow:trow + 128, :], q="sp")
            em.dma(a0_t[:], IN["a0_s"][trow:trow + 128, :], q="sp")
            for oc in range(2):
                bank = po[oc]
                for h in range(32):
                    em.mm(bank[:], mqT[h][:], w2sb[h][:, oc * 512:(oc + 1) * 512],
                          start=(h == 0), stop=False, inc=0)
                em.mm(bank[:], s2r[0:1, :], b2row[0:1, oc * 512:(oc + 1) * 512],
                      start=False, stop=True)
                sl_ = slice(oc * 512, (oc + 1) * 512)
                em.stt(fA[:, 0:512], bank[:], rs2[:], x_t[:, sl_], ALU.mult, ALU.add)
                em.tt(x1_t[:, sl_], fA[:, 0:512], a0_t[:, sl_], ALU.add)
            em.dma(OUT["x1_s"][trow:trow + 128, :], x1_t[:], q="sp")


L0F_INS = dict(
    x=([T, D], F32), xn_s=([T, D], BF16), a0_s=([T, D], BF16),
    w1q_rows=([4096, 1024], BF16), w2q_rows=([1024, 4096], BF16),
    L0_ffn_ln1_w=([1024], F32), L0_ffn_ln1_b=([1024], F32),
    L0_ffn_ln2_w=([4096], F32), L0_ffn_ln2_b=([4096], F32),
    L0_ffn_b1=([4096], F32), L0_ffn_b2=([1024], F32),
    c_ones=([1, 128], BF16), c_ident=([128, 128], BF16),
)
L0F_OUTS = dict(x1_s=([T, D], BF16),
                mq_s=([T, HID], BF16), s2_s=([T, 1], F32))



def emit_l0a(em, IN, OUT, n_chunks=8, n_sub=8, scan_secs=99):
    nc = em.nc
    ident = em.sb("la_ident", [128, 128], BF16)
    em.dma(ident[:], IN["c_ident"], q="sp")
    onesf = em.sb("la_onesf", [1, 128], F32)
    em.memset(onesf[:], 1.0)
    onescol = em.sb("la_onescol", [64, 1], F32)
    em.memset(onescol[:], 1.0)
    negones64 = em.sb("la_no64", [1, 64], F32)
    em.memset(negones64[:], -1.0)
    c_ms = em.sb("la_ms", [CS, CS], BF16)
    em.dma(c_ms[:], IN["c_ms"], q="sp")
    c_bigL = em.sb("la_bigL", [CS, CS], F32)
    em.dma(c_bigL[:], IN["c_bigL"], q="sp")
    c_bigUi = em.sb("la_bigUi", [CS, CS], F32)
    em.dma(c_bigUi[:], IN["c_bigUi"], q="sp")
    c_bigUs = em.sb("la_bigUs", [CS, CS], F32)
    em.dma(c_bigUs[:], IN["c_bigUs"], q="sp")

    alog = em.sb("la_alog", [12, 1], F32)
    em.dma(alog[:], IN["L0_attn_A_log"].rearrange("(p c) -> p c", c=1), q="sp")
    dtb = em.sb("la_dtb", [12, 1], F32)
    em.dma(dtb[:], IN["L0_attn_dt_bias"].rearrange("(p c) -> p c", c=1), q="sp")
    bbc = em.sb("la_bb", [12, 1], F32)
    em.dma(bbc[:], IN["L0_attn_b_b"].rearrange("(p c) -> p c", c=1), q="sp")
    negA = em.sb("la_negA", [12, 1], F32)
    em.act(negA[:], alog[:], AF.Exp)
    em.ts(negA[:], negA[:], -1.0, op0=ALU.mult)
    negbb = em.sb("la_negbb", [12, 1], F32)
    em.ts(negbb[:], bbc[:], -1.0, op0=ALU.mult)

    gnwrow = em.sb("la_gnwrow", [1, 128], F32)
    em.dma(gnwrow[:], IN["L0_attn_g_norm_weight"].rearrange("(r c) -> r c", r=1), q="sp")
    gnwmat = em.sb("la_gnwmat", [64, 1536], BF16)
    ppg = em.ps("la_ppg", [128, 128])
    for h in range(12):
        em.mm(ppg[0:64, 0:128], onesf[0:1, 0:64], gnwrow[:])
        em.copy(gnwmat[:, h * 128:(h + 1) * 128], ppg[0:64, 0:128])

    convqk = em.sb("la_convqk", [64, 96], F32)
    for h in range(12):
        em.dma(convqk[:, h * 4:(h + 1) * 4], IN["L0_attn_conv_q"][h * 64:(h + 1) * 64, :], q="sp")
        em.dma(convqk[:, 48 + h * 4:48 + (h + 1) * 4], IN["L0_attn_conv_k"][h * 64:(h + 1) * 64, :], q="sp")
    convv_all = em.sb("la_convvall", [128, 48], F32)
    for vt in range(12):
        em.dma(convv_all[:, vt * 4:(vt + 1) * 4], IN["L0_attn_conv_v"][vt * 128:(vt + 1) * 128, :], q="sp")
    dgqk = []
    for hh in range(24):
        tile4 = []
        for j in range(4):
            d = em.sb(f"la_dg{hh}_{j}", [64, 64], BF16)
            em.ts(d[:], ident[0:64, 0:64], convqk[:, hh * 4 + j:hh * 4 + j + 1], op0=ALU.mult)
            tile4.append(d)
        dgqk.append(tile4)
    vdg = [[em.sb(f"la_vdg{i}_{j}", [128, 128], BF16) for j in range(4)] for i in range(2)]

    w_sb = [em.sb(f"la_w{d}", [128, 4608], BF16) for d in range(8)]
    wgkb_sb = em.sb("la_wgkb", [128, 352], BF16)
    for d in range(8):
        em.dma(w_sb[d][:, 0:768], IN["wq_bf"][d * 128:(d + 1) * 128, :], q="sp")
        em.dma(w_sb[d][:, 768:1536], IN["wk_bf"][d * 128:(d + 1) * 128, :], q="sp")
        em.dma(w_sb[d][:, 1536:3072], IN["wv_bf"][d * 128:(d + 1) * 128, :], q="sp")
        em.dma(w_sb[d][:, 3072:4608], IN["wg_bf"][d * 128:(d + 1) * 128, :], q="sp")
        em.dma(wgkb_sb[:, d * 44:(d + 1) * 44], IN["wgkb_bf"][d * 128:(d + 1) * 128, :], q="sp")

    pp = [em.ps(f"la_pp{i}", [128, 512]) for i in range(6)]
    ps_trf = em.ps("la_pstrf", [128, 128])
    identf = em.sb("la_identf", [128, 128], F32)
    em.copy(identf[:], ident[:])

    with emscope(em):
        x_t = em.sb("la_xt", [128, 512], F32)
        xn_t = em.sb("la_xnt", [128, 1024], BF16)
        st = em.sb("la_st", [128, 16], F32)
        sqf = em.sb("la_sqf", [64, 512], F32)
        xnT = [em.sb(f"la_xnT{d}", [128, 512], BF16, tdest=True) for d in range(8)]
        preqk = [em.sb(f"la_preqk{t}", [64, 516], BF16) for t in range(2)]
        histqk = [em.sb(f"la_histqk{t}", [64, 4], BF16) for t in range(24)]
        prev = [em.sb(f"la_prev{t}", [128, 516], BF16) for t in range(2)]
        histv = [em.sb(f"la_histv{t}", [128, 4], BF16) for t in range(12)]
        qn = [em.sb(f"la_qn{t}", [64, 512], BF16) for t in range(12)]
        kn = [em.sb(f"la_kn{t}", [64, 512], BF16) for t in range(12)]
        vf = [em.sb(f"la_vf{t}", [128, 512], BF16) for t in range(2)]
        rsq1 = em.sb("la_rsq1", [1, 512], F32)
        g_sc = em.sb("la_gsc", [64, 1536], BF16)
        gkb = em.sb("la_gkb", [12, 512], F32)
        beta_f = em.sb("la_betaf", [12, 512], F32)
        S = em.sb("la_S", [64, 1536], F32)
        Sbf = em.sb("la_Sbf", [64, 1536], BF16)
        em.memset(S[:], 0.0)
        em.memset(Sbf[:], 0.0)
        gam = em.sb("la_gam", [12, 64], F32)
        b12 = em.sb("la_b12", [12, 64], F32)
        bb12 = em.sb("la_bb12", [12, 64], F32)
        kds12 = em.sb("la_kds12", [12, 64], F32)
        flatg = em.sb("la_flatg", [1, 768], F32)
        flatbeta = em.sb("la_flatbeta", [1, 768], F32)
        beta_tok = em.sb("la_btok", [64, 12], F32)
        bb_tok = em.sb("la_bbtok", [64, 12], F32)
        b_tok = em.sb("la_btok2", [64, 12], F32)
        kds_tok = em.sb("la_kdstok", [64, 12], F32)
        bc_tok = em.sb("la_bctok", [64, 12], F32)
        bcrow = em.sb("la_bcrow", [1, 12], F32)
        E0 = em.sb("la_E0", [64, 768], BF16)
        E0U = em.sb("la_E0U", [64, 768], BF16)
        Amat = em.sb("la_Amat", [64, 768], BF16)
        ATmat = em.sb("la_ATmat", [64, 768], BF16)
        WTmat = em.sb("la_WTmat", [64, 768], BF16)
        powP = [em.sb(f"la_powP{i}", [64, 768], BF16) for i in range(2)]
        powPT = [None] + [em.sb(f"la_powPT{i}", [64, 768], BF16) for i in range(1, 6)]
        y_t = em.sb("la_y", [64, 1536], BF16)
        y2_t = em.sb("la_y2", [64, 1536], BF16)
        Dl = em.sb("la_Dl", [64, 1536], BF16)
        Ot = em.sb("la_O", [64, 1536], BF16)
        o2 = em.sb("la_o2", [64, 1536], BF16)
        ovar = em.sb("la_ovar", [64, 16], F32)
        ktokG = em.sb("la_ktokG", [128, 768], BF16, tdest=True)
        vtokG = em.sb("la_vtokG", [128, 1536], BF16, tdest=True)
        ktokB = em.sb("la_ktokB", [64, 768], BF16)
        vtokB = em.sb("la_vtokB", [64, 1536], BF16)

        xn_s, a0_s = OUT["xn_s"], OUT["a0_s"]
        kn_s, vf_s, og_s = OUT["kn_s"], OUT["vf_s"], OUT["og_s"]

        def stretch(tile, h0, nh, rep):
            return tile[:, h0:h0 + nh].unsqueeze(2).broadcast_to((64, nh, rep))

        def rep_mask(msk, nh):
            return msk[:].unsqueeze(1).broadcast_to((64, nh, CS))

        def h3(ap2d, nh, blk):
            return ap2d.rearrange("p (h s) -> p h s", s=blk)

        for ch in range(n_chunks):
            tbase = ch * 512
            for tt in range(4):
                trow = tbase + tt * 128
                em.dma(x_t[:], IN["x"][trow:trow + 128, 0:512], q="sp")
                em.red(st[:, 0:1], x_t[:], ALU.add)
                em.act(sqf[0:64, :], x_t[0:64, :], AF.Square, accum_out=st[0:64, 1:2])
                em.act(sqf[0:64, :], x_t[64:128, :], AF.Square, accum_out=st[64:128, 1:2])
                em.dma(x_t[:], IN["x"][trow:trow + 128, 512:1024], q="sp")
                em.red(st[:, 9:10], x_t[:], ALU.add)
                em.act(sqf[0:64, :], x_t[0:64, :], AF.Square, accum_out=st[0:64, 10:11])
                em.act(sqf[0:64, :], x_t[64:128, :], AF.Square, accum_out=st[64:128, 10:11])
                em.tt(st[:, 0:1], st[:, 0:1], st[:, 9:10], ALU.add)
                em.tt(st[:, 1:2], st[:, 1:2], st[:, 10:11], ALU.add)
                em.ts(st[:, 2:3], st[:, 0:1], 1.0 / 1024, op0=ALU.mult)
                em.ts(st[:, 3:4], st[:, 1:2], 1.0 / 1024, op0=ALU.mult)
                em.ts(st[:, 4:5], st[:, 2:3], st[:, 2:3], op0=ALU.mult)
                em.tt(st[:, 5:6], st[:, 3:4], st[:, 4:5], ALU.subtract)
                em.act(st[:, 6:7], st[:, 5:6], AF.Ln, bias=EPS_LN)
                em.act(st[:, 7:8], st[:, 6:7], AF.Exp, scale=-0.5)
                em.ts(st[:, 8:9], st[:, 2:3], st[:, 7:8], s2=-1.0, op0=ALU.mult, op1=ALU.mult)
                em.act(xn_t[:, 512:1024], x_t[:], AF.Identity, scale=st[:, 7:8], bias=st[:, 8:9])
                em.dma(x_t[:], IN["x"][trow:trow + 128, 0:512], q="sp")
                em.act(xn_t[:, 0:512], x_t[:], AF.Identity, scale=st[:, 7:8], bias=st[:, 8:9])
                em.dma(xn_s[trow:trow + 128, :], xn_t[:], q="sp")
            for d in range(8):
                em.dmat(xnT[d][:], xn_s[tbase:tbase + 512, d * 128:(d + 1) * 128], q="sp")
            for hh in range(24):
                bank = pp[hh % 2]
                prt = preqk[hh % 2]
                for d in range(8):
                    em.mm(bank[0:64, :], w_sb[d][:, hh * 64:(hh + 1) * 64], xnT[d][:],
                          start=(d == 0), stop=(d == 7), inc=(1 if d == 7 else 0))
                if ch > 0:
                    em.copy(prt[:, 0:3], histqk[hh][:, 0:3], q="pool")
                else:
                    em.memset(prt[:, 0:3], 0.0, q="pool")
                em.copy(prt[:, 3:515], bank[0:64, :])
                em.copy(histqk[hh][:, 0:3], prt[:, 512:515], q="pool")
                bank2 = pp[2 + hh % 2]
                for j in range(4):
                    em.mm(bank2[0:64, :], dgqk[hh][j][:], prt[:, j:j + 512],
                          start=(j == 0), stop=(j == 3), inc=(1 if j == 3 else 0))
                dst = qn[hh] if hh < 12 else kn[hh - 12]
                em.act(dst[:], bank2[0:64, :], AF.Silu)
                em.act(sqf[:], dst[:], AF.Square)
                em.mm(pp[4][0:1, :], onescol[:], sqf[:])
                em.act(rsq1[:], pp[4][0:1, :], AF.Ln, bias=1e-6)
                em.act(rsq1[:], rsq1[:], AF.Exp, scale=-0.5)
                if hh < 12:
                    em.ts(rsq1[:], rsq1[:], 0.125, op0=ALU.mult)
                em.mm(pp[5][0:64, 0:512], onesf[0:1, 0:64], rsq1[0:1, :])
                em.tt(dst[:], dst[:], pp[5][0:64, 0:512], ALU.mult)
                if hh >= 12:
                    em.dma(kn_s[(hh - 12) * 64:(hh - 11) * 64, tbase:tbase + 512], dst[:], q="sp")
            for ft in range(12):
                bank = pp[ft % 2]
                prt = prev[ft % 2]
                for d in range(8):
                    em.mm(bank[:], w_sb[d][:, 1536 + ft * 128:1536 + (ft + 1) * 128], xnT[d][:],
                          start=(d == 0), stop=(d == 7), inc=(1 if d == 7 else 0))
                if ch > 0:
                    em.copy(prt[:, 0:3], histv[ft][:, 0:3], q="pool")
                else:
                    em.memset(prt[:, 0:3], 0.0, q="pool")
                em.copy(prt[:, 3:515], bank[:])
                em.copy(histv[ft][:, 0:3], prt[:, 512:515], q="pool")
                dg = vdg[ft % 2]
                for j in range(4):
                    em.ts(dg[j][:], ident[:], convv_all[:, ft * 4 + j:ft * 4 + j + 1],
                          op0=ALU.mult, q="pool")
                bank2 = pp[2 + ft % 2]
                for j in range(4):
                    em.mm(bank2[:], dg[j][:], prt[:, j:j + 512],
                          start=(j == 0), stop=(j == 3), inc=(1 if j == 3 else 0))
                vt = vf[ft % 2]
                em.act(vt[:], bank2[:], AF.Silu)
                em.dma(vf_s[ft * 128:(ft + 1) * 128, tbase:tbase + 512], vt[:], q="sp")
            bank = pp[2]
            for d in range(8):
                em.mm(bank[0:44, :], wgkb_sb[:, d * 44:(d + 1) * 44], xnT[d][:],
                      start=(d == 0), stop=(d == 7), inc=(1 if d == 7 else 0))
            em.act(gkb[:], bank[0:12, :], AF.Abs, bias=dtb[:, 0:1])
            em.act(gkb[:], gkb[:], AF.Exp, scale=-1.0)
            em.act(gkb[:], gkb[:], AF.Ln, bias=1.0)
            em.act(beta_f[:], bank[0:12, :], AF.Identity, bias=dtb[:, 0:1])
            em.ts(beta_f[:], beta_f[:], 0.0, op0=ALU.max)
            em.tt(gkb[:], gkb[:], beta_f[:], ALU.add)
            em.ts(gkb[:], gkb[:], negA[:, 0:1], op0=ALU.mult)
            em.act(beta_f[:], bank[32:44, :], AF.Exp, scale=-1.0, bias=negbb[:, 0:1])
            em.ts(beta_f[:], beta_f[:], 1.0, op0=ALU.add)
            em.recip(beta_f[:], beta_f[:])
            for sc in range(n_sub):
                c0 = sc * 64
                grp = sc // 2
                if sc % 2 == 0:
                    trow = tbase + grp * 128
                    em.dmat(ktokG[:], kn_s[0:768, trow:trow + 128], q="sp")
                    em.dmat(vtokG[:], vf_s[0:1536, trow:trow + 128], q="sp")
                    em.dma(OUT["ktb_s"][:], ktokG[64:128, :], q="pool")
                    em.dma(OUT["vtb_s"][:], vtokG[64:128, :], q="pool")
                    em.dma(ktokB[:], OUT["ktb_s"][:], q="pool")
                    em.dma(vtokB[:], OUT["vtb_s"][:], q="pool")
                ktok = ktokG if sc % 2 == 0 else ktokB
                vtok = vtokG if sc % 2 == 0 else vtokB
                for fs in range(3):
                    gb = pp[3 + fs]
                    for d in range(8):
                        em.mm(gb[0:64, :], xnT[d][:, c0:c0 + 64],
                              w_sb[d][:, 3072 + fs * 512:3072 + (fs + 1) * 512],
                              start=(d == 0), stop=(d == 7), inc=(1 if d == 7 else 0))
                    em.copy(g_sc[:, fs * 512:(fs + 1) * 512], gb[0:64, :])
                if scan_secs < 2:
                    continue
                em.scan(gam[:], gkb[:, c0:c0 + 64], gkb[:, c0:c0 + 64], 0.0,
                        ALU.add, ALU.bypass)
                em.act(b12[:], gam[:], AF.Exp)
                em.tt(bb12[:], beta_f[:, c0:c0 + 64], b12[:], ALU.mult)
                em.ts(kds12[:], gam[:], gam[:, 63:64], op0=ALU.subtract)
                em.act(kds12[:], kds12[:], AF.Exp, scale=-1.0)
                em.dma(OUT["flat_s"][0:12, 0:64], gam[:], q="pool")
                em.dma(OUT["flat_s"][12:24, 0:64], beta_f[:, c0:c0 + 64], q="pool")
                em.dma(flatg[:], OUT["flat_s"][0:12, :].rearrange("a b -> (a b)").rearrange("(r c) -> r c", r=1), q="pool")
                em.dma(flatbeta[:], OUT["flat_s"][12:24, :].rearrange("a b -> (a b)").rearrange("(r c) -> r c", r=1), q="pool")
                for tsrc, tdst in ((beta_f[:, c0:c0 + 64], beta_tok), (bb12[:], bb_tok),
                                   (b12[:], b_tok), (kds12[:], kds_tok)):
                    em.tr(ps_trf[0:64, 0:12], tsrc, identf[0:12, 0:12])
                    em.copy(tdst[:], ps_trf[0:64, 0:12])
                em.act(bcrow[:], flatg[0:1, 63:768:64], AF.Exp)
                em.mm(pp[5][0:64, 0:12], onesf[0:1, 0:64], bcrow[:])
                em.copy(bc_tok[:], pp[5][0:64, 0:12])
                if scan_secs < 3:
                    continue
                for h in range(12):
                    b_, cs_ = (pp[0], h * 64) if h < 8 else (pp[1], (h - 8) * 64)
                    em.mm(b_[0:64, cs_:cs_ + 64], flatg[0:1, 64 * h:64 * h + 64],
                          negones64[0:1, 0:64], start=(h in (0, 8)), stop=False, inc=0)
                    em.mm(b_[0:64, cs_:cs_ + 64], onesf[0:1, 0:64],
                          flatg[0:1, 64 * h:64 * h + 64], start=False,
                          stop=(h in (7, 11)), inc=(1 if h in (7, 11) else 0))
                scr = ((pp[0], pp[4], 512, 0), (pp[1], pp[5], 256, 512))
                for (b_, s_, w_, off) in scr:
                    em.tt(h3(s_[0:64, 0:w_], w_ // 64, 64), h3(b_[0:64, 0:w_], w_ // 64, 64),
                          rep_mask(c_bigL, w_ // 64), ALU.add)
                    em.act(E0[:, off:off + w_], s_[0:64, 0:w_], AF.Exp, scale=-1.0)
                    em.tt(h3(s_[0:64, 0:w_], w_ // 64, 64), h3(b_[0:64, 0:w_], w_ // 64, 64),
                          rep_mask(c_bigUi, w_ // 64), ALU.add)
                    em.act(E0U[:, off:off + w_], s_[0:64, 0:w_], AF.Exp)
                if scan_secs < 4:
                    continue
                for h in range(12):
                    b_, cs_ = (pp[2], h * 64) if h < 8 else (pp[3], (h - 8) * 64)
                    kf = kn[h][:, c0:c0 + 64]
                    em.mm(b_[0:64, cs_:cs_ + 64], kf, kf, inc=(1 if h in (7, 11) else 0))
                if scan_secs < 41:
                    continue
                for (b_, w_, off) in ((pp[2], 512, 0), (pp[3], 256, 512)):
                    nh = w_ // 64
                    em.tt(Amat[:, off:off + w_], E0[:, off:off + w_], b_[0:64, 0:w_], ALU.mult)
                    em.tt(h3(Amat[:, off:off + w_], nh, 64), h3(Amat[:, off:off + w_], nh, 64),
                          stretch(beta_tok, 0 + off // 64, nh, 64), ALU.mult)
                    em.tt(h3(Amat[:, off:off + w_], nh, 64), h3(Amat[:, off:off + w_], nh, 64),
                          rep_mask(c_ms, nh), ALU.mult)
                if scan_secs < 42:
                    continue
                for (b_, s_, w_, off) in scr:
                    em.tt(h3(s_[0:64, 0:w_], w_ // 64, 64), h3(b_[0:64, 0:w_], w_ // 64, 64),
                          rep_mask(c_bigUs, w_ // 64), ALU.add)
                    em.act(E0[:, off:off + w_], s_[0:64, 0:w_], AF.Exp)
                for h in range(12):
                    b_, cs_ = (pp[0], h * 64) if h < 8 else (pp[1], (h - 8) * 64)
                    em.mm(b_[0:64, cs_:cs_ + 64], onesf[0:1, 0:64],
                          flatbeta[0:1, 64 * h:64 * h + 64],
                          inc=(1 if h in (7, 11) else 0))
                if scan_secs < 43:
                    continue
                for (b_, w_, off) in ((pp[2], 512, 0), (pp[3], 256, 512)):
                    em.tt(ATmat[:, off:off + w_], E0[:, off:off + w_], b_[0:64, 0:w_], ALU.mult)
                for (b_, w_, off) in ((pp[0], 512, 0), (pp[1], 256, 512)):
                    em.tt(ATmat[:, off:off + w_], ATmat[:, off:off + w_], b_[0:64, 0:w_], ALU.mult)
                if scan_secs < 5:
                    continue
                for h in range(12):
                    b_, cs_ = (pp[2], h * 64) if h < 8 else (pp[3], (h - 8) * 64)
                    em.mm(b_[0:64, cs_:cs_ + 64], kn[h][:, c0:c0 + 64], qn[h][:, c0:c0 + 64],
                          inc=(1 if h in (7, 11) else 0))
                for (b_, w_, off) in ((pp[2], 512, 0), (pp[3], 256, 512)):
                    em.tt(WTmat[:, off:off + w_], E0U[:, off:off + w_], b_[0:64, 0:w_], ALU.mult)
                if scan_secs < 6:
                    continue
                for h in range(12):
                    b_ = (pp[2], pp[3], pp[4])[h // 4]
                    cs_ = (h % 4) * 128
                    em.mm(b_[0:64, cs_:cs_ + 128], kn[h][:, c0:c0 + 64],
                          Sbf[:, h * 128:(h + 1) * 128], inc=(1 if h % 4 == 3 else 0))
                em.tt(h3(y_t[:], 12, 128), h3(vtok[0:64, :], 12, 128),
                      stretch(beta_tok, 0, 12, 128), ALU.mult)
                for bi_, b_ in enumerate((pp[2], pp[3], pp[4])):
                    off = bi_ * 512
                    em.tt(h3(y2_t[:, off:off + 512], 4, 128), h3(b_[0:64, 0:512], 4, 128),
                          stretch(bb_tok, bi_ * 4, 4, 128), ALU.mult)
                em.tt(y_t[:], y_t[:], y2_t[:], ALU.subtract)
                if scan_secs < 7:
                    continue
                em.copy(powP[0][:], Amat[:], q="pool")
                for lv in range(1, 6):
                    pprev = powP[(lv - 1) % 2]
                    ptprev = ATmat if lv == 1 else powPT[lv - 1]
                    pcur, ptcur = powP[lv % 2], powPT[lv]
                    for h in range(12):
                        b_, cs_ = (pp[0], h * 64) if h < 8 else (pp[1], (h - 8) * 64)
                        hb = slice(h * 64, h * 64 + 64)
                        em.mm(b_[0:64, cs_:cs_ + 64], ptprev[:, hb], pprev[:, hb],
                              inc=(1 if h in (7, 11) else 0))
                    for h in range(12):
                        b_, cs_ = (pp[2], h * 64) if h < 8 else (pp[3], (h - 8) * 64)
                        hb = slice(h * 64, h * 64 + 64)
                        em.mm(b_[0:64, cs_:cs_ + 64], pprev[:, hb], ptprev[:, hb],
                              inc=(1 if h in (7, 11) else 0))
                    em.act(pcur[:, 0:512], pp[0][0:64, 0:512], AF.Identity)
                    em.act(pcur[:, 512:768], pp[1][0:64, 0:256], AF.Identity)
                    em.act(ptcur[:, 0:512], pp[2][0:64, 0:512], AF.Identity)
                    em.act(ptcur[:, 512:768], pp[3][0:64, 0:256], AF.Identity)
                if scan_secs < 8:
                    continue
                for lv in range(5, 0, -1):
                    pt = ATmat if lv == 1 else powPT[lv]
                    pt = powPT[lv]
                    for h in range(12):
                        b_ = (pp[0], pp[1], pp[2])[h // 4]
                        cs_ = (h % 4) * 128
                        em.mm(b_[0:64, cs_:cs_ + 128], pt[:, h * 64:h * 64 + 64],
                              y_t[:, h * 128:(h + 1) * 128], inc=(1 if h % 4 == 3 else 0))
                    for bi_, b_ in enumerate((pp[0], pp[1], pp[2])):
                        off = bi_ * 512
                        em.tt(y_t[:, off:off + 512], y_t[:, off:off + 512], b_[0:64, 0:512], ALU.add)
                for h in range(12):
                    b_ = (pp[0], pp[1], pp[2])[h // 4]
                    cs_ = (h % 4) * 128
                    em.mm(b_[0:64, cs_:cs_ + 128], ATmat[:, h * 64:h * 64 + 64],
                          y_t[:, h * 128:(h + 1) * 128], inc=(1 if h % 4 == 3 else 0))
                for bi_, b_ in enumerate((pp[0], pp[1], pp[2])):
                    off = bi_ * 512
                    em.tt(Dl[:, off:off + 512], y_t[:, off:off + 512], b_[0:64, 0:512], ALU.subtract)
                if scan_secs < 9:
                    continue
                for h in range(12):
                    b_ = (pp[0], pp[1], pp[2])[h // 4]
                    cs_ = (h % 4) * 128
                    em.mm(b_[0:64, cs_:cs_ + 128], qn[h][:, c0:c0 + 64],
                          Sbf[:, h * 128:(h + 1) * 128], inc=(1 if h % 4 == 3 else 0))
                for bi_, b_ in enumerate((pp[0], pp[1], pp[2])):
                    off = bi_ * 512
                    em.tt(h3(Ot[:, off:off + 512], 4, 128), h3(b_[0:64, 0:512], 4, 128),
                          stretch(b_tok, bi_ * 4, 4, 128), ALU.mult)
                for h in range(12):
                    b_ = (pp[3], pp[4], pp[5])[h // 4]
                    cs_ = (h % 4) * 128
                    em.mm(b_[0:64, cs_:cs_ + 128], WTmat[:, h * 64:h * 64 + 64],
                          Dl[:, h * 128:(h + 1) * 128], inc=(1 if h % 4 == 3 else 0))
                for bi_, b_ in enumerate((pp[3], pp[4], pp[5])):
                    off = bi_ * 512
                    em.tt(Ot[:, off:off + 512], Ot[:, off:off + 512], b_[0:64, 0:512], ALU.add)
                em.act(o2[:], Ot[:], AF.Square)
                for h in range(12):
                    em.red(ovar[:, h:h + 1], o2[:, h * 128:(h + 1) * 128], ALU.add)
                em.ts(ovar[:, 0:12], ovar[:, 0:12], 1.0 / 128, op0=ALU.mult)
                em.act(ovar[:, 0:12], ovar[:, 0:12], AF.Ln, bias=1e-5)
                em.act(ovar[:, 0:12], ovar[:, 0:12], AF.Exp, scale=-0.5)
                em.tt(h3(o2[:], 12, 128), h3(Ot[:], 12, 128),
                      ovar[:, 0:12].unsqueeze(2).broadcast_to((64, 12, 128)), ALU.mult)
                em.tt(o2[:], o2[:], gnwmat[:], ALU.mult)
                em.act(y_t[:], g_sc[:], AF.Silu)
                em.tt(y2_t[:], o2[:], y_t[:], ALU.mult)
                em.dma(og_s[tbase + c0:tbase + c0 + 64, :], y2_t[:], q="sp")
                if scan_secs < 10:
                    continue
                em.tt(h3(y_t[:, 0:768], 12, 64), h3(ktok[0:64, :], 12, 64),
                      stretch(kds_tok, 0, 12, 64), ALU.mult)
                for h in range(12):
                    b_ = (pp[0], pp[1], pp[2])[h // 4]
                    cs_ = (h % 4) * 128
                    em.mm(b_[0:64, cs_:cs_ + 128], y_t[:, h * 64:h * 64 + 64],
                          Dl[:, h * 128:(h + 1) * 128], inc=(1 if h % 4 == 3 else 0))
                em.tt(h3(S[:], 12, 128), h3(S[:], 12, 128),
                      stretch(bc_tok, 0, 12, 128), ALU.mult)
                for bi_, b_ in enumerate((pp[0], pp[1], pp[2])):
                    off = bi_ * 512
                    em.tt(S[:, off:off + 512], S[:, off:off + 512], b_[0:64, 0:512], ALU.add)
                em.copy(Sbf[:], S[:])
                if ch == 0 and sc == 0 and "d_gam" in OUT:
                    em.dma(OUT["d_gam"], gam[:], q="sp")
                    em.dma(OUT["d_beta"], beta_f[:, 0:64], q="sp")
                    em.dma(OUT["d_E0"], E0U[:], q="sp")
                    em.dma(OUT["d_Amat"], Amat[:], q="sp")
                    em.dma(OUT["d_ATmat"], ATmat[:], q="sp")
                    em.dma(OUT["d_WTmat"], WTmat[:], q="sp")
                    em.dma(OUT["d_Dl"], Dl[:], q="sp")
                    em.dma(OUT["d_Ot"], Ot[:], q="sp")
                    em.dma(OUT["d_S"], S[0:64, :], q="sp")
                    em.dma(OUT["d_gk"], gkb[:], q="sp")
    with emscope(em):
        ogT = [em.sb(f"la_ogT{v}", [128, 512], BF16, tdest=True) for v in range(12)]
        a0t = em.sb("la_a0t", [128, 1024], BF16)
        wo_sb = [em.sb(f"la_wo{v}", [128, 1024], BF16) for v in range(12)]
        for v in range(12):
            em.dma(wo_sb[v][:], IN["wo_bf"][v * 128:(v + 1) * 128, :], q="sp")
        for gg in range(n_chunks):
            t0 = gg * 512
            for v in range(12):
                em.dmat(ogT[v][:], og_s[t0:t0 + 512, v * 128:(v + 1) * 128], q="sp")
            for tt in range(4):
                for oc in range(2):
                    bank = pp[oc]
                    for v in range(12):
                        em.mm(bank[:], ogT[v][:, tt * 128:(tt + 1) * 128],
                              wo_sb[v][:, oc * 512:(oc + 1) * 512],
                              start=(v == 0), stop=(v == 11), inc=(1 if v == 11 else 0))
                    em.copy(a0t[:, oc * 512:(oc + 1) * 512], bank[:])
                em.dma(a0_s[t0 + tt * 128:t0 + tt * 128 + 128, :], a0t[:], q="sp")


L0A_INS = dict(
    x=([T, D], F32),
    wq_bf=([1024, 768], BF16), wk_bf=([1024, 768], BF16), wv_bf=([1024, 1536], BF16),
    wg_bf=([1024, 1536], BF16), wo_bf=([1536, 1024], BF16), wgkb_bf=([1024, 44], BF16),
    L0_attn_conv_q=([768, 4], F32), L0_attn_conv_k=([768, 4], F32),
    L0_attn_conv_v=([1536, 4], F32),
    L0_attn_A_log=([12], F32), L0_attn_dt_bias=([12], F32), L0_attn_b_b=([12], F32),
    L0_attn_g_norm_weight=([128], F32),
    c_ident=([128, 128], BF16), c_ms=([CS, CS], BF16), c_bigL=([CS, CS], F32),
    c_bigUi=([CS, CS], F32), c_bigUs=([CS, CS], F32),
)
L0A_OUTS = dict(xn_s=([T, D], BF16), a0_s=([T, D], BF16),
                kn_s=([768, T], BF16), vf_s=([1536, T], BF16), og_s=([T, 1536], BF16),
                flat_s=([24, 64], F32), ktb_s=([64, 768], BF16), vtb_s=([64, 1536], BF16))
L0A_DBG = dict(d_gam=([12, 64], F32), d_beta=([12, 64], F32), d_E0=([64, 768], BF16),
               d_Amat=([64, 768], BF16), d_ATmat=([64, 768], BF16), d_WTmat=([64, 768], BF16),
               d_Dl=([64, 1536], BF16), d_Ot=([64, 1536], BF16), d_S=([128, 768], F32),
               d_gk=([12, 512], F32))



def emit_l1(em, IN, OUT, n_chunks=8):
    nc = em.nc
    ident = em.sb("l1_ident", [128, 128], BF16)
    em.dma(ident[:], IN["c_ident"], q="sp")
    onesb = em.sb("l1_onesb", [1, 128], BF16)
    em.dma(onesb[:], IN["c_ones"], q="sp")
    onescol = em.sb("l1_onescol", [128, 1], BF16)
    em.memset(onescol[:], 1.0)
    pp = [em.ps(f"l1_pp{i}", [128, 512]) for i in range(6)]
    ps_tr = em.ps("l1_pstr", [128, 128], BF16)

    x_t = em.sb("l1_xt", [128, 1024], BF16)
    yn_t = em.sb("l1_ynt", [128, 1024], BF16)
    st = em.sb("l1_st", [128, 32], F32)
    sqf = em.sb("l1_sqf", [64, 512], F32)
    fA = em.sb("l1_fA", [128, 1024], F32)
    fB = em.sb("l1_fB", [128, 1024], F32)
    fC = em.sb("l1_fC", [128, 1024], F32)
    fD = em.sb("l1_fD", [128, 1024], F32)
    iA = em.sb("l1_iA", [128, 1024], I32)
    ynT = [em.sb(f"l1_ynT{d}", [128, 512], BF16, tdest=True) for d in range(8)]
    yn_s, a1_s = OUT["yn_s"], OUT["a1_s"]
    sc_s, xq_s = OUT["sc_s"], OUT["xq_s"]

    for tt in range(32):
        trow = tt * 128
        em.dma(x_t[:], IN["x1_s"][trow:trow + 128, :], q="sp")
        em.red(st[:, 0:1], x_t[:], ALU.add)
        em.act(sqf[0:64, :], x_t[0:64, 0:512], AF.Square, accum_out=st[0:64, 1:2])
        em.act(sqf[0:64, :], x_t[64:128, 0:512], AF.Square, accum_out=st[64:128, 1:2])
        em.act(sqf[0:64, :], x_t[0:64, 512:1024], AF.Square, accum_out=st[0:64, 2:3])
        em.act(sqf[0:64, :], x_t[64:128, 512:1024], AF.Square, accum_out=st[64:128, 2:3])
        em.tt(st[:, 1:2], st[:, 1:2], st[:, 2:3], ALU.add)
        em.ts(st[:, 3:4], st[:, 0:1], 1.0 / 1024, op0=ALU.mult)
        em.ts(st[:, 4:5], st[:, 1:2], 1.0 / 1024, op0=ALU.mult)
        em.ts(st[:, 5:6], st[:, 3:4], st[:, 3:4], op0=ALU.mult)
        em.tt(st[:, 6:7], st[:, 4:5], st[:, 5:6], ALU.subtract)
        em.act(st[:, 7:8], st[:, 6:7], AF.Ln, bias=EPS_LN)
        em.act(st[:, 8:9], st[:, 7:8], AF.Exp, scale=-0.5)
        em.ts(st[:, 9:10], st[:, 3:4], st[:, 8:9], s2=-1.0, op0=ALU.mult, op1=ALU.mult)
        em.act(yn_t[:], x_t[:], AF.Identity, scale=st[:, 8:9], bias=st[:, 9:10])
        em.dma(yn_s[trow:trow + 128, :], yn_t[:], q="sp")

    with emscope(em):
        wk_sb = [em.sb(f"l1_wk{d}", [128, 1024], BF16) for d in range(8)]
        wv_sb = [em.sb(f"l1_wv{d}", [128, 1024], BF16) for d in range(8)]
        for d in range(8):
            em.dma(wk_sb[d][:], IN["wk1_bf"][d * 128:(d + 1) * 128, :], q="sp")
            em.dma(wv_sb[d][:], IN["wv1_bf"][d * 128:(d + 1) * 128, :], q="sp")
        bkrow = em.sb("l1_bkrow", [1, 1024], BF16)
        em.dma(bkrow[:], IN["L1_attn_bk"].rearrange("(r c) -> r c", r=1), q="pool")
        bvrow = em.sb("l1_bvrow", [1, 1024], BF16)
        em.dma(bvrow[:], IN["L1_attn_bv"].rearrange("(r c) -> r c", r=1), q="pool")
        k_t = em.sb("l1_kt", [128, 1024], BF16)
        v_t = em.sb("l1_vt", [128, 1024], BF16)
        for tt in range(32):
            trow = tt * 128
            for d in range(8):
                em.dmat(ynT[d][:, 0:128], yn_s[trow:trow + 128, d * 128:(d + 1) * 128], q="sp")
            for half, (wsb, brow, dst, do_elu) in enumerate(
                    [(wk_sb, bkrow, k_t, True), (wv_sb, bvrow, v_t, False)]):
                for oc in range(2):
                    bank = pp[4 + oc]
                    for d in range(8):
                        em.mm(bank[:], ynT[d][:, 0:128], wsb[d][:, oc * 512:(oc + 1) * 512],
                              start=(d == 0), stop=False, inc=0)
                    em.mm(bank[:], onesb[0:1, 0:128], brow[0:1, oc * 512:(oc + 1) * 512],
                          start=False, stop=True)
                    sl_ = slice(oc * 512, (oc + 1) * 512)
                    if do_elu:
                        em.ts(fA[:, 0:512], bank[:], 0.0, op0=ALU.min)
                        em.act(fB[:, 0:512], fA[:, 0:512], AF.Exp)
                        em.stt(dst[:, sl_], bank[:], 0.0, fB[:, 0:512], ALU.max, ALU.add)
                    else:
                        em.copy(dst[:, sl_], bank[:])
            for h in range(16):
                b_ = pp[0] if h < 8 else pp[1]
                cs_ = (h % 8) * 64
                em.mm(b_[0:64, cs_:cs_ + 64], k_t[:, h * 64:(h + 1) * 64],
                      v_t[:, h * 64:(h + 1) * 64], start=(tt == 0), stop=(tt == 31),
                      inc=(1 if (tt == 31 and h in (7, 15)) else 0))
            for oc in range(2):
                em.mm(pp[2 + 0][0:1, 0:512] if oc == 0 else pp[3][0:1, 0:512],
                      onescol[:, 0:1], k_t[:, oc * 512:(oc + 1) * 512],
                      start=(tt == 0), stop=(tt == 31),
                      inc=(1 if tt == 31 else 0))
        kv_sb = em.sb("l1_kvsb", [64, 1024], BF16)
        em.copy(kv_sb[:, 0:512], pp[0][0:64, :])
        em.copy(kv_sb[:, 512:1024], pp[1][0:64, :])
        ks_row = em.sb("l1_ksrow", [1, 1024], BF16)
        em.copy(ks_row[0:1, 0:512], pp[2][0:1, 0:512])
        em.copy(ks_row[0:1, 512:1024], pp[3][0:1, 0:512])
        ks_col = em.sb("l1_kscol", [64, 16], BF16)
        for h in range(16):
            em.tr(ps_tr[0:64, 0:1], ks_row[0:1, h * 64:(h + 1) * 64], ident[0:1, 0:1])
            em.copy(ks_col[:, h:h + 1], ps_tr[0:64, 0:1])

        wq_sb = [em.sb(f"l1_wq{d}", [128, 1024], BF16) for d in range(8)]
        for d in range(8):
            em.dma(wq_sb[d][:], IN["wq1_bf"][d * 128:(d + 1) * 128, :], q="sp")
        bq_col = em.sb("l1_bqcol", [64, 16], F32)
        for hh in range(16):
            em.dma(bq_col[:, hh:hh + 1],
                   IN["L1_attn_bq"][hh * 64:(hh + 1) * 64].rearrange("(p c) -> p c", c=1), q="sp")
        qh = [em.sb(f"l1_qh{h}", [64, 512], BF16) for h in range(16)]
        zt = em.sb("l1_zt", [128, 16], F32)
        sc_t = em.sb("l1_sct", [128, 1024], BF16)
        for chk in range(n_chunks):
            tbase = chk * 512
            for d in range(8):
                em.dmat(ynT[d][:], yn_s[tbase:tbase + 512, d * 128:(d + 1) * 128], q="sp")
            for h in range(16):
                bank = pp[4 + h % 2]
                for d in range(8):
                    em.mm(bank[0:64, :], wq_sb[d][:, h * 64:(h + 1) * 64], ynT[d][:],
                          start=(d == 0), stop=(d == 7), inc=(1 if d == 7 else 0))
                em.act(fA[0:64, 0:512], bank[0:64, :], AF.Identity,
                       bias=bq_col[:, h:h + 1])
                em.ts(fB[0:64, 0:512], fA[0:64, 0:512], 0.0, op0=ALU.min)
                em.act(fC[0:64, 0:512], fB[0:64, 0:512], AF.Exp)
                em.stt(qh[h][:], fA[0:64, 0:512], 0.0, fC[0:64, 0:512], ALU.max, ALU.add)
            for ts4 in range(4):
                t0 = ts4 * 128
                for h in range(16):
                    em.mm(pp[2][:, h:h + 1], qh[h][:, t0:t0 + 128], ks_col[:, h:h + 1],
                          inc=(1 if h == 15 else 0))
                em.ts(zt[:], pp[2][:, 0:16], 1e-6, op0=ALU.add)
                em.recip(zt[:], zt[:])
                for h in range(16):
                    b_ = pp[0] if h < 8 else pp[1]
                    cs_ = (h % 8) * 64
                    em.mm(b_[:, cs_:cs_ + 64], qh[h][:, t0:t0 + 128],
                          kv_sb[:, h * 64:(h + 1) * 64], inc=(1 if h in (7, 15) else 0))
                for bi_, b_ in enumerate((pp[0], pp[1])):
                    em.tt(sc_t[:, bi_ * 512:(bi_ + 1) * 512].rearrange("p (h s) -> p h s", s=64),
                          b_[:, 0:512].rearrange("p (h s) -> p h s", s=64),
                          zt[:, bi_ * 8:(bi_ + 1) * 8].unsqueeze(2).broadcast_to((128, 8, 64)),
                          ALU.mult)
                em.dma(sc_s[tbase + t0:tbase + t0 + 128, :], sc_t[:], q="sp")

    with emscope(em):
        wo_sb = [em.sb(f"l1_wo{d}", [128, 1024], BF16) for d in range(8)]
        for d in range(8):
            em.dma(wo_sb[d][:], IN["wo1_bf"][d * 128:(d + 1) * 128, :], q="sp")
        borow = em.sb("l1_borow", [1, 1024], BF16)
        em.dma(borow[:], IN["L1_attn_bo"].rearrange("(r c) -> r c", r=1), q="pool")
        scT = [em.sb(f"l1_scT{d}", [128, 512], BF16, tdest=True) for d in range(8)]
        a1_t = em.sb("l1_a1t", [128, 1024], BF16)
        for chk in range(n_chunks):
            tbase = chk * 512
            for d in range(8):
                em.dmat(scT[d][:], sc_s[tbase:tbase + 512, d * 128:(d + 1) * 128], q="sp")
            for tt in range(4):
                for oc in range(2):
                    bank = pp[4 + oc]
                    for d in range(8):
                        em.mm(bank[:], scT[d][:, tt * 128:(tt + 1) * 128],
                              wo_sb[d][:, oc * 512:(oc + 1) * 512],
                              start=(d == 0), stop=False, inc=0)
                    em.mm(bank[:], onesb[0:1, 0:128], borow[0:1, oc * 512:(oc + 1) * 512],
                          start=False, stop=True)
                    em.copy(a1_t[:, oc * 512:(oc + 1) * 512], bank[:])
                em.dma(a1_s[tbase + tt * 128:tbase + tt * 128 + 128, :], a1_t[:], q="sp")

    with emscope(em):
        h_sb = [em.sb(f"l1_h{d}", [128, 1024], BF16) for d in range(8)]
        for d in range(8):
            em.dma(h_sb[d][:], IN["c_had"][d * 128:(d + 1) * 128, :], q="sp")
        xq_t = em.sb("l1_xqt", [128, 1024], BF16)
        for chk in range(n_chunks):
            tbase = chk * 512
            for d in range(8):
                em.dmat(ynT[d][:], yn_s[tbase:tbase + 512, d * 128:(d + 1) * 128], q="sp")
            for tt in range(4):
                for oc in range(2):
                    bank = pp[4 + oc]
                    for d in range(8):
                        em.mm(bank[:], ynT[d][:, tt * 128:(tt + 1) * 128],
                              h_sb[d][:, oc * 512:(oc + 1) * 512],
                              start=(d == 0), stop=(d == 7), inc=(1 if d == 7 else 0))
                    em.copy(fA[:, oc * 512:(oc + 1) * 512], bank[:])
                em.red(st[:, 0:1], fA[:], ALU.max, absval=True)
                em.ts(st[:, 1:2], st[:, 0:1], 1e-5, op0=ALU.max)
                em.ts(st[:, 2:3], st[:, 1:2], 1.0 / 7.0, op0=ALU.mult)
                em.recip(st[:, 3:4], st[:, 2:3])
                em.ts(fB[:], fA[:], st[:, 3:4], op0=ALU.mult)
                em.copy(iA[:], fB[:])
                em.copy(fC[:], iA[:])
                e = iA[:].bitcast(F32)
                em.tt(e, fB[:], fC[:], ALU.subtract)
                em.ts(fD[:], e, 0.5, op0=ALU.is_ge)
                em.stt(fD[:], e, -0.5, fD[:], ALU.is_lt, ALU.subtract)
                em.tt(fC[:], fC[:], fD[:], ALU.subtract)
                em.ts(xq_t[:], fC[:], 7.0, -8.0, op0=ALU.min, op1=ALU.max)
                trow = tbase + tt * 128
                em.dma(xq_s[trow:trow + 128, :], xq_t[:], q="sp")
                em.dma(OUT["s1q_s"][trow:trow + 128, :], st[:, 2:3], q="sp")

    with emscope(em):
        wf_sb = [em.sb(f"l1_wf{d}", [128, 1024], BF16, tdest=True) for d in range(8)]
        for d in range(8):
            em.dmat(wf_sb[d][:], IN["wf1q_rows"][:, d * 128:(d + 1) * 128], q="sp")
        xqT = [em.sb(f"l1_xqT{d}", [128, 512], BF16, tdest=True) for d in range(8)]
        x1_t = em.sb("l1_x1t", [128, 1024], BF16)
        a1r_t = em.sb("l1_a1r", [128, 1024], BF16)
        sq_c = em.sb("l1_sqc", [128, 1], F32)
        out_t = em.sb("l1_outt", [128, 1024], F32)
        for chk in range(n_chunks):
            tbase = chk * 512
            for d in range(8):
                em.dmat(xqT[d][:], xq_s[tbase:tbase + 512, d * 128:(d + 1) * 128], q="sp")
            for tt in range(4):
                trow = tbase + tt * 128
                em.dma(x1_t[:], IN["x1_s"][trow:trow + 128, :], q="sp")
                em.dma(a1r_t[:], a1_s[trow:trow + 128, :], q="sp")
                em.dma(sq_c[:], OUT["s1q_s"][trow:trow + 128, :], q="sp")
                for oc in range(2):
                    bank = pp[4 + oc]
                    for d in range(8):
                        em.mm(bank[:], xqT[d][:, tt * 128:(tt + 1) * 128],
                              wf_sb[d][:, oc * 512:(oc + 1) * 512],
                              start=(d == 0), stop=(d == 7), inc=(1 if d == 7 else 0))
                    sl_ = slice(oc * 512, (oc + 1) * 512)
                    em.stt(fA[:, 0:512], bank[:], sq_c[:, 0:1], x1_t[:, sl_], ALU.mult, ALU.add)
                    em.tt(out_t[:, sl_], fA[:, 0:512], a1r_t[:, sl_], ALU.add)
                em.dma(OUT["out"][trow:trow + 128, :], out_t[:], q="sp")


L1_INS = dict(
    x1_s=([T, D], BF16),
    wq1_bf=([1024, 1024], BF16), wk1_bf=([1024, 1024], BF16),
    wv1_bf=([1024, 1024], BF16), wo1_bf=([1024, 1024], BF16),
    L1_attn_bq=([1024], F32), L1_attn_bk=([1024], F32),
    L1_attn_bv=([1024], F32), L1_attn_bo=([1024], F32),
    wf1q_rows=([1024, 1024], BF16),
    c_ident=([128, 128], BF16), c_ones=([1, 128], BF16), c_had=([1024, 1024], BF16),
)
L1_OUTS = dict(out=([T, D], F32), yn_s=([T, D], BF16), a1_s=([T, D], BF16),
               sc_s=([T, D], BF16), xq_s=([T, D], BF16), s1q_s=([T, 1], F32))



_CACHE = {}


def _phase(name, emit_fn, ins_spec, outs_spec):
    if name not in _CACHE:
        _CACHE[name] = build_phase(emit_fn, ins_spec, outs_spec)
    return _CACHE[name]


def kernel_multilaunch(**inputs):
    inputs = {k: np.ascontiguousarray(v) for k, v in inputs.items()}
    consts = host_consts()
    NCORE = 8
    cores = list(range(NCORE))

    prep_nc = _phase("prep", emit_prep, {n: (INPUT_SHAPES[n], F32) for n in PREP_INS}, PREP_OUTS)
    prep_in = {n: inputs[n].astype(np.float32) for n in PREP_INS}
    prep_res = run_bass_kernel_spmd(prep_nc, [prep_in] * NCORE, core_ids=cores)
    pw = prep_res.results

    l0a_nc = _phase("l0a", emit_l0a, L0A_INS, L0A_OUTS)
    l0a_maps = []
    for c in cores:
        m = dict(x=inputs["L0_attn_hidden_states"][c].astype(np.float32),
                 wq_bf=pw[c]["wq_bf"], wk_bf=pw[c]["wk_bf"], wv_bf=pw[c]["wv_bf"],
                 wg_bf=pw[c]["wg_bf"], wo_bf=pw[c]["wo_bf"], wgkb_bf=pw[c]["wgkb_bf"],
                 L0_attn_conv_q=inputs["L0_attn_conv_q"], L0_attn_conv_k=inputs["L0_attn_conv_k"],
                 L0_attn_conv_v=inputs["L0_attn_conv_v"],
                 L0_attn_A_log=inputs["L0_attn_A_log"], L0_attn_dt_bias=inputs["L0_attn_dt_bias"],
                 L0_attn_b_b=inputs["L0_attn_b_b"],
                 L0_attn_g_norm_weight=inputs["L0_attn_g_norm_weight"],
                 c_ident=consts["c_ident"], c_ms=consts["c_ms"], c_bigL=consts["c_bigL"],
                 c_bigUi=consts["c_bigUi"], c_bigUs=consts["c_bigUs"])
        l0a_maps.append(m)
    l0a_res = run_bass_kernel_spmd(l0a_nc, l0a_maps, core_ids=cores).results

    l0f_nc = _phase("l0f", emit_l0f, L0F_INS, L0F_OUTS)
    l0f_maps = []
    for c in cores:
        m = dict(x=inputs["L0_attn_hidden_states"][c].astype(np.float32),
                 xn_s=l0a_res[c]["xn_s"], a0_s=l0a_res[c]["a0_s"],
                 w1q_rows=pw[c]["w1q_rows"], w2q_rows=pw[c]["w2q_rows"],
                 L0_ffn_ln1_w=inputs["L0_ffn_ln1_w"], L0_ffn_ln1_b=inputs["L0_ffn_ln1_b"],
                 L0_ffn_ln2_w=inputs["L0_ffn_ln2_w"], L0_ffn_ln2_b=inputs["L0_ffn_ln2_b"],
                 L0_ffn_b1=inputs["L0_ffn_b1"], L0_ffn_b2=inputs["L0_ffn_b2"],
                 c_ones=consts["c_ones"], c_ident=consts["c_ident"])
        l0f_maps.append(m)
    l0f_res = run_bass_kernel_spmd(l0f_nc, l0f_maps, core_ids=cores).results

    l1_nc = _phase("l1", emit_l1, L1_INS, L1_OUTS)
    l1_maps = []
    for c in cores:
        m = dict(x1_s=l0f_res[c]["x1_s"],
                 wq1_bf=pw[c]["wq1_bf"], wk1_bf=pw[c]["wk1_bf"],
                 wv1_bf=pw[c]["wv1_bf"], wo1_bf=pw[c]["wo1_bf"],
                 L1_attn_bq=inputs["L1_attn_bq"], L1_attn_bk=inputs["L1_attn_bk"],
                 L1_attn_bv=inputs["L1_attn_bv"], L1_attn_bo=inputs["L1_attn_bo"],
                 wf1q_rows=pw[c]["wf1q_rows"],
                 c_ident=consts["c_ident"], c_ones=consts["c_ones"], c_had=consts["c_had"])
        l1_maps.append(m)
    l1_res = run_bass_kernel_spmd(l1_nc, l1_maps, core_ids=cores).results

    out = np.stack([l1_res[c]["out"] for c in cores], axis=0)
    return out



def build_fused():
    nc = bass.Bass("TRN2", target_bir_lowering=False)
    ext_in = {}
    for n in PREP_INS:
        ext_in[n] = nc.dram_tensor(n, INPUT_SHAPES[n], F32, kind="ExternalInput").ap()
    for n, sh in [("x", [T, D]), ("L0_attn_A_log", [12]), ("L0_attn_dt_bias", [12]),
                  ("L0_attn_b_b", [12]), ("L0_attn_g_norm_weight", [128]),
                  ("L0_ffn_ln1_w", [1024]), ("L0_ffn_ln1_b", [1024]),
                  ("L0_ffn_ln2_w", [4096]), ("L0_ffn_ln2_b", [4096]),
                  ("L0_ffn_b1", [4096]), ("L0_ffn_b2", [1024]),
                  ("L1_attn_bq", [1024]), ("L1_attn_bk", [1024]),
                  ("L1_attn_bv", [1024]), ("L1_attn_bo", [1024])]:
        if n not in ext_in:
            ext_in[n] = nc.dram_tensor(n, sh, F32, kind="ExternalInput").ap()
    for n, (sh, dt) in CONST_SPECS.items():
        ext_in[n] = nc.dram_tensor(n, list(sh), dt, kind="ExternalInput").ap()
    out = nc.dram_tensor("out", [T, D], F32, kind="ExternalOutput").ap()

    MID = {}
    for spec in (PREP_OUTS, L0A_OUTS, L0F_OUTS, L1_OUTS):
        for n, (sh, dt) in spec.items():
            if n == "out":
                continue
            if n not in MID:
                MID[n] = nc.dram_tensor(n, list(sh), dt, kind="Internal").ap()
    MID["out"] = out

    keys = list(Em.QUEUES) + [f"{q}{l}" for q in ("sp", "pool") for l in range(Em.NLANES)]
    sem_ctx = [nc.semaphore(f"q_{k}") for k in keys]
    sems = {}
    import contextlib
    with contextlib.ExitStack() as stack:
        for k, ctx in zip(keys, sem_ctx):
            sems[k] = stack.enter_context(ctx)
        with nc.Block() as block:
            @block.vector
            def _(vector):
                em = Em(nc, sems)
                with contextlib.ExitStack() as tstack:
                    em.stack = tstack
                    allin = {**ext_in, **MID}
                    with emscope(em):
                        emit_prep(em, ext_in, MID)
                    with emscope(em):
                        emit_l0a(em, allin, MID)
                    with emscope(em):
                        emit_l0f(em, allin, MID)
                    with emscope(em):
                        emit_l1(em, allin, MID)
                    em.barrier_store_done()
    return nc


def kernel(**inputs):
    inputs = {k: np.ascontiguousarray(v) for k, v in inputs.items()}
    consts = host_consts()
    NCORE = 8
    if "fused" not in _CACHE:
        _CACHE["fused"] = build_fused()
    nc = _CACHE["fused"]
    maps = []
    for c in range(NCORE):
        m = {n: inputs[n].astype(np.float32) for n in PREP_INS}
        m["x"] = inputs["L0_attn_hidden_states"][c].astype(np.float32)
        for n in ["L0_attn_A_log", "L0_attn_dt_bias", "L0_attn_b_b", "L0_attn_g_norm_weight",
                  "L0_ffn_ln1_w", "L0_ffn_ln1_b", "L0_ffn_ln2_w", "L0_ffn_ln2_b",
                  "L0_ffn_b1", "L0_ffn_b2", "L1_attn_bq", "L1_attn_bk",
                  "L1_attn_bv", "L1_attn_bo"]:
            m[n] = inputs[n].astype(np.float32)
        for n in CONST_SPECS:
            m[n] = consts[n]
        maps.append(m)
    res = run_bass_kernel_spmd(nc, maps, core_ids=list(range(NCORE)))
    return np.stack([res.results[c]["out"] for c in range(NCORE)], axis=0)


# version 04
# speedup vs baseline: 1.1486x; 1.1438x over previous
import numpy as np
import ml_dtypes
import concourse.bass as bass
import concourse.mybir as mybir
from concourse.bass_utils import run_bass_kernel_spmd

F32 = mybir.dt.float32
BF16 = mybir.dt.bfloat16
I32 = mybir.dt.int32
AF = mybir.ActivationFunctionType
ALU = mybir.AluOpType
AX = mybir.AxisListType
BF = ml_dtypes.bfloat16

T = 4096
D = 1024
NH, DK, DV = 12, 64, 128
KD, VD = NH * DK, NH * DV
CS = 64
HID = 4096
H1, DH1 = 16, 64
EPS_LN = 1e-5
LN8TH = float(np.log(np.float64(0.125)))



class Em:
    QUEUES = ("pe", "act", "dve", "pool", "sp")
    NLANES = 8

    def __init__(self, nc, sems):
        self.nc = nc
        self.eng = dict(pe=nc.tensor, act=nc.scalar, dve=nc.vector,
                        pool=nc.gpsimd, sp=nc.sync)
        self.sems = sems
        self.cnt = {k: 0 for k in sems}
        self.lane = {"sp": 0, "pool": 0}
        self.acc = {}
        self.waited = {q: {} for q in self.QUEUES}

    @staticmethod
    def _rng(ap):
        t = ap.tensor
        sp = t.space.name
        dims = ap.ap
        if sp in ("SBUF", "PSUM") and len(dims) >= 1 and dims[0][0] > 0:
            rs = dims[0][0]
            p0 = ap.offset // rs
            p1 = p0 + dims[0][1] - 1
            flo = fhi = ap.offset % rs
            for step, count in dims[1:]:
                d = step * (count - 1)
                if d >= 0:
                    fhi += d
                else:
                    flo += d
            return t.name, p0, p1, flo, fhi
        lo = hi = ap.offset
        for step, count in dims:
            d = step * (count - 1)
            if d >= 0:
                hi += d
            else:
                lo += d
        return t.name, 0, 0, lo, hi

    def _deps_for(self, ap, write):
        name, p0, p1, lo, hi = self._rng(ap)
        deps = []
        for rec in self.acc.get(name, ()):
            if rec[3] < lo or rec[2] > hi or rec[1] < p0 or rec[0] > p1:
                continue
            if rec[6] or write:
                deps.append((rec[4], rec[5]))
        return deps

    def _record(self, ap, q, cnt, write):
        name, p0, p1, lo, hi = self._rng(ap)
        lst = self.acc.setdefault(name, [])
        if write:
            lst[:] = [r for r in lst
                      if not (p0 <= r[0] and r[1] <= p1 and lo <= r[2] and r[3] <= hi)]
        lst.append([p0, p1, lo, hi, q, cnt, write])
        if len(lst) > 400:
            del lst[: len(lst) - 250]

    def _wait(self, q, deps):
        need = {}
        for dq, c in deps:
            if dq == q == "pe":
                continue
            if c is None:
                raise RuntimeError("dependency on un-inc'd (lazy) op")
            if need.get(dq, 0) < c:
                need[dq] = c
        for dq, c in need.items():
            if self.waited[q].get(dq, 0) >= c:
                continue
            self.eng[q].wait_ge(self.sems[dq], c)
            self.waited[q][dq] = c

    def op(self, q, fname, outs, ins, inc=1, _call=None, **kw):
        deps = []
        for ap in ins:
            if isinstance(ap, (int, float)) or ap is None:
                continue
            deps += self._deps_for(ap, False)
        for ap in outs:
            if ap is None:
                continue
            deps += self._deps_for(ap, True)
        self._wait(q, deps)
        inst = (_call or getattr(self.eng[q], fname))(**kw)
        c = None
        if inc:
            inst.then_inc(self.sems[q], inc)
            self.cnt[q] += inc
            c = self.cnt[q]
        else:
            c = self.cnt[q] + 1
        for ap in ins:
            if isinstance(ap, (int, float)) or ap is None:
                continue
            self._record(ap, q, c, False)
        for ap in outs:
            if ap is None:
                continue
            self._record(ap, q, c, True)
        return inst

    def _dma_issue(self, q, outs, ins, transpose=False, **kw):
        ln = self.lane[q]
        self.lane[q] = (ln + 1) % self.NLANES
        key = f"{q}{ln}"
        deps = []
        for ap in ins:
            deps += self._deps_for(ap, False)
        for ap in outs:
            deps += self._deps_for(ap, True)
        deps.append((key, self.cnt[key]))
        if not hasattr(self, "_cat"):
            self._cat = {"tr": {}, "ss": {}}
        is_ss = (not transpose and ins[0].tensor.space.name == "SBUF"
                 and outs[0].tensor.space.name == "SBUF")
        if transpose:
            deps += list(self._cat["ss"].items())
        elif is_ss:
            deps += list(self._cat["tr"].items())
        self._wait(q, deps)
        if transpose:
            kw["transpose"] = True
        inst = self.eng[q].dma_start(**kw)
        inst.then_inc(self.sems[key], 16)
        self.cnt[key] += 16
        c = self.cnt[key]
        if transpose:
            self._cat["tr"][key] = c
        elif is_ss:
            self._cat["ss"][key] = c
        for ap in ins:
            self._record(ap, key, c, False)
        for ap in outs:
            self._record(ap, key, c, True)
        return inst

    def dma(self, out, in_, q="sp", **kw):
        return self._dma_issue(q, [out], [in_], out=out, in_=in_, **kw)

    def dmat(self, out, in_, q="sp"):
        return self._dma_issue(q, [out], [in_], transpose=True, out=out, in_=in_)

    def mm(self, out, lhsT, rhs, start=True, stop=True, inc=1, tile_position=None):
        return self.op("pe", "matmul", [out], [lhsT, rhs], inc=inc, out=out,
                       lhsT=lhsT, rhs=rhs, start=start, stop=stop,
                       tile_position=tile_position)

    def tr(self, out, in_, ident, inc=1):
        return self.op("pe", "transpose", [out], [in_, ident], inc=inc,
                       out=out, in_=in_, identity=ident)

    def constf(self, value):
        value = float(value)
        if not hasattr(self, "_consts"):
            self._consts = {}
        if value not in self._consts:
            t = self.stack.enter_context(
                self.nc.sbuf_tensor(f"c_f32_{len(self._consts)}", [128, 1], F32))
            self.memset(t[:], value)
            self._consts[value] = t
        return self._consts[value][:]

    def act(self, out, in_, func, bias=None, scale=None, accum_out=None, q="act"):
        npart = out.ap[0][1]
        if isinstance(bias, (int, float)) and float(bias) != 0.0:
            bias = self.constf(bias)[0:npart, 0:1]
        if isinstance(scale, (int, float)) and float(scale) == 1.0:
            scale = None
        if isinstance(scale, (int, float)):
            scale = self.constf(scale)[0:npart, 0:1]
        kw = dict(out=out, in_=in_, func=func)
        ins = [in_]
        if bias is not None and not (isinstance(bias, float) and bias == 0.0):
            kw["bias"] = bias
            if not isinstance(bias, (int, float)):
                ins.append(bias)
        if scale is not None:
            kw["scale"] = scale
            if not isinstance(scale, (int, float)):
                ins.append(scale)
        if accum_out is not None:
            kw["accum_out"] = accum_out
        return self.op(q, "activation", [out, accum_out], ins, **kw)

    def ts(self, out, in0, s1, s2=None, op0=ALU.mult, op1=None, q="dve", accum_out=None):
        ins = [in0] + [s for s in (s1, s2) if s is not None and not isinstance(s, (int, float))]
        kw = dict(out=out, in0=in0, scalar1=s1, scalar2=s2, op0=op0)
        if op1 is not None:
            kw["op1"] = op1
        if accum_out is not None:
            kw["accum_out"] = accum_out
        return self.op(q, "tensor_scalar", [out, accum_out], ins, **kw)

    def tt(self, out, in0, in1, alu, q="dve"):
        return self.op(q, "tensor_tensor", [out], [in0, in1], out=out, in0=in0, in1=in1, op=alu)

    def stt(self, out, in0, s, in1, op0, op1, q="dve"):
        ins = [in0, in1] + ([s] if not isinstance(s, (int, float)) else [])
        return self.op(q, "scalar_tensor_tensor", [out], ins, out=out, in0=in0,
                       scalar=s, in1=in1, op0=op0, op1=op1)

    def copy(self, out, in_, q="dve"):
        return self.op(q, "tensor_copy", [out], [in_], out=out, in_=in_)

    def red(self, out, in_, alu, absval=None, q="dve"):
        return self.op(q, "tensor_reduce", [out], [in_], out=out, in_=in_,
                       axis=AX.X, op=alu, apply_absolute_value=absval)

    def scan(self, out, d0, d1, initial, op0, op1, q="dve"):
        return self.op(q, "tensor_tensor_scan", [out], [d0, d1], out=out, data0=d0,
                       data1=d1, initial=initial, op0=op0, op1=op1)

    def memset(self, ap, val, q="dve"):
        return self.op(q, "memset", [ap], [], ap=ap, constant=val)

    def recip(self, out, in_, q="dve"):
        return self.op(q, "reciprocal", [out], [in_], out=out, in_=in_)

    def sb(self, name, shape, dtype, tdest=False):
        return self.stack.enter_context(
            self.nc.sbuf_tensor(name, list(shape), dtype, is_transpose_dest=tdest))

    def ps(self, name, shape, dtype=F32):
        return self.stack.enter_context(self.nc.psum_tensor(name, list(shape), dtype))

    def quiesce(self):
        for q in self.QUEUES:
            for dk, c in self.cnt.items():
                if dk != q and c and self.waited[q].get(dk, 0) < c:
                    self.eng[q].wait_ge(self.sems[dk], c)
                    self.waited[q][dk] = c

    def barrier_store_done(self):
        for dk, c in self.cnt.items():
            if dk != "sp" and c:
                if self.waited["sp"].get(dk, 0) < c:
                    self.eng["sp"].wait_ge(self.sems[dk], c)
                    self.waited["sp"][dk] = c
        for ln in range(self.NLANES):
            key = f"sp{ln}"
            if self.cnt.get(key, 0):
                self.eng["sp"].wait_ge(self.sems[key], self.cnt[key])


def rsqrt_em(em, out, in_, eps, q="act", scratch=None):
    s = scratch if scratch is not None else out
    em.act(s, in_, AF.Sqrt, bias=float(eps), q=q)
    em.recip(out, s)



def host_consts():
    ident = np.eye(128, dtype=BF)
    ones_row = np.ones((1, 128), dtype=BF)
    t_idx = np.arange(CS)[:, None]
    s_idx = np.arange(CS)[None, :]
    ms = np.tril(np.ones((CS, CS), np.float32), -1).astype(BF)
    bigL = (30000.0 * (s_idx > t_idx)).astype(np.float32)
    bigUi = (-30000.0 * (t_idx > s_idx)).astype(np.float32)
    bigUs = (-30000.0 * (t_idx >= s_idx)).astype(np.float32)
    h = np.array([[1.0]], dtype=np.float64)
    while h.shape[0] < 1024:
        h = np.block([[h, h], [h, -h]])
    had = (h / np.sqrt(1024)).astype(np.float32).astype(BF)
    return dict(c_ident=ident, c_ones=ones_row, c_ms=ms, c_bigL=bigL,
                c_bigUi=bigUi, c_bigUs=bigUs, c_had=had)


CONST_SPECS = dict(c_ident=([128, 128], BF16), c_ones=([1, 128], BF16),
                   c_ms=([CS, CS], BF16), c_bigL=([CS, CS], F32),
                   c_bigUi=([CS, CS], F32), c_bigUs=([CS, CS], F32),
                   c_had=([1024, 1024], BF16))



def round_chain(em, out_bf, x_f32, tmp_i32, tmp_a, tmp_b, lo, hi, scale_col=None,
                post_mul_col=None, q1="dve", q2="dve"):
    if scale_col is not None:
        em.ts(tmp_a, x_f32, scale_col, op0=ALU.mult, q=q1)
        src = tmp_a
    else:
        src = x_f32
    em.copy(tmp_i32, src, q=q1)
    em.copy(tmp_b, tmp_i32, q=q1)
    em.tt(tmp_i32.bitcast(F32), src, tmp_b, ALU.subtract, q=q2)
    e = tmp_i32.bitcast(F32)
    em.ts(tmp_a, e, 0.5, op0=ALU.is_ge, q=q2)
    em.stt(tmp_a, e, -0.5, tmp_a, ALU.is_lt, ALU.subtract, q=q2)
    em.tt(tmp_b, tmp_b, tmp_a, ALU.subtract, q=q2)
    if post_mul_col is None:
        em.ts(out_bf, tmp_b, float(hi), float(lo), op0=ALU.min, op1=ALU.max, q=q1)
    else:
        em.ts(tmp_b, tmp_b, float(hi), float(lo), op0=ALU.min, op1=ALU.max, q=q1)
        em.ts(out_bf, tmp_b, post_mul_col, op0=ALU.mult, q=q1)


def emit_prep(em, IN, OUT):
    nc = em.nc
    sw = em.sb("pr_sw", [128, 4096], F32)
    swb = em.sb("pr_swb", [128, 4096], F32)
    swc = em.sb("pr_swc", [128, 4096], F32)
    swi = em.sb("pr_swi", [128, 4096], I32)
    bo0 = em.sb("pr_bo0", [128, 4096], BF16)
    bo1 = em.sb("pr_bo1", [128, 4096], BF16)
    acc = em.sb("pr_acc", [128, 1], F32)
    tpart = em.sb("pr_tp", [128, 1], F32)
    sc = em.sb("pr_sc", [128, 8], F32)
    onesf = em.sb("pr_onesf", [128, 1], F32)
    onesr = em.sb("pr_onesr", [1, 128], F32)
    ps = em.ps("pr_ps", [1, 8])
    ps2 = em.ps("pr_ps2", [128, 8])
    em.memset(onesf[:], 1.0)
    em.memset(onesr[:], 1.0)

    casts = [
        ("L0_attn_Wq", "wq_bf", 1024, 768), ("L0_attn_Wk", "wk_bf", 1024, 768),
        ("L0_attn_Wv", "wv_bf", 1024, 1536), ("L0_attn_Wg", "wg_bf", 1024, 1536),
        ("L0_attn_Wo", "wo_bf", 1536, 1024),
        ("L1_attn_wq", "wq1_bf", 1024, 1024), ("L1_attn_wk", "wk1_bf", 1024, 1024),
        ("L1_attn_wv", "wv1_bf", 1024, 1024), ("L1_attn_wo", "wo1_bf", 1024, 1024),
    ]
    bi = 0
    for src_, dst, rows, cols in casts:
        s, d = IN[src_], OUT[dst]
        for r0 in range(0, rows, 128):
            b = (bo0, bo1)[bi % 2]; bi += 1
            em.dma(b[:, 0:cols], s[r0:r0 + 128, 0:cols], q="pool")
            em.dma(d[r0:r0 + 128, 0:cols], b[:, 0:cols], q="sp")

    for src_, dst, rows in [("L0_attn_conv_q", "convq_bf", 768),
                            ("L0_attn_conv_k", "convk_bf", 768),
                            ("L0_attn_conv_v", "convv_bf", 1536)]:
        ncol = rows * 4 // 128
        em.dma(sw[:, 0:ncol], IN[src_].rearrange("a b -> (a b)").rearrange("(p n) -> p n", p=128), q="sp")
        em.copy(bo0[:, 0:ncol], sw[:, 0:ncol])
        em.dma(OUT[dst].rearrange("a b -> (a b)").rearrange("(p n) -> p n", p=128), bo0[:, 0:ncol], q="sp")

    em.memset(bo1[:, 0:352], 0.0)
    em.dma(OUT["wgkb_bf"].rearrange("a b -> (a b)").rearrange("(p n) -> p n", p=128),
           bo1[:, 0:352], q="sp")
    for src_, c0 in [("L0_attn_Wgk", 0), ("L0_attn_Wb", 32)]:
        em.dma(sw[:, 0:96], IN[src_].rearrange("a b -> (a b)").rearrange("(p n) -> p n", p=128), q="sp")
        em.copy(bo0[:, 0:96], sw[:, 0:96])
        em.dma(OUT["wgkb_bf"][:, c0:c0 + 12].rearrange("(p r) c -> p r c", p=128),
               bo0[:, 0:96].rearrange("p (r c) -> p r c", c=12), q="sp")

    for wname, oname, nelem in [("L0_ffn_w1", "w1q_rows", 4096 * 1024),
                                ("L0_ffn_w2", "w2q_rows", 1024 * 4096)]:
        w = IN[wname]
        wflat = w.rearrange("a b -> (a b)").rearrange("(p n) -> p n", p=128)
        o = OUT[oname]
        oflat = o.rearrange("a b -> (a b)").rearrange("(p n) -> p n", p=128)
        NCOL = nelem // 128
        TL = 4096
        em.memset(acc[:], 0.0)
        for c0 in range(0, NCOL, TL):
            em.dma(sw[:], wflat[:, c0:c0 + TL], q="sp")
            em.act(swb[:], sw[:], AF.Abs, accum_out=tpart[:])
            em.tt(acc[:], acc[:], tpart[:], ALU.add)
        em.mm(ps[0:1, 0:1], acc[:], onesf[:])
        em.copy(sc[0:1, 4:5], ps[0:1, 0:1])
        em.mm(ps2[:, 0:1], onesr[:], sc[0:1, 4:5])
        em.ts(sc[:, 0:1], ps2[:, 0:1], 1.0 / nelem, 1e-5, op0=ALU.mult, op1=ALU.max)
        em.recip(sc[:, 1:2], sc[:, 0:1])
        for c0 in range(0, NCOL, TL):
            em.dma(sw[:], wflat[:, c0:c0 + TL], q="sp")
            round_chain(em, bo0[:], sw[:], swi[:], swb[:], swc[:], -1.0, 1.0,
                        scale_col=sc[:, 1:2], post_mul_col=sc[:, 0:1])
            em.dma(oflat[:, c0:c0 + TL], bo0[:], q="sp")

    w = IN["L1_ffn_weight"]
    o = OUT["wf1q_rows"]
    for r0 in range(0, 1024, 128):
        em.dma(sw[:, 0:1024], w[r0:r0 + 128, :], q="sp")
        em.act(swb[:, 0:1024], sw[:, 0:1024], AF.Abs, accum_out=sc[:, 0:1])
        em.ts(sc[:, 1:2], sc[:, 0:1], 1.0 / 1024, 1e-5, op0=ALU.mult, op1=ALU.max)
        em.recip(sc[:, 2:3], sc[:, 1:2])
        em.ts(swc[:, 0:1024], sw[:, 0:1024], sc[:, 2:3], op0=ALU.mult)
        em.ts(swb[:, 0:1024], swc[:, 0:1024], 0.5, op0=ALU.is_gt)
        em.stt(swb[:, 0:1024], swc[:, 0:1024], -0.5, swb[:, 0:1024], ALU.is_lt, ALU.subtract)
        em.ts(sc[:, 3:4], sc[:, 1:2], -1.0, op0=ALU.mult)
        em.ts(bo0[:, 0:1024], swb[:, 0:1024], sc[:, 3:4], op0=ALU.mult)
        em.dma(o[r0:r0 + 128, :], bo0[:, 0:1024], q="sp")


PREP_OUTS = dict(
    wq_bf=([1024, 768], BF16), wk_bf=([1024, 768], BF16), wv_bf=([1024, 1536], BF16),
    wg_bf=([1024, 1536], BF16), wo_bf=([1536, 1024], BF16), wgkb_bf=([1024, 44], BF16),
    convq_bf=([768, 4], BF16), convk_bf=([768, 4], BF16), convv_bf=([1536, 4], BF16),
    w1q_rows=([4096, 1024], BF16), w2q_rows=([1024, 4096], BF16),
    wf1q_rows=([1024, 1024], BF16),
    wq1_bf=([1024, 1024], BF16), wk1_bf=([1024, 1024], BF16),
    wv1_bf=([1024, 1024], BF16), wo1_bf=([1024, 1024], BF16),
)

PREP_INS = ["L0_attn_Wq", "L0_attn_Wk", "L0_attn_Wv", "L0_attn_Wg", "L0_attn_Wo",
            "L0_attn_conv_q", "L0_attn_conv_k", "L0_attn_conv_v", "L0_attn_Wgk",
            "L0_attn_Wb", "L0_ffn_w1", "L0_ffn_w2", "L1_attn_wq", "L1_attn_wk",
            "L1_attn_wv", "L1_attn_wo", "L1_ffn_weight"]

INPUT_SHAPES = dict(
    L0_attn_Wq=[1024, 768], L0_attn_Wk=[1024, 768], L0_attn_Wv=[1024, 1536],
    L0_attn_Wg=[1024, 1536], L0_attn_Wo=[1536, 1024], L0_attn_conv_q=[768, 4],
    L0_attn_conv_k=[768, 4], L0_attn_conv_v=[1536, 4], L0_attn_Wgk=[1024, 12],
    L0_attn_Wb=[1024, 12], L0_ffn_w1=[4096, 1024], L0_ffn_w2=[1024, 4096],
    L1_attn_wq=[1024, 1024], L1_attn_wk=[1024, 1024], L1_attn_wv=[1024, 1024],
    L1_attn_wo=[1024, 1024], L1_ffn_weight=[1024, 1024],
)


def build_phase(emit_fn, ext_ins, ext_outs, consts=()):
    nc = bass.Bass("TRN2", target_bir_lowering=False)
    IN = {n: nc.dram_tensor(n, list(sh), dt, kind="ExternalInput").ap()
          for n, (sh, dt) in ext_ins.items()}
    OUT = {n: nc.dram_tensor(n, list(sh), dt, kind="ExternalOutput").ap()
           for n, (sh, dt) in ext_outs.items()}
    keys = list(Em.QUEUES) + [f"{q}{l}" for q in ("sp", "pool") for l in range(Em.NLANES)]
    sem_ctx = [nc.semaphore(f"q_{k}") for k in keys]
    sems = {}
    import contextlib
    with contextlib.ExitStack() as stack:
        for k, ctx in zip(keys, sem_ctx):
            sems[k] = stack.enter_context(ctx)
        with nc.Block() as block:
            @block.vector
            def _(vector):
                em = Em(nc, sems)
                with contextlib.ExitStack() as tstack:
                    em.stack = tstack
                    emit_fn(em, IN, OUT)
                    em.barrier_store_done()
    return nc


import contextlib

@contextlib.contextmanager
def emscope(em):
    with contextlib.ExitStack() as inner:
        outer = em.stack
        em.stack = inner
        try:
            yield
        finally:
            em.quiesce()
            em.stack = outer



def _bcast_const_mat(em, out_sb, vec_dram, width, ones_f, ps_bc, tmp_row):
    em.dma(tmp_row[0:1, 0:width], vec_dram.rearrange("(r c) -> r c", r=1), q="sp")
    for c0 in range(0, width, 512):
        em.mm(ps_bc[:, 0:512], ones_f[0:1, :], tmp_row[0:1, c0:c0 + 512])
        em.copy(out_sb[:, c0:c0 + 512], ps_bc[:, 0:512])


def _round_quant(em, out_bf, x_f32, scol, lo, hi, fB, fC, fD, iA, n):
    em.ts(fB[:, 0:n], x_f32, scol, op0=ALU.mult)
    em.copy(iA[:, 0:n], fB[:, 0:n])
    em.copy(fC[:, 0:n], iA[:, 0:n])
    e = iA[:, 0:n].bitcast(F32)
    em.tt(e, fB[:, 0:n], fC[:, 0:n], ALU.subtract)
    em.ts(fD[:, 0:n], e, 0.5, op0=ALU.is_ge)
    em.stt(fD[:, 0:n], e, -0.5, fD[:, 0:n], ALU.is_lt, ALU.subtract)
    em.tt(fC[:, 0:n], fC[:, 0:n], fD[:, 0:n], ALU.subtract)
    em.ts(out_bf, fC[:, 0:n], float(hi), float(lo), op0=ALU.min, op1=ALU.max)


def emit_l0f(em, IN, OUT):
    nc = em.nc
    ones = em.sb("ff_ones", [1, 128], BF16)
    em.dma(ones[:], IN["c_ones"], q="sp")
    onesf = em.sb("ff_onesf", [1, 128], F32)
    em.memset(onesf[:], 1.0)
    ident = em.sb("ff_ident", [128, 128], BF16)
    em.dma(ident[:], IN["c_ident"], q="sp")
    ps_bc = em.ps("ff_psbc", [128, 512])
    tmp_row = em.sb("ff_trow", [1, 4096], F32)

    c_ln1w = em.sb("ff_ln1w", [128, 1024], F32)
    c_ln1b = em.sb("ff_ln1b", [128, 1024], F32)
    c_ln2w = em.sb("ff_ln2w", [128, 4096], F32)
    c_ln2b = em.sb("ff_ln2b", [128, 4096], F32)
    for mat, vec, w in [(c_ln1w, "L0_ffn_ln1_w", 1024), (c_ln1b, "L0_ffn_ln1_b", 1024),
                        (c_ln2w, "L0_ffn_ln2_w", 4096), (c_ln2b, "L0_ffn_ln2_b", 4096)]:
        _bcast_const_mat(em, mat, IN[vec], w, onesf, ps_bc, tmp_row)
    b1row = em.sb("ff_b1row", [1, 4096], BF16)
    em.dma(b1row[:], IN["L0_ffn_b1"].rearrange("(r c) -> r c", r=1), q="pool")
    b2row = em.sb("ff_b2row", [1, 1024], BF16)
    em.dma(b2row[:], IN["L0_ffn_b2"].rearrange("(r c) -> r c", r=1), q="pool")

    xn_t = em.sb("ff_xn", [128, 1024], BF16)
    fA = em.sb("ff_fA", [128, 1024], F32)
    fB = em.sb("ff_fB", [128, 1024], F32)
    fC = em.sb("ff_fC", [128, 1024], F32)
    fD = em.sb("ff_fD", [128, 1024], F32)
    iA = em.sb("ff_iA", [128, 1024], I32)
    hq_t = em.sb("ff_hq", [128, 1024], BF16)
    st = em.sb("ff_st", [128, 32], F32)
    gacc = em.sb("ff_gacc", [128, 32], F32)
    sqs = em.sb("ff_sqs", [128, 512], F32)
    mq_t = em.sb("ff_mqt", [128, 1024], BF16)
    srow = em.sb("ff_srow", [1, 128], BF16)
    sbfc = em.sb("ff_sbfc", [128, 1], BF16)
    ps_tr = em.ps("ff_pstr", [128, 128], BF16)
    pu = [em.ps(f"ff_pu{i}", [128, 512]) for i in range(2)]

    xn_s, mq_s, s2_s = IN["xn_s"], OUT["mq_s"], OUT["s2_s"]

    with emscope(em):
        w1sb = [em.sb(f"ff_w1_{d}", [128, 4096], BF16, tdest=True) for d in range(8)]
        for d in range(8):
            em.dmat(w1sb[d][:], IN["w1q_rows"][:, d * 128:(d + 1) * 128], q="sp")
        hqT = [em.sb(f"ff_hqT{d}", [128, 128], BF16, tdest=True) for d in range(8)]
        gm = em.sb("ff_gm", [128, 4096], BF16)

        for tt in range(32):
            trow = tt * 128
            em.dma(xn_t[:], xn_s[trow:trow + 128, :], q="sp")
            em.red(st[:, 0:1], xn_t[:], ALU.add)
            em.act(fA[:], xn_t[:], AF.Square, accum_out=st[:, 1:2])
            em.ts(st[:, 2:3], st[:, 0:1], 1.0 / 1024, op0=ALU.mult)
            em.ts(st[:, 3:4], st[:, 1:2], 1.0 / 1024, op0=ALU.mult)
            em.ts(st[:, 4:5], st[:, 2:3], st[:, 2:3], op0=ALU.mult)
            em.tt(st[:, 5:6], st[:, 3:4], st[:, 4:5], ALU.subtract)
            em.act(st[:, 6:7], st[:, 5:6], AF.Ln, bias=EPS_LN)
            em.act(st[:, 7:8], st[:, 6:7], AF.Exp, scale=-0.5)
            em.ts(st[:, 8:9], st[:, 2:3], st[:, 7:8], s2=-1.0, op0=ALU.mult, op1=ALU.mult)
            em.act(fA[:], xn_t[:], AF.Identity, scale=st[:, 7:8], bias=st[:, 8:9])
            em.stt(fB[:], fA[:], 0.0, c_ln1w[:], ALU.bypass, ALU.mult)
            em.tt(fA[:], fB[:], c_ln1b[:], ALU.add)
            em.red(st[:, 9:10], fA[:], ALU.max, absval=True)
            em.ts(st[:, 10:11], st[:, 9:10], 1e-5, op0=ALU.max)
            em.recip(st[:, 11:12], st[:, 10:11])
            em.ts(st[:, 12:13], st[:, 11:12], 127.0, op0=ALU.mult)
            em.ts(st[:, 13:14], st[:, 10:11], 1.0 / 127.0, op0=ALU.mult)
            _round_quant(em, hq_t[:], fA[:], st[:, 12:13], -128.0, 127.0, fA2 := fB, fC, fD, iA, 1024)
            em.copy(sbfc[:], st[:, 12:13])
            em.tr(ps_tr[0:1, 0:128], sbfc[:], ident[:])
            em.copy(srow[:], ps_tr[0:1, 0:128])
            for d in range(8):
                em.tr(ps_tr[:], hq_t[:, d * 128:(d + 1) * 128], ident[:])
                if d % 2:
                    em.act(hqT[d][:], ps_tr[:], AF.Identity)
                else:
                    em.copy(hqT[d][:], ps_tr[:])
            for fc in range(8):
                bank = pu[fc % 2]
                for d in range(8):
                    em.mm(bank[:], hqT[d][:], w1sb[d][:, fc * 512:(fc + 1) * 512],
                          start=(d == 0), stop=False, inc=0)
                em.mm(bank[:], srow[0:1, :], b1row[0:1, fc * 512:(fc + 1) * 512],
                      start=False, stop=True)
                em.act(gm[:, fc * 512:(fc + 1) * 512], bank[:], AF.Gelu,
                       scale=st[:, 13:14], accum_out=gacc[:, fc:fc + 1])
                em.act(sqs[:], gm[:, fc * 512:(fc + 1) * 512], AF.Square,
                       accum_out=gacc[:, 8 + fc:9 + fc])
            em.red(st[:, 16:17], gacc[:, 0:8], ALU.add)
            em.red(st[:, 17:18], gacc[:, 8:16], ALU.add)
            em.ts(st[:, 18:19], st[:, 16:17], 1.0 / 4096, op0=ALU.mult)
            em.ts(st[:, 19:20], st[:, 17:18], 1.0 / 4096, op0=ALU.mult)
            em.ts(st[:, 20:21], st[:, 18:19], st[:, 18:19], op0=ALU.mult)
            em.tt(st[:, 21:22], st[:, 19:20], st[:, 20:21], ALU.subtract)
            em.act(st[:, 22:23], st[:, 21:22], AF.Ln, bias=EPS_LN)
            em.act(st[:, 23:24], st[:, 22:23], AF.Exp, scale=-0.5)
            em.ts(st[:, 24:25], st[:, 18:19], st[:, 23:24], s2=-1.0, op0=ALU.mult, op1=ALU.mult)
            for fc in range(8):
                sl_ = slice(fc * 512, (fc + 1) * 512)
                em.act(fA[:, 0:512], gm[:, sl_], AF.Identity, scale=st[:, 23:24], bias=st[:, 24:25])
                em.stt(fB[:, 0:512], fA[:, 0:512], 0.0, c_ln2w[:, sl_], ALU.bypass, ALU.mult)
                em.tt(gm[:, sl_], fB[:, 0:512], c_ln2b[:, sl_], ALU.add)
                em.red(gacc[:, 16 + fc:17 + fc], gm[:, sl_], ALU.max, absval=True)
            em.red(st[:, 25:26], gacc[:, 16:24], ALU.max)
            em.ts(st[:, 26:27], st[:, 25:26], 1e-5, op0=ALU.max)
            em.recip(st[:, 27:28], st[:, 26:27])
            em.ts(st[:, 28:29], st[:, 27:28], 127.0, op0=ALU.mult)
            em.dma(s2_s[trow:trow + 128, :], st[:, 28:29], q="sp")
            for cc in range(4):
                sl_ = slice(cc * 1024, (cc + 1) * 1024)
                _round_quant(em, mq_t[:], gm[:, sl_], st[:, 28:29], -128.0, 127.0, fB, fC, fD, iA, 1024)
                em.dma(mq_s[trow:trow + 128, sl_], mq_t[:], q="sp")

    with emscope(em):
        w2sb = [em.sb(f"ff_w2_{h}", [128, 1024], BF16, tdest=True) for h in range(32)]
        for h in range(32):
            em.dmat(w2sb[h][:], IN["w2q_rows"][:, h * 128:(h + 1) * 128], q="sp")
        mqT = [em.sb(f"ff_mqT{h}", [128, 128], BF16, tdest=True) for h in range(32)]
        x_t = em.sb("ff_xt", [128, 1024], F32)
        a0_t = em.sb("ff_a0", [128, 1024], BF16)
        x1_t = em.sb("ff_x1", [128, 1024], BF16)
        s2c = em.sb("ff_s2c", [128, 1], F32)
        s2r = em.sb("ff_s2r", [1, 128], BF16)
        s2cb = em.sb("ff_s2cb", [128, 1], BF16)
        rs2 = em.sb("ff_rs2", [128, 1], F32)
        po = [em.ps(f"ff_po{i}", [128, 512]) for i in range(2)]
        for tt in range(32):
            trow = tt * 128
            em.dma(s2c[:], s2_s[trow:trow + 128, :], q="sp")
            em.recip(rs2[:], s2c[:])
            em.copy(s2cb[:], s2c[:])
            em.tr(ps_tr[0:1, 0:128], s2cb[:], ident[:])
            em.copy(s2r[:], ps_tr[0:1, 0:128])
            for h in range(32):
                em.dmat(mqT[h][:], mq_s[trow:trow + 128, h * 128:(h + 1) * 128], q="sp")
            em.dma(x_t[:], IN["x"][trow:trow + 128, :], q="sp")
            em.dma(a0_t[:], IN["a0_s"][trow:trow + 128, :], q="sp")
            for oc in range(2):
                bank = po[oc]
                for h in range(32):
                    em.mm(bank[:], mqT[h][:], w2sb[h][:, oc * 512:(oc + 1) * 512],
                          start=(h == 0), stop=False, inc=0)
                em.mm(bank[:], s2r[0:1, :], b2row[0:1, oc * 512:(oc + 1) * 512],
                      start=False, stop=True)
                sl_ = slice(oc * 512, (oc + 1) * 512)
                em.stt(fA[:, 0:512], bank[:], rs2[:], x_t[:, sl_], ALU.mult, ALU.add)
                em.tt(x1_t[:, sl_], fA[:, 0:512], a0_t[:, sl_], ALU.add)
            em.dma(OUT["x1_s"][trow:trow + 128, :], x1_t[:], q="sp")


L0F_INS = dict(
    x=([T, D], F32), xn_s=([T, D], BF16), a0_s=([T, D], BF16),
    w1q_rows=([4096, 1024], BF16), w2q_rows=([1024, 4096], BF16),
    L0_ffn_ln1_w=([1024], F32), L0_ffn_ln1_b=([1024], F32),
    L0_ffn_ln2_w=([4096], F32), L0_ffn_ln2_b=([4096], F32),
    L0_ffn_b1=([4096], F32), L0_ffn_b2=([1024], F32),
    c_ones=([1, 128], BF16), c_ident=([128, 128], BF16),
)
L0F_OUTS = dict(x1_s=([T, D], BF16),
                mq_s=([T, HID], BF16), s2_s=([T, 1], F32))



def emit_l0a(em, IN, OUT, n_chunks=8, n_sub=8, scan_secs=99):
    nc = em.nc
    ident = em.sb("la_ident", [128, 128], BF16)
    em.dma(ident[:], IN["c_ident"], q="sp")
    onesf = em.sb("la_onesf", [1, 128], F32)
    em.memset(onesf[:], 1.0)
    onescol = em.sb("la_onescol", [64, 1], F32)
    em.memset(onescol[:], 1.0)
    negones64 = em.sb("la_no64", [1, 64], F32)
    em.memset(negones64[:], -1.0)
    c_ms = em.sb("la_ms", [CS, CS], BF16)
    em.dma(c_ms[:], IN["c_ms"], q="sp")
    c_bigL = em.sb("la_bigL", [CS, CS], F32)
    em.dma(c_bigL[:], IN["c_bigL"], q="sp")
    c_bigUi = em.sb("la_bigUi", [CS, CS], F32)
    em.dma(c_bigUi[:], IN["c_bigUi"], q="sp")
    c_bigUs = em.sb("la_bigUs", [CS, CS], F32)
    em.dma(c_bigUs[:], IN["c_bigUs"], q="sp")

    alog = em.sb("la_alog", [12, 1], F32)
    em.dma(alog[:], IN["L0_attn_A_log"].rearrange("(p c) -> p c", c=1), q="sp")
    dtb = em.sb("la_dtb", [12, 1], F32)
    em.dma(dtb[:], IN["L0_attn_dt_bias"].rearrange("(p c) -> p c", c=1), q="sp")
    bbc = em.sb("la_bb", [12, 1], F32)
    em.dma(bbc[:], IN["L0_attn_b_b"].rearrange("(p c) -> p c", c=1), q="sp")
    negA = em.sb("la_negA", [12, 1], F32)
    em.act(negA[:], alog[:], AF.Exp)
    em.ts(negA[:], negA[:], -1.0, op0=ALU.mult)
    negbb = em.sb("la_negbb", [12, 1], F32)
    em.ts(negbb[:], bbc[:], -1.0, op0=ALU.mult)

    gnwrow = em.sb("la_gnwrow", [1, 128], F32)
    em.dma(gnwrow[:], IN["L0_attn_g_norm_weight"].rearrange("(r c) -> r c", r=1), q="sp")
    gnwmat = em.sb("la_gnwmat", [64, 1536], BF16)
    ppg = em.ps("la_ppg", [128, 128])
    for h in range(12):
        em.mm(ppg[0:64, 0:128], onesf[0:1, 0:64], gnwrow[:])
        em.copy(gnwmat[:, h * 128:(h + 1) * 128], ppg[0:64, 0:128])

    convqk = em.sb("la_convqk", [64, 96], F32)
    for h in range(12):
        em.dma(convqk[:, h * 4:(h + 1) * 4], IN["L0_attn_conv_q"][h * 64:(h + 1) * 64, :], q="sp")
        em.dma(convqk[:, 48 + h * 4:48 + (h + 1) * 4], IN["L0_attn_conv_k"][h * 64:(h + 1) * 64, :], q="sp")
    convv_all = em.sb("la_convvall", [128, 48], F32)
    for vt in range(12):
        em.dma(convv_all[:, vt * 4:(vt + 1) * 4], IN["L0_attn_conv_v"][vt * 128:(vt + 1) * 128, :], q="sp")

    w_sb = [em.sb(f"la_w{d}", [128, 4608], BF16) for d in range(8)]
    wgkb_sb = em.sb("la_wgkb", [128, 352], BF16)
    for d in range(8):
        em.dma(w_sb[d][:, 0:768], IN["wq_bf"][d * 128:(d + 1) * 128, :], q="sp")
        em.dma(w_sb[d][:, 768:1536], IN["wk_bf"][d * 128:(d + 1) * 128, :], q="sp")
        em.dma(w_sb[d][:, 1536:3072], IN["wv_bf"][d * 128:(d + 1) * 128, :], q="sp")
        em.dma(w_sb[d][:, 3072:4608], IN["wg_bf"][d * 128:(d + 1) * 128, :], q="sp")
        em.dma(wgkb_sb[:, d * 44:(d + 1) * 44], IN["wgkb_bf"][d * 128:(d + 1) * 128, :], q="sp")

    pp = [em.ps(f"la_pp{i}", [128, 512]) for i in range(6)]
    ps_trf = em.ps("la_pstrf", [128, 128])
    identf = em.sb("la_identf", [12, 12], F32)
    em.copy(identf[:], ident[0:12, 0:12])

    with emscope(em):
        x_t = em.sb("la_xt", [128, 512], F32)
        xn_t = em.sb("la_xnt", [128, 1024], BF16)
        st = em.sb("la_st", [128, 16], F32)
        sqf2 = [em.sb(f"la_sqf{t}", [64, 512], F32) for t in range(2)]
        xnT = [em.sb(f"la_xnT{d}", [128, 512], BF16, tdest=True) for d in range(8)]
        preqk = [em.sb(f"la_preqk{t}", [64, 516], BF16) for t in range(2)]
        histqk = [em.sb(f"la_histqk{t}", [64, 4], BF16) for t in range(24)]
        prev = [em.sb(f"la_prev{t}", [128, 516], BF16) for t in range(2)]
        histv = [em.sb(f"la_histv{t}", [128, 4], BF16) for t in range(12)]
        qn = [em.sb(f"la_qn{t}", [64, 512], BF16) for t in range(12)]
        kn = [em.sb(f"la_kn{t}", [64, 512], BF16) for t in range(12)]
        vf = [em.sb(f"la_vf{t}", [128, 512], BF16) for t in range(2)]
        rowA = [em.sb(f"la_row{t}", [1, 512], F32) for t in range(2)]
        accv = [em.sb(f"la_accv{t}", [128, 512], BF16) for t in range(2)]
        accqk = [a[0:64, :] for a in accv]
        g_sc = em.sb("la_gsc", [64, 1536], BF16)
        gkb = em.sb("la_gkb", [12, 512], F32)
        beta_f = em.sb("la_betaf", [12, 512], F32)
        S = em.sb("la_S", [64, 1536], F32)
        Sbf = em.sb("la_Sbf", [64, 1536], BF16)
        em.memset(S[:], 0.0)
        em.memset(Sbf[:], 0.0)
        gam8 = em.sb("la_gam8", [12, 512], F32)
        b8 = em.sb("la_b8", [12, 512], F32)
        bb_sc = em.sb("la_bbsc", [12, 64], F32)
        flatg = em.sb("la_flatg", [1, 768], F32)
        tok_beta = em.sb("la_tbeta", [64, 96], F32)
        tok_bb = em.sb("la_tbb", [64, 96], F32)
        tok_b = em.sb("la_tb", [64, 96], F32)
        tok_kds = em.sb("la_tkds", [64, 96], F32)
        tok_bc = em.sb("la_tbc", [64, 96], F32)
        bcrow = em.sb("la_bcrow", [1, 12], F32)
        E0 = em.sb("la_E0", [64, 768], BF16)
        E0U = em.sb("la_E0U", [64, 768], BF16)
        Amat = em.sb("la_Amat", [64, 768], BF16)
        ATmat = em.sb("la_ATmat", [64, 768], BF16)
        WTmat = em.sb("la_WTmat", [64, 768], BF16)
        powP = [em.sb(f"la_powP{i}", [64, 768], BF16) for i in range(2)]
        powPT = [None] + [em.sb(f"la_powPT{i}", [64, 768], BF16) for i in range(1, 6)]
        y_t = em.sb("la_y", [64, 1536], BF16)
        grP = [em.sb(f"la_gr{t}", [64, 768], F32) for t in range(2)]
        y2_t = em.sb("la_y2", [64, 1536], BF16)
        Dl = em.sb("la_Dl", [64, 1536], BF16)
        OtP = [em.sb(f"la_O{t}", [64, 1536], BF16) for t in range(2)]
        ovarP = [em.sb(f"la_ovar{t}", [64, 16], F32) for t in range(2)]
        ktokG = em.sb("la_ktokG", [128, 768], BF16, tdest=True)
        vtokG = em.sb("la_vtokG", [128, 1536], BF16, tdest=True)
        ktokB = em.sb("la_ktokB", [64, 768], BF16)
        vtokB = em.sb("la_vtokB", [64, 1536], BF16)

        xn_s, a0_s = OUT["xn_s"], OUT["a0_s"]
        kn_s, vf_s, og_s = OUT["kn_s"], OUT["vf_s"], OUT["og_s"]

        def stretch(tile, h0, nh, rep):
            return tile[:, h0:h0 + nh].unsqueeze(2).broadcast_to((64, nh, rep))

        def rep_mask(msk, nh):
            return msk[:].unsqueeze(1).broadcast_to((64, nh, CS))

        def h3(ap2d, nh, blk):
            return ap2d.rearrange("p (h s) -> p h s", s=blk)

        for ch in range(n_chunks):
            tbase = ch * 512
            for tt in range(4):
                trow = tbase + tt * 128
                em.dma(x_t[:], IN["x"][trow:trow + 128, 0:512], q="sp")
                em.red(st[:, 0:1], x_t[:], ALU.add)
                em.act(sqf2[0][0:64, :], x_t[0:64, :], AF.Square, accum_out=st[0:64, 1:2])
                em.act(sqf2[0][0:64, :], x_t[64:128, :], AF.Square, accum_out=st[64:128, 1:2])
                em.dma(x_t[:], IN["x"][trow:trow + 128, 512:1024], q="sp")
                em.red(st[:, 9:10], x_t[:], ALU.add)
                em.act(sqf2[0][0:64, :], x_t[0:64, :], AF.Square, accum_out=st[0:64, 10:11])
                em.act(sqf2[0][0:64, :], x_t[64:128, :], AF.Square, accum_out=st[64:128, 10:11])
                em.tt(st[:, 0:1], st[:, 0:1], st[:, 9:10], ALU.add)
                em.tt(st[:, 1:2], st[:, 1:2], st[:, 10:11], ALU.add)
                em.ts(st[:, 2:3], st[:, 0:1], 1.0 / 1024, op0=ALU.mult)
                em.ts(st[:, 3:4], st[:, 1:2], 1.0 / 1024, op0=ALU.mult)
                em.ts(st[:, 4:5], st[:, 2:3], st[:, 2:3], op0=ALU.mult)
                em.tt(st[:, 5:6], st[:, 3:4], st[:, 4:5], ALU.subtract)
                em.act(st[:, 6:7], st[:, 5:6], AF.Ln, bias=EPS_LN)
                em.act(st[:, 7:8], st[:, 6:7], AF.Exp, scale=-0.5)
                em.ts(st[:, 8:9], st[:, 2:3], st[:, 7:8], s2=-1.0, op0=ALU.mult, op1=ALU.mult)
                em.act(xn_t[:, 512:1024], x_t[:], AF.Identity, scale=st[:, 7:8], bias=st[:, 8:9])
                em.dma(x_t[:], IN["x"][trow:trow + 128, 0:512], q="sp")
                em.act(xn_t[:, 0:512], x_t[:], AF.Identity, scale=st[:, 7:8], bias=st[:, 8:9])
                em.dma(xn_s[trow:trow + 128, :], xn_t[:], q="sp")
            for d in range(8):
                em.dmat(xnT[d][:], xn_s[tbase:tbase + 512, d * 128:(d + 1) * 128], q="sp")
            for hh in range(24):
                bank = pp[hh % 2]
                prt = preqk[hh % 2]
                for d in range(8):
                    em.mm(bank[0:64, :], w_sb[d][:, hh * 64:(hh + 1) * 64], xnT[d][:],
                          start=(d == 0), stop=(d == 7), inc=(1 if d == 7 else 0))
                if ch > 0:
                    em.copy(prt[:, 0:3], histqk[hh][:, 0:3], q="pool")
                else:
                    em.memset(prt[:, 0:3], 0.0, q="pool")
                em.copy(prt[:, 3:515], bank[0:64, :])
                em.copy(histqk[hh][:, 0:3], prt[:, 512:515], q="pool")
                acc = accqk[hh % 2]
                em.ts(acc[:, :], prt[:, 0:512], convqk[:, hh * 4:hh * 4 + 1],
                      op0=ALU.mult)
                for j in range(1, 4):
                    em.stt(acc[:, :], prt[:, j:j + 512], convqk[:, hh * 4 + j:hh * 4 + j + 1],
                           acc[:, :], ALU.mult, ALU.add)
                dst = qn[hh] if hh < 12 else kn[hh - 12]
                em.act(dst[:], acc[:, :], AF.Silu)
                sq = sqf2[hh % 2]
                em.act(sq[:], dst[:], AF.Square)
                pb = pp[4 + hh % 2]
                em.mm(pb[0:1, :], onescol[:], sq[:])
                em.act(rowA[hh % 2][:], pb[0:1, :], AF.Identity)
                em.dma(OUT["rsq_s"][hh:hh + 1, :], rowA[hh % 2][:], q="pool")
            for ft in range(12):
                bank = pp[ft % 2]
                prt = prev[ft % 2]
                for d in range(8):
                    em.mm(bank[:], w_sb[d][:, 1536 + ft * 128:1536 + (ft + 1) * 128], xnT[d][:],
                          start=(d == 0), stop=(d == 7), inc=(1 if d == 7 else 0))
                if ch > 0:
                    em.copy(prt[:, 0:3], histv[ft][:, 0:3], q="pool")
                else:
                    em.memset(prt[:, 0:3], 0.0, q="pool")
                em.copy(prt[:, 3:515], bank[:])
                em.copy(histv[ft][:, 0:3], prt[:, 512:515], q="pool")
                acc = accv[ft % 2]
                em.ts(acc[:], prt[:, 0:512], convv_all[:, ft * 4:ft * 4 + 1],
                      op0=ALU.mult)
                for j in range(1, 4):
                    em.stt(acc[:], prt[:, j:j + 512], convv_all[:, ft * 4 + j:ft * 4 + j + 1],
                           acc[:], ALU.mult, ALU.add)
                vt = vf[ft % 2]
                em.act(vt[:], acc[:], AF.Silu)
                em.dma(vf_s[ft * 128:(ft + 1) * 128, tbase:tbase + 512], vt[:], q="sp")
            em.dma(gkb[:], OUT["rsq_s"][0:12, :], q="pool")
            em.dma(beta_f[:], OUT["rsq_s"][12:24, :], q="pool")
            em.act(gkb[:], gkb[:], AF.Ln, bias=1e-6)
            em.act(gkb[:], gkb[:], AF.Exp, scale=-0.5, bias=LN8TH)
            em.act(beta_f[:], beta_f[:], AF.Ln, bias=1e-6)
            em.act(beta_f[:], beta_f[:], AF.Exp, scale=-0.5)
            em.dma(OUT["rsq_s"][0:12, :], gkb[:], q="pool")
            em.dma(OUT["rsq_s"][12:24, :], beta_f[:], q="pool")
            for hh in range(24):
                rw = rowA[hh % 2]
                em.dma(rw[:], OUT["rsq_s"][hh:hh + 1, :], q="sp")
                pb = pp[4 + hh % 2]
                em.mm(pb[0:64, 0:512], onesf[0:1, 0:64], rw[0:1, :])
                dst = qn[hh] if hh < 12 else kn[hh - 12]
                em.tt(dst[:], dst[:], pb[0:64, 0:512], ALU.mult)
                if hh >= 12:
                    em.dma(kn_s[(hh - 12) * 64:(hh - 11) * 64, tbase:tbase + 512], dst[:], q="sp")
            bank = pp[2]
            for d in range(8):
                em.mm(bank[0:44, :], wgkb_sb[:, d * 44:(d + 1) * 44], xnT[d][:],
                      start=(d == 0), stop=(d == 7), inc=(1 if d == 7 else 0))
            em.act(gkb[:], bank[0:12, :], AF.Abs, bias=dtb[:, 0:1])
            em.act(gkb[:], gkb[:], AF.Exp, scale=-1.0)
            em.act(gkb[:], gkb[:], AF.Ln, bias=1.0)
            em.act(beta_f[:], bank[0:12, :], AF.Identity, bias=dtb[:, 0:1])
            em.ts(beta_f[:], beta_f[:], 0.0, op0=ALU.max)
            em.tt(gkb[:], gkb[:], beta_f[:], ALU.add)
            em.ts(gkb[:], gkb[:], negA[:, 0:1], op0=ALU.mult)
            em.act(beta_f[:], bank[32:44, :], AF.Exp, scale=-1.0, bias=negbb[:, 0:1])
            em.act(beta_f[:], beta_f[:], AF.Ln, bias=1.0)
            em.act(beta_f[:], beta_f[:], AF.Exp, scale=-1.0)
            for sc in range(n_sub):
                c0 = sc * 64
                em.scan(gam8[:, c0:c0 + 64], gkb[:, c0:c0 + 64], gkb[:, c0:c0 + 64], 0.0,
                        ALU.add, ALU.bypass)
            em.act(b8[:, 0:n_sub * 64], gam8[:, 0:n_sub * 64], AF.Exp)
            for sc in range(n_sub):
                c0 = sc * 64
                em.dma(OUT["flat_s"][sc:sc + 1, :].rearrange("r (a b) -> (r a) b", a=12),
                       gam8[:, c0:c0 + 64], q="pool")
                em.dma(OUT["flat_s"][8 + sc:9 + sc, :].rearrange("r (a b) -> (r a) b", a=12),
                       beta_f[:, c0:c0 + 64], q="pool")
            for sc in range(n_sub):
                c0 = sc * 64
                em.ts(gkb[:, c0:c0 + 64], gam8[:, c0:c0 + 64], gam8[:, c0 + 63:c0 + 64],
                      op0=ALU.subtract)
                em.act(gkb[:, c0:c0 + 64], gkb[:, c0:c0 + 64], AF.Exp, scale=-1.0)
                em.tt(bb_sc[:], beta_f[:, c0:c0 + 64], b8[:, c0:c0 + 64], ALU.mult)
                for tsrc, tdst in ((beta_f[:, c0:c0 + 64], tok_beta), (bb_sc[:], tok_bb),
                                   (b8[:, c0:c0 + 64], tok_b), (gkb[:, c0:c0 + 64], tok_kds)):
                    em.tr(ps_trf[0:64, 0:12], tsrc, identf[0:12, 0:12])
                    em.copy(tdst[:, sc * 12:sc * 12 + 12], ps_trf[0:64, 0:12])
                em.tr(ps_trf[0:1, 0:12], b8[0:12, c0 + 63:c0 + 64], identf[0:12, 0:12])
                em.copy(bcrow[:], ps_trf[0:1, 0:12])
                em.mm(pp[5][0:64, 0:12], onesf[0:1, 0:64], bcrow[:])
                em.copy(tok_bc[:, sc * 12:sc * 12 + 12], pp[5][0:64, 0:12])
            for sc in range(n_sub):
                c0 = sc * 64
                grp = sc // 2
                if sc % 2 == 0:
                    trow = tbase + grp * 128
                    em.dmat(ktokG[:], kn_s[0:768, trow:trow + 128], q="sp")
                    em.dmat(vtokG[:], vf_s[0:1536, trow:trow + 128], q="sp")
                    em.dma(OUT["ktb_s"][:], ktokG[64:128, :], q="pool")
                    em.dma(OUT["vtb_s"][:], vtokG[64:128, :], q="pool")
                    em.dma(ktokB[:], OUT["ktb_s"][:], q="pool")
                    em.dma(vtokB[:], OUT["vtb_s"][:], q="pool")
                ktok = ktokG if sc % 2 == 0 else ktokB
                vtok = vtokG if sc % 2 == 0 else vtokB
                for fs in range(3):
                    gb = pp[3 + fs]
                    for d in range(8):
                        em.mm(gb[0:64, :], xnT[d][:, c0:c0 + 64],
                              w_sb[d][:, 3072 + fs * 512:3072 + (fs + 1) * 512],
                              start=(d == 0), stop=(d == 7), inc=(1 if d == 7 else 0))
                    em.copy(g_sc[:, fs * 512:(fs + 1) * 512], gb[0:64, :])
                em.dma(flatg[:], OUT["flat_s"][sc:sc + 1, :], q="pool")
                Ot, ovar, gr = OtP[sc % 2], ovarP[sc % 2], grP[sc % 2]
                beta_tok = tok_beta[:, sc * 12:sc * 12 + 12]
                bb_tok = tok_bb[:, sc * 12:sc * 12 + 12]
                b_tok = tok_b[:, sc * 12:sc * 12 + 12]
                kds_tok = tok_kds[:, sc * 12:sc * 12 + 12]
                bc_tok = tok_bc[:, sc * 12:sc * 12 + 12]

                def stream(g):
                    bx, bk, bs = pp[3 * g], pp[3 * g + 1], pp[3 * g + 2]
                    h0 = 6 * g
                    m384 = slice(384 * g, 384 * g + 384)
                    m768 = slice(768 * g, 768 * g + 768)
                    T2 = []

                    def xmms():
                        for hl in range(6):
                            h = h0 + hl
                            em.mm(bx[0:64, hl * 64:hl * 64 + 64], flatg[0:1, 64 * h:64 * h + 64],
                                  negones64[0:1, 0:64], start=(hl == 0), stop=False, inc=0)
                            em.mm(bx[0:64, hl * 64:hl * 64 + 64], onesf[0:1, 0:64],
                                  flatg[0:1, 64 * h:64 * h + 64], start=False,
                                  stop=(hl == 5), inc=(1 if hl == 5 else 0))
                    T2.append(xmms)

                    def e0b():
                        em.tt(h3(bs[0:64, 0:384], 6, 64), h3(bx[0:64, 0:384], 6, 64),
                              rep_mask(c_bigL, 6), ALU.add)
                        em.act(E0[:, m384], bs[0:64, 0:384], AF.Exp, scale=-1.0)
                    T2.append(e0b)

                    def e0ub():
                        em.tt(h3(bs[0:64, 0:384], 6, 64), h3(bx[0:64, 0:384], 6, 64),
                              rep_mask(c_bigUi, 6), ALU.add)
                        em.act(E0U[:, m384], bs[0:64, 0:384], AF.Exp)
                    T2.append(e0ub)

                    def kktm():
                        for hl in range(6):
                            kf = kn[h0 + hl][:, c0:c0 + 64]
                            em.mm(bk[0:64, hl * 64:hl * 64 + 64], kf, kf, inc=(1 if hl == 5 else 0))
                    T2.append(kktm)

                    def abuild():
                        em.tt(Amat[:, m384], E0[:, m384], bk[0:64, 0:384], ALU.mult)
                        em.tt(h3(Amat[:, m384], 6, 64), h3(Amat[:, m384], 6, 64),
                              stretch(beta_tok, h0, 6, 64), ALU.mult)
                        em.tt(h3(Amat[:, m384], 6, 64), h3(Amat[:, m384], 6, 64),
                              rep_mask(c_ms, 6), ALU.mult)
                    T2.append(abuild)

                    def eatb():
                        em.tt(h3(bs[0:64, 0:384], 6, 64), h3(bx[0:64, 0:384], 6, 64),
                              rep_mask(c_bigUs, 6), ALU.add)
                        em.act(E0[:, m384], bs[0:64, 0:384], AF.Exp)
                    T2.append(eatb)

                    def betab():
                        if g == 0:
                            em.dma(flatg[:], OUT["flat_s"][8 + sc:9 + sc, :], q="pool")
                        for hl in range(6):
                            h = h0 + hl
                            em.mm(bx[0:64, hl * 64:hl * 64 + 64], onesf[0:1, 0:64],
                                  flatg[0:1, 64 * h:64 * h + 64], inc=(1 if hl == 5 else 0))
                    T2.append(betab)

                    def atb():
                        em.tt(ATmat[:, m384], E0[:, m384], bk[0:64, 0:384], ALU.mult)
                        em.tt(ATmat[:, m384], ATmat[:, m384], bx[0:64, 0:384], ALU.mult)
                    T2.append(atb)

                    def kqtm():
                        for hl in range(6):
                            h = h0 + hl
                            em.mm(bk[0:64, hl * 64:hl * 64 + 64], kn[h][:, c0:c0 + 64],
                                  qn[h][:, c0:c0 + 64], inc=(1 if hl == 5 else 0))
                    T2.append(kqtm)

                    def wtb():
                        em.tt(WTmat[:, m384], E0U[:, m384], bk[0:64, 0:384], ALU.mult)
                    T2.append(wtb)

                    def rmms():
                        for hl in range(6):
                            h = h0 + hl
                            b_, cs_ = (bx, hl * 128) if hl < 4 else (bk, (hl - 4) * 128)
                            em.mm(b_[0:64, cs_:cs_ + 128], kn[h][:, c0:c0 + 64],
                                  Sbf[:, h * 128:(h + 1) * 128], inc=(1 if hl in (3, 5) else 0))
                    T2.append(rmms)

                    def rbuild():
                        em.tt(h3(y_t[:, m768], 6, 128), h3(vtok[0:64, m768], 6, 128),
                              stretch(beta_tok, h0, 6, 128), ALU.mult)
                        em.tt(h3(y2_t[:, 768 * g:768 * g + 512], 4, 128), h3(bx[0:64, 0:512], 4, 128),
                              stretch(bb_tok, h0, 4, 128), ALU.mult)
                        em.tt(h3(y2_t[:, 768 * g + 512:768 * g + 768], 2, 128), h3(bk[0:64, 0:256], 2, 128),
                              stretch(bb_tok, h0 + 4, 2, 128), ALU.mult)
                        em.tt(y_t[:, m768], y_t[:, m768], y2_t[:, m768], ALU.subtract)
                    T2.append(rbuild)

                    def powcopy():
                        em.copy(powP[0][:, m384], Amat[:, m384], q="pool")
                    T2.append(powcopy)

                    for lv in range(1, 6):
                        def sq(lv=lv):
                            pprev = powP[(lv - 1) % 2]
                            ptprev = ATmat if lv == 1 else powPT[lv - 1]
                            pcur, ptcur = powP[lv % 2], powPT[lv]
                            for hl in range(6):
                                hb = slice((h0 + hl) * 64, (h0 + hl) * 64 + 64)
                                em.mm(bx[0:64, hl * 64:hl * 64 + 64], ptprev[:, hb], pprev[:, hb],
                                      inc=(1 if hl == 5 else 0))
                            for hl in range(6):
                                hb = slice((h0 + hl) * 64, (h0 + hl) * 64 + 64)
                                em.mm(bk[0:64, hl * 64:hl * 64 + 64], pprev[:, hb], ptprev[:, hb],
                                      inc=(1 if hl == 5 else 0))
                            em.act(pcur[:, m384], bx[0:64, 0:384], AF.Identity)
                            em.act(ptcur[:, m384], bk[0:64, 0:384], AF.Identity)
                        T2.append(sq)

                    for lv in range(5, 0, -1):
                        def app(lv=lv):
                            pt = powPT[lv]
                            for hl in range(6):
                                h = h0 + hl
                                b_, cs_ = (bx, hl * 128) if hl < 4 else (bk, (hl - 4) * 128)
                                em.mm(b_[0:64, cs_:cs_ + 128], pt[:, h * 64:h * 64 + 64],
                                      y_t[:, h * 128:(h + 1) * 128], inc=(1 if hl in (3, 5) else 0))
                            em.tt(y_t[:, 768 * g:768 * g + 512], y_t[:, 768 * g:768 * g + 512],
                                  bx[0:64, 0:512], ALU.add)
                            em.tt(y_t[:, 768 * g + 512:768 * g + 768], y_t[:, 768 * g + 512:768 * g + 768],
                                  bk[0:64, 0:256], ALU.add)
                        T2.append(app)

                    def dlb():
                        for hl in range(6):
                            h = h0 + hl
                            b_, cs_ = (bx, hl * 128) if hl < 4 else (bk, (hl - 4) * 128)
                            em.mm(b_[0:64, cs_:cs_ + 128], ATmat[:, h * 64:h * 64 + 64],
                                  y_t[:, h * 128:(h + 1) * 128], inc=(1 if hl in (3, 5) else 0))
                        em.tt(Dl[:, 768 * g:768 * g + 512], y_t[:, 768 * g:768 * g + 512],
                              bx[0:64, 0:512], ALU.subtract)
                        em.tt(Dl[:, 768 * g + 512:768 * g + 768], y_t[:, 768 * g + 512:768 * g + 768],
                              bk[0:64, 0:256], ALU.subtract)
                    T2.append(dlb)

                    def omms():
                        for hl in range(6):
                            h = h0 + hl
                            b_, cs_ = (bx, hl * 128) if hl < 4 else (bk, (hl - 4) * 128)
                            em.mm(b_[0:64, cs_:cs_ + 128], qn[h][:, c0:c0 + 64],
                                  Sbf[:, h * 128:(h + 1) * 128], inc=(1 if hl in (3, 5) else 0))
                        em.tt(h3(Ot[:, 768 * g:768 * g + 512], 4, 128), h3(bx[0:64, 0:512], 4, 128),
                              stretch(b_tok, h0, 4, 128), ALU.mult)
                        em.tt(h3(Ot[:, 768 * g + 512:768 * g + 768], 2, 128), h3(bk[0:64, 0:256], 2, 128),
                              stretch(b_tok, h0 + 4, 2, 128), ALU.mult)
                    T2.append(omms)

                    def wdl():
                        for hl in range(6):
                            h = h0 + hl
                            b_, cs_ = (bx, hl * 128) if hl < 4 else (bk, (hl - 4) * 128)
                            em.mm(b_[0:64, cs_:cs_ + 128], WTmat[:, h * 64:h * 64 + 64],
                                  Dl[:, h * 128:(h + 1) * 128], inc=(1 if hl in (3, 5) else 0))
                        em.tt(Ot[:, 768 * g:768 * g + 512], Ot[:, 768 * g:768 * g + 512],
                              bx[0:64, 0:512], ALU.add)
                        em.tt(Ot[:, 768 * g + 512:768 * g + 768], Ot[:, 768 * g + 512:768 * g + 768],
                              bk[0:64, 0:256], ALU.add)
                    T2.append(wdl)

                    def supd():
                        em.tt(h3(y_t[:, 384 * g:384 * g + 384], 6, 64), h3(ktok[0:64, m384], 6, 64),
                              stretch(kds_tok, h0, 6, 64), ALU.mult)
                        for hl in range(6):
                            h = h0 + hl
                            b_, cs_ = (bx, hl * 128) if hl < 4 else (bk, (hl - 4) * 128)
                            em.mm(b_[0:64, cs_:cs_ + 128], y_t[:, 384 * g + hl * 64:384 * g + hl * 64 + 64],
                                  Dl[:, h * 128:(h + 1) * 128], inc=(1 if hl in (3, 5) else 0))
                        em.tt(h3(S[:, m768], 6, 128), h3(S[:, m768], 6, 128),
                              stretch(bc_tok, h0, 6, 128), ALU.mult)
                        em.tt(S[:, 768 * g:768 * g + 512], S[:, 768 * g:768 * g + 512],
                              bx[0:64, 0:512], ALU.add)
                        em.tt(S[:, 768 * g + 512:768 * g + 768], S[:, 768 * g + 512:768 * g + 768],
                              bk[0:64, 0:256], ALU.add)
                        em.copy(Sbf[:, m768], S[:, m768])
                    T2.append(supd)

                    def onorm():
                        em.act(gr[:], Ot[:, m768], AF.Square)
                        for hl in range(6):
                            h = h0 + hl
                            em.red(ovar[:, h:h + 1], gr[:, hl * 128:(hl + 1) * 128], ALU.add)
                        em.ts(ovar[:, h0:h0 + 6], ovar[:, h0:h0 + 6], 1.0 / 128, op0=ALU.mult)
                        em.act(ovar[:, h0:h0 + 6], ovar[:, h0:h0 + 6], AF.Ln, bias=1e-5)
                        em.act(ovar[:, h0:h0 + 6], ovar[:, h0:h0 + 6], AF.Exp, scale=-0.5)
                        em.tt(h3(Ot[:, m768], 6, 128), h3(Ot[:, m768], 6, 128),
                              ovar[:, h0:h0 + 6].unsqueeze(2).broadcast_to((64, 6, 128)), ALU.mult)
                        em.tt(Ot[:, m768], Ot[:, m768], gnwmat[:, m768], ALU.mult)
                    T2.append(onorm)

                    def gate():
                        em.tt(Ot[:, m768], Ot[:, m768], g_sc[:, m768], ALU.mult)
                        em.act(gr[:], g_sc[:, m768], AF.Exp, scale=-1.0)
                        em.act(gr[:], gr[:], AF.Ln, bias=1.0)
                        em.act(gr[:], gr[:], AF.Exp, scale=-1.0)
                        em.tt(Ot[:, m768], Ot[:, m768], gr[:], ALU.mult)
                        em.dma(og_s[tbase + c0:tbase + c0 + 64, 768 * g:768 * g + 768],
                               Ot[:, m768], q="sp")
                    T2.append(gate)
                    return T2

                s1, s2 = stream(0), stream(1)
                for a, b in zip(s1, s2):
                    a()
                    b()
                if ch == 0 and sc == 0 and "d_gam" in OUT:
                    em.dma(OUT["d_gam"], gam8[:, 0:64], q="sp")
                    em.dma(OUT["d_Amat"], Amat[:], q="sp")
                    em.dma(OUT["d_ATmat"], ATmat[:], q="sp")
                    em.dma(OUT["d_WTmat"], WTmat[:], q="sp")
                    em.dma(OUT["d_Dl"], Dl[:], q="sp")
                    em.dma(OUT["d_Ot"], Ot[:], q="sp")
                    em.dma(OUT["d_S"], S[:], q="sp")
    with emscope(em):
        ogT = [em.sb(f"la_ogT{v}", [128, 512], BF16, tdest=True) for v in range(12)]
        a0t = em.sb("la_a0t", [128, 1024], BF16)
        wo_sb = [em.sb(f"la_wo{v}", [128, 1024], BF16) for v in range(12)]
        for v in range(12):
            em.dma(wo_sb[v][:], IN["wo_bf"][v * 128:(v + 1) * 128, :], q="sp")
        for gg in range(n_chunks):
            t0 = gg * 512
            for v in range(12):
                em.dmat(ogT[v][:], og_s[t0:t0 + 512, v * 128:(v + 1) * 128], q="sp")
            for tt in range(4):
                for oc in range(2):
                    bank = pp[oc]
                    for v in range(12):
                        em.mm(bank[:], ogT[v][:, tt * 128:(tt + 1) * 128],
                              wo_sb[v][:, oc * 512:(oc + 1) * 512],
                              start=(v == 0), stop=(v == 11), inc=(1 if v == 11 else 0))
                    em.copy(a0t[:, oc * 512:(oc + 1) * 512], bank[:])
                em.dma(a0_s[t0 + tt * 128:t0 + tt * 128 + 128, :], a0t[:], q="sp")


L0A_INS = dict(
    x=([T, D], F32),
    wq_bf=([1024, 768], BF16), wk_bf=([1024, 768], BF16), wv_bf=([1024, 1536], BF16),
    wg_bf=([1024, 1536], BF16), wo_bf=([1536, 1024], BF16), wgkb_bf=([1024, 44], BF16),
    L0_attn_conv_q=([768, 4], F32), L0_attn_conv_k=([768, 4], F32),
    L0_attn_conv_v=([1536, 4], F32),
    L0_attn_A_log=([12], F32), L0_attn_dt_bias=([12], F32), L0_attn_b_b=([12], F32),
    L0_attn_g_norm_weight=([128], F32),
    c_ident=([128, 128], BF16), c_ms=([CS, CS], BF16), c_bigL=([CS, CS], F32),
    c_bigUi=([CS, CS], F32), c_bigUs=([CS, CS], F32),
)
L0A_OUTS = dict(xn_s=([T, D], BF16), a0_s=([T, D], BF16),
                kn_s=([768, T], BF16), vf_s=([1536, T], BF16), og_s=([T, 1536], BF16),
                flat_s=([16, 768], F32), rsq_s=([24, 512], F32),
                ktb_s=([64, 768], BF16), vtb_s=([64, 1536], BF16))
L0A_DBG = dict(d_gam=([12, 64], F32),
               d_Amat=([64, 768], BF16), d_ATmat=([64, 768], BF16), d_WTmat=([64, 768], BF16),
               d_Dl=([64, 1536], BF16), d_Ot=([64, 1536], BF16), d_S=([64, 1536], F32))



def emit_l1(em, IN, OUT, n_chunks=8):
    nc = em.nc
    ident = em.sb("l1_ident", [128, 128], BF16)
    em.dma(ident[:], IN["c_ident"], q="sp")
    onesb = em.sb("l1_onesb", [1, 128], BF16)
    em.dma(onesb[:], IN["c_ones"], q="sp")
    onescol = em.sb("l1_onescol", [128, 1], BF16)
    em.memset(onescol[:], 1.0)
    pp = [em.ps(f"l1_pp{i}", [128, 512]) for i in range(6)]
    ps_tr = em.ps("l1_pstr", [128, 128], BF16)

    x_t = em.sb("l1_xt", [128, 1024], BF16)
    yn_t = em.sb("l1_ynt", [128, 1024], BF16)
    st = em.sb("l1_st", [128, 32], F32)
    sqf = em.sb("l1_sqf", [64, 512], F32)
    fA = em.sb("l1_fA", [128, 1024], F32)
    fB = em.sb("l1_fB", [128, 1024], F32)
    fC = em.sb("l1_fC", [128, 1024], F32)
    fD = em.sb("l1_fD", [128, 1024], F32)
    iA = em.sb("l1_iA", [128, 1024], I32)
    ynT = [em.sb(f"l1_ynT{d}", [128, 512], BF16, tdest=True) for d in range(8)]
    yn_s, a1_s = OUT["yn_s"], OUT["a1_s"]
    sc_s, xq_s = OUT["sc_s"], OUT["xq_s"]

    for tt in range(32):
        trow = tt * 128
        em.dma(x_t[:], IN["x1_s"][trow:trow + 128, :], q="sp")
        em.red(st[:, 0:1], x_t[:], ALU.add)
        em.act(sqf[0:64, :], x_t[0:64, 0:512], AF.Square, accum_out=st[0:64, 1:2])
        em.act(sqf[0:64, :], x_t[64:128, 0:512], AF.Square, accum_out=st[64:128, 1:2])
        em.act(sqf[0:64, :], x_t[0:64, 512:1024], AF.Square, accum_out=st[0:64, 2:3])
        em.act(sqf[0:64, :], x_t[64:128, 512:1024], AF.Square, accum_out=st[64:128, 2:3])
        em.tt(st[:, 1:2], st[:, 1:2], st[:, 2:3], ALU.add)
        em.ts(st[:, 3:4], st[:, 0:1], 1.0 / 1024, op0=ALU.mult)
        em.ts(st[:, 4:5], st[:, 1:2], 1.0 / 1024, op0=ALU.mult)
        em.ts(st[:, 5:6], st[:, 3:4], st[:, 3:4], op0=ALU.mult)
        em.tt(st[:, 6:7], st[:, 4:5], st[:, 5:6], ALU.subtract)
        em.act(st[:, 7:8], st[:, 6:7], AF.Ln, bias=EPS_LN)
        em.act(st[:, 8:9], st[:, 7:8], AF.Exp, scale=-0.5)
        em.ts(st[:, 9:10], st[:, 3:4], st[:, 8:9], s2=-1.0, op0=ALU.mult, op1=ALU.mult)
        em.act(yn_t[:], x_t[:], AF.Identity, scale=st[:, 8:9], bias=st[:, 9:10])
        em.dma(yn_s[trow:trow + 128, :], yn_t[:], q="sp")

    with emscope(em):
        wk_sb = [em.sb(f"l1_wk{d}", [128, 1024], BF16) for d in range(8)]
        wv_sb = [em.sb(f"l1_wv{d}", [128, 1024], BF16) for d in range(8)]
        for d in range(8):
            em.dma(wk_sb[d][:], IN["wk1_bf"][d * 128:(d + 1) * 128, :], q="sp")
            em.dma(wv_sb[d][:], IN["wv1_bf"][d * 128:(d + 1) * 128, :], q="sp")
        bkrow = em.sb("l1_bkrow", [1, 1024], BF16)
        em.dma(bkrow[:], IN["L1_attn_bk"].rearrange("(r c) -> r c", r=1), q="pool")
        bvrow = em.sb("l1_bvrow", [1, 1024], BF16)
        em.dma(bvrow[:], IN["L1_attn_bv"].rearrange("(r c) -> r c", r=1), q="pool")
        k_t = em.sb("l1_kt", [128, 1024], BF16)
        v_t = em.sb("l1_vt", [128, 1024], BF16)
        for tt in range(32):
            trow = tt * 128
            for d in range(8):
                em.dmat(ynT[d][:, 0:128], yn_s[trow:trow + 128, d * 128:(d + 1) * 128], q="sp")
            for half, (wsb, brow, dst, do_elu) in enumerate(
                    [(wk_sb, bkrow, k_t, True), (wv_sb, bvrow, v_t, False)]):
                for oc in range(2):
                    bank = pp[4 + oc]
                    for d in range(8):
                        em.mm(bank[:], ynT[d][:, 0:128], wsb[d][:, oc * 512:(oc + 1) * 512],
                              start=(d == 0), stop=False, inc=0)
                    em.mm(bank[:], onesb[0:1, 0:128], brow[0:1, oc * 512:(oc + 1) * 512],
                          start=False, stop=True)
                    sl_ = slice(oc * 512, (oc + 1) * 512)
                    if do_elu:
                        em.ts(fA[:, 0:512], bank[:], 0.0, op0=ALU.min)
                        em.act(fB[:, 0:512], fA[:, 0:512], AF.Exp)
                        em.stt(dst[:, sl_], bank[:], 0.0, fB[:, 0:512], ALU.max, ALU.add)
                    else:
                        em.copy(dst[:, sl_], bank[:])
            for h in range(16):
                b_ = pp[0] if h < 8 else pp[1]
                cs_ = (h % 8) * 64
                em.mm(b_[0:64, cs_:cs_ + 64], k_t[:, h * 64:(h + 1) * 64],
                      v_t[:, h * 64:(h + 1) * 64], start=(tt == 0), stop=(tt == 31),
                      inc=(1 if (tt == 31 and h in (7, 15)) else 0))
            for oc in range(2):
                em.mm(pp[2 + 0][0:1, 0:512] if oc == 0 else pp[3][0:1, 0:512],
                      onescol[:, 0:1], k_t[:, oc * 512:(oc + 1) * 512],
                      start=(tt == 0), stop=(tt == 31),
                      inc=(1 if tt == 31 else 0))
        kv_sb = em.sb("l1_kvsb", [64, 1024], BF16)
        em.copy(kv_sb[:, 0:512], pp[0][0:64, :])
        em.copy(kv_sb[:, 512:1024], pp[1][0:64, :])
        ks_row = em.sb("l1_ksrow", [1, 1024], BF16)
        em.copy(ks_row[0:1, 0:512], pp[2][0:1, 0:512])
        em.copy(ks_row[0:1, 512:1024], pp[3][0:1, 0:512])
        ks_col = em.sb("l1_kscol", [64, 16], BF16)
        for h in range(16):
            em.tr(ps_tr[0:64, 0:1], ks_row[0:1, h * 64:(h + 1) * 64], ident[0:1, 0:1])
            em.copy(ks_col[:, h:h + 1], ps_tr[0:64, 0:1])

        wq_sb = [em.sb(f"l1_wq{d}", [128, 1024], BF16) for d in range(8)]
        for d in range(8):
            em.dma(wq_sb[d][:], IN["wq1_bf"][d * 128:(d + 1) * 128, :], q="sp")
        bq_col = em.sb("l1_bqcol", [64, 16], F32)
        for hh in range(16):
            em.dma(bq_col[:, hh:hh + 1],
                   IN["L1_attn_bq"][hh * 64:(hh + 1) * 64].rearrange("(p c) -> p c", c=1), q="sp")
        qh = [em.sb(f"l1_qh{h}", [64, 512], BF16) for h in range(16)]
        zt = em.sb("l1_zt", [128, 16], F32)
        sc_t = em.sb("l1_sct", [128, 1024], BF16)
        for chk in range(n_chunks):
            tbase = chk * 512
            for d in range(8):
                em.dmat(ynT[d][:], yn_s[tbase:tbase + 512, d * 128:(d + 1) * 128], q="sp")
            for h in range(16):
                bank = pp[4 + h % 2]
                for d in range(8):
                    em.mm(bank[0:64, :], wq_sb[d][:, h * 64:(h + 1) * 64], ynT[d][:],
                          start=(d == 0), stop=(d == 7), inc=(1 if d == 7 else 0))
                em.act(fA[0:64, 0:512], bank[0:64, :], AF.Identity,
                       bias=bq_col[:, h:h + 1])
                em.ts(fB[0:64, 0:512], fA[0:64, 0:512], 0.0, op0=ALU.min)
                em.act(fC[0:64, 0:512], fB[0:64, 0:512], AF.Exp)
                em.stt(qh[h][:], fA[0:64, 0:512], 0.0, fC[0:64, 0:512], ALU.max, ALU.add)
            for ts4 in range(4):
                t0 = ts4 * 128
                for h in range(16):
                    em.mm(pp[2][:, h:h + 1], qh[h][:, t0:t0 + 128], ks_col[:, h:h + 1],
                          inc=(1 if h == 15 else 0))
                em.ts(zt[:], pp[2][:, 0:16], 1e-6, op0=ALU.add)
                em.recip(zt[:], zt[:])
                for h in range(16):
                    b_ = pp[0] if h < 8 else pp[1]
                    cs_ = (h % 8) * 64
                    em.mm(b_[:, cs_:cs_ + 64], qh[h][:, t0:t0 + 128],
                          kv_sb[:, h * 64:(h + 1) * 64], inc=(1 if h in (7, 15) else 0))
                for bi_, b_ in enumerate((pp[0], pp[1])):
                    em.tt(sc_t[:, bi_ * 512:(bi_ + 1) * 512].rearrange("p (h s) -> p h s", s=64),
                          b_[:, 0:512].rearrange("p (h s) -> p h s", s=64),
                          zt[:, bi_ * 8:(bi_ + 1) * 8].unsqueeze(2).broadcast_to((128, 8, 64)),
                          ALU.mult)
                em.dma(sc_s[tbase + t0:tbase + t0 + 128, :], sc_t[:], q="sp")

    with emscope(em):
        wo_sb = [em.sb(f"l1_wo{d}", [128, 1024], BF16) for d in range(8)]
        for d in range(8):
            em.dma(wo_sb[d][:], IN["wo1_bf"][d * 128:(d + 1) * 128, :], q="sp")
        borow = em.sb("l1_borow", [1, 1024], BF16)
        em.dma(borow[:], IN["L1_attn_bo"].rearrange("(r c) -> r c", r=1), q="pool")
        scT = [em.sb(f"l1_scT{d}", [128, 512], BF16, tdest=True) for d in range(8)]
        a1_t = em.sb("l1_a1t", [128, 1024], BF16)
        for chk in range(n_chunks):
            tbase = chk * 512
            for d in range(8):
                em.dmat(scT[d][:], sc_s[tbase:tbase + 512, d * 128:(d + 1) * 128], q="sp")
            for tt in range(4):
                for oc in range(2):
                    bank = pp[4 + oc]
                    for d in range(8):
                        em.mm(bank[:], scT[d][:, tt * 128:(tt + 1) * 128],
                              wo_sb[d][:, oc * 512:(oc + 1) * 512],
                              start=(d == 0), stop=False, inc=0)
                    em.mm(bank[:], onesb[0:1, 0:128], borow[0:1, oc * 512:(oc + 1) * 512],
                          start=False, stop=True)
                    em.copy(a1_t[:, oc * 512:(oc + 1) * 512], bank[:])
                em.dma(a1_s[tbase + tt * 128:tbase + tt * 128 + 128, :], a1_t[:], q="sp")

    with emscope(em):
        h_sb = [em.sb(f"l1_h{d}", [128, 1024], BF16) for d in range(8)]
        for d in range(8):
            em.dma(h_sb[d][:], IN["c_had"][d * 128:(d + 1) * 128, :], q="sp")
        xq_t = em.sb("l1_xqt", [128, 1024], BF16)
        for chk in range(n_chunks):
            tbase = chk * 512
            for d in range(8):
                em.dmat(ynT[d][:], yn_s[tbase:tbase + 512, d * 128:(d + 1) * 128], q="sp")
            for tt in range(4):
                for oc in range(2):
                    bank = pp[4 + oc]
                    for d in range(8):
                        em.mm(bank[:], ynT[d][:, tt * 128:(tt + 1) * 128],
                              h_sb[d][:, oc * 512:(oc + 1) * 512],
                              start=(d == 0), stop=(d == 7), inc=(1 if d == 7 else 0))
                    em.copy(fA[:, oc * 512:(oc + 1) * 512], bank[:])
                em.red(st[:, 0:1], fA[:], ALU.max, absval=True)
                em.ts(st[:, 1:2], st[:, 0:1], 1e-5, op0=ALU.max)
                em.ts(st[:, 2:3], st[:, 1:2], 1.0 / 7.0, op0=ALU.mult)
                em.recip(st[:, 3:4], st[:, 2:3])
                em.ts(fB[:], fA[:], st[:, 3:4], op0=ALU.mult)
                em.copy(iA[:], fB[:])
                em.copy(fC[:], iA[:])
                e = iA[:].bitcast(F32)
                em.tt(e, fB[:], fC[:], ALU.subtract)
                em.ts(fD[:], e, 0.5, op0=ALU.is_ge)
                em.stt(fD[:], e, -0.5, fD[:], ALU.is_lt, ALU.subtract)
                em.tt(fC[:], fC[:], fD[:], ALU.subtract)
                em.ts(xq_t[:], fC[:], 7.0, -8.0, op0=ALU.min, op1=ALU.max)
                trow = tbase + tt * 128
                em.dma(xq_s[trow:trow + 128, :], xq_t[:], q="sp")
                em.dma(OUT["s1q_s"][trow:trow + 128, :], st[:, 2:3], q="sp")

    with emscope(em):
        wf_sb = [em.sb(f"l1_wf{d}", [128, 1024], BF16, tdest=True) for d in range(8)]
        for d in range(8):
            em.dmat(wf_sb[d][:], IN["wf1q_rows"][:, d * 128:(d + 1) * 128], q="sp")
        xqT = [em.sb(f"l1_xqT{d}", [128, 512], BF16, tdest=True) for d in range(8)]
        x1_t = em.sb("l1_x1t", [128, 1024], BF16)
        a1r_t = em.sb("l1_a1r", [128, 1024], BF16)
        sq_c = em.sb("l1_sqc", [128, 1], F32)
        out_t = em.sb("l1_outt", [128, 1024], F32)
        for chk in range(n_chunks):
            tbase = chk * 512
            for d in range(8):
                em.dmat(xqT[d][:], xq_s[tbase:tbase + 512, d * 128:(d + 1) * 128], q="sp")
            for tt in range(4):
                trow = tbase + tt * 128
                em.dma(x1_t[:], IN["x1_s"][trow:trow + 128, :], q="sp")
                em.dma(a1r_t[:], a1_s[trow:trow + 128, :], q="sp")
                em.dma(sq_c[:], OUT["s1q_s"][trow:trow + 128, :], q="sp")
                for oc in range(2):
                    bank = pp[4 + oc]
                    for d in range(8):
                        em.mm(bank[:], xqT[d][:, tt * 128:(tt + 1) * 128],
                              wf_sb[d][:, oc * 512:(oc + 1) * 512],
                              start=(d == 0), stop=(d == 7), inc=(1 if d == 7 else 0))
                    sl_ = slice(oc * 512, (oc + 1) * 512)
                    em.stt(fA[:, 0:512], bank[:], sq_c[:, 0:1], x1_t[:, sl_], ALU.mult, ALU.add)
                    em.tt(out_t[:, sl_], fA[:, 0:512], a1r_t[:, sl_], ALU.add)
                em.dma(OUT["out"][trow:trow + 128, :], out_t[:], q="sp")


L1_INS = dict(
    x1_s=([T, D], BF16),
    wq1_bf=([1024, 1024], BF16), wk1_bf=([1024, 1024], BF16),
    wv1_bf=([1024, 1024], BF16), wo1_bf=([1024, 1024], BF16),
    L1_attn_bq=([1024], F32), L1_attn_bk=([1024], F32),
    L1_attn_bv=([1024], F32), L1_attn_bo=([1024], F32),
    wf1q_rows=([1024, 1024], BF16),
    c_ident=([128, 128], BF16), c_ones=([1, 128], BF16), c_had=([1024, 1024], BF16),
)
L1_OUTS = dict(out=([T, D], F32), yn_s=([T, D], BF16), a1_s=([T, D], BF16),
               sc_s=([T, D], BF16), xq_s=([T, D], BF16), s1q_s=([T, 1], F32))



_CACHE = {}


def _phase(name, emit_fn, ins_spec, outs_spec):
    if name not in _CACHE:
        _CACHE[name] = build_phase(emit_fn, ins_spec, outs_spec)
    return _CACHE[name]


def kernel_multilaunch(**inputs):
    inputs = {k: np.ascontiguousarray(v) for k, v in inputs.items()}
    consts = host_consts()
    NCORE = 8
    cores = list(range(NCORE))

    prep_nc = _phase("prep", emit_prep, {n: (INPUT_SHAPES[n], F32) for n in PREP_INS}, PREP_OUTS)
    prep_in = {n: inputs[n].astype(np.float32) for n in PREP_INS}
    prep_res = run_bass_kernel_spmd(prep_nc, [prep_in] * NCORE, core_ids=cores)
    pw = prep_res.results

    l0a_nc = _phase("l0a", emit_l0a, L0A_INS, L0A_OUTS)
    l0a_maps = []
    for c in cores:
        m = dict(x=inputs["L0_attn_hidden_states"][c].astype(np.float32),
                 wq_bf=pw[c]["wq_bf"], wk_bf=pw[c]["wk_bf"], wv_bf=pw[c]["wv_bf"],
                 wg_bf=pw[c]["wg_bf"], wo_bf=pw[c]["wo_bf"], wgkb_bf=pw[c]["wgkb_bf"],
                 L0_attn_conv_q=inputs["L0_attn_conv_q"], L0_attn_conv_k=inputs["L0_attn_conv_k"],
                 L0_attn_conv_v=inputs["L0_attn_conv_v"],
                 L0_attn_A_log=inputs["L0_attn_A_log"], L0_attn_dt_bias=inputs["L0_attn_dt_bias"],
                 L0_attn_b_b=inputs["L0_attn_b_b"],
                 L0_attn_g_norm_weight=inputs["L0_attn_g_norm_weight"],
                 c_ident=consts["c_ident"], c_ms=consts["c_ms"], c_bigL=consts["c_bigL"],
                 c_bigUi=consts["c_bigUi"], c_bigUs=consts["c_bigUs"])
        l0a_maps.append(m)
    l0a_res = run_bass_kernel_spmd(l0a_nc, l0a_maps, core_ids=cores).results

    l0f_nc = _phase("l0f", emit_l0f, L0F_INS, L0F_OUTS)
    l0f_maps = []
    for c in cores:
        m = dict(x=inputs["L0_attn_hidden_states"][c].astype(np.float32),
                 xn_s=l0a_res[c]["xn_s"], a0_s=l0a_res[c]["a0_s"],
                 w1q_rows=pw[c]["w1q_rows"], w2q_rows=pw[c]["w2q_rows"],
                 L0_ffn_ln1_w=inputs["L0_ffn_ln1_w"], L0_ffn_ln1_b=inputs["L0_ffn_ln1_b"],
                 L0_ffn_ln2_w=inputs["L0_ffn_ln2_w"], L0_ffn_ln2_b=inputs["L0_ffn_ln2_b"],
                 L0_ffn_b1=inputs["L0_ffn_b1"], L0_ffn_b2=inputs["L0_ffn_b2"],
                 c_ones=consts["c_ones"], c_ident=consts["c_ident"])
        l0f_maps.append(m)
    l0f_res = run_bass_kernel_spmd(l0f_nc, l0f_maps, core_ids=cores).results

    l1_nc = _phase("l1", emit_l1, L1_INS, L1_OUTS)
    l1_maps = []
    for c in cores:
        m = dict(x1_s=l0f_res[c]["x1_s"],
                 wq1_bf=pw[c]["wq1_bf"], wk1_bf=pw[c]["wk1_bf"],
                 wv1_bf=pw[c]["wv1_bf"], wo1_bf=pw[c]["wo1_bf"],
                 L1_attn_bq=inputs["L1_attn_bq"], L1_attn_bk=inputs["L1_attn_bk"],
                 L1_attn_bv=inputs["L1_attn_bv"], L1_attn_bo=inputs["L1_attn_bo"],
                 wf1q_rows=pw[c]["wf1q_rows"],
                 c_ident=consts["c_ident"], c_ones=consts["c_ones"], c_had=consts["c_had"])
        l1_maps.append(m)
    l1_res = run_bass_kernel_spmd(l1_nc, l1_maps, core_ids=cores).results

    out = np.stack([l1_res[c]["out"] for c in cores], axis=0)
    return out



def build_fused():
    nc = bass.Bass("TRN2", target_bir_lowering=False)
    ext_in = {}
    for n in PREP_INS:
        ext_in[n] = nc.dram_tensor(n, INPUT_SHAPES[n], F32, kind="ExternalInput").ap()
    for n, sh in [("x", [T, D]), ("L0_attn_A_log", [12]), ("L0_attn_dt_bias", [12]),
                  ("L0_attn_b_b", [12]), ("L0_attn_g_norm_weight", [128]),
                  ("L0_ffn_ln1_w", [1024]), ("L0_ffn_ln1_b", [1024]),
                  ("L0_ffn_ln2_w", [4096]), ("L0_ffn_ln2_b", [4096]),
                  ("L0_ffn_b1", [4096]), ("L0_ffn_b2", [1024]),
                  ("L1_attn_bq", [1024]), ("L1_attn_bk", [1024]),
                  ("L1_attn_bv", [1024]), ("L1_attn_bo", [1024])]:
        if n not in ext_in:
            ext_in[n] = nc.dram_tensor(n, sh, F32, kind="ExternalInput").ap()
    for n, (sh, dt) in CONST_SPECS.items():
        ext_in[n] = nc.dram_tensor(n, list(sh), dt, kind="ExternalInput").ap()
    out = nc.dram_tensor("out", [T, D], F32, kind="ExternalOutput").ap()

    MID = {}
    for spec in (PREP_OUTS, L0A_OUTS, L0F_OUTS, L1_OUTS):
        for n, (sh, dt) in spec.items():
            if n == "out":
                continue
            if n not in MID:
                MID[n] = nc.dram_tensor(n, list(sh), dt, kind="Internal").ap()
    MID["out"] = out

    keys = list(Em.QUEUES) + [f"{q}{l}" for q in ("sp", "pool") for l in range(Em.NLANES)]
    sem_ctx = [nc.semaphore(f"q_{k}") for k in keys]
    sems = {}
    import contextlib
    with contextlib.ExitStack() as stack:
        for k, ctx in zip(keys, sem_ctx):
            sems[k] = stack.enter_context(ctx)
        with nc.Block() as block:
            @block.vector
            def _(vector):
                em = Em(nc, sems)
                with contextlib.ExitStack() as tstack:
                    em.stack = tstack
                    allin = {**ext_in, **MID}
                    with emscope(em):
                        emit_prep(em, ext_in, MID)
                    with emscope(em):
                        emit_l0a(em, allin, MID)
                    with emscope(em):
                        emit_l0f(em, allin, MID)
                    with emscope(em):
                        emit_l1(em, allin, MID)
                    em.barrier_store_done()
    return nc


def kernel(**inputs):
    inputs = {k: np.ascontiguousarray(v) for k, v in inputs.items()}
    consts = host_consts()
    NCORE = 8
    if "fused" not in _CACHE:
        _CACHE["fused"] = build_fused()
    nc = _CACHE["fused"]
    maps = []
    for c in range(NCORE):
        m = {n: inputs[n].astype(np.float32) for n in PREP_INS}
        m["x"] = inputs["L0_attn_hidden_states"][c].astype(np.float32)
        for n in ["L0_attn_A_log", "L0_attn_dt_bias", "L0_attn_b_b", "L0_attn_g_norm_weight",
                  "L0_ffn_ln1_w", "L0_ffn_ln1_b", "L0_ffn_ln2_w", "L0_ffn_ln2_b",
                  "L0_ffn_b1", "L0_ffn_b2", "L1_attn_bq", "L1_attn_bk",
                  "L1_attn_bv", "L1_attn_bo"]:
            m[n] = inputs[n].astype(np.float32)
        for n in CONST_SPECS:
            m[n] = consts[n]
        maps.append(m)
    res = run_bass_kernel_spmd(nc, maps, core_ids=list(range(NCORE)))
    return np.stack([res.results[c]["out"] for c in range(NCORE)], axis=0)


# version 06 (final)
# speedup vs baseline: 1.2163x; 1.0590x over previous
import numpy as np
import ml_dtypes
import concourse.bass as bass
import concourse.mybir as mybir
from concourse.bass_utils import run_bass_kernel_spmd

F32 = mybir.dt.float32
BF16 = mybir.dt.bfloat16
I32 = mybir.dt.int32
AF = mybir.ActivationFunctionType
ALU = mybir.AluOpType
AX = mybir.AxisListType
BF = ml_dtypes.bfloat16

T = 4096
D = 1024
NH, DK, DV = 12, 64, 128
KD, VD = NH * DK, NH * DV
CS = 64
HID = 4096
H1, DH1 = 16, 64
EPS_LN = 1e-5
LN8TH = float(np.log(np.float64(0.125)))



class Em:
    QUEUES = ("pe", "act", "dve", "pool", "sp")
    NLANES = 8

    def __init__(self, nc, sems):
        self.nc = nc
        self.eng = dict(pe=nc.tensor, act=nc.scalar, dve=nc.vector,
                        pool=nc.gpsimd, sp=nc.sync)
        self.sems = sems
        self.cnt = {k: 0 for k in sems}
        self.lane = {"sp": 0, "pool": 0}
        self.acc = {}
        self.waited = {q: {} for q in self.QUEUES}

    @staticmethod
    def _rng(ap):
        t = ap.tensor
        sp = t.space.name
        dims = ap.ap
        if sp in ("SBUF", "PSUM") and len(dims) >= 1 and dims[0][0] > 0:
            rs = dims[0][0]
            p0 = ap.offset // rs
            p1 = p0 + dims[0][1] - 1
            flo = fhi = ap.offset % rs
            for step, count in dims[1:]:
                d = step * (count - 1)
                if d >= 0:
                    fhi += d
                else:
                    flo += d
            return t.name, p0, p1, flo, fhi
        lo = hi = ap.offset
        for step, count in dims:
            d = step * (count - 1)
            if d >= 0:
                hi += d
            else:
                lo += d
        return t.name, 0, 0, lo, hi

    def _deps_for(self, ap, write):
        name, p0, p1, lo, hi = self._rng(ap)
        deps = []
        for rec in self.acc.get(name, ()):
            if rec[3] < lo or rec[2] > hi or rec[1] < p0 or rec[0] > p1:
                continue
            if rec[6] or write:
                deps.append((rec[4], rec[5]))
        return deps

    def _record(self, ap, q, cnt, write):
        name, p0, p1, lo, hi = self._rng(ap)
        lst = self.acc.setdefault(name, [])
        if write:
            lst[:] = [r for r in lst
                      if not (p0 <= r[0] and r[1] <= p1 and lo <= r[2] and r[3] <= hi)]
        lst.append([p0, p1, lo, hi, q, cnt, write])
        if len(lst) > 400:
            del lst[: len(lst) - 250]

    def _wait(self, q, deps):
        need = {}
        for dq, c in deps:
            if dq == q == "pe":
                continue
            if c is None:
                raise RuntimeError("dependency on un-inc'd (lazy) op")
            if need.get(dq, 0) < c:
                need[dq] = c
        for dq, c in need.items():
            if self.waited[q].get(dq, 0) >= c:
                continue
            self.eng[q].wait_ge(self.sems[dq], c)
            self.waited[q][dq] = c

    def op(self, q, fname, outs, ins, inc=1, _call=None, **kw):
        deps = []
        for ap in ins:
            if isinstance(ap, (int, float)) or ap is None:
                continue
            deps += self._deps_for(ap, False)
        for ap in outs:
            if ap is None:
                continue
            deps += self._deps_for(ap, True)
        self._wait(q, deps)
        inst = (_call or getattr(self.eng[q], fname))(**kw)
        c = None
        if inc:
            inst.then_inc(self.sems[q], inc)
            self.cnt[q] += inc
            c = self.cnt[q]
        else:
            c = self.cnt[q] + 1
        for ap in ins:
            if isinstance(ap, (int, float)) or ap is None:
                continue
            self._record(ap, q, c, False)
        for ap in outs:
            if ap is None:
                continue
            self._record(ap, q, c, True)
        return inst

    def _dma_issue(self, q, outs, ins, transpose=False, **kw):
        ln = self.lane[q]
        self.lane[q] = (ln + 1) % self.NLANES
        key = f"{q}{ln}"
        deps = []
        for ap in ins:
            deps += self._deps_for(ap, False)
        for ap in outs:
            deps += self._deps_for(ap, True)
        deps.append((key, self.cnt[key]))
        if not hasattr(self, "_cat"):
            self._cat = {"tr": {}, "ss": {}}
        is_ss = (not transpose and ins[0].tensor.space.name == "SBUF"
                 and outs[0].tensor.space.name == "SBUF")
        if transpose:
            deps += list(self._cat["ss"].items())
        elif is_ss:
            deps += list(self._cat["tr"].items())
        self._wait(q, deps)
        if transpose:
            kw["transpose"] = True
        inst = self.eng[q].dma_start(**kw)
        inst.then_inc(self.sems[key], 16)
        self.cnt[key] += 16
        c = self.cnt[key]
        if transpose:
            self._cat["tr"][key] = c
        elif is_ss:
            self._cat["ss"][key] = c
        for ap in ins:
            self._record(ap, key, c, False)
        for ap in outs:
            self._record(ap, key, c, True)
        return inst

    def dma(self, out, in_, q="sp", **kw):
        return self._dma_issue(q, [out], [in_], out=out, in_=in_, **kw)

    def dmat(self, out, in_, q="sp"):
        return self._dma_issue(q, [out], [in_], transpose=True, out=out, in_=in_)

    def mm(self, out, lhsT, rhs, start=True, stop=True, inc=1, tile_position=None):
        return self.op("pe", "matmul", [out], [lhsT, rhs], inc=inc, out=out,
                       lhsT=lhsT, rhs=rhs, start=start, stop=stop,
                       tile_position=tile_position)

    def tr(self, out, in_, ident, inc=1):
        return self.op("pe", "transpose", [out], [in_, ident], inc=inc,
                       out=out, in_=in_, identity=ident)

    def prealloc_consts(self):
        for v in (1.0, 1e-5, 1e-6, LN8TH, -0.5, -1.0, 0.5):
            self.constf(v)

    def constf(self, value):
        value = float(value)
        if not hasattr(self, "_consts"):
            self._consts = {}
        if value not in self._consts:
            t = self.stack.enter_context(
                self.nc.sbuf_tensor(f"c_f32_{len(self._consts)}", [128, 1], F32))
            self.memset(t[:], value)
            self._consts[value] = t
        return self._consts[value][:]

    def act(self, out, in_, func, bias=None, scale=None, accum_out=None, q="act"):
        npart = out.ap[0][1]
        if isinstance(bias, (int, float)) and float(bias) != 0.0:
            bias = self.constf(bias)[0:npart, 0:1]
        if isinstance(scale, (int, float)) and float(scale) == 1.0:
            scale = None
        if isinstance(scale, (int, float)):
            scale = self.constf(scale)[0:npart, 0:1]
        kw = dict(out=out, in_=in_, func=func)
        ins = [in_]
        if bias is not None and not (isinstance(bias, float) and bias == 0.0):
            kw["bias"] = bias
            if not isinstance(bias, (int, float)):
                ins.append(bias)
        if scale is not None:
            kw["scale"] = scale
            if not isinstance(scale, (int, float)):
                ins.append(scale)
        if accum_out is not None:
            kw["accum_out"] = accum_out
        return self.op(q, "activation", [out, accum_out], ins, **kw)

    def ts(self, out, in0, s1, s2=None, op0=ALU.mult, op1=None, q="dve", accum_out=None):
        ins = [in0] + [s for s in (s1, s2) if s is not None and not isinstance(s, (int, float))]
        kw = dict(out=out, in0=in0, scalar1=s1, scalar2=s2, op0=op0)
        if op1 is not None:
            kw["op1"] = op1
        if accum_out is not None:
            kw["accum_out"] = accum_out
        return self.op(q, "tensor_scalar", [out, accum_out], ins, **kw)

    def tt(self, out, in0, in1, alu, q="dve"):
        return self.op(q, "tensor_tensor", [out], [in0, in1], out=out, in0=in0, in1=in1, op=alu)

    def stt(self, out, in0, s, in1, op0, op1, q="dve"):
        ins = [in0, in1] + ([s] if not isinstance(s, (int, float)) else [])
        return self.op(q, "scalar_tensor_tensor", [out], ins, out=out, in0=in0,
                       scalar=s, in1=in1, op0=op0, op1=op1)

    def copy(self, out, in_, q="dve"):
        return self.op(q, "tensor_copy", [out], [in_], out=out, in_=in_)

    def red(self, out, in_, alu, absval=None, q="dve"):
        return self.op(q, "tensor_reduce", [out], [in_], out=out, in_=in_,
                       axis=AX.X, op=alu, apply_absolute_value=absval)

    def scan(self, out, d0, d1, initial, op0, op1, q="dve"):
        return self.op(q, "tensor_tensor_scan", [out], [d0, d1], out=out, data0=d0,
                       data1=d1, initial=initial, op0=op0, op1=op1)

    def memset(self, ap, val, q="dve"):
        return self.op(q, "memset", [ap], [], ap=ap, constant=val)

    def recip(self, out, in_, q="dve"):
        return self.op(q, "reciprocal", [out], [in_], out=out, in_=in_)

    def sb(self, name, shape, dtype, tdest=False):
        return self.stack.enter_context(
            self.nc.sbuf_tensor(name, list(shape), dtype, is_transpose_dest=tdest))

    def ps(self, name, shape, dtype=F32):
        return self.stack.enter_context(self.nc.psum_tensor(name, list(shape), dtype))

    def quiesce(self):
        for q in self.QUEUES:
            for dk, c in self.cnt.items():
                if dk != q and c and self.waited[q].get(dk, 0) < c:
                    self.eng[q].wait_ge(self.sems[dk], c)
                    self.waited[q][dk] = c

    def barrier_store_done(self):
        for dk, c in self.cnt.items():
            if dk != "sp" and c:
                if self.waited["sp"].get(dk, 0) < c:
                    self.eng["sp"].wait_ge(self.sems[dk], c)
                    self.waited["sp"][dk] = c
        for ln in range(self.NLANES):
            key = f"sp{ln}"
            if self.cnt.get(key, 0):
                self.eng["sp"].wait_ge(self.sems[key], self.cnt[key])


def rsqrt_em(em, out, in_, eps, q="act", scratch=None):
    s = scratch if scratch is not None else out
    em.act(s, in_, AF.Sqrt, bias=float(eps), q=q)
    em.recip(out, s)



def host_consts():
    ident = np.eye(128, dtype=BF)
    ones_row = np.ones((1, 128), dtype=BF)
    t_idx = np.arange(CS)[:, None]
    s_idx = np.arange(CS)[None, :]
    ms = np.tril(np.ones((CS, CS), np.float32), -1).astype(BF)
    bigL = (30000.0 * (s_idx > t_idx)).astype(np.float32)
    bigUi = (-30000.0 * (t_idx > s_idx)).astype(np.float32)
    bigUs = (-30000.0 * (t_idx >= s_idx)).astype(np.float32)
    h = np.array([[1.0]], dtype=np.float64)
    while h.shape[0] < 1024:
        h = np.block([[h, h], [h, -h]])
    had = (h / np.sqrt(1024)).astype(np.float32).astype(BF)
    return dict(c_ident=ident, c_ones=ones_row, c_ms=ms, c_bigL=bigL,
                c_bigUi=bigUi, c_bigUs=bigUs, c_had=had)


CONST_SPECS = dict(c_ident=([128, 128], BF16), c_ones=([1, 128], BF16),
                   c_ms=([CS, CS], BF16), c_bigL=([CS, CS], F32),
                   c_bigUi=([CS, CS], F32), c_bigUs=([CS, CS], F32),
                   c_had=([1024, 1024], BF16))



def round_chain(em, out_bf, x_f32, tmp_i32, tmp_a, tmp_b, lo, hi, scale_col=None,
                post_mul_col=None, q1="dve", q2="dve"):
    if scale_col is not None:
        em.ts(tmp_a, x_f32, scale_col, op0=ALU.mult, q=q1)
        src = tmp_a
    else:
        src = x_f32
    em.copy(tmp_i32, src, q=q1)
    em.copy(tmp_b, tmp_i32, q=q1)
    em.tt(tmp_i32.bitcast(F32), src, tmp_b, ALU.subtract, q=q2)
    e = tmp_i32.bitcast(F32)
    em.ts(tmp_a, e, 0.5, op0=ALU.is_ge, q=q2)
    em.stt(tmp_a, e, -0.5, tmp_a, ALU.is_lt, ALU.subtract, q=q2)
    em.tt(tmp_b, tmp_b, tmp_a, ALU.subtract, q=q2)
    if post_mul_col is None:
        em.ts(out_bf, tmp_b, float(hi), float(lo), op0=ALU.min, op1=ALU.max, q=q1)
    else:
        em.ts(tmp_b, tmp_b, float(hi), float(lo), op0=ALU.min, op1=ALU.max, q=q1)
        em.ts(out_bf, tmp_b, post_mul_col, op0=ALU.mult, q=q1)


def emit_prep(em, IN, OUT):
    nc = em.nc
    sw = em.sb("pr_sw", [128, 4096], F32)
    swb = em.sb("pr_swb", [128, 4096], F32)
    swc = em.sb("pr_swc", [128, 4096], F32)
    swi = em.sb("pr_swi", [128, 4096], I32)
    bo0 = em.sb("pr_bo0", [128, 4096], BF16)
    bo1 = em.sb("pr_bo1", [128, 4096], BF16)
    acc = em.sb("pr_acc", [128, 1], F32)
    tpart = em.sb("pr_tp", [128, 1], F32)
    sc = em.sb("pr_sc", [128, 8], F32)
    onesf = em.sb("pr_onesf", [128, 1], F32)
    onesr = em.sb("pr_onesr", [1, 128], F32)
    ps = em.ps("pr_ps", [1, 8])
    ps2 = em.ps("pr_ps2", [128, 8])
    em.memset(onesf[:], 1.0)
    em.memset(onesr[:], 1.0)

    casts = [
        ("L0_attn_Wq", "wq_bf", 1024, 768), ("L0_attn_Wk", "wk_bf", 1024, 768),
        ("L0_attn_Wv", "wv_bf", 1024, 1536), ("L0_attn_Wg", "wg_bf", 1024, 1536),
        ("L0_attn_Wo", "wo_bf", 1536, 1024),
        ("L1_attn_wq", "wq1_bf", 1024, 1024), ("L1_attn_wk", "wk1_bf", 1024, 1024),
        ("L1_attn_wv", "wv1_bf", 1024, 1024), ("L1_attn_wo", "wo1_bf", 1024, 1024),
    ]
    bi = 0
    for src_, dst, rows, cols in casts:
        s, d = IN[src_], OUT[dst]
        for r0 in range(0, rows, 128):
            b = (bo0, bo1)[bi % 2]; bi += 1
            em.dma(b[:, 0:cols], s[r0:r0 + 128, 0:cols], q="pool")
            em.dma(d[r0:r0 + 128, 0:cols], b[:, 0:cols], q="sp")

    for src_, dst, rows in [("L0_attn_conv_q", "convq_bf", 768),
                            ("L0_attn_conv_k", "convk_bf", 768),
                            ("L0_attn_conv_v", "convv_bf", 1536)]:
        ncol = rows * 4 // 128
        em.dma(sw[:, 0:ncol], IN[src_].rearrange("a b -> (a b)").rearrange("(p n) -> p n", p=128), q="sp")
        em.copy(bo0[:, 0:ncol], sw[:, 0:ncol])
        em.dma(OUT[dst].rearrange("a b -> (a b)").rearrange("(p n) -> p n", p=128), bo0[:, 0:ncol], q="sp")

    em.memset(bo1[:, 0:352], 0.0)
    em.dma(OUT["wgkb_bf"].rearrange("a b -> (a b)").rearrange("(p n) -> p n", p=128),
           bo1[:, 0:352], q="sp")
    for src_, c0 in [("L0_attn_Wgk", 0), ("L0_attn_Wb", 32)]:
        em.dma(sw[:, 0:96], IN[src_].rearrange("a b -> (a b)").rearrange("(p n) -> p n", p=128), q="sp")
        em.copy(bo0[:, 0:96], sw[:, 0:96])
        em.dma(OUT["wgkb_bf"][:, c0:c0 + 12].rearrange("(p r) c -> p r c", p=128),
               bo0[:, 0:96].rearrange("p (r c) -> p r c", c=12), q="sp")

    for wname, oname, nelem in [("L0_ffn_w1", "w1q_rows", 4096 * 1024),
                                ("L0_ffn_w2", "w2q_rows", 1024 * 4096)]:
        w = IN[wname]
        wflat = w.rearrange("a b -> (a b)").rearrange("(p n) -> p n", p=128)
        o = OUT[oname]
        oflat = o.rearrange("a b -> (a b)").rearrange("(p n) -> p n", p=128)
        NCOL = nelem // 128
        TL = 4096
        em.memset(acc[:], 0.0)
        for c0 in range(0, NCOL, TL):
            em.dma(sw[:], wflat[:, c0:c0 + TL], q="sp")
            em.act(swb[:], sw[:], AF.Abs, accum_out=tpart[:])
            em.tt(acc[:], acc[:], tpart[:], ALU.add)
        em.mm(ps[0:1, 0:1], acc[:], onesf[:])
        em.copy(sc[0:1, 4:5], ps[0:1, 0:1])
        em.mm(ps2[:, 0:1], onesr[:], sc[0:1, 4:5])
        em.ts(sc[:, 0:1], ps2[:, 0:1], 1.0 / nelem, 1e-5, op0=ALU.mult, op1=ALU.max)
        em.recip(sc[:, 1:2], sc[:, 0:1])
        for c0 in range(0, NCOL, TL):
            em.dma(sw[:], wflat[:, c0:c0 + TL], q="sp")
            round_chain(em, bo0[:], sw[:], swi[:], swb[:], swc[:], -1.0, 1.0,
                        scale_col=sc[:, 1:2], post_mul_col=sc[:, 0:1])
            em.dma(oflat[:, c0:c0 + TL], bo0[:], q="sp")

    w = IN["L1_ffn_weight"]
    o = OUT["wf1q_rows"]
    for r0 in range(0, 1024, 128):
        em.dma(sw[:, 0:1024], w[r0:r0 + 128, :], q="sp")
        em.act(swb[:, 0:1024], sw[:, 0:1024], AF.Abs, accum_out=sc[:, 0:1])
        em.ts(sc[:, 1:2], sc[:, 0:1], 1.0 / 1024, 1e-5, op0=ALU.mult, op1=ALU.max)
        em.recip(sc[:, 2:3], sc[:, 1:2])
        em.ts(swc[:, 0:1024], sw[:, 0:1024], sc[:, 2:3], op0=ALU.mult)
        em.ts(swb[:, 0:1024], swc[:, 0:1024], 0.5, op0=ALU.is_gt)
        em.stt(swb[:, 0:1024], swc[:, 0:1024], -0.5, swb[:, 0:1024], ALU.is_lt, ALU.subtract)
        em.ts(sc[:, 3:4], sc[:, 1:2], -1.0, op0=ALU.mult)
        em.ts(bo0[:, 0:1024], swb[:, 0:1024], sc[:, 3:4], op0=ALU.mult)
        em.dma(o[r0:r0 + 128, :], bo0[:, 0:1024], q="sp")


PREP_OUTS = dict(
    wq_bf=([1024, 768], BF16), wk_bf=([1024, 768], BF16), wv_bf=([1024, 1536], BF16),
    wg_bf=([1024, 1536], BF16), wo_bf=([1536, 1024], BF16), wgkb_bf=([1024, 44], BF16),
    convq_bf=([768, 4], BF16), convk_bf=([768, 4], BF16), convv_bf=([1536, 4], BF16),
    w1q_rows=([4096, 1024], BF16), w2q_rows=([1024, 4096], BF16),
    wf1q_rows=([1024, 1024], BF16),
    wq1_bf=([1024, 1024], BF16), wk1_bf=([1024, 1024], BF16),
    wv1_bf=([1024, 1024], BF16), wo1_bf=([1024, 1024], BF16),
)

PREP_INS = ["L0_attn_Wq", "L0_attn_Wk", "L0_attn_Wv", "L0_attn_Wg", "L0_attn_Wo",
            "L0_attn_conv_q", "L0_attn_conv_k", "L0_attn_conv_v", "L0_attn_Wgk",
            "L0_attn_Wb", "L0_ffn_w1", "L0_ffn_w2", "L1_attn_wq", "L1_attn_wk",
            "L1_attn_wv", "L1_attn_wo", "L1_ffn_weight"]

INPUT_SHAPES = dict(
    L0_attn_Wq=[1024, 768], L0_attn_Wk=[1024, 768], L0_attn_Wv=[1024, 1536],
    L0_attn_Wg=[1024, 1536], L0_attn_Wo=[1536, 1024], L0_attn_conv_q=[768, 4],
    L0_attn_conv_k=[768, 4], L0_attn_conv_v=[1536, 4], L0_attn_Wgk=[1024, 12],
    L0_attn_Wb=[1024, 12], L0_ffn_w1=[4096, 1024], L0_ffn_w2=[1024, 4096],
    L1_attn_wq=[1024, 1024], L1_attn_wk=[1024, 1024], L1_attn_wv=[1024, 1024],
    L1_attn_wo=[1024, 1024], L1_ffn_weight=[1024, 1024],
)


def build_phase(emit_fn, ext_ins, ext_outs, consts=()):
    nc = bass.Bass("TRN2", target_bir_lowering=False)
    IN = {n: nc.dram_tensor(n, list(sh), dt, kind="ExternalInput").ap()
          for n, (sh, dt) in ext_ins.items()}
    OUT = {n: nc.dram_tensor(n, list(sh), dt, kind="ExternalOutput").ap()
           for n, (sh, dt) in ext_outs.items()}
    keys = list(Em.QUEUES) + [f"{q}{l}" for q in ("sp", "pool") for l in range(Em.NLANES)]
    sem_ctx = [nc.semaphore(f"q_{k}") for k in keys]
    sems = {}
    import contextlib
    with contextlib.ExitStack() as stack:
        for k, ctx in zip(keys, sem_ctx):
            sems[k] = stack.enter_context(ctx)
        with nc.Block() as block:
            @block.vector
            def _(vector):
                em = Em(nc, sems)
                with contextlib.ExitStack() as tstack:
                    em.stack = tstack
                    em.prealloc_consts()
                    emit_fn(em, IN, OUT)
                    em.barrier_store_done()
    return nc


import contextlib

@contextlib.contextmanager
def emscope(em):
    with contextlib.ExitStack() as inner:
        outer = em.stack
        em.stack = inner
        try:
            yield
        finally:
            em.quiesce()
            em.stack = outer



def _bcast_const_mat(em, out_sb, vec_dram, width, ones_f, ps_bc, tmp_row):
    em.dma(tmp_row[0:1, 0:width], vec_dram.rearrange("(r c) -> r c", r=1), q="sp")
    for c0 in range(0, width, 512):
        em.mm(ps_bc[:, 0:512], ones_f[0:1, :], tmp_row[0:1, c0:c0 + 512])
        em.copy(out_sb[:, c0:c0 + 512], ps_bc[:, 0:512])


def _round_quant(em, out_bf, x_f32, scol, lo, hi, fB, fC, fD, iA, n):
    em.ts(fB[:, 0:n], x_f32, scol, op0=ALU.mult)
    em.copy(iA[:, 0:n], fB[:, 0:n])
    em.copy(fC[:, 0:n], iA[:, 0:n])
    e = iA[:, 0:n].bitcast(F32)
    em.tt(e, fB[:, 0:n], fC[:, 0:n], ALU.subtract)
    em.ts(fD[:, 0:n], e, 0.5, op0=ALU.is_ge)
    em.stt(fD[:, 0:n], e, -0.5, fD[:, 0:n], ALU.is_lt, ALU.subtract)
    em.tt(out_bf, fC[:, 0:n], fD[:, 0:n], ALU.subtract)


def emit_l0f(em, IN, OUT):
    nc = em.nc
    ones = em.sb("ff_ones", [1, 128], BF16)
    em.dma(ones[:], IN["c_ones"], q="sp")
    onesf = em.sb("ff_onesf", [1, 128], F32)
    em.memset(onesf[:], 1.0)
    ident = em.sb("ff_ident", [128, 128], BF16)
    em.dma(ident[:], IN["c_ident"], q="sp")
    c_ln1w = em.sb("ff_ln1w", [128, 1024], F32)
    c_ln1b = em.sb("ff_ln1b", [128, 1024], F32)
    b2row = em.sb("ff_b2row", [1, 1024], BF16)
    em.dma(b2row[:], IN["L0_ffn_b2"].rearrange("(r c) -> r c", r=1), q="pool")

    w1sb = [em.sb(f"ff_w1_{d}", [128, 4096], BF16, tdest=True) for d in range(8)]
    w2sb = [em.sb(f"ff_w2_{h}", [128, 1024], BF16, tdest=True) for h in range(32)]
    for d in range(8):
        em.dmat(w1sb[d][:], IN["w1q_rows"][:, d * 128:(d + 1) * 128], q="sp")
    for h in range(32):
        em.dmat(w2sb[h][:], IN["w2q_rows"][:, h * 128:(h + 1) * 128], q="sp")

    pu = [em.ps(f"ff_pu{i}", [128, 512]) for i in range(3)]
    po = [em.ps(f"ff_po{i}", [128, 512]) for i in range(2)]
    ps_trP = [em.ps(f"ff_pstr{i}", [128, 128], BF16) for i in range(3)]

    with emscope(em):
        tmp_row = em.sb("ff_trow", [1, 4096], F32)
        stg = em.sb("ff_stg", [128, 512], F32)
        for mat, vec in [(c_ln1w, "L0_ffn_ln1_w"), (c_ln1b, "L0_ffn_ln1_b")]:
            em.dma(tmp_row[0:1, 0:1024], IN[vec].rearrange("(r c) -> r c", r=1), q="sp")
            for c0 in range(0, 1024, 512):
                em.mm(pu[0][:, 0:512], onesf[0:1, :], tmp_row[0:1, c0:c0 + 512])
                em.copy(mat[:, c0:c0 + 512], pu[0][:, 0:512])
        for dst, vec in [(OUT["ln2w_s"], "L0_ffn_ln2_w"), (OUT["ln2b_s"], "L0_ffn_ln2_b")]:
            em.dma(tmp_row[0:1, :], IN[vec].rearrange("(r c) -> r c", r=1), q="sp")
            for c0 in range(0, 4096, 512):
                em.mm(pu[0][:, 0:512], onesf[0:1, :], tmp_row[0:1, c0:c0 + 512])
                em.copy(stg[:], pu[0][:, 0:512])
                em.dma(dst[:, c0:c0 + 512], stg[:], q="pool")

    xnB = [em.sb(f"ff_xn{t}", [128, 1024], BF16) for t in range(2)]
    xB = [em.sb(f"ff_x{t}", [128, 1024], F32) for t in range(2)]
    a0B = [em.sb(f"ff_a0{t}", [128, 1024], BF16) for t in range(2)]
    x1B = [em.sb(f"ff_x1{t}", [128, 1024], BF16) for t in range(2)]
    fA = em.sb("ff_fA", [128, 1024], F32)
    fB = em.sb("ff_fB", [128, 512], F32)
    fC = em.sb("ff_fC", [128, 512], F32)
    fD = em.sb("ff_fD", [128, 512], F32)
    fE = em.sb("ff_fE", [128, 512], F32)
    iA = em.sb("ff_iA", [128, 512], I32)
    hq_t = em.sb("ff_hq", [128, 1024], BF16)
    hqT = [em.sb(f"ff_hqT{d}", [128, 128], BF16) for d in range(8)]
    gm = em.sb("ff_gm", [128, 4096], BF16)
    mqB = [em.sb(f"ff_mq{t}", [128, 1024], BF16) for t in range(4)]
    mqT = [em.sb(f"ff_mqT{t}", [128, 128], BF16) for t in range(4)]
    stP = [em.sb(f"ff_st{t}", [128, 32], F32) for t in range(2)]
    gaccP = [em.sb(f"ff_gacc{t}", [128, 32], F32) for t in range(2)]
    srowP = [em.sb(f"ff_srow{t}", [1, 128], BF16) for t in range(2)]
    s2rowP = [em.sb(f"ff_s2row{t}", [1, 128], BF16) for t in range(2)]
    sbfcP = [em.sb(f"ff_sbfc{t}", [128, 1], BF16) for t in range(2)]
    lw2 = [em.sb(f"ff_lw2{t}", [128, 512], F32) for t in range(2)]
    lb2 = [em.sb(f"ff_lb2{t}", [128, 512], F32) for t in range(2)]
    b1b = [em.sb(f"ff_b1b{t}", [1, 512], BF16) for t in range(2)]

    xn_s = IN["xn_s"]
    em.dma(xnB[0][:], xn_s[0:128, :], q="sp")
    em.dma(xB[0][:], IN["x"][0:128, :], q="sp")
    em.dma(a0B[0][:], IN["a0_s"][0:128, :], q="sp")

    def make_F(tt):
        trow = tt * 128
        p = tt % 2
        xn_t, x_t = xnB[p], xB[p]
        st, gacc = stP[p], gaccP[p]
        srow, s2row, sbfc = srowP[p], s2rowP[p], sbfcP[p]

        def f1():
            if tt < 31:
                em.dma(xnB[1 - p][:], xn_s[trow + 128:trow + 256, :], q="sp")
            em.red(st[:, 0:1], xn_t[:], ALU.add)
            em.act(fA[:], xn_t[:], AF.Square, accum_out=st[:, 1:2])
            em.ts(st[:, 2:3], st[:, 0:1], 1.0 / 1024, op0=ALU.mult)
            em.ts(st[:, 3:4], st[:, 1:2], 1.0 / 1024, op0=ALU.mult)
            em.ts(st[:, 4:5], st[:, 2:3], st[:, 2:3], op0=ALU.mult)
            em.tt(st[:, 5:6], st[:, 3:4], st[:, 4:5], ALU.subtract)
            em.act(st[:, 6:7], st[:, 5:6], AF.Ln, bias=EPS_LN)
            em.act(st[:, 7:8], st[:, 6:7], AF.Exp, scale=-0.5)
            em.ts(st[:, 8:9], st[:, 2:3], st[:, 7:8], s2=-1.0, op0=ALU.mult, op1=ALU.mult)
            em.act(fA[:], xn_t[:], AF.Identity, scale=st[:, 7:8], bias=st[:, 8:9])
            em.stt(fA[:], fA[:], 0.0, c_ln1w[:], ALU.bypass, ALU.mult)
            em.tt(fA[:], fA[:], c_ln1b[:], ALU.add)

        def f2():
            em.red(st[:, 9:10], fA[:], ALU.max, absval=True)
            em.ts(st[:, 10:11], st[:, 9:10], 1e-5, op0=ALU.max)
            em.recip(st[:, 11:12], st[:, 10:11])
            em.ts(st[:, 12:13], st[:, 11:12], 127.0, op0=ALU.mult)
            em.ts(st[:, 13:14], st[:, 10:11], 1.0 / 127.0, op0=ALU.mult)
            _round_quant(em, hq_t[:, 0:512], fA[:, 0:512],
                         st[:, 12:13], -128.0, 127.0, fB, fC, fD, iA, 512)

        def f3():
            _round_quant(em, hq_t[:, 512:1024], fA[:, 512:1024],
                         st[:, 12:13], -128.0, 127.0, fB, fC, fD, iA, 512)
            em.copy(sbfc[:], st[:, 12:13])
            em.tr(ps_trP[0][0:1, 0:128], sbfc[:], ident[:])
            em.copy(srow[:], ps_trP[0][0:1, 0:128])
            for d in range(8):
                em.tr(ps_trP[d % 2][:], hq_t[:, d * 128:(d + 1) * 128], ident[:])
                if d % 2:
                    em.act(hqT[d][:], ps_trP[1][:], AF.Identity)
                else:
                    em.copy(hqT[d][:], ps_trP[0][:])

        def mk_fc(fc):
            def f():
                bank = pu[fc % 3]
                em.dma(b1b[fc % 2][:], IN["L0_ffn_b1"][fc * 512:(fc + 1) * 512]
                       .rearrange("(r c) -> r c", r=1), q="pool")
                for d in range(8):
                    em.mm(bank[:], hqT[d][:], w1sb[d][:, fc * 512:(fc + 1) * 512],
                          start=(d == 0), stop=False, inc=0)
                em.mm(bank[:], srow[0:1, :], b1b[fc % 2][0:1, :], start=False, stop=True)
                em.act(gm[:, fc * 512:(fc + 1) * 512], bank[:], AF.Gelu,
                       scale=st[:, 13:14], accum_out=gacc[:, fc:fc + 1])
                em.act(fB[:], gm[:, fc * 512:(fc + 1) * 512], AF.Square,
                       accum_out=gacc[:, 8 + fc:9 + fc])
            return f

        def f8():
            em.red(st[:, 16:17], gacc[:, 0:8], ALU.add)
            em.red(st[:, 17:18], gacc[:, 8:16], ALU.add)
            em.ts(st[:, 18:19], st[:, 16:17], 1.0 / 4096, op0=ALU.mult)
            em.ts(st[:, 19:20], st[:, 17:18], 1.0 / 4096, op0=ALU.mult)
            em.ts(st[:, 20:21], st[:, 18:19], st[:, 18:19], op0=ALU.mult)
            em.tt(st[:, 21:22], st[:, 19:20], st[:, 20:21], ALU.subtract)
            em.act(st[:, 22:23], st[:, 21:22], AF.Ln, bias=EPS_LN)
            em.act(st[:, 23:24], st[:, 22:23], AF.Exp, scale=-0.5)
            em.ts(st[:, 24:25], st[:, 18:19], st[:, 23:24], s2=-1.0, op0=ALU.mult, op1=ALU.mult)

        def mk_ln2(fc):
            def f():
                sl_ = slice(fc * 512, (fc + 1) * 512)
                em.dma(lw2[fc % 2][:], OUT["ln2w_s"][:, sl_], q="sp")
                em.dma(lb2[fc % 2][:], OUT["ln2b_s"][:, sl_], q="sp")
                em.act(fB[:], gm[:, sl_], AF.Identity, scale=st[:, 23:24], bias=st[:, 24:25])
                em.tt(fC[:], fB[:], lw2[fc % 2][:], ALU.mult)
                em.tt(gm[:, sl_], fC[:], lb2[fc % 2][:], ALU.add)
                em.red(gacc[:, 16 + fc:17 + fc], gm[:, sl_], ALU.max, absval=True)
            return f

        def f11():
            em.red(st[:, 25:26], gacc[:, 16:24], ALU.max)
            em.ts(st[:, 26:27], st[:, 25:26], 1e-5, op0=ALU.max)
            em.recip(st[:, 27:28], st[:, 26:27])
            em.ts(st[:, 28:29], st[:, 27:28], 127.0, op0=ALU.mult)
            em.ts(st[:, 29:30], st[:, 26:27], 1.0 / 127.0, op0=ALU.mult)
            em.copy(sbfc[:], st[:, 28:29])
            em.tr(ps_trP[0][0:1, 0:128], sbfc[:], ident[:])
            em.copy(s2row[:], ps_trP[0][0:1, 0:128])

        def mk_q(cc):
            def f():
                for half in range(2):
                    _round_quant(em, mqB[cc][:, half * 512:(half + 1) * 512],
                                 gm[:, cc * 1024 + half * 512:cc * 1024 + (half + 1) * 512],
                                 st[:, 28:29], -128.0, 127.0, fB, fC, fD, iA, 512)
            return f

        return ([f1, f2, f3] + [mk_fc(fc) for fc in range(8)] + [f8]
                + [mk_ln2(fc) for fc in range(8)] + [f11],
                [mk_q(cc) for cc in range(4)])

    def make_B(tt):
        trow = tt * 128
        p = tt % 2
        x_t, a0_t, x1_t = xB[p], a0B[p], x1B[p]
        st, s2row = stP[p], s2rowP[p]

        def mk_cc(cc):
            def f():
                def emit_mm(j):
                    h = cc * 8 + j
                    em.mm(po[0][:], mqT[j % 4][:], w2sb[h][:, 0:512],
                          start=(h == 0), stop=False, inc=0)
                    em.mm(po[1][:], mqT[j % 4][:], w2sb[h][:, 512:1024],
                          start=(h == 0), stop=False, inc=0)
                for j in range(8):
                    em.tr(ps_trP[j % 3][:], mqB[cc][:, j * 128:(j + 1) * 128], ident[:])
                    if j % 2:
                        em.act(mqT[j % 4][:], ps_trP[j % 3][:], AF.Identity)
                    else:
                        em.copy(mqT[j % 4][:], ps_trP[j % 3][:])
                    if j >= 2:
                        emit_mm(j - 2)
                emit_mm(6)
                emit_mm(7)
            return f

        def b5():
            em.mm(po[0][:], s2row[0:1, :], b2row[0:1, 0:512], start=False, stop=True)
            em.mm(po[1][:], s2row[0:1, :], b2row[0:1, 512:1024], start=False, stop=True)
            for oc in range(2):
                sl_ = slice(oc * 512, (oc + 1) * 512)
                em.stt(fE[:], po[oc][:], st[:, 29:30], x_t[:, sl_], ALU.mult, ALU.add)
                em.tt(x1_t[:, sl_], fE[:], a0_t[:, sl_], ALU.add)
            em.dma(OUT["x1_s"][trow:trow + 128, :], x1_t[:], q="sp")
            if tt < 31:
                em.dma(xB[1 - p][:], IN["x"][trow + 128:trow + 256, :], q="sp")
                em.dma(a0B[1 - p][:], IN["a0_s"][trow + 128:trow + 256, :], q="sp")

        return [mk_cc(cc) for cc in range(4)] + [b5]

    front0, qprev = make_F(0)
    for f in front0:
        f()
    for tt in range(32):
        bs = make_B(tt)
        ffront, fq = make_F(tt + 1) if tt < 31 else ([], [])
        for cc in range(4):
            qprev[cc]()
            bs[cc]()
        bs[4]()
        for f in ffront:
            f()
        qprev = fq


L0F_INS = dict(
    x=([T, D], F32), xn_s=([T, D], BF16), a0_s=([T, D], BF16),
    w1q_rows=([4096, 1024], BF16), w2q_rows=([1024, 4096], BF16),
    L0_ffn_ln1_w=([1024], F32), L0_ffn_ln1_b=([1024], F32),
    L0_ffn_ln2_w=([4096], F32), L0_ffn_ln2_b=([4096], F32),
    L0_ffn_b1=([4096], F32), L0_ffn_b2=([1024], F32),
    c_ones=([1, 128], BF16), c_ident=([128, 128], BF16),
)
L0F_OUTS = dict(x1_s=([T, D], BF16),
                ln2w_s=([128, HID], F32), ln2b_s=([128, HID], F32))



def emit_l0a(em, IN, OUT, n_chunks=8, n_sub=8, scan_secs=99):
    nc = em.nc
    ident = em.sb("la_ident", [128, 128], BF16)
    em.dma(ident[:], IN["c_ident"], q="sp")
    onesf = em.sb("la_onesf", [1, 128], F32)
    em.memset(onesf[:], 1.0)
    onescol = em.sb("la_onescol", [64, 1], F32)
    em.memset(onescol[:], 1.0)
    negones64 = em.sb("la_no64", [1, 64], F32)
    em.memset(negones64[:], -1.0)
    c_ms = em.sb("la_ms", [CS, CS], BF16)
    em.dma(c_ms[:], IN["c_ms"], q="sp")
    c_bigL = em.sb("la_bigL", [CS, CS], F32)
    em.dma(c_bigL[:], IN["c_bigL"], q="sp")
    c_bigUi = em.sb("la_bigUi", [CS, CS], F32)
    em.dma(c_bigUi[:], IN["c_bigUi"], q="sp")
    c_bigUs = em.sb("la_bigUs", [CS, CS], F32)
    em.dma(c_bigUs[:], IN["c_bigUs"], q="sp")

    alog = em.sb("la_alog", [12, 1], F32)
    em.dma(alog[:], IN["L0_attn_A_log"].rearrange("(p c) -> p c", c=1), q="sp")
    dtb = em.sb("la_dtb", [12, 1], F32)
    em.dma(dtb[:], IN["L0_attn_dt_bias"].rearrange("(p c) -> p c", c=1), q="sp")
    bbc = em.sb("la_bb", [12, 1], F32)
    em.dma(bbc[:], IN["L0_attn_b_b"].rearrange("(p c) -> p c", c=1), q="sp")
    negA = em.sb("la_negA", [12, 1], F32)
    em.act(negA[:], alog[:], AF.Exp)
    em.ts(negA[:], negA[:], -1.0, op0=ALU.mult)
    negbb = em.sb("la_negbb", [12, 1], F32)
    em.ts(negbb[:], bbc[:], -1.0, op0=ALU.mult)

    gnwrow = em.sb("la_gnwrow", [1, 128], F32)
    em.dma(gnwrow[:], IN["L0_attn_g_norm_weight"].rearrange("(r c) -> r c", r=1), q="sp")
    gnwmat = em.sb("la_gnwmat", [64, 1536], BF16)
    ppg = em.ps("la_ppg", [128, 128])
    for h in range(12):
        em.mm(ppg[0:64, 0:128], onesf[0:1, 0:64], gnwrow[:])
        em.copy(gnwmat[:, h * 128:(h + 1) * 128], ppg[0:64, 0:128])

    convqk = em.sb("la_convqk", [64, 96], F32)
    for h in range(12):
        em.dma(convqk[:, h * 4:(h + 1) * 4], IN["L0_attn_conv_q"][h * 64:(h + 1) * 64, :], q="sp")
        em.dma(convqk[:, 48 + h * 4:48 + (h + 1) * 4], IN["L0_attn_conv_k"][h * 64:(h + 1) * 64, :], q="sp")
    convv_all = em.sb("la_convvall", [128, 48], F32)
    for vt in range(12):
        em.dma(convv_all[:, vt * 4:(vt + 1) * 4], IN["L0_attn_conv_v"][vt * 128:(vt + 1) * 128, :], q="sp")

    w_sb = [em.sb(f"la_w{d}", [128, 4608], BF16) for d in range(8)]
    wgkb_sb = em.sb("la_wgkb", [128, 352], BF16)
    for d in range(8):
        em.dma(w_sb[d][:, 0:768], IN["wq_bf"][d * 128:(d + 1) * 128, :], q="sp")
        em.dma(w_sb[d][:, 768:1536], IN["wk_bf"][d * 128:(d + 1) * 128, :], q="sp")
        em.dma(w_sb[d][:, 1536:3072], IN["wv_bf"][d * 128:(d + 1) * 128, :], q="sp")
        em.dma(w_sb[d][:, 3072:4608], IN["wg_bf"][d * 128:(d + 1) * 128, :], q="sp")
        em.dma(wgkb_sb[:, d * 44:(d + 1) * 44], IN["wgkb_bf"][d * 128:(d + 1) * 128, :], q="sp")

    pp = [em.ps(f"la_pp{i}", [128, 512]) for i in range(6)]
    ps_trf = em.ps("la_pstrf", [128, 128])
    identf = em.sb("la_identf", [12, 12], F32)
    em.copy(identf[:], ident[0:12, 0:12])

    with emscope(em):
        x_t = em.sb("la_xt", [128, 512], F32)
        xn_t = em.sb("la_xnt", [128, 1024], BF16)
        st = em.sb("la_st", [128, 16], F32)
        sqf2 = [em.sb(f"la_sqf{t}", [64, 512], F32) for t in range(2)]
        xnT = [em.sb(f"la_xnT{d}", [128, 512], BF16, tdest=True) for d in range(8)]
        preqk = [em.sb(f"la_preqk{t}", [64, 516], BF16) for t in range(2)]
        histqk = [em.sb(f"la_histqk{t}", [64, 4], BF16) for t in range(24)]
        prev = [em.sb(f"la_prev{t}", [128, 516], BF16) for t in range(2)]
        histv = [em.sb(f"la_histv{t}", [128, 4], BF16) for t in range(12)]
        qn = [em.sb(f"la_qn{t}", [64, 512], BF16) for t in range(12)]
        kn = [em.sb(f"la_kn{t}", [64, 512], BF16) for t in range(12)]
        vf = [em.sb(f"la_vf{t}", [128, 512], BF16) for t in range(2)]
        rowA = [em.sb(f"la_row{t}", [1, 512], F32) for t in range(2)]
        accv = [em.sb(f"la_accv{t}", [128, 512], BF16) for t in range(2)]
        accqk = [a[0:64, :] for a in accv]
        g_sc = em.sb("la_gsc", [64, 1536], BF16)
        gkb = em.sb("la_gkb", [12, 512], F32)
        beta_f = em.sb("la_betaf", [12, 512], F32)
        S = em.sb("la_S", [64, 1536], F32)
        Sbf = em.sb("la_Sbf", [64, 1536], BF16)
        em.memset(S[:], 0.0)
        em.memset(Sbf[:], 0.0)
        gam8 = em.sb("la_gam8", [12, 512], F32)
        b8 = em.sb("la_b8", [12, 512], F32)
        bb_sc = em.sb("la_bbsc", [12, 64], F32)
        flatg = em.sb("la_flatg", [1, 768], F32)
        tok_beta = em.sb("la_tbeta", [64, 96], F32)
        tok_bb = em.sb("la_tbb", [64, 96], F32)
        tok_b = em.sb("la_tb", [64, 96], F32)
        tok_kds = em.sb("la_tkds", [64, 96], F32)
        tok_bc = em.sb("la_tbc", [64, 96], F32)
        bcrow = em.sb("la_bcrow", [1, 12], F32)
        E0 = em.sb("la_E0", [64, 768], BF16)
        E0U = em.sb("la_E0U", [64, 768], BF16)
        Amat = em.sb("la_Amat", [64, 768], BF16)
        ATmat = em.sb("la_ATmat", [64, 768], BF16)
        WTmat = em.sb("la_WTmat", [64, 768], BF16)
        powP = [em.sb(f"la_powP{i}", [64, 768], BF16) for i in range(2)]
        powPT = [None] + [em.sb(f"la_powPT{i}", [64, 768], BF16) for i in range(1, 6)]
        y_t = em.sb("la_y", [64, 1536], BF16)
        grP = [em.sb(f"la_gr{t}", [64, 768], F32) for t in range(2)]
        y2_t = em.sb("la_y2", [64, 1536], BF16)
        Dl = em.sb("la_Dl", [64, 1536], BF16)
        OtP = [em.sb(f"la_O{t}", [64, 1536], BF16) for t in range(2)]
        ovarP = [em.sb(f"la_ovar{t}", [64, 16], F32) for t in range(2)]
        ktokG = em.sb("la_ktokG", [128, 768], BF16, tdest=True)
        vtokG = em.sb("la_vtokG", [128, 1536], BF16, tdest=True)
        ktokB = em.sb("la_ktokB", [64, 768], BF16)
        vtokB = em.sb("la_vtokB", [64, 1536], BF16)

        xn_s, a0_s = OUT["xn_s"], OUT["a0_s"]
        kn_s, vf_s, og_s = OUT["kn_s"], OUT["vf_s"], OUT["og_s"]

        def stretch(tile, h0, nh, rep):
            return tile[:, h0:h0 + nh].unsqueeze(2).broadcast_to((64, nh, rep))

        def rep_mask(msk, nh):
            return msk[:].unsqueeze(1).broadcast_to((64, nh, CS))

        def h3(ap2d, nh, blk):
            return ap2d.rearrange("p (h s) -> p h s", s=blk)

        for ch in range(n_chunks):
            tbase = ch * 512
            for tt in range(4):
                trow = tbase + tt * 128
                em.dma(x_t[:], IN["x"][trow:trow + 128, 0:512], q="sp")
                em.red(st[:, 0:1], x_t[:], ALU.add)
                em.act(sqf2[0][0:64, :], x_t[0:64, :], AF.Square, accum_out=st[0:64, 1:2])
                em.act(sqf2[0][0:64, :], x_t[64:128, :], AF.Square, accum_out=st[64:128, 1:2])
                em.dma(x_t[:], IN["x"][trow:trow + 128, 512:1024], q="sp")
                em.red(st[:, 9:10], x_t[:], ALU.add)
                em.act(sqf2[0][0:64, :], x_t[0:64, :], AF.Square, accum_out=st[0:64, 10:11])
                em.act(sqf2[0][0:64, :], x_t[64:128, :], AF.Square, accum_out=st[64:128, 10:11])
                em.tt(st[:, 0:1], st[:, 0:1], st[:, 9:10], ALU.add)
                em.tt(st[:, 1:2], st[:, 1:2], st[:, 10:11], ALU.add)
                em.ts(st[:, 2:3], st[:, 0:1], 1.0 / 1024, op0=ALU.mult)
                em.ts(st[:, 3:4], st[:, 1:2], 1.0 / 1024, op0=ALU.mult)
                em.ts(st[:, 4:5], st[:, 2:3], st[:, 2:3], op0=ALU.mult)
                em.tt(st[:, 5:6], st[:, 3:4], st[:, 4:5], ALU.subtract)
                em.act(st[:, 6:7], st[:, 5:6], AF.Ln, bias=EPS_LN)
                em.act(st[:, 7:8], st[:, 6:7], AF.Exp, scale=-0.5)
                em.ts(st[:, 8:9], st[:, 2:3], st[:, 7:8], s2=-1.0, op0=ALU.mult, op1=ALU.mult)
                em.act(xn_t[:, 512:1024], x_t[:], AF.Identity, scale=st[:, 7:8], bias=st[:, 8:9])
                em.dma(x_t[:], IN["x"][trow:trow + 128, 0:512], q="sp")
                em.act(xn_t[:, 0:512], x_t[:], AF.Identity, scale=st[:, 7:8], bias=st[:, 8:9])
                em.dma(xn_s[trow:trow + 128, :], xn_t[:], q="sp")
            for d in range(8):
                em.dmat(xnT[d][:], xn_s[tbase:tbase + 512, d * 128:(d + 1) * 128], q="sp")
            for hh in range(24):
                bank = pp[hh % 2]
                prt = preqk[hh % 2]
                for d in range(8):
                    em.mm(bank[0:64, :], w_sb[d][:, hh * 64:(hh + 1) * 64], xnT[d][:],
                          start=(d == 0), stop=(d == 7), inc=(1 if d == 7 else 0))
                if ch > 0:
                    em.copy(prt[:, 0:3], histqk[hh][:, 0:3], q="pool")
                else:
                    em.memset(prt[:, 0:3], 0.0, q="pool")
                em.copy(prt[:, 3:515], bank[0:64, :])
                em.copy(histqk[hh][:, 0:3], prt[:, 512:515], q="pool")
                acc = accqk[hh % 2]
                em.ts(acc[:, :], prt[:, 0:512], convqk[:, hh * 4:hh * 4 + 1],
                      op0=ALU.mult)
                for j in range(1, 4):
                    em.stt(acc[:, :], prt[:, j:j + 512], convqk[:, hh * 4 + j:hh * 4 + j + 1],
                           acc[:, :], ALU.mult, ALU.add)
                dst = qn[hh] if hh < 12 else kn[hh - 12]
                em.act(dst[:], acc[:, :], AF.Silu)
                sq = sqf2[hh % 2]
                em.act(sq[:], dst[:], AF.Square)
                pb = pp[4 + hh % 2]
                em.mm(pb[0:1, :], onescol[:], sq[:])
                em.act(rowA[hh % 2][:], pb[0:1, :], AF.Identity)
                em.dma(OUT["rsq_s"][hh:hh + 1, :], rowA[hh % 2][:], q="pool")
            for ft in range(12):
                bank = pp[ft % 2]
                prt = prev[ft % 2]
                for d in range(8):
                    em.mm(bank[:], w_sb[d][:, 1536 + ft * 128:1536 + (ft + 1) * 128], xnT[d][:],
                          start=(d == 0), stop=(d == 7), inc=(1 if d == 7 else 0))
                if ch > 0:
                    em.copy(prt[:, 0:3], histv[ft][:, 0:3], q="pool")
                else:
                    em.memset(prt[:, 0:3], 0.0, q="pool")
                em.copy(prt[:, 3:515], bank[:])
                em.copy(histv[ft][:, 0:3], prt[:, 512:515], q="pool")
                acc = accv[ft % 2]
                em.ts(acc[:], prt[:, 0:512], convv_all[:, ft * 4:ft * 4 + 1],
                      op0=ALU.mult)
                for j in range(1, 4):
                    em.stt(acc[:], prt[:, j:j + 512], convv_all[:, ft * 4 + j:ft * 4 + j + 1],
                           acc[:], ALU.mult, ALU.add)
                vt = vf[ft % 2]
                em.act(vt[:], acc[:], AF.Silu)
                em.dma(vf_s[ft * 128:(ft + 1) * 128, tbase:tbase + 512], vt[:], q="sp")
            em.dma(gkb[:], OUT["rsq_s"][0:12, :], q="pool")
            em.dma(beta_f[:], OUT["rsq_s"][12:24, :], q="pool")
            em.act(gkb[:], gkb[:], AF.Ln, bias=1e-6)
            em.act(gkb[:], gkb[:], AF.Exp, scale=-0.5, bias=LN8TH)
            em.act(beta_f[:], beta_f[:], AF.Ln, bias=1e-6)
            em.act(beta_f[:], beta_f[:], AF.Exp, scale=-0.5)
            em.dma(OUT["rsq_s"][0:12, :], gkb[:], q="pool")
            em.dma(OUT["rsq_s"][12:24, :], beta_f[:], q="pool")
            for hh in range(24):
                rw = rowA[hh % 2]
                em.dma(rw[:], OUT["rsq_s"][hh:hh + 1, :], q="sp")
                pb = pp[4 + hh % 2]
                em.mm(pb[0:64, 0:512], onesf[0:1, 0:64], rw[0:1, :])
                dst = qn[hh] if hh < 12 else kn[hh - 12]
                em.tt(dst[:], dst[:], pb[0:64, 0:512], ALU.mult)
                if hh >= 12:
                    em.dma(kn_s[(hh - 12) * 64:(hh - 11) * 64, tbase:tbase + 512], dst[:], q="sp")
            bank = pp[2]
            for d in range(8):
                em.mm(bank[0:44, :], wgkb_sb[:, d * 44:(d + 1) * 44], xnT[d][:],
                      start=(d == 0), stop=(d == 7), inc=(1 if d == 7 else 0))
            em.act(gkb[:], bank[0:12, :], AF.Abs, bias=dtb[:, 0:1])
            em.act(gkb[:], gkb[:], AF.Exp, scale=-1.0)
            em.act(gkb[:], gkb[:], AF.Ln, bias=1.0)
            em.act(beta_f[:], bank[0:12, :], AF.Identity, bias=dtb[:, 0:1])
            em.ts(beta_f[:], beta_f[:], 0.0, op0=ALU.max)
            em.tt(gkb[:], gkb[:], beta_f[:], ALU.add)
            em.ts(gkb[:], gkb[:], negA[:, 0:1], op0=ALU.mult)
            em.act(beta_f[:], bank[32:44, :], AF.Exp, scale=-1.0, bias=negbb[:, 0:1])
            em.act(beta_f[:], beta_f[:], AF.Ln, bias=1.0)
            em.act(beta_f[:], beta_f[:], AF.Exp, scale=-1.0)
            for sc in range(n_sub):
                c0 = sc * 64
                em.scan(gam8[:, c0:c0 + 64], gkb[:, c0:c0 + 64], gkb[:, c0:c0 + 64], 0.0,
                        ALU.add, ALU.bypass)
            em.act(b8[:, 0:n_sub * 64], gam8[:, 0:n_sub * 64], AF.Exp)
            for sc in range(n_sub):
                c0 = sc * 64
                em.dma(OUT["flat_s"][sc:sc + 1, :].rearrange("r (a b) -> (r a) b", a=12),
                       gam8[:, c0:c0 + 64], q="pool")
                em.dma(OUT["flat_s"][8 + sc:9 + sc, :].rearrange("r (a b) -> (r a) b", a=12),
                       beta_f[:, c0:c0 + 64], q="pool")
            for sc in range(n_sub):
                c0 = sc * 64
                em.ts(gkb[:, c0:c0 + 64], gam8[:, c0:c0 + 64], gam8[:, c0 + 63:c0 + 64],
                      op0=ALU.subtract)
                em.act(gkb[:, c0:c0 + 64], gkb[:, c0:c0 + 64], AF.Exp, scale=-1.0)
                em.tt(bb_sc[:], beta_f[:, c0:c0 + 64], b8[:, c0:c0 + 64], ALU.mult)
                for tsrc, tdst in ((beta_f[:, c0:c0 + 64], tok_beta), (bb_sc[:], tok_bb),
                                   (b8[:, c0:c0 + 64], tok_b), (gkb[:, c0:c0 + 64], tok_kds)):
                    em.tr(ps_trf[0:64, 0:12], tsrc, identf[0:12, 0:12])
                    em.copy(tdst[:, sc * 12:sc * 12 + 12], ps_trf[0:64, 0:12])
                em.tr(ps_trf[0:1, 0:12], b8[0:12, c0 + 63:c0 + 64], identf[0:12, 0:12])
                em.copy(bcrow[:], ps_trf[0:1, 0:12])
                em.mm(pp[5][0:64, 0:12], onesf[0:1, 0:64], bcrow[:])
                em.copy(tok_bc[:, sc * 12:sc * 12 + 12], pp[5][0:64, 0:12])
            for sc in range(n_sub):
                c0 = sc * 64
                grp = sc // 2
                if sc % 2 == 0:
                    trow = tbase + grp * 128
                    em.dmat(ktokG[:], kn_s[0:768, trow:trow + 128], q="sp")
                    em.dmat(vtokG[:], vf_s[0:1536, trow:trow + 128], q="sp")
                    em.dma(OUT["ktb_s"][:], ktokG[64:128, :], q="pool")
                    em.dma(OUT["vtb_s"][:], vtokG[64:128, :], q="pool")
                    em.dma(ktokB[:], OUT["ktb_s"][:], q="pool")
                    em.dma(vtokB[:], OUT["vtb_s"][:], q="pool")
                ktok = ktokG if sc % 2 == 0 else ktokB
                vtok = vtokG if sc % 2 == 0 else vtokB
                for fs in range(3):
                    gb = pp[3 + fs]
                    for d in range(8):
                        em.mm(gb[0:64, :], xnT[d][:, c0:c0 + 64],
                              w_sb[d][:, 3072 + fs * 512:3072 + (fs + 1) * 512],
                              start=(d == 0), stop=(d == 7), inc=(1 if d == 7 else 0))
                    em.copy(g_sc[:, fs * 512:(fs + 1) * 512], gb[0:64, :])
                em.dma(flatg[:], OUT["flat_s"][sc:sc + 1, :], q="pool")
                Ot, ovar, gr = OtP[sc % 2], ovarP[sc % 2], grP[sc % 2]
                beta_tok = tok_beta[:, sc * 12:sc * 12 + 12]
                bb_tok = tok_bb[:, sc * 12:sc * 12 + 12]
                b_tok = tok_b[:, sc * 12:sc * 12 + 12]
                kds_tok = tok_kds[:, sc * 12:sc * 12 + 12]
                bc_tok = tok_bc[:, sc * 12:sc * 12 + 12]

                def stream(g):
                    bx, bk, bs = pp[3 * g], pp[3 * g + 1], pp[3 * g + 2]
                    h0 = 6 * g
                    m384 = slice(384 * g, 384 * g + 384)
                    m768 = slice(768 * g, 768 * g + 768)
                    T2 = []

                    def xmms():
                        for hl in range(6):
                            h = h0 + hl
                            em.mm(bx[0:64, hl * 64:hl * 64 + 64], flatg[0:1, 64 * h:64 * h + 64],
                                  negones64[0:1, 0:64], start=(hl == 0), stop=False, inc=0)
                            em.mm(bx[0:64, hl * 64:hl * 64 + 64], onesf[0:1, 0:64],
                                  flatg[0:1, 64 * h:64 * h + 64], start=False,
                                  stop=(hl == 5), inc=(1 if hl == 5 else 0))
                    T2.append(xmms)

                    def e0b():
                        em.tt(h3(bs[0:64, 0:384], 6, 64), h3(bx[0:64, 0:384], 6, 64),
                              rep_mask(c_bigL, 6), ALU.add)
                        em.act(E0[:, m384], bs[0:64, 0:384], AF.Exp, scale=-1.0)
                    T2.append(e0b)

                    def e0ub():
                        em.tt(h3(bs[0:64, 0:384], 6, 64), h3(bx[0:64, 0:384], 6, 64),
                              rep_mask(c_bigUi, 6), ALU.add)
                        em.act(E0U[:, m384], bs[0:64, 0:384], AF.Exp)
                    T2.append(e0ub)

                    def kktm():
                        for hl in range(6):
                            kf = kn[h0 + hl][:, c0:c0 + 64]
                            em.mm(bk[0:64, hl * 64:hl * 64 + 64], kf, kf, inc=(1 if hl == 5 else 0))
                    T2.append(kktm)

                    def abuild():
                        em.tt(Amat[:, m384], E0[:, m384], bk[0:64, 0:384], ALU.mult)
                        em.tt(h3(Amat[:, m384], 6, 64), h3(Amat[:, m384], 6, 64),
                              stretch(beta_tok, h0, 6, 64), ALU.mult)
                        em.tt(h3(Amat[:, m384], 6, 64), h3(Amat[:, m384], 6, 64),
                              rep_mask(c_ms, 6), ALU.mult)
                    T2.append(abuild)

                    def eatb():
                        em.tt(h3(bs[0:64, 0:384], 6, 64), h3(bx[0:64, 0:384], 6, 64),
                              rep_mask(c_bigUs, 6), ALU.add)
                        em.act(E0[:, m384], bs[0:64, 0:384], AF.Exp)
                    T2.append(eatb)

                    def betab():
                        if g == 0:
                            em.dma(flatg[:], OUT["flat_s"][8 + sc:9 + sc, :], q="pool")
                        for hl in range(6):
                            h = h0 + hl
                            em.mm(bx[0:64, hl * 64:hl * 64 + 64], onesf[0:1, 0:64],
                                  flatg[0:1, 64 * h:64 * h + 64], inc=(1 if hl == 5 else 0))
                    T2.append(betab)

                    def atb():
                        em.tt(ATmat[:, m384], E0[:, m384], bk[0:64, 0:384], ALU.mult)
                        em.tt(ATmat[:, m384], ATmat[:, m384], bx[0:64, 0:384], ALU.mult)
                    T2.append(atb)

                    def kqtm():
                        for hl in range(6):
                            h = h0 + hl
                            em.mm(bk[0:64, hl * 64:hl * 64 + 64], kn[h][:, c0:c0 + 64],
                                  qn[h][:, c0:c0 + 64], inc=(1 if hl == 5 else 0))
                    T2.append(kqtm)

                    def wtb():
                        em.tt(WTmat[:, m384], E0U[:, m384], bk[0:64, 0:384], ALU.mult)
                    T2.append(wtb)

                    def rmms():
                        for hl in range(6):
                            h = h0 + hl
                            b_, cs_ = (bx, hl * 128) if hl < 4 else (bk, (hl - 4) * 128)
                            em.mm(b_[0:64, cs_:cs_ + 128], kn[h][:, c0:c0 + 64],
                                  Sbf[:, h * 128:(h + 1) * 128], inc=(1 if hl in (3, 5) else 0))
                    T2.append(rmms)

                    def rbuild():
                        em.tt(h3(y_t[:, m768], 6, 128), h3(vtok[0:64, m768], 6, 128),
                              stretch(beta_tok, h0, 6, 128), ALU.mult)
                        em.tt(h3(y2_t[:, 768 * g:768 * g + 512], 4, 128), h3(bx[0:64, 0:512], 4, 128),
                              stretch(bb_tok, h0, 4, 128), ALU.mult)
                        em.tt(h3(y2_t[:, 768 * g + 512:768 * g + 768], 2, 128), h3(bk[0:64, 0:256], 2, 128),
                              stretch(bb_tok, h0 + 4, 2, 128), ALU.mult)
                        em.tt(y_t[:, m768], y_t[:, m768], y2_t[:, m768], ALU.subtract)
                    T2.append(rbuild)

                    def powcopy():
                        em.copy(powP[0][:, m384], Amat[:, m384], q="pool")
                    T2.append(powcopy)

                    for lv in range(1, 6):
                        def sq(lv=lv):
                            pprev = powP[(lv - 1) % 2]
                            ptprev = ATmat if lv == 1 else powPT[lv - 1]
                            pcur, ptcur = powP[lv % 2], powPT[lv]
                            for hl in range(6):
                                hb = slice((h0 + hl) * 64, (h0 + hl) * 64 + 64)
                                em.mm(bx[0:64, hl * 64:hl * 64 + 64], ptprev[:, hb], pprev[:, hb],
                                      inc=(1 if hl == 5 else 0))
                            for hl in range(6):
                                hb = slice((h0 + hl) * 64, (h0 + hl) * 64 + 64)
                                em.mm(bk[0:64, hl * 64:hl * 64 + 64], pprev[:, hb], ptprev[:, hb],
                                      inc=(1 if hl == 5 else 0))
                            em.act(pcur[:, m384], bx[0:64, 0:384], AF.Identity)
                            em.act(ptcur[:, m384], bk[0:64, 0:384], AF.Identity)
                        T2.append(sq)

                    for lv in range(5, 0, -1):
                        def app(lv=lv):
                            pt = powPT[lv]
                            for hl in range(6):
                                h = h0 + hl
                                b_, cs_ = (bx, hl * 128) if hl < 4 else (bk, (hl - 4) * 128)
                                em.mm(b_[0:64, cs_:cs_ + 128], pt[:, h * 64:h * 64 + 64],
                                      y_t[:, h * 128:(h + 1) * 128], inc=(1 if hl in (3, 5) else 0))
                            em.tt(y_t[:, 768 * g:768 * g + 512], y_t[:, 768 * g:768 * g + 512],
                                  bx[0:64, 0:512], ALU.add)
                            em.tt(y_t[:, 768 * g + 512:768 * g + 768], y_t[:, 768 * g + 512:768 * g + 768],
                                  bk[0:64, 0:256], ALU.add)
                        T2.append(app)

                    def dlb():
                        for hl in range(6):
                            h = h0 + hl
                            b_, cs_ = (bx, hl * 128) if hl < 4 else (bk, (hl - 4) * 128)
                            em.mm(b_[0:64, cs_:cs_ + 128], ATmat[:, h * 64:h * 64 + 64],
                                  y_t[:, h * 128:(h + 1) * 128], inc=(1 if hl in (3, 5) else 0))
                        em.tt(Dl[:, 768 * g:768 * g + 512], y_t[:, 768 * g:768 * g + 512],
                              bx[0:64, 0:512], ALU.subtract)
                        em.tt(Dl[:, 768 * g + 512:768 * g + 768], y_t[:, 768 * g + 512:768 * g + 768],
                              bk[0:64, 0:256], ALU.subtract)
                    T2.append(dlb)

                    def omms():
                        for hl in range(6):
                            h = h0 + hl
                            b_, cs_ = (bx, hl * 128) if hl < 4 else (bk, (hl - 4) * 128)
                            em.mm(b_[0:64, cs_:cs_ + 128], qn[h][:, c0:c0 + 64],
                                  Sbf[:, h * 128:(h + 1) * 128], inc=(1 if hl in (3, 5) else 0))
                        em.tt(h3(Ot[:, 768 * g:768 * g + 512], 4, 128), h3(bx[0:64, 0:512], 4, 128),
                              stretch(b_tok, h0, 4, 128), ALU.mult)
                        em.tt(h3(Ot[:, 768 * g + 512:768 * g + 768], 2, 128), h3(bk[0:64, 0:256], 2, 128),
                              stretch(b_tok, h0 + 4, 2, 128), ALU.mult)
                    T2.append(omms)

                    def wdl():
                        for hl in range(6):
                            h = h0 + hl
                            b_, cs_ = (bx, hl * 128) if hl < 4 else (bk, (hl - 4) * 128)
                            em.mm(b_[0:64, cs_:cs_ + 128], WTmat[:, h * 64:h * 64 + 64],
                                  Dl[:, h * 128:(h + 1) * 128], inc=(1 if hl in (3, 5) else 0))
                        em.tt(Ot[:, 768 * g:768 * g + 512], Ot[:, 768 * g:768 * g + 512],
                              bx[0:64, 0:512], ALU.add)
                        em.tt(Ot[:, 768 * g + 512:768 * g + 768], Ot[:, 768 * g + 512:768 * g + 768],
                              bk[0:64, 0:256], ALU.add)
                    T2.append(wdl)

                    def supd():
                        em.tt(h3(y_t[:, 384 * g:384 * g + 384], 6, 64), h3(ktok[0:64, m384], 6, 64),
                              stretch(kds_tok, h0, 6, 64), ALU.mult)
                        for hl in range(6):
                            h = h0 + hl
                            b_, cs_ = (bx, hl * 128) if hl < 4 else (bk, (hl - 4) * 128)
                            em.mm(b_[0:64, cs_:cs_ + 128], y_t[:, 384 * g + hl * 64:384 * g + hl * 64 + 64],
                                  Dl[:, h * 128:(h + 1) * 128], inc=(1 if hl in (3, 5) else 0))
                        em.tt(h3(S[:, m768], 6, 128), h3(S[:, m768], 6, 128),
                              stretch(bc_tok, h0, 6, 128), ALU.mult)
                        em.tt(S[:, 768 * g:768 * g + 512], S[:, 768 * g:768 * g + 512],
                              bx[0:64, 0:512], ALU.add)
                        em.tt(S[:, 768 * g + 512:768 * g + 768], S[:, 768 * g + 512:768 * g + 768],
                              bk[0:64, 0:256], ALU.add)
                        em.copy(Sbf[:, m768], S[:, m768])
                    T2.append(supd)

                    def onorm():
                        em.act(gr[:], Ot[:, m768], AF.Square)
                        for hl in range(6):
                            h = h0 + hl
                            em.red(ovar[:, h:h + 1], gr[:, hl * 128:(hl + 1) * 128], ALU.add)
                        em.ts(ovar[:, h0:h0 + 6], ovar[:, h0:h0 + 6], 1.0 / 128, op0=ALU.mult)
                        em.act(ovar[:, h0:h0 + 6], ovar[:, h0:h0 + 6], AF.Ln, bias=1e-5)
                        em.act(ovar[:, h0:h0 + 6], ovar[:, h0:h0 + 6], AF.Exp, scale=-0.5)
                        em.tt(h3(Ot[:, m768], 6, 128), h3(Ot[:, m768], 6, 128),
                              ovar[:, h0:h0 + 6].unsqueeze(2).broadcast_to((64, 6, 128)), ALU.mult)
                        em.tt(Ot[:, m768], Ot[:, m768], gnwmat[:, m768], ALU.mult)
                    T2.append(onorm)

                    def gate():
                        em.tt(Ot[:, m768], Ot[:, m768], g_sc[:, m768], ALU.mult)
                        em.act(gr[:], g_sc[:, m768], AF.Exp, scale=-1.0)
                        em.act(gr[:], gr[:], AF.Ln, bias=1.0)
                        em.act(gr[:], gr[:], AF.Exp, scale=-1.0)
                        em.tt(Ot[:, m768], Ot[:, m768], gr[:], ALU.mult)
                        em.dma(og_s[tbase + c0:tbase + c0 + 64, 768 * g:768 * g + 768],
                               Ot[:, m768], q="sp")
                    T2.append(gate)
                    return T2

                s1, s2 = stream(0), stream(1)
                for a, b in zip(s1, s2):
                    a()
                    b()
                if ch == 0 and sc == 0 and "d_gam" in OUT:
                    em.dma(OUT["d_gam"], gam8[:, 0:64], q="sp")
                    em.dma(OUT["d_Amat"], Amat[:], q="sp")
                    em.dma(OUT["d_ATmat"], ATmat[:], q="sp")
                    em.dma(OUT["d_WTmat"], WTmat[:], q="sp")
                    em.dma(OUT["d_Dl"], Dl[:], q="sp")
                    em.dma(OUT["d_Ot"], Ot[:], q="sp")
                    em.dma(OUT["d_S"], S[:], q="sp")
    with emscope(em):
        ogT = [em.sb(f"la_ogT{v}", [128, 512], BF16, tdest=True) for v in range(12)]
        a0t = em.sb("la_a0t", [128, 1024], BF16)
        wo_sb = [em.sb(f"la_wo{v}", [128, 1024], BF16) for v in range(12)]
        for v in range(12):
            em.dma(wo_sb[v][:], IN["wo_bf"][v * 128:(v + 1) * 128, :], q="sp")
        for gg in range(n_chunks):
            t0 = gg * 512
            for v in range(12):
                em.dmat(ogT[v][:], og_s[t0:t0 + 512, v * 128:(v + 1) * 128], q="sp")
            for tt in range(4):
                for oc in range(2):
                    bank = pp[oc]
                    for v in range(12):
                        em.mm(bank[:], ogT[v][:, tt * 128:(tt + 1) * 128],
                              wo_sb[v][:, oc * 512:(oc + 1) * 512],
                              start=(v == 0), stop=(v == 11), inc=(1 if v == 11 else 0))
                    em.copy(a0t[:, oc * 512:(oc + 1) * 512], bank[:])
                em.dma(a0_s[t0 + tt * 128:t0 + tt * 128 + 128, :], a0t[:], q="sp")


L0A_INS = dict(
    x=([T, D], F32),
    wq_bf=([1024, 768], BF16), wk_bf=([1024, 768], BF16), wv_bf=([1024, 1536], BF16),
    wg_bf=([1024, 1536], BF16), wo_bf=([1536, 1024], BF16), wgkb_bf=([1024, 44], BF16),
    L0_attn_conv_q=([768, 4], F32), L0_attn_conv_k=([768, 4], F32),
    L0_attn_conv_v=([1536, 4], F32),
    L0_attn_A_log=([12], F32), L0_attn_dt_bias=([12], F32), L0_attn_b_b=([12], F32),
    L0_attn_g_norm_weight=([128], F32),
    c_ident=([128, 128], BF16), c_ms=([CS, CS], BF16), c_bigL=([CS, CS], F32),
    c_bigUi=([CS, CS], F32), c_bigUs=([CS, CS], F32),
)
L0A_OUTS = dict(xn_s=([T, D], BF16), a0_s=([T, D], BF16),
                kn_s=([768, T], BF16), vf_s=([1536, T], BF16), og_s=([T, 1536], BF16),
                flat_s=([16, 768], F32), rsq_s=([24, 512], F32),
                ktb_s=([64, 768], BF16), vtb_s=([64, 1536], BF16))
L0A_DBG = dict(d_gam=([12, 64], F32),
               d_Amat=([64, 768], BF16), d_ATmat=([64, 768], BF16), d_WTmat=([64, 768], BF16),
               d_Dl=([64, 1536], BF16), d_Ot=([64, 1536], BF16), d_S=([64, 1536], F32))



def emit_l1(em, IN, OUT, n_chunks=8):
    nc = em.nc
    ident = em.sb("l1_ident", [128, 128], BF16)
    em.dma(ident[:], IN["c_ident"], q="sp")
    onesb = em.sb("l1_onesb", [1, 128], BF16)
    em.dma(onesb[:], IN["c_ones"], q="sp")
    onescol = em.sb("l1_onescol", [128, 1], BF16)
    em.memset(onescol[:], 1.0)
    pp = [em.ps(f"l1_pp{i}", [128, 512]) for i in range(6)]
    ps_tr = em.ps("l1_pstr", [128, 128], BF16)

    x_t = em.sb("l1_xt", [128, 1024], BF16)
    yn_t = em.sb("l1_ynt", [128, 1024], BF16)
    st = em.sb("l1_st", [128, 32], F32)
    sqf = em.sb("l1_sqf", [64, 512], F32)
    fA = em.sb("l1_fA", [128, 1024], F32)
    fB = em.sb("l1_fB", [128, 1024], F32)
    fC = em.sb("l1_fC", [128, 1024], F32)
    fD = em.sb("l1_fD", [128, 1024], F32)
    iA = em.sb("l1_iA", [128, 1024], I32)
    ynT = [em.sb(f"l1_ynT{d}", [128, 512], BF16, tdest=True) for d in range(8)]
    yn_s, a1_s = OUT["yn_s"], OUT["a1_s"]
    sc_s, xq_s = OUT["sc_s"], OUT["xq_s"]

    for tt in range(32):
        trow = tt * 128
        em.dma(x_t[:], IN["x1_s"][trow:trow + 128, :], q="sp")
        em.red(st[:, 0:1], x_t[:], ALU.add)
        em.act(sqf[0:64, :], x_t[0:64, 0:512], AF.Square, accum_out=st[0:64, 1:2])
        em.act(sqf[0:64, :], x_t[64:128, 0:512], AF.Square, accum_out=st[64:128, 1:2])
        em.act(sqf[0:64, :], x_t[0:64, 512:1024], AF.Square, accum_out=st[0:64, 2:3])
        em.act(sqf[0:64, :], x_t[64:128, 512:1024], AF.Square, accum_out=st[64:128, 2:3])
        em.tt(st[:, 1:2], st[:, 1:2], st[:, 2:3], ALU.add)
        em.ts(st[:, 3:4], st[:, 0:1], 1.0 / 1024, op0=ALU.mult)
        em.ts(st[:, 4:5], st[:, 1:2], 1.0 / 1024, op0=ALU.mult)
        em.ts(st[:, 5:6], st[:, 3:4], st[:, 3:4], op0=ALU.mult)
        em.tt(st[:, 6:7], st[:, 4:5], st[:, 5:6], ALU.subtract)
        em.act(st[:, 7:8], st[:, 6:7], AF.Ln, bias=EPS_LN)
        em.act(st[:, 8:9], st[:, 7:8], AF.Exp, scale=-0.5)
        em.ts(st[:, 9:10], st[:, 3:4], st[:, 8:9], s2=-1.0, op0=ALU.mult, op1=ALU.mult)
        em.act(yn_t[:], x_t[:], AF.Identity, scale=st[:, 8:9], bias=st[:, 9:10])
        em.dma(yn_s[trow:trow + 128, :], yn_t[:], q="sp")

    with emscope(em):
        wk_sb = [em.sb(f"l1_wk{d}", [128, 1024], BF16) for d in range(8)]
        wv_sb = [em.sb(f"l1_wv{d}", [128, 1024], BF16) for d in range(8)]
        for d in range(8):
            em.dma(wk_sb[d][:], IN["wk1_bf"][d * 128:(d + 1) * 128, :], q="sp")
            em.dma(wv_sb[d][:], IN["wv1_bf"][d * 128:(d + 1) * 128, :], q="sp")
        bkrow = em.sb("l1_bkrow", [1, 1024], BF16)
        em.dma(bkrow[:], IN["L1_attn_bk"].rearrange("(r c) -> r c", r=1), q="pool")
        bvrow = em.sb("l1_bvrow", [1, 1024], BF16)
        em.dma(bvrow[:], IN["L1_attn_bv"].rearrange("(r c) -> r c", r=1), q="pool")
        k_t = em.sb("l1_kt", [128, 1024], BF16)
        v_t = em.sb("l1_vt", [128, 1024], BF16)
        for tt in range(32):
            trow = tt * 128
            for d in range(8):
                em.dmat(ynT[d][:, 0:128], yn_s[trow:trow + 128, d * 128:(d + 1) * 128], q="sp")
            for half, (wsb, brow, dst, do_elu) in enumerate(
                    [(wk_sb, bkrow, k_t, True), (wv_sb, bvrow, v_t, False)]):
                for oc in range(2):
                    bank = pp[4 + oc]
                    for d in range(8):
                        em.mm(bank[:], ynT[d][:, 0:128], wsb[d][:, oc * 512:(oc + 1) * 512],
                              start=(d == 0), stop=False, inc=0)
                    em.mm(bank[:], onesb[0:1, 0:128], brow[0:1, oc * 512:(oc + 1) * 512],
                          start=False, stop=True)
                    sl_ = slice(oc * 512, (oc + 1) * 512)
                    if do_elu:
                        em.ts(fA[:, 0:512], bank[:], 0.0, op0=ALU.min)
                        em.act(fB[:, 0:512], fA[:, 0:512], AF.Exp)
                        em.stt(dst[:, sl_], bank[:], 0.0, fB[:, 0:512], ALU.max, ALU.add)
                    else:
                        em.copy(dst[:, sl_], bank[:])
            for h in range(16):
                b_ = pp[0] if h < 8 else pp[1]
                cs_ = (h % 8) * 64
                em.mm(b_[0:64, cs_:cs_ + 64], k_t[:, h * 64:(h + 1) * 64],
                      v_t[:, h * 64:(h + 1) * 64], start=(tt == 0), stop=(tt == 31),
                      inc=(1 if (tt == 31 and h in (7, 15)) else 0))
            for oc in range(2):
                em.mm(pp[2 + 0][0:1, 0:512] if oc == 0 else pp[3][0:1, 0:512],
                      onescol[:, 0:1], k_t[:, oc * 512:(oc + 1) * 512],
                      start=(tt == 0), stop=(tt == 31),
                      inc=(1 if tt == 31 else 0))
        kv_sb = em.sb("l1_kvsb", [64, 1024], BF16)
        em.copy(kv_sb[:, 0:512], pp[0][0:64, :])
        em.copy(kv_sb[:, 512:1024], pp[1][0:64, :])
        ks_row = em.sb("l1_ksrow", [1, 1024], BF16)
        em.copy(ks_row[0:1, 0:512], pp[2][0:1, 0:512])
        em.copy(ks_row[0:1, 512:1024], pp[3][0:1, 0:512])
        ks_col = em.sb("l1_kscol", [64, 16], BF16)
        for h in range(16):
            em.tr(ps_tr[0:64, 0:1], ks_row[0:1, h * 64:(h + 1) * 64], ident[0:1, 0:1])
            em.copy(ks_col[:, h:h + 1], ps_tr[0:64, 0:1])

        wq_sb = [em.sb(f"l1_wq{d}", [128, 1024], BF16) for d in range(8)]
        for d in range(8):
            em.dma(wq_sb[d][:], IN["wq1_bf"][d * 128:(d + 1) * 128, :], q="sp")
        bq_col = em.sb("l1_bqcol", [64, 16], F32)
        for hh in range(16):
            em.dma(bq_col[:, hh:hh + 1],
                   IN["L1_attn_bq"][hh * 64:(hh + 1) * 64].rearrange("(p c) -> p c", c=1), q="sp")
        qh = [em.sb(f"l1_qh{h}", [64, 512], BF16) for h in range(16)]
        zt = em.sb("l1_zt", [128, 16], F32)
        sc_t = em.sb("l1_sct", [128, 1024], BF16)
        for chk in range(n_chunks):
            tbase = chk * 512
            for d in range(8):
                em.dmat(ynT[d][:], yn_s[tbase:tbase + 512, d * 128:(d + 1) * 128], q="sp")
            for h in range(16):
                bank = pp[4 + h % 2]
                for d in range(8):
                    em.mm(bank[0:64, :], wq_sb[d][:, h * 64:(h + 1) * 64], ynT[d][:],
                          start=(d == 0), stop=(d == 7), inc=(1 if d == 7 else 0))
                em.act(fA[0:64, 0:512], bank[0:64, :], AF.Identity,
                       bias=bq_col[:, h:h + 1])
                em.ts(fB[0:64, 0:512], fA[0:64, 0:512], 0.0, op0=ALU.min)
                em.act(fC[0:64, 0:512], fB[0:64, 0:512], AF.Exp)
                em.stt(qh[h][:], fA[0:64, 0:512], 0.0, fC[0:64, 0:512], ALU.max, ALU.add)
            for ts4 in range(4):
                t0 = ts4 * 128
                for h in range(16):
                    em.mm(pp[2][:, h:h + 1], qh[h][:, t0:t0 + 128], ks_col[:, h:h + 1],
                          inc=(1 if h == 15 else 0))
                em.ts(zt[:], pp[2][:, 0:16], 1e-6, op0=ALU.add)
                em.recip(zt[:], zt[:])
                for h in range(16):
                    b_ = pp[0] if h < 8 else pp[1]
                    cs_ = (h % 8) * 64
                    em.mm(b_[:, cs_:cs_ + 64], qh[h][:, t0:t0 + 128],
                          kv_sb[:, h * 64:(h + 1) * 64], inc=(1 if h in (7, 15) else 0))
                for bi_, b_ in enumerate((pp[0], pp[1])):
                    em.tt(sc_t[:, bi_ * 512:(bi_ + 1) * 512].rearrange("p (h s) -> p h s", s=64),
                          b_[:, 0:512].rearrange("p (h s) -> p h s", s=64),
                          zt[:, bi_ * 8:(bi_ + 1) * 8].unsqueeze(2).broadcast_to((128, 8, 64)),
                          ALU.mult)
                em.dma(sc_s[tbase + t0:tbase + t0 + 128, :], sc_t[:], q="sp")

    with emscope(em):
        wo_sb = [em.sb(f"l1_wo{d}", [128, 1024], BF16) for d in range(8)]
        for d in range(8):
            em.dma(wo_sb[d][:], IN["wo1_bf"][d * 128:(d + 1) * 128, :], q="sp")
        borow = em.sb("l1_borow", [1, 1024], BF16)
        em.dma(borow[:], IN["L1_attn_bo"].rearrange("(r c) -> r c", r=1), q="pool")
        scT = [em.sb(f"l1_scT{d}", [128, 512], BF16, tdest=True) for d in range(8)]
        a1_t = em.sb("l1_a1t", [128, 1024], BF16)
        for chk in range(n_chunks):
            tbase = chk * 512
            for d in range(8):
                em.dmat(scT[d][:], sc_s[tbase:tbase + 512, d * 128:(d + 1) * 128], q="sp")
            for tt in range(4):
                for oc in range(2):
                    bank = pp[4 + oc]
                    for d in range(8):
                        em.mm(bank[:], scT[d][:, tt * 128:(tt + 1) * 128],
                              wo_sb[d][:, oc * 512:(oc + 1) * 512],
                              start=(d == 0), stop=False, inc=0)
                    em.mm(bank[:], onesb[0:1, 0:128], borow[0:1, oc * 512:(oc + 1) * 512],
                          start=False, stop=True)
                    em.copy(a1_t[:, oc * 512:(oc + 1) * 512], bank[:])
                em.dma(a1_s[tbase + tt * 128:tbase + tt * 128 + 128, :], a1_t[:], q="sp")

    with emscope(em):
        h_sb = [em.sb(f"l1_h{d}", [128, 1024], BF16) for d in range(8)]
        for d in range(8):
            em.dma(h_sb[d][:], IN["c_had"][d * 128:(d + 1) * 128, :], q="sp")
        xq_t = em.sb("l1_xqt", [128, 1024], BF16)
        for chk in range(n_chunks):
            tbase = chk * 512
            for d in range(8):
                em.dmat(ynT[d][:], yn_s[tbase:tbase + 512, d * 128:(d + 1) * 128], q="sp")
            for tt in range(4):
                for oc in range(2):
                    bank = pp[4 + oc]
                    for d in range(8):
                        em.mm(bank[:], ynT[d][:, tt * 128:(tt + 1) * 128],
                              h_sb[d][:, oc * 512:(oc + 1) * 512],
                              start=(d == 0), stop=(d == 7), inc=(1 if d == 7 else 0))
                    em.copy(fA[:, oc * 512:(oc + 1) * 512], bank[:])
                em.red(st[:, 0:1], fA[:], ALU.max, absval=True)
                em.ts(st[:, 1:2], st[:, 0:1], 1e-5, op0=ALU.max)
                em.ts(st[:, 2:3], st[:, 1:2], 1.0 / 7.0, op0=ALU.mult)
                em.recip(st[:, 3:4], st[:, 2:3])
                em.ts(fB[:], fA[:], st[:, 3:4], op0=ALU.mult)
                em.copy(iA[:], fB[:])
                em.copy(fC[:], iA[:])
                e = iA[:].bitcast(F32)
                em.tt(e, fB[:], fC[:], ALU.subtract)
                em.ts(fD[:], e, 0.5, op0=ALU.is_ge)
                em.stt(fD[:], e, -0.5, fD[:], ALU.is_lt, ALU.subtract)
                em.tt(fC[:], fC[:], fD[:], ALU.subtract)
                em.ts(xq_t[:], fC[:], 7.0, -8.0, op0=ALU.min, op1=ALU.max)
                trow = tbase + tt * 128
                em.dma(xq_s[trow:trow + 128, :], xq_t[:], q="sp")
                em.dma(OUT["s1q_s"][trow:trow + 128, :], st[:, 2:3], q="sp")

    with emscope(em):
        wf_sb = [em.sb(f"l1_wf{d}", [128, 1024], BF16, tdest=True) for d in range(8)]
        for d in range(8):
            em.dmat(wf_sb[d][:], IN["wf1q_rows"][:, d * 128:(d + 1) * 128], q="sp")
        xqT = [em.sb(f"l1_xqT{d}", [128, 512], BF16, tdest=True) for d in range(8)]
        x1_t = em.sb("l1_x1t", [128, 1024], BF16)
        a1r_t = em.sb("l1_a1r", [128, 1024], BF16)
        sq_c = em.sb("l1_sqc", [128, 1], F32)
        out_t = em.sb("l1_outt", [128, 1024], F32)
        for chk in range(n_chunks):
            tbase = chk * 512
            for d in range(8):
                em.dmat(xqT[d][:], xq_s[tbase:tbase + 512, d * 128:(d + 1) * 128], q="sp")
            for tt in range(4):
                trow = tbase + tt * 128
                em.dma(x1_t[:], IN["x1_s"][trow:trow + 128, :], q="sp")
                em.dma(a1r_t[:], a1_s[trow:trow + 128, :], q="sp")
                em.dma(sq_c[:], OUT["s1q_s"][trow:trow + 128, :], q="sp")
                for oc in range(2):
                    bank = pp[4 + oc]
                    for d in range(8):
                        em.mm(bank[:], xqT[d][:, tt * 128:(tt + 1) * 128],
                              wf_sb[d][:, oc * 512:(oc + 1) * 512],
                              start=(d == 0), stop=(d == 7), inc=(1 if d == 7 else 0))
                    sl_ = slice(oc * 512, (oc + 1) * 512)
                    em.stt(fA[:, 0:512], bank[:], sq_c[:, 0:1], x1_t[:, sl_], ALU.mult, ALU.add)
                    em.tt(out_t[:, sl_], fA[:, 0:512], a1r_t[:, sl_], ALU.add)
                em.dma(OUT["out"][trow:trow + 128, :], out_t[:], q="sp")


L1_INS = dict(
    x1_s=([T, D], BF16),
    wq1_bf=([1024, 1024], BF16), wk1_bf=([1024, 1024], BF16),
    wv1_bf=([1024, 1024], BF16), wo1_bf=([1024, 1024], BF16),
    L1_attn_bq=([1024], F32), L1_attn_bk=([1024], F32),
    L1_attn_bv=([1024], F32), L1_attn_bo=([1024], F32),
    wf1q_rows=([1024, 1024], BF16),
    c_ident=([128, 128], BF16), c_ones=([1, 128], BF16), c_had=([1024, 1024], BF16),
)
L1_OUTS = dict(out=([T, D], F32), yn_s=([T, D], BF16), a1_s=([T, D], BF16),
               sc_s=([T, D], BF16), xq_s=([T, D], BF16), s1q_s=([T, 1], F32))



_CACHE = {}


def _phase(name, emit_fn, ins_spec, outs_spec):
    if name not in _CACHE:
        _CACHE[name] = build_phase(emit_fn, ins_spec, outs_spec)
    return _CACHE[name]


def kernel_multilaunch(**inputs):
    inputs = {k: np.ascontiguousarray(v) for k, v in inputs.items()}
    consts = host_consts()
    NCORE = 8
    cores = list(range(NCORE))

    prep_nc = _phase("prep", emit_prep, {n: (INPUT_SHAPES[n], F32) for n in PREP_INS}, PREP_OUTS)
    prep_in = {n: inputs[n].astype(np.float32) for n in PREP_INS}
    prep_res = run_bass_kernel_spmd(prep_nc, [prep_in] * NCORE, core_ids=cores)
    pw = prep_res.results

    l0a_nc = _phase("l0a", emit_l0a, L0A_INS, L0A_OUTS)
    l0a_maps = []
    for c in cores:
        m = dict(x=inputs["L0_attn_hidden_states"][c].astype(np.float32),
                 wq_bf=pw[c]["wq_bf"], wk_bf=pw[c]["wk_bf"], wv_bf=pw[c]["wv_bf"],
                 wg_bf=pw[c]["wg_bf"], wo_bf=pw[c]["wo_bf"], wgkb_bf=pw[c]["wgkb_bf"],
                 L0_attn_conv_q=inputs["L0_attn_conv_q"], L0_attn_conv_k=inputs["L0_attn_conv_k"],
                 L0_attn_conv_v=inputs["L0_attn_conv_v"],
                 L0_attn_A_log=inputs["L0_attn_A_log"], L0_attn_dt_bias=inputs["L0_attn_dt_bias"],
                 L0_attn_b_b=inputs["L0_attn_b_b"],
                 L0_attn_g_norm_weight=inputs["L0_attn_g_norm_weight"],
                 c_ident=consts["c_ident"], c_ms=consts["c_ms"], c_bigL=consts["c_bigL"],
                 c_bigUi=consts["c_bigUi"], c_bigUs=consts["c_bigUs"])
        l0a_maps.append(m)
    l0a_res = run_bass_kernel_spmd(l0a_nc, l0a_maps, core_ids=cores).results

    l0f_nc = _phase("l0f", emit_l0f, L0F_INS, L0F_OUTS)
    l0f_maps = []
    for c in cores:
        m = dict(x=inputs["L0_attn_hidden_states"][c].astype(np.float32),
                 xn_s=l0a_res[c]["xn_s"], a0_s=l0a_res[c]["a0_s"],
                 w1q_rows=pw[c]["w1q_rows"], w2q_rows=pw[c]["w2q_rows"],
                 L0_ffn_ln1_w=inputs["L0_ffn_ln1_w"], L0_ffn_ln1_b=inputs["L0_ffn_ln1_b"],
                 L0_ffn_ln2_w=inputs["L0_ffn_ln2_w"], L0_ffn_ln2_b=inputs["L0_ffn_ln2_b"],
                 L0_ffn_b1=inputs["L0_ffn_b1"], L0_ffn_b2=inputs["L0_ffn_b2"],
                 c_ones=consts["c_ones"], c_ident=consts["c_ident"])
        l0f_maps.append(m)
    l0f_res = run_bass_kernel_spmd(l0f_nc, l0f_maps, core_ids=cores).results

    l1_nc = _phase("l1", emit_l1, L1_INS, L1_OUTS)
    l1_maps = []
    for c in cores:
        m = dict(x1_s=l0f_res[c]["x1_s"],
                 wq1_bf=pw[c]["wq1_bf"], wk1_bf=pw[c]["wk1_bf"],
                 wv1_bf=pw[c]["wv1_bf"], wo1_bf=pw[c]["wo1_bf"],
                 L1_attn_bq=inputs["L1_attn_bq"], L1_attn_bk=inputs["L1_attn_bk"],
                 L1_attn_bv=inputs["L1_attn_bv"], L1_attn_bo=inputs["L1_attn_bo"],
                 wf1q_rows=pw[c]["wf1q_rows"],
                 c_ident=consts["c_ident"], c_ones=consts["c_ones"], c_had=consts["c_had"])
        l1_maps.append(m)
    l1_res = run_bass_kernel_spmd(l1_nc, l1_maps, core_ids=cores).results

    out = np.stack([l1_res[c]["out"] for c in cores], axis=0)
    return out



def build_fused():
    nc = bass.Bass("TRN2", target_bir_lowering=False)
    ext_in = {}
    for n in PREP_INS:
        ext_in[n] = nc.dram_tensor(n, INPUT_SHAPES[n], F32, kind="ExternalInput").ap()
    for n, sh in [("x", [T, D]), ("L0_attn_A_log", [12]), ("L0_attn_dt_bias", [12]),
                  ("L0_attn_b_b", [12]), ("L0_attn_g_norm_weight", [128]),
                  ("L0_ffn_ln1_w", [1024]), ("L0_ffn_ln1_b", [1024]),
                  ("L0_ffn_ln2_w", [4096]), ("L0_ffn_ln2_b", [4096]),
                  ("L0_ffn_b1", [4096]), ("L0_ffn_b2", [1024]),
                  ("L1_attn_bq", [1024]), ("L1_attn_bk", [1024]),
                  ("L1_attn_bv", [1024]), ("L1_attn_bo", [1024])]:
        if n not in ext_in:
            ext_in[n] = nc.dram_tensor(n, sh, F32, kind="ExternalInput").ap()
    for n, (sh, dt) in CONST_SPECS.items():
        ext_in[n] = nc.dram_tensor(n, list(sh), dt, kind="ExternalInput").ap()
    out = nc.dram_tensor("out", [T, D], F32, kind="ExternalOutput").ap()

    MID = {}
    for spec in (PREP_OUTS, L0A_OUTS, L0F_OUTS, L1_OUTS):
        for n, (sh, dt) in spec.items():
            if n == "out":
                continue
            if n not in MID:
                MID[n] = nc.dram_tensor(n, list(sh), dt, kind="Internal").ap()
    MID["out"] = out

    keys = list(Em.QUEUES) + [f"{q}{l}" for q in ("sp", "pool") for l in range(Em.NLANES)]
    sem_ctx = [nc.semaphore(f"q_{k}") for k in keys]
    sems = {}
    import contextlib
    with contextlib.ExitStack() as stack:
        for k, ctx in zip(keys, sem_ctx):
            sems[k] = stack.enter_context(ctx)
        with nc.Block() as block:
            @block.vector
            def _(vector):
                em = Em(nc, sems)
                with contextlib.ExitStack() as tstack:
                    em.stack = tstack
                    em.prealloc_consts()
                    allin = {**ext_in, **MID}
                    with emscope(em):
                        emit_prep(em, ext_in, MID)
                    with emscope(em):
                        emit_l0a(em, allin, MID)
                    with emscope(em):
                        emit_l0f(em, allin, MID)
                    with emscope(em):
                        emit_l1(em, allin, MID)
                    em.barrier_store_done()
    return nc


def kernel(**inputs):
    inputs = {k: np.ascontiguousarray(v) for k, v in inputs.items()}
    consts = host_consts()
    NCORE = 8
    if "fused" not in _CACHE:
        _CACHE["fused"] = build_fused()
    nc = _CACHE["fused"]
    maps = []
    for c in range(NCORE):
        m = {n: inputs[n].astype(np.float32) for n in PREP_INS}
        m["x"] = inputs["L0_attn_hidden_states"][c].astype(np.float32)
        for n in ["L0_attn_A_log", "L0_attn_dt_bias", "L0_attn_b_b", "L0_attn_g_norm_weight",
                  "L0_ffn_ln1_w", "L0_ffn_ln1_b", "L0_ffn_ln2_w", "L0_ffn_ln2_b",
                  "L0_ffn_b1", "L0_ffn_b2", "L1_attn_bq", "L1_attn_bk",
                  "L1_attn_bv", "L1_attn_bo"]:
            m[n] = inputs[n].astype(np.float32)
        for n in CONST_SPECS:
            m[n] = consts[n]
        maps.append(m)
    res = run_bass_kernel_spmd(nc, maps, core_ids=list(range(NCORE)))
    return np.stack([res.results[c]["out"] for c in range(NCORE)], axis=0)
